# version 17
# baseline (speedup 1.0000x reference)
import numpy as np
import ml_dtypes
from contextlib import ExitStack

import concourse.bass as bass
import concourse.mybir as mybir
from concourse.bass_utils import run_bass_kernel_spmd
from concourse.alu_op_type import AluOpType as Op

F32 = mybir.dt.float32
BF16 = mybir.dt.bfloat16
FP16 = mybir.dt.float16
I32 = mybir.dt.int32
U8 = mybir.dt.uint8
AF = mybir.ActivationFunctionType
AX = mybir.AxisListType
BIG = float(2 ** 23)
LN2 = float(np.log(2.0))

NCORES = 8
B = 2
T = 8192
D = 2048
TOK = B * T
TSH = TOK // NCORES
bf16 = ml_dtypes.bfloat16


def cdiv(a, b):
    return (a + b - 1) // b


class Seq:

    ENGS = ("sync", "vector", "scalar", "gpsimd", "tensor")

    def __init__(self, nc, es, name="seq"):
        self.nc = nc
        self.es = es
        self.name = name
        self.hs = {}
        for e in self.ENGS:
            self._lane(e)
        self.steps = []
        self.info = []
        self.cum = {}
        self.cur_label = None
        self.last_id = None
        self.last_per_lane = {}
        self._bar = []

    def _lane(self, lane):
        if lane not in self.hs:
            self.hs[lane] = self.es.enter_context(self.nc.semaphore(self.name + "_" + lane))
        return self.hs[lane]

    def label(self, name):
        self.cur_label = name

    def step(self, eng, inc, fn, deps=None, async_=False, lane=None):
        lane = lane or eng
        self._lane(lane)
        if deps is None:
            deps = [self.last_id] if self.last_id is not None else list(self._bar)
        sid = len(self.steps)
        self.cum[lane] = self.cum.get(lane, 0) + inc
        self.steps.append((eng, inc, fn, self.cur_label, tuple(d for d in deps if d is not None), lane))
        self.info.append((lane, self.cum[lane], async_))
        self.last_id = sid
        self.last_per_lane[lane] = sid
        return sid

    def barrier(self):
        self._bar = list(self.last_per_lane.values())
        self.last_id = None

    def dma(self, eng, dst, src, deps=None, lane=None):
        def fn(e, h, tgt):
            e.dma_start(out=dst, in_=src).then_inc(h, 16)
        return self.step(eng, 16, fn, deps=deps, async_=True, lane=lane)

    def dma_t(self, eng, dst, src, deps=None, lane=None):
        def fn(e, h, tgt):
            e.dma_start_transpose(dst, src).then_inc(h, 16)
        return self.step(eng, 16, fn, deps=deps, async_=True, lane=lane)

    def dmas(self, eng, pairs, deps=None, lane=None):
        def fn(e, h, tgt):
            for dst, src in pairs:
                e.dma_start(out=dst, in_=src).then_inc(h, 16)
        return self.step(eng, 16 * len(pairs), fn, deps=deps, async_=True, lane=lane)

    def comp(self, eng, fn, deps=None):
        def wrap(e, h, tgt):
            inst = fn(e)
            d = e.drain()
            (d if d is not None else inst).then_inc(h, 1)
        return self.step(eng, 1, wrap, deps=deps, async_=False)

    def finish(self):
        tot = dict(self.cum)

        def fn(e, h, tgt):
            for ln, t in tot.items():
                if t > 0:
                    e.wait_ge(self.hs[ln], t)
        self.step("scalar", 0, fn, deps=[])

    def emit(self, eng_name, eng):
        waited = {}
        acc = 0
        open_lab = None
        ctx = None
        for e, inc, fn, lab, deps, lane in self.steps:
            if e == eng_name:
                if lab != open_lab:
                    if ctx is not None:
                        ctx.__exit__(None, None, None)
                    ctx = self.nc.named_scope(lab) if lab else None
                    if ctx is not None:
                        ctx.__enter__()
                    open_lab = lab
                for d in deps:
                    dl, dc, dasync = self.info[d]
                    if dl == eng_name and not dasync:
                        continue
                    if dc > waited.get(dl, 0):
                        eng.wait_ge(self.hs[dl], dc)
                        waited[dl] = dc
                fn(eng, self.hs[lane], acc + inc)
                acc += inc
        if ctx is not None:
            ctx.__exit__(None, None, None)


P0_SPECS = [
    ("l0wgu", 16384, 2048, "tern_ste", True),
    ("l0wpr", 2048, 8192, "tern_ste", True),
    ("l1wfc", 8192, 2048, "tern", True),
    ("l1wpr", 2048, 8192, "tern", True),
    ("l2wfc", 5120, 2048, "tern_g", True),
    ("l2wpr", 2048, 5120, "tern_g", True),
    ("ipwT", 2048, 6144, "cast", False),
    ("opwT", 2048, 2048, "cast", False),
    ("wqT", 2048, 2048, "cast", False),
    ("wkT", 2048, 2048, "cast", False),
    ("wvT", 2048, 2048, "cast", False),
    ("woT", 2048, 2048, "cast", False),
    ("wxT", 2048, 2080, "cast", False),
    ("l3wq", 2048, 2048, "cast", False),
    ("l3wk", 2048, 512, "cast", False),
    ("l3wv", 2048, 512, "cast", False),
    ("l3wo", 2048, 2048, "cast", False),
]
P0_NV = [
    ("l3g", 16384, 1024, 2048),
    ("l3u", 16384, 1024, 2048),
    ("l3d", 2048, 8192, 16384),
]
CW0 = 4096


def emit_p0(nc, seq, io):
    es = ExitStack()
    ins, outs = io["ins"], io["outs"]
    iden_in, icst_in = io["iden_in"], io["icst_in"]
    gpart, gsum = io["gpart"], io["gsum"]
    CW = CW0
    sbuf = nc.sbuf_tensor
    w0 = es.enter_context(sbuf("w0", [128, CW], F32))
    wbf2 = es.enter_context(sbuf("wbf2", [128, CW], BF16))
    wbf = es.enter_context(sbuf("wbf", [128, CW], BF16))
    sc1 = es.enter_context(sbuf("sc1", [128, CW // 64], F32))
    wn = es.enter_context(sbuf("wn", [128, CW], F32))
    sg = es.enter_context(sbuf("sg", [128, CW], F32))
    t1 = es.enter_context(sbuf("t1", [128, CW], F32))
    t2 = es.enter_context(sbuf("t2", [128, CW], F32))
    qb = es.enter_context(sbuf("qb", [128, CW], BF16))
    qb2 = es.enter_context(sbuf("qb2", [128, CW], BF16))
    scB = es.enter_context(sbuf("scB", [128, CW // 64], BF16))
    onesw = es.enter_context(sbuf("onesw", [128, CW], F32))
    sA = es.enter_context(sbuf("sA", [128, CW], F32))
    wnG = es.enter_context(sbuf("wnG", [128, CW], F32))
    gq = es.enter_context(sbuf("gq", [128, CW], BF16))
    iden = es.enter_context(sbuf("iden_s", [128, 128], BF16))
    icst = es.enter_context(sbuf("icst_s", [128, 2], I32))
    gsc = es.enter_context(sbuf("gsc", [128, 4], F32))
    ones = es.enter_context(sbuf("ones", [128, 1], F32))
    red = es.enter_context(sbuf("red", [128, 1], F32))
    red2 = es.enter_context(sbuf("red2", [1, 16], F32))
    pk = es.enter_context(sbuf("pk", [128, CW // 2], I32))
    cod = es.enter_context(sbuf("cod", [128, CW], I32))
    nsc = es.enter_context(sbuf("nsc", [128, 512], F32))
    tT = es.enter_context(sbuf("tT", [128, 1024], BF16))
    ptr = es.enter_context(nc.psum_tensor("ptr", [128, 1024], BF16))
    pred = es.enter_context(nc.psum_tensor("pred", [1, 2], F32))

    seq.dma("sync", iden[:], iden_in[:])
    seq.dma("sync", icst[:], icst_in[:])
    seq.comp("vector", lambda v: v.memset(ones[:], 1.0))
    seq.comp("vector", lambda v: v.memset(onesw[:], 1.0))

    for j, (name, R, K) in enumerate([("l2wfc", 5120, 2048), ("l2wpr", 2048, 5120)]):
        Rs = R // NCORES
        for rt in range(Rs // 128):
            for kc in range(0, K, CW):
                w = min(CW, K - kc)
                seq.dma("sync", w0[:, :w], ins[name][rt * 128:(rt + 1) * 128, kc:kc + w])

                def red_fn(v, w=w):
                    if w < CW:
                        v.memset(w0[:, w:], 0.0)
                    return v.tensor_reduce(red[:, :], w0[:, :], axis=AX.X, op=Op.add,
                                           apply_absolute_value=True)
                seq.comp("vector", red_fn)
                first = (rt == 0 and kc == 0)
                seq.comp("tensor", lambda t, j=j, first=first: t.matmul(
                    pred[:, j:j + 1], red[:], ones[:], start=first, stop=True))
    def red2_fn(v):
        v.memset(red2[:], 0.0)
        return v.tensor_copy(red2[:, 0:2], pred[:])
    seq.comp("vector", red2_fn)
    seq.dma("scalar", gpart.ap(), red2[:])

    def cc_fn(g, h, tgt):
        g.collective_compute("AllReduce", Op.add, replica_groups=[list(range(NCORES))],
                             ins=[gpart.ap().opt()], outs=[gsum.ap().opt()]).then_inc(h, 1)
    seq.step("gpsimd", 1, cc_fn, async_=True)
    seq.dma("sync", gsc[:, 0:2], gsum.ap()[0, 0:2].partition_broadcast(128))

    def gsc_fn(v):
        NE = float(5120 * 2048)
        v.tensor_scalar(gsc[:, 2:3], gsc[:, 1:2], 1.0 / NE, 1e-5, Op.mult, Op.max)
        return v.tensor_scalar(gsc[:, 0:1], gsc[:, 0:1], 1.0 / NE, 1e-5, Op.mult, Op.max)
    seq.comp("vector", gsc_fn)

    def store_chunk(name, rt, kc, w, tr, wb):
        if not tr:
            seq.dma("scalar", outs[name][rt * 128:(rt + 1) * 128, kc:kc + w], wb[:, :w])
            return
        for b0 in range(0, w, 1024):
            nb = min(8, (w - b0) // 128)

            def tr_fn(t, b0=b0, nb=nb, wb=wb):
                for i in range(nb):
                    inst = t.transpose(ptr[:, i * 128:(i + 1) * 128],
                                       wb[:, b0 + i * 128: b0 + (i + 1) * 128], iden[:])
                return inst
            seq.comp("tensor", tr_fn)
            seq.comp("vector", lambda v, nb=nb: v.tensor_copy(tT[:, :nb * 128], ptr[:, :nb * 128]))
            for i in range(nb):
                seq.dma("scalar", outs[name][kc + b0 + i * 128: kc + b0 + (i + 1) * 128,
                                             rt * 128:(rt + 1) * 128], tT[:, i * 128:(i + 1) * 128])

    tctr = [0]

    def wbuf():
        return wbf if tctr[0] % 2 == 0 else wbf2

    for name, R, K, kind, tr in P0_SPECS:
        Rs = R // NCORES
        gcol = 0 if name == "l2wfc" else 2
        for rt in range(Rs // 128):
            for kc in range(0, K, CW):
                tctr[0] += 1
                wb = wbuf()
                w = min(CW, K - kc)
                G = w // 64
                seq.dma("sync", w0[:, :w], ins[name][rt * 128:(rt + 1) * 128, kc:kc + w])
                if kind == "cast":
                    seq.comp("vector", lambda v, w=w, wb=wb: v.tensor_copy(wb[:, :w], w0[:, :w]))
                elif kind in ("tern", "tern_ste"):
                    def t_fn1(v, w=w, G=G):
                        v.tensor_copy(qb[:, :w], w0[:, :w])
                        v.tensor_copy(t2[:, :w], qb[:, :w])
                        if w < CW:
                            v.memset(t2[:, w:], 0.0)
                        v.tensor_reduce(sc1[:, :], t2[:, :].rearrange("p (g x) -> p g x", x=64),
                                        axis=AX.X, op=Op.add, apply_absolute_value=True)
                        v.tensor_scalar(sc1[:, :], sc1[:, :], 1.0 / 64, 1e-8, Op.mult, Op.max)
                        v.scalar_tensor_tensor(scB[:, :], sc1[:, :], 1.0, onesw[:, 0:64], Op.mult, Op.mult)
                        return v.scalar_tensor_tensor(sc1[:, :], scB[:, :], 1.0, onesw[:, 0:64], Op.mult, Op.mult)
                    seq.comp("vector", t_fn1)

                    def t_sg(s, w=w):
                        s.activation(out=sA[:, :w], in_=t2[:, :w], func=AF.Abs,
                                     scale=1.0 / 0.5009765625)
                        return s.activation(out=sg[:, :w], in_=t2[:, :w], func=AF.Sign)
                    seq.comp("scalar", t_sg)
                    seq.comp("gpsimd", lambda g, w=w, G=G: g.tensor_tensor(
                        wnG[:, :w].rearrange("p (g x) -> p g x", x=64),
                        onesw[:, :w].rearrange("p (g x) -> p g x", x=64),
                        sc1[:, :G].unsqueeze(2).to_broadcast((128, G, 64)), Op.mult))

                    def cmp_fn(v, w=w):
                        v.tensor_tensor(t2[:, :w], sA[:, :w], wnG[:, :w], Op.is_ge)
                        return v.tensor_tensor(wn[:, :w], sg[:, :w], t2[:, :w], Op.mult)
                    seq.comp("vector", cmp_fn)
                    seq.comp("gpsimd", lambda g, w=w, G=G: g.tensor_tensor(
                        gq[:, :w].rearrange("p (g x) -> p g x", x=64),
                        wn[:, :w].rearrange("p (g x) -> p g x", x=64),
                        scB[:, :G].unsqueeze(2).to_broadcast((128, G, 64)), Op.mult))
                    if kind == "tern_ste":
                        def ste(v, w=w, wb=wb):
                            v.tensor_tensor(qb2[:, :w], gq[:, :w], qb[:, :w], Op.subtract)
                            return v.tensor_tensor(wb[:, :w], qb[:, :w], qb2[:, :w], Op.add)
                        seq.comp("vector", ste)
                    else:
                        seq.comp("vector", lambda v, w=w, wb=wb: v.tensor_copy(wb[:, :w], gq[:, :w]))
                else:
                    def tg_sg(s, w=w, gcol=gcol):
                        s.activation(out=sg[:, :w], in_=w0[:, :w], func=AF.Sign)
                        s.activation(out=sg[:, :w], in_=sg[:, :w], func=AF.Copy,
                                     scale=gsc[:, gcol:gcol + 1])
                        return s.activation(out=sA[:, :w], in_=w0[:, :w], func=AF.Abs, scale=1.0 / 0.7)
                    seq.comp("scalar", tg_sg)
                    seq.comp("vector", lambda v, w=w, gcol=gcol, wb=wb: v.scalar_tensor_tensor(
                        wb[:, :w], sA[:, :w], gsc[:, gcol:gcol + 1], sg[:, :w], Op.is_gt, Op.mult))
                store_chunk(name, rt, kc, w, tr, wb)

    for name, R, Kp, Kd in P0_NV:
        Rs = R // NCORES
        for rt in range(Rs // 128):
            seq.dma("sync", nsc[:, :Kd // 32], ins[name + "_sc"][rt * 128:(rt + 1) * 128, :])
            for kc in range(0, Kd, CW):
                tctr[0] += 1
                wb = wbuf()
                w = min(CW, Kd - kc)
                GS = w // 32
                seq.dma("sync", pk[:, :w // 2], ins[name + "_pk"][rt * 128:(rt + 1) * 128, kc // 2:(kc + w) // 2])

                def nv1(v, w=w):
                    c3 = cod[:, :w].rearrange("p (k two) -> p k two", two=2)
                    v.tensor_scalar(c3[:, :, 0:1], pk[:, :w // 2].unsqueeze(2), icst[:, 0:1], None, Op.bitwise_and)
                    v.tensor_scalar(c3[:, :, 1:2], pk[:, :w // 2].unsqueeze(2), icst[:, 1:2], icst[:, 0:1],
                                    Op.arith_shift_right, Op.bitwise_and)
                    v.tensor_copy(t2[:, :w], cod[:, :w])
                    v.tensor_scalar(wn[:, :w], t2[:, :w], 8.0, None, Op.is_ge)
                    v.scalar_tensor_tensor(t1[:, :w], wn[:, :w], -8.0, t2[:, :w], Op.mult, Op.add)
                    v.tensor_scalar(wn[:, :w], wn[:, :w], -2.0, 1.0, Op.mult, Op.add)
                    v.tensor_scalar(t2[:, :w], t1[:, :w], 0.5, -0.25, Op.mult, Op.add)
                    return v.tensor_scalar(t2[:, :w], t2[:, :w], BIG, BIG, Op.add, Op.subtract)
                seq.comp("vector", nv1)
                seq.comp("scalar", lambda s, w=w: s.activation(out=sg[:, :w], in_=t2[:, :w], func=AF.Exp, scale=LN2))

                def nv2(v, w=w):
                    lsb = cod[:, :w].bitcast(F32)
                    v.scalar_tensor_tensor(lsb, t2[:, :w], -2.0, t1[:, :w], Op.mult, Op.add)
                    v.tensor_scalar(t2[:, :w], lsb, 2.0, 0.25, Op.add, Op.mult)
                    v.tensor_tensor(t2[:, :w], t2[:, :w], sg[:, :w], Op.mult)
                    v.tensor_scalar(sg[:, :w], t1[:, :w], 2.0, None, Op.is_lt)
                    v.tensor_scalar(t1[:, :w], t1[:, :w], 0.5, None, Op.mult)
                    v.tensor_tensor(t1[:, :w], t1[:, :w], t2[:, :w], Op.subtract)
                    v.tensor_tensor(t1[:, :w], t1[:, :w], sg[:, :w], Op.mult)
                    v.tensor_tensor(t2[:, :w], t2[:, :w], t1[:, :w], Op.add)
                    return v.tensor_tensor(t2[:, :w], t2[:, :w], wn[:, :w], Op.mult)
                seq.comp("vector", nv2)
                seq.comp("gpsimd", lambda g, w=w, GS=GS, kc=kc, wb=wb: g.tensor_tensor(
                    wb[:, :w].rearrange("p (g x) -> p g x", x=32),
                    t2[:, :w].rearrange("p (g x) -> p g x", x=32),
                    nsc[:, kc // 32: kc // 32 + GS].unsqueeze(2).to_broadcast((128, GS, 32)), Op.mult))
                store_chunk(name, rt, kc, w, True, wb)


    es.close()


def emit_p0p(nc, seq, io):
    es = ExitStack()
    ins, outs = io["ins"], io["outs"]
    iden_in, icst_in = io["iden_in"], io["icst_in"]
    gpart, gsum = io["gpart"], io["gsum"]
    CW = CW0
    sbuf = nc.sbuf_tensor
    w02 = [es.enter_context(sbuf(f"w0{i}", [128, CW], F32)) for i in range(2)]
    wbfs = [es.enter_context(sbuf(f"wbf{i}", [128, CW], BF16)) for i in range(2)]
    sc1 = es.enter_context(sbuf("sc1", [128, CW // 64], F32))
    wn = es.enter_context(sbuf("wn", [128, CW], F32))
    sg = es.enter_context(sbuf("sg", [128, CW], F32))
    t2 = es.enter_context(sbuf("t2", [128, CW], F32))
    qb = es.enter_context(sbuf("qb", [128, CW], BF16))
    qb2 = es.enter_context(sbuf("qb2", [128, CW], BF16))
    scB = es.enter_context(sbuf("scB", [128, CW // 64], BF16))
    onesw = es.enter_context(sbuf("onesw", [128, CW], F32))
    sA = es.enter_context(sbuf("sA", [128, CW], F32))
    wnG = es.enter_context(sbuf("wnG", [128, CW], F32))
    gq = es.enter_context(sbuf("gq", [128, CW], BF16))
    t1 = wnG
    cod = sA[:].bitcast(I32)
    iden = es.enter_context(sbuf("iden_s", [128, 128], BF16))
    icst = es.enter_context(sbuf("icst_s", [128, 2], I32))
    gsc = es.enter_context(sbuf("gsc", [128, 4], F32))
    ones = es.enter_context(sbuf("ones", [128, 1], F32))
    red = es.enter_context(sbuf("red", [128, 1], F32))
    red2 = es.enter_context(sbuf("red2", [1, 16], F32))
    pk2 = [es.enter_context(sbuf(f"pk{i}", [128, CW // 2], I32)) for i in range(2)]
    nsc = es.enter_context(sbuf("nsc", [128, 512], F32))
    tT2 = [es.enter_context(sbuf(f"tT{i}", [128, 1024], BF16)) for i in range(2)]
    ptr2 = [es.enter_context(nc.psum_tensor(f"ptr{i}", [128, 1024], BF16)) for i in range(2)]
    pred = es.enter_context(nc.psum_tensor("pred", [1, 2], F32))

    seq.dma("sync", iden[:], iden_in[:])
    seq.dma("sync", icst[:], icst_in[:])
    seq.comp("vector", lambda v: v.memset(ones[:], 1.0))
    seq.comp("vector", lambda v: v.memset(onesw[:], 1.0))

    for j, (name, R, K) in enumerate([("l2wfc", 5120, 2048), ("l2wpr", 2048, 5120)]):
        Rs = R // NCORES
        for rt in range(Rs // 128):
            for kc in range(0, K, CW):
                w = min(CW, K - kc)
                seq.dma("sync", w02[0][:, :w], ins[name][rt * 128:(rt + 1) * 128, kc:kc + w])

                def red_fn(v, w=w):
                    if w < CW:
                        v.memset(w02[0][:, w:], 0.0)
                    return v.tensor_reduce(red[:, :], w02[0][:, :], axis=AX.X, op=Op.add,
                                           apply_absolute_value=True)
                seq.comp("vector", red_fn)
                first = (rt == 0 and kc == 0)
                seq.comp("tensor", lambda t, j=j, first=first: t.matmul(
                    pred[:, j:j + 1], red[:], ones[:], start=first, stop=True))

    def red2_fn(v):
        v.memset(red2[:], 0.0)
        return v.tensor_copy(red2[:, 0:2], pred[:])
    seq.comp("vector", red2_fn)
    seq.dma("scalar", gpart.ap(), red2[:])

    def cc_fn(g, h, tgt):
        g.collective_compute("AllReduce", Op.add, replica_groups=[list(range(NCORES))],
                             ins=[gpart.ap().opt()], outs=[gsum.ap().opt()]).then_inc(h, 1)
    seq.step("gpsimd", 1, cc_fn, async_=True)
    seq.dma("sync", gsc[:, 0:2], gsum.ap()[0, 0:2].partition_broadcast(128))

    def gsc_fn(v):
        NE = float(5120 * 2048)
        v.tensor_scalar(gsc[:, 2:3], gsc[:, 1:2], 1.0 / NE, 1e-5, Op.mult, Op.max)
        return v.tensor_scalar(gsc[:, 0:1], gsc[:, 0:1], 1.0 / NE, 1e-5, Op.mult, Op.max)
    gid = seq.comp("vector", gsc_fn)
    seq.barrier()

    LR = {"t2rd": None, "sgrd": None, "wnrd": None, "wnGrd": None, "scrd": [],
          "gqrd": None, "w0rd": [gid, gid], "pkrd": [None, None],
          "wbfree": [None, None], "ptr_ev": {}, "tst": {}, "nscrd": None}
    ci = [0]

    def store_chunk(name, rt, kc, w, tr, wb, wb_ready):
        p = ci[0] % 2
        if not tr:
            sid = seq.dma("gpsimd", outs[name][rt * 128:(rt + 1) * 128, kc:kc + w], wb[:, :w],
                          deps=[wb_ready], lane=f"st{p}")
            LR["wbfree"][p] = sid
            return
        last_tr = None
        for gi, b0 in enumerate(range(0, w, 1024)):
            nb = min(8, (w - b0) // 128)
            pb = gi % 2

            def tr_fn(t, b0=b0, nb=nb, wb=wb, pb=pb):
                for i in range(nb):
                    inst = t.transpose(ptr2[pb][:, i * 128:(i + 1) * 128],
                                       wb[:, b0 + i * 128:b0 + (i + 1) * 128], iden[:])
                return inst
            ttr = seq.comp("tensor", tr_fn, deps=[wb_ready, LR["ptr_ev"].get(pb)])
            ev = seq.comp("scalar", lambda s, nb=nb, pb=pb: s.activation(
                out=tT2[pb][:, :nb * 128], in_=ptr2[pb][:, :nb * 128], func=AF.Identity),
                deps=[ttr, LR["tst"].get(pb)])
            LR["ptr_ev"][pb] = ev
            prs = [(outs[name][kc + b0 + i * 128:kc + b0 + (i + 1) * 128,
                               rt * 128:(rt + 1) * 128], tT2[pb][:, i * 128:(i + 1) * 128])
                   for i in range(nb)]
            st = seq.dmas("gpsimd", prs, deps=[ev], lane=f"so{pb}")
            LR["tst"][pb] = st
            last_tr = ttr
        LR["wbfree"][p] = last_tr

    for name, R, K, kind, tr in P0_SPECS:
        Rs = R // NCORES
        gcol = 0 if name == "l2wfc" else 2
        for rt in range(Rs // 128):
            for kc in range(0, K, CW):
                p = ci[0] % 2
                w0 = w02[p]
                wb = wbfs[p]
                w = min(CW, K - kc)
                G = w // 64
                ld = seq.dma("sync", w0[:, :w], ins[name][rt * 128:(rt + 1) * 128, kc:kc + w],
                             deps=[LR["w0rd"][p]], lane=f"w0{p}")
                if kind == "cast":
                    cp = seq.comp("vector", lambda v, w=w, wb=wb, w0=w0: v.tensor_copy(
                        wb[:, :w], w0[:, :w]), deps=[ld, LR["wbfree"][p]])
                    LR["w0rd"][p] = cp
                    store_chunk(name, rt, kc, w, tr, wb, cp)
                elif kind in ("tern", "tern_ste"):
                    def t_fn1(v, w=w, G=G, w0=w0):
                        v.tensor_copy(qb[:, :w], w0[:, :w])
                        v.tensor_copy(t2[:, :w], qb[:, :w])
                        if w < CW:
                            v.memset(t2[:, w:], 0.0)
                        v.tensor_reduce(sc1[:, :], t2[:, :].rearrange("p (g x) -> p g x", x=64),
                                        axis=AX.X, op=Op.add, apply_absolute_value=True)
                        v.tensor_scalar(sc1[:, :], sc1[:, :], 1.0 / 64, 1e-8, Op.mult, Op.max)
                        v.scalar_tensor_tensor(scB[:, :], sc1[:, :], 1.0, onesw[:, 0:64], Op.mult, Op.mult)
                        return v.scalar_tensor_tensor(sc1[:, :], scB[:, :], 1.0, onesw[:, 0:64], Op.mult, Op.mult)
                    t1id = seq.comp("vector", t_fn1,
                                    deps=[ld, LR["t2rd"]] + LR["scrd"])
                    LR["w0rd"][p] = t1id

                    def t_sg(s, w=w):
                        s.activation(out=sA[:, :w], in_=t2[:, :w], func=AF.Abs,
                                     scale=1.0 / 0.5009765625)
                        return s.activation(out=sg[:, :w], in_=t2[:, :w], func=AF.Sign)
                    tsg = seq.comp("scalar", t_sg, deps=[t1id, LR["sgrd"]])
                    gp1 = seq.comp("gpsimd", lambda g, w=w, G=G: g.tensor_tensor(
                        wnG[:, :w].rearrange("p (g x) -> p g x", x=64),
                        onesw[:, :w].rearrange("p (g x) -> p g x", x=64),
                        sc1[:, :G].unsqueeze(2).to_broadcast((128, G, 64)), Op.mult),
                        deps=[t1id, LR["wnGrd"]])

                    def cmp_fn(v, w=w):
                        v.tensor_tensor(t2[:, :w], sA[:, :w], wnG[:, :w], Op.is_ge)
                        return v.tensor_tensor(wn[:, :w], sg[:, :w], t2[:, :w], Op.mult)
                    cmpid = seq.comp("vector", cmp_fn, deps=[tsg, gp1, LR["wnrd"]])
                    LR["t2rd"] = tsg
                    LR["sgrd"] = cmpid
                    LR["wnGrd"] = cmpid
                    gp2 = seq.comp("gpsimd", lambda g, w=w, G=G: g.tensor_tensor(
                        gq[:, :w].rearrange("p (g x) -> p g x", x=64),
                        wn[:, :w].rearrange("p (g x) -> p g x", x=64),
                        scB[:, :G].unsqueeze(2).to_broadcast((128, G, 64)), Op.mult),
                        deps=[cmpid, LR["gqrd"]])
                    LR["wnrd"] = gp2
                    LR["scrd"] = [gp1, gp2]
                    if kind == "tern_ste":
                        def ste(v, w=w, wb=wb):
                            v.tensor_tensor(qb2[:, :w], gq[:, :w], qb[:, :w], Op.subtract)
                            return v.tensor_tensor(wb[:, :w], qb[:, :w], qb2[:, :w], Op.add)
                        steid = seq.comp("vector", ste, deps=[gp2, LR["wbfree"][p]])
                    else:
                        steid = seq.comp("vector", lambda v, w=w, wb=wb: v.tensor_copy(
                            wb[:, :w], gq[:, :w]), deps=[gp2, LR["wbfree"][p]])
                    LR["gqrd"] = steid
                    store_chunk(name, rt, kc, w, tr, wb, steid)
                else:
                    def tg_sg(s, w=w, gcol=gcol, w0=w0):
                        s.activation(out=sg[:, :w], in_=w0[:, :w], func=AF.Sign)
                        s.activation(out=sg[:, :w], in_=sg[:, :w], func=AF.Copy,
                                     scale=gsc[:, gcol:gcol + 1])
                        return s.activation(out=sA[:, :w], in_=w0[:, :w], func=AF.Abs, scale=1.0 / 0.7)
                    tgid = seq.comp("scalar", tg_sg, deps=[ld, LR["sgrd"]])
                    LR["w0rd"][p] = tgid
                    wbstt = seq.comp("vector", lambda v, w=w, gcol=gcol, wb=wb: v.scalar_tensor_tensor(
                        wb[:, :w], sA[:, :w], gsc[:, gcol:gcol + 1], sg[:, :w], Op.is_gt, Op.mult),
                        deps=[tgid, LR["wbfree"][p]])
                    LR["sgrd"] = wbstt
                    store_chunk(name, rt, kc, w, tr, wb, wbstt)
                ci[0] += 1

    for name, R, Kp, Kd in P0_NV:
        Rs = R // NCORES
        for rt in range(Rs // 128):
            ldn = seq.dma("sync", nsc[:, :Kd // 32], ins[name + "_sc"][rt * 128:(rt + 1) * 128, :],
                          deps=[LR["nscrd"]], lane="nsc")
            for kc in range(0, Kd, CW):
                p = ci[0] % 2
                wb = wbfs[p]
                pk = pk2[p]
                w = min(CW, Kd - kc)
                GS = w // 32
                ldp = seq.dma("sync", pk[:, :w // 2],
                              ins[name + "_pk"][rt * 128:(rt + 1) * 128, kc // 2:(kc + w) // 2],
                              deps=[LR["pkrd"][p]], lane=f"pk{p}")

                def nv1(v, w=w, pk=pk):
                    c3 = cod[:, :w].rearrange("p (k two) -> p k two", two=2)
                    v.tensor_scalar(c3[:, :, 0:1], pk[:, :w // 2].unsqueeze(2), icst[:, 0:1], None, Op.bitwise_and)
                    v.tensor_scalar(c3[:, :, 1:2], pk[:, :w // 2].unsqueeze(2), icst[:, 1:2], icst[:, 0:1],
                                    Op.arith_shift_right, Op.bitwise_and)
                    v.tensor_copy(t2[:, :w], cod[:, :w])
                    v.tensor_scalar(wn[:, :w], t2[:, :w], 8.0, None, Op.is_ge)
                    v.scalar_tensor_tensor(t1[:, :w], wn[:, :w], -8.0, t2[:, :w], Op.mult, Op.add)
                    v.tensor_scalar(wn[:, :w], wn[:, :w], -2.0, 1.0, Op.mult, Op.add)
                    v.tensor_scalar(t2[:, :w], t1[:, :w], 0.5, -0.25, Op.mult, Op.add)
                    return v.tensor_scalar(t2[:, :w], t2[:, :w], BIG, BIG, Op.add, Op.subtract)
                n1 = seq.comp("vector", nv1, deps=[ldp, LR["t2rd"], LR["wnrd"], LR["sgrd"],
                                                   LR["wnGrd"]])
                LR["pkrd"][p] = n1
                sexp = seq.comp("scalar", lambda s, w=w: s.activation(
                    out=sg[:, :w], in_=t2[:, :w], func=AF.Exp, scale=LN2),
                    deps=[n1, LR["sgrd"]])

                def nv2(v, w=w):
                    lsb = cod[:, :w].bitcast(F32)
                    v.scalar_tensor_tensor(lsb, t2[:, :w], -2.0, t1[:, :w], Op.mult, Op.add)
                    v.tensor_scalar(t2[:, :w], lsb, 2.0, 0.25, Op.add, Op.mult)
                    v.tensor_tensor(t2[:, :w], t2[:, :w], sg[:, :w], Op.mult)
                    v.tensor_scalar(sg[:, :w], t1[:, :w], 2.0, None, Op.is_lt)
                    v.tensor_scalar(t1[:, :w], t1[:, :w], 0.5, None, Op.mult)
                    v.tensor_tensor(t1[:, :w], t1[:, :w], t2[:, :w], Op.subtract)
                    v.tensor_tensor(t1[:, :w], t1[:, :w], sg[:, :w], Op.mult)
                    v.tensor_tensor(t2[:, :w], t2[:, :w], t1[:, :w], Op.add)
                    return v.tensor_tensor(t2[:, :w], t2[:, :w], wn[:, :w], Op.mult)
                n2 = seq.comp("vector", nv2, deps=[sexp])
                LR["sgrd"] = n2
                LR["wnrd"] = n2
                gp3 = seq.comp("gpsimd", lambda g, w=w, GS=GS, kc=kc, wb=wb: g.tensor_tensor(
                    wb[:, :w].rearrange("p (g x) -> p g x", x=32),
                    t2[:, :w].rearrange("p (g x) -> p g x", x=32),
                    nsc[:, kc // 32:kc // 32 + GS].unsqueeze(2).to_broadcast((128, GS, 32)), Op.mult),
                    deps=[n2, ldn, LR["wbfree"][p]])
                LR["t2rd"] = gp3
                LR["nscrd"] = gp3 if kc + w >= Kd else LR["nscrd"]
                store_chunk(name, rt, kc, w, True, wb, gp3)
                ci[0] += 1
    seq.barrier()
    es.close()


def build_p0():
    nc = bass.Bass("TRN2", target_bir_lowering=False, debug=False)
    es = ExitStack()
    ins, outs = {}, {}
    for name, R, K, kind, tr in P0_SPECS:
        Rs = R // NCORES
        ins[name] = nc.dram_tensor(name, [Rs, K], F32, kind="ExternalInput").ap()
        oshape = [K, Rs] if tr else [Rs, K]
        outs[name] = nc.dram_tensor(name + "_o", oshape, BF16, kind="ExternalOutput").ap()
    for name, R, Kp, Kd in P0_NV:
        Rs = R // NCORES
        ins[name + "_pk"] = nc.dram_tensor(name + "_pk", [Rs, Kp], I32, kind="ExternalInput").ap()
        ins[name + "_sc"] = nc.dram_tensor(name + "_sc", [Rs, Kd // 32], F32, kind="ExternalInput").ap()
        outs[name] = nc.dram_tensor(name + "_o", [Kd, Rs], BF16, kind="ExternalOutput").ap()
    dbg_out = nc.dram_tensor("dbg_o", [128, 4], F32, kind="ExternalOutput").ap()

    dbg2_out = nc.dram_tensor("dbg2_o", [128, 2048], F32, kind="ExternalOutput").ap()
    dbg3_out = nc.dram_tensor("dbg3_o", [128, 2048], F32, kind="ExternalOutput").ap()
    iden_in = nc.dram_tensor("iden", [128, 128], BF16, kind="ExternalInput").ap()
    icst_in = nc.dram_tensor("icst", [128, 2], I32, kind="ExternalInput").ap()
    gpart = nc.dram_tensor("gpart", [1, 16], F32)
    gsum = nc.dram_tensor("gsum", [1, 16], F32)


    seq = Seq(nc, es, "p0seq")
    io = dict(ins=ins, outs=outs, iden_in=iden_in, icst_in=icst_in, gpart=gpart, gsum=gsum)
    emit_p0(nc, seq, io)
    seq.finish()
    block = es.enter_context(nc.Block())

    @block.sync
    def _(sync):
        seq.emit("sync", sync)

    @block.vector
    def _(vector):
        seq.emit("vector", vector)

    @block.scalar
    def _(scalar):
        seq.emit("scalar", scalar)

    @block.gpsimd
    def _(gpsimd):
        seq.emit("gpsimd", gpsimd)

    @block.tensor
    def _(tensor):
        seq.emit("tensor", tensor)

    es.close()
    return nc


def p0_host_inputs(inp):
    tr = lambda a: np.ascontiguousarray(np.asarray(a).T)
    asc = lambda a: np.ascontiguousarray(np.asarray(a))
    full = {
        "l0wgu": asc(inp["L0_ffn_w_gate_up"]),
        "l0wpr": asc(inp["L0_ffn_w_proj"]),
        "l1wfc": asc(inp["L1_ffn_W_fc"]),
        "l1wpr": asc(inp["L1_ffn_W_proj"]),
        "l2wfc": asc(inp["L2_ffn_w_fc"]),
        "l2wpr": asc(inp["L2_ffn_w_proj"]),
        "ipwT": tr(inp["L0_attn_in_proj_w"]),
        "opwT": tr(inp["L0_attn_out_proj_w"]),
        "wqT": tr(inp["L1_attn_Wq"]),
        "wkT": tr(inp["L1_attn_Wk"]),
        "wvT": tr(inp["L1_attn_Wv"]),
        "woT": tr(inp["L1_attn_Wo"]),
        "wxT": tr(inp["L2_attn_W_xproj"]),
        "l3wq": asc(inp["L3_attn_Wq"]),
        "l3wk": asc(inp["L3_attn_Wk"]),
        "l3wv": asc(inp["L3_attn_Wv"]),
        "l3wo": asc(inp["L3_attn_Wo"]),
        "l3g_pk": asc(inp["L3_ffn_gate_W_packed"]),
        "l3g_sc": asc(inp["L3_ffn_gate_scales"]),
        "l3u_pk": asc(inp["L3_ffn_up_W_packed"]),
        "l3u_sc": asc(inp["L3_ffn_up_scales"]),
        "l3d_pk": asc(inp["L3_ffn_down_W_packed"]),
        "l3d_sc": asc(inp["L3_ffn_down_scales"]),
    }
    iden = np.eye(128, dtype=bf16)
    icst = np.tile(np.array([[15, 4]], np.int32), (128, 1))
    maps = []
    for c in range(NCORES):
        m = {"iden": iden, "icst": icst}
        for k, v in full.items():
            R = v.shape[0]
            Rs = R // NCORES
            m[k] = np.ascontiguousarray(v[c * Rs:(c + 1) * Rs])
        maps.append(m)
    return maps


def p0_collect(results):
    out = {}
    for name, R, K, kind, tr in P0_SPECS:
        out[name] = np.concatenate([r[name + "_o"] for r in results], axis=1 if tr else 0)
    for name, R, Kp, Kd in P0_NV:
        out[name] = np.concatenate([r[name + "_o"] for r in results], axis=1)
    return out


if __name__ == "__main__":
    print("kernel library")


def emit_a1(nc, seq, io):
    es = ExitStack()
    W = TSH
    WH = W + 2
    xfm_in = io["xfm_in"]
    ipw_in = io["ipw_in"]
    ipb_in = io["ipb_in"]
    shw_in = io["shw_in"]
    shb_in = io["shb_in"]
    hsc_in = io["hsc_in"]
    o1_in = io["o1_in"]
    xn_out = io["xn_out"]
    x0_out = io["x0_out"]
    v2_out = io["v2_out"]
    sbuf = nc.sbuf_tensor
    xsb = es.enter_context(sbuf("xsb", [128, 16 * 512], F32))
    sq = es.enter_context(sbuf("sq", [128, 16 * 512], F32))
    xnr = es.enter_context(sbuf("xnr", [128, 16 * WH], BF16))
    strow = es.enter_context(sbuf("strow", [1, 6 * 512], F32))
    cst = es.enter_context(sbuf("cst", [1, 2], F32))
    ones = es.enter_context(sbuf("ones", [128, 2], F32))
    ones1 = es.enter_context(sbuf("ones1_s", [1, 128], F32))
    hsc = es.enter_context(sbuf("hsc_s", [128, 2], F32))
    wsb2 = [es.enter_context(sbuf(f"wsb{i}", [128, 16 * 128], BF16)) for i in range(2)]
    usb = es.enter_context(sbuf("usb", [128, WH], F32))
    ucv = es.enter_context(sbuf("ucv", [128, W], F32))
    x1sb = es.enter_context(sbuf("x1sb", [128, W], F32))
    obuf2 = [es.enter_context(sbuf(f"obuf{i}", [128, W], BF16)) for i in range(2)]
    ipb = es.enter_context(sbuf("ipb_s", [128, 96], F32))
    shw = es.enter_context(sbuf("shw_s", [128, 48 * 8], F32))
    shb = es.enter_context(sbuf("shb_s", [128, 48 * 8], F32))
    pst0 = es.enter_context(nc.psum_tensor("pst0", [1, 512], F32))
    pst1 = es.enter_context(nc.psum_tensor("pst1", [1, 512], F32))
    pbc = es.enter_context(nc.psum_tensor("pbc", [128, 1024], F32))
    pu = [es.enter_context(nc.psum_tensor(f"pu{i}", [128, 512], F32)) for i in range(4)]

    seq.dma("sync", ones1[:], o1_in[:])
    seq.dma("sync", hsc[:], hsc_in[:])
    seq.dma("sync", ipb[:].rearrange("p (h two) -> p h two", two=2),
            ipb_in.rearrange("(h p) two -> p h two", p=128))
    seq.dma("sync", shw[:].rearrange("p (h x) -> p h x", x=8),
            shw_in.rearrange("(h p) x -> p h x", p=128))
    seq.dma("sync", shb[:].rearrange("p (h x) -> p h x", x=8),
            shb_in.rearrange("(h p) x -> p h x", p=128))
    seq.comp("vector", lambda v: v.memset(ones[:], 1.0))
    seq.comp("vector", lambda v: v.memset(cst[:, 0:1], 1e-5))

    ln_blocks = [(i * 512, 512, 0) for i in range(4)] + [(WH - 512, 512, 0)]
    for t0, bw, _ in ln_blocks:
        seq.dma("sync", xsb[:].rearrange("p (k x) -> p k x", x=512)[:, :, :bw],
                xfm_in.rearrange("(k p) t -> p k t", p=128)[:, :, t0:t0 + bw])

        def sq_fn(s, bw=bw):
            return s.activation(out=sq[:].rearrange("p (k x) -> p k x", x=512)[:, :, :bw],
                                in_=xsb[:].rearrange("p (k x) -> p k x", x=512)[:, :, :bw],
                                func=AF.Square)
        seq.comp("scalar", sq_fn)

        def st_fn(t, bw=bw):
            for k in range(16):
                t.matmul(pst0[:, :bw], ones[:, 0:1], xsb[:, k * 512:k * 512 + bw],
                         start=(k == 0), stop=(k == 15))
            for k in range(16):
                inst = t.matmul(pst1[:, :bw], ones[:, 0:1], sq[:, k * 512:k * 512 + bw],
                                start=(k == 0), stop=(k == 15))
            return inst
        seq.comp("tensor", st_fn)

        def row_fn(v, bw=bw):
            m = strow[0:1, 0:bw]
            var = strow[0:1, 512:512 + bw]
            v.tensor_scalar(m, pst0[:, :bw], 1.0 / 2048, None, Op.mult)
            v.tensor_tensor(var, m, m, Op.mult)
            v.scalar_tensor_tensor(var, pst1[:, :bw], 1.0 / 2048, var, Op.mult, Op.subtract)
            return v.tensor_copy(strow[0:1, 2048:2048 + bw], var)
        seq.comp("vector", row_fn)

        def rstd_fn(s, bw=bw):
            v = strow[0:1, 512:512 + bw]
            s.activation(out=v, in_=v, func=AF.Ln, bias=cst[0:1, 0:1])
            return s.activation(out=strow[0:1, 1024:1024 + bw], in_=v, func=AF.Exp, scale=-0.5)
        seq.comp("scalar", rstd_fn)
        seq.comp("vector", lambda v, bw=bw: v.tensor_tensor(
            strow[0:1, 1536:1536 + bw], strow[0:1, 0:bw], strow[0:1, 1024:1024 + bw], Op.mult))

        def bc_fn(t, bw=bw):
            t.matmul(pbc[:, 0:bw], ones1[:], strow[0:1, 1024:1024 + bw], start=True, stop=True)
            return t.matmul(pbc[:, 512:512 + bw], ones1[:], strow[0:1, 1536:1536 + bw], start=True, stop=True)
        seq.comp("tensor", bc_fn)

        def ap_fn(v, t0=t0, bw=bw):
            for k in range(16):
                v.tensor_tensor(sq[:, k * 512:k * 512 + bw], xsb[:, k * 512:k * 512 + bw],
                                pbc[:, 0:bw], Op.mult)
                inst = v.tensor_tensor(xnr[:, k * WH + t0:k * WH + t0 + bw],
                                       sq[:, k * 512:k * 512 + bw], pbc[:, 512:512 + bw],
                                       Op.subtract)
            return inst
        seq.comp("vector", ap_fn)
    seq.dma("scalar", xn_out.rearrange("(k p) t -> p k t", p=128),
            xnr[:].rearrange("p (k t) -> p k t", t=WH)[:, :, 2:])

    xn_store = len(seq.steps) - 1
    TR = {"ld": [None, None], "mm": [None, None], "ev": xn_store, "evh": xn_store,
          "conv": xn_store, "st": [None, None], "ti": 0}

    def do_htile(h, kind, i):
        ti = TR["ti"]
        p = ti % 2
        wsbp = wsb2[p]
        ldid = seq.dma("sync", wsbp[:].rearrange("p (k m) -> p k m", m=128),
                       ipw_in.rearrange("(k p) m -> p k m", p=128)[:, :, h * 128:(h + 1) * 128],
                       deps=[TR["mm"][p]], lane=f"ws{p}")

        def mm_fn(t, h=h, wsbp=wsbp):
            for c in range(4):
                c0 = c * 512
                for k in range(16):
                    inst = t.matmul(pu[c][:, :512], wsbp[:, k * 128:(k + 1) * 128],
                                    xnr[:, k * WH + c0:k * WH + c0 + 512],
                                    start=(k == 0), stop=(k == 15))
            return inst
        mmid = seq.comp("tensor", mm_fn, deps=[ldid, TR["ev"], xn_store])

        def ev_fn(s, h=h):
            for c in range(4):
                inst = s.activation(out=usb[:, c * 512:(c + 1) * 512], in_=pu[c][:, :512],
                                    func=AF.Identity, bias=ipb[:, 2 * h:2 * h + 1])
            return inst
        evid = seq.comp("scalar", ev_fn, deps=[mmid, TR["conv"]])

        def mmh_fn(t, h=h, wsbp=wsbp):
            for k in range(16):
                inst = t.matmul(pu[0][:, :2], wsbp[:, k * 128:(k + 1) * 128],
                                xnr[:, k * WH + 2048:k * WH + 2050],
                                start=(k == 0), stop=(k == 15))
            return inst
        mmhid = seq.comp("tensor", mmh_fn, deps=[evid])
        TR["mm"][p] = mmhid
        evhid = seq.comp("scalar", lambda s, h=h: s.activation(
            out=usb[:, 2048:2050], in_=pu[0][:, :2], func=AF.Identity,
            bias=ipb[:, 2 * h:2 * h + 1]), deps=[mmhid])
        TR["ev"] = evhid
        ob = obuf2[ti % 2]

        def cv_fn(v, h=h, kind=kind, ob=ob):
            v.tensor_tensor(usb[:, 0:2], usb[:, 0:2], hsc[:, 0:2], Op.mult)
            v.tensor_scalar(ucv[:], usb[:, 0:W], shw[:, h * 8:h * 8 + 1], None, Op.mult)
            v.scalar_tensor_tensor(ucv[:], usb[:, 1:1 + W], shw[:, h * 8 + 2:h * 8 + 3], ucv[:], Op.mult, Op.add)
            v.scalar_tensor_tensor(ucv[:], usb[:, 2:2 + W], shw[:, h * 8 + 4:h * 8 + 5], ucv[:], Op.mult, Op.add)
            if kind == "x1":
                return v.tensor_scalar(x1sb[:], ucv[:], shb[:, h * 8:h * 8 + 1], None, Op.add)
            elif kind == "v":
                v.tensor_scalar(ucv[:], ucv[:], shb[:, h * 8:h * 8 + 1], None, Op.add)
                return v.tensor_tensor(ob[:], ucv[:], x1sb[:], Op.mult)
            else:
                return v.tensor_scalar(ob[:], ucv[:], shb[:, h * 8:h * 8 + 1], None, Op.add)
        cvid = seq.comp("vector", cv_fn, deps=[evhid, TR["st"][ti % 2]])
        TR["conv"] = cvid
        if kind == "v":
            TR["st"][ti % 2] = seq.dma("scalar", v2_out[i * 128:(i + 1) * 128, :], ob[:],
                                       deps=[cvid], lane=f"ob{ti % 2}")
        elif kind == "x0":
            TR["st"][ti % 2] = seq.dma("scalar", x0_out[i * 128:(i + 1) * 128, :], ob[:],
                                       deps=[cvid], lane=f"ob{ti % 2}")
        TR["ti"] += 1

    for i in range(16):
        do_htile(16 + i, "x1", i)
        do_htile(32 + i, "v", i)
        do_htile(0 + i, "x0", i)
    seq.barrier()


    es.close()


def build_a1():
    nc = bass.Bass("TRN2", target_bir_lowering=False, debug=False)
    es = ExitStack()
    W = TSH
    WH = W + 2
    xfm_in = nc.dram_tensor("xfm", [D, WH], F32, kind="ExternalInput").ap()
    ipw_in = nc.dram_tensor("ipwT", [D, 3 * D], BF16, kind="ExternalInput").ap()
    ipb_in = nc.dram_tensor("ipb", [3 * D, 2], F32, kind="ExternalInput").ap()
    shw_in = nc.dram_tensor("shw", [3 * D, 8], F32, kind="ExternalInput").ap()
    shb_in = nc.dram_tensor("shb", [3 * D, 8], F32, kind="ExternalInput").ap()
    hsc_in = nc.dram_tensor("hsc", [128, 2], F32, kind="ExternalInput").ap()
    o1_in = nc.dram_tensor("ones1", [1, 128], F32, kind="ExternalInput").ap()
    xn_out = nc.dram_tensor("xn_o", [D, W], BF16, kind="ExternalOutput").ap()
    x0_out = nc.dram_tensor("x0_o", [D, W], BF16, kind="ExternalOutput").ap()
    v2_out = nc.dram_tensor("v2_o", [D, W], BF16, kind="ExternalOutput").ap()


    seq = Seq(nc, es, "a1seq")
    io = dict(xfm_in=xfm_in, ipw_in=ipw_in, ipb_in=ipb_in, shw_in=shw_in, shb_in=shb_in, hsc_in=hsc_in, o1_in=o1_in, xn_out=xn_out, x0_out=x0_out, v2_out=v2_out)
    emit_a1(nc, seq, io)
    seq.finish()
    block = es.enter_context(nc.Block())
    for en in ["sync", "vector", "scalar", "gpsimd", "tensor"]:
        getattr(block, en)(lambda e, en=en: seq.emit(en, e))
    es.close()
    return nc


def a1_host_inputs(inp, prep):
    x = np.asarray(inp["L0_attn_x"], np.float32).reshape(TOK, D)
    ipb = np.zeros((3 * D, 2), np.float32)
    ipb[:, 0] = np.asarray(inp["L0_attn_in_proj_b"], np.float32)
    shw_r = np.asarray(inp["L0_attn_short_w"], np.float32).reshape(3 * D, 3)
    shw = np.zeros((3 * D, 8), np.float32)
    shw[:, 0] = shw_r[:, 0]
    shw[:, 2] = shw_r[:, 1]
    shw[:, 4] = shw_r[:, 2]
    shb = np.zeros((3 * D, 8), np.float32)
    shb[:, 0] = np.asarray(inp["L0_attn_short_b"], np.float32)
    ones1 = np.ones((1, 128), np.float32)
    maps = []
    for c in range(NCORES):
        t0 = c * TSH
        xfm = np.zeros((D, TSH + 2), np.float32)
        xfm[:, 2:] = x[t0:t0 + TSH].T
        hs = np.zeros((128, 2), np.float32)
        if c % 4 != 0:
            xfm[:, 0:2] = x[t0 - 2:t0].T
            hs[:] = 1.0
        maps.append(dict(xfm=np.ascontiguousarray(xfm), ipwT=np.ascontiguousarray(prep["ipwT"]),
                         ipb=ipb, shw=shw, shb=shb, hsc=hs, ones1=ones1))
    return maps


def emit_a2(nc, seq, io):
    es = ExitStack()
    CS = D // NCORES
    NQ = 4
    QC = 64
    L = T
    v2_in = io["v2_in"]
    x0_in = io["x0_in"]
    zT_in = io["zT_in"]
    w0T_in = io["w0T_in"]
    w1T_in = io["w1T_in"]
    w2T_in = io["w2T_in"]
    w3T_in = io["w3T_in"]
    fvec_in = io["fvec_in"]
    trow_in = io["trow_in"]
    nds_in = io["nds_in"]
    fb_in = io["fb_in"]
    wre_in = io["wre_in"]
    wim_in = io["wim_in"]
    wimn_in = io["wimn_in"]
    wc64_in = io["wc64_in"]
    wpd_in = io["wpd_in"]
    wpd2_in = io["wpd2_in"]
    twfr_in = io["twfr_in"]
    twfi_in = io["twfi_in"]
    twir_in = io["twir_in"]
    twii_in = io["twii_in"]
    y_out = io["y_out"]
    DBGON = io.get("dbg", True)
    kdbg_out = io.get("kdbg_out")
    h1dbg_out = io.get("h1dbg_out")
    h3dbg_out = io.get("h3dbg_out")
    kfdbg_out = io.get("kfdbg_out")
    kscr = io["kscr"]
    kfscr = io["kfscr"]
    yscr = io["yscr"]
    sbuf = nc.sbuf_tensor
    S = es.enter_context(sbuf("Sb", [128, 16384], BF16))
    Z = es.enter_context(sbuf("Zb", [128, 16384], BF16))
    T1b = es.enter_context(sbuf("T1b", [128, 8192], BF16))
    T2b = es.enter_context(sbuf("T2b", [128, 8192], BF16))
    T3b = es.enter_context(sbuf("T3b", [128, 8192], BF16))
    KF = es.enter_context(sbuf("KFb", [128, 16384], BF16))
    wre = es.enter_context(sbuf("wre_s", [128, 128], BF16))
    wim = es.enter_context(sbuf("wim_s", [128, 128], BF16))
    wimn = es.enter_context(sbuf("wimn_s", [128, 128], BF16))
    wc64 = es.enter_context(sbuf("wc64_s", [64, 256], BF16))
    wpd = es.enter_context(sbuf("wpd_s", [128, 256], BF16))
    wpd2 = es.enter_context(sbuf("wpd2_s", [128, 256], BF16))
    twfr = es.enter_context(sbuf("twfr_s", [128, 128], F32))
    twfi = es.enter_context(sbuf("twfi_s", [128, 128], F32))
    twir = es.enter_context(sbuf("twir_s", [128, 128], F32))
    twii = es.enter_context(sbuf("twii_s", [128, 128], F32))
    w0T = es.enter_context(sbuf("w0T_s", [3, 64], F32))
    w1T = es.enter_context(sbuf("w1T_s", [64, 64], F32))
    w2T = es.enter_context(sbuf("w2T_s", [64, 64], F32))
    w3T = es.enter_context(sbuf("w3T_s", [64, CS], F32))
    fvec = es.enter_context(sbuf("fvec_s", [64, 8], F32))
    nds = es.enter_context(sbuf("nds_s", [128, 4], F32))
    fbv = es.enter_context(sbuf("fbv_s", [128, 4], F32))
    pA = [es.enter_context(nc.psum_tensor(f"pA{i}", [128, 512], F32)) for i in range(4)]
    pE = es.enter_context(nc.psum_tensor("pE", [64, 512], F32))
    pE2 = es.enter_context(nc.psum_tensor("pE2", [64, 512], F32))

    hga_lo = S[0:64, :].bitcast(F32)
    hga_hi = S[64:128, :].bitcast(F32)
    hgb_hi = Z[64:128, :].bitcast(F32)
    zTv = Z[0:3, :].bitcast(F32)
    kpre = Z[0:128, :].bitcast(F32)
    trowB = KF[0:128, :].bitcast(F32)
    X2 = Z
    Y2 = KF
    X0v = KF

    for dst, src2 in [(wre, wre_in), (wim, wim_in), (wimn, wimn_in), (wc64, wc64_in),
                      (wpd, wpd_in), (wpd2, wpd2_in), (twfr, twfr_in), (twfi, twfi_in),
                      (twir, twir_in), (twii, twii_in), (w0T, w0T_in),
                      (w1T, w1T_in), (w2T, w2T_in), (w3T, w3T_in), (fvec, fvec_in),
                      ]:
        seq.dma("sync", dst[:], src2[:])
    seq.dma("sync", nds[:].rearrange("p (ct two) -> p ct two", two=2),
            nds_in.rearrange("(ct p) two -> p ct two", p=128))
    seq.dma("sync", fbv[:].rearrange("p (ct two) -> p ct two", two=2),
            fb_in.rearrange("(ct p) two -> p ct two", p=128))
    seq.dma("sync", zTv, zT_in[:])

    TWO_PI = float(2 * np.pi)

    def fv_fn(v):
        v.tensor_tensor(fvec[:, 2:3], fvec[:, 0:1], fvec[:, 2:3], Op.mult)
        v.tensor_tensor(fvec[:, 4:5], fvec[:, 0:1], fvec[:, 4:5], Op.mult)
        return v.tensor_tensor(fvec[:, 6:7], fvec[:, 0:1], fvec[:, 6:7], Op.mult)
    seq.comp("vector", fv_fn)

    def mlp_layer(lhsT, rhs, bcol, last=False):
        prev_tail = len(seq.steps) - 1
        acts = {}
        for ch in range(16):
            mid = seq.comp("tensor", lambda t, ch=ch: t.matmul(
                pA[ch % 2][:64, :], lhsT, rhs[:, ch * 512:(ch + 1) * 512],
                start=True, stop=True), deps=[acts.get(ch - 2, prev_tail)])
            acts[ch] = seq.comp("scalar", lambda s, ch=ch: s.activation(
                out=hga_hi[:, ch * 512:(ch + 1) * 512], in_=pA[ch % 2][:64, :],
                func=AF.Identity, scale=fvec[:, 0:1], bias=fvec[:, bcol:bcol + 1]),
                deps=[mid])

        def rr(v):
            MAGIC = 1.5 * BIG
            v.tensor_scalar(hgb_hi, hga_hi, 1.0 / TWO_PI, MAGIC, Op.mult, Op.add)
            v.tensor_scalar(hgb_hi, hgb_hi, MAGIC, None, Op.subtract)
            return v.scalar_tensor_tensor(hga_hi, hgb_hi, -TWO_PI, hga_hi, Op.mult, Op.add)
        seq.comp("vector", rr)
        seq.comp("scalar", lambda s: s.activation(out=hga_lo, in_=hga_hi, func=AF.Sin))

    mlp_layer(w0T[:, :], zTv, 2)
    if DBGON:
        seq.dma("scalar", h1dbg_out, hga_lo)
    mlp_layer(w1T[:, :], hga_lo, 4)
    mlp_layer(w2T[:, :], hga_lo, 6)
    if DBGON:
        seq.dma("scalar", h3dbg_out, hga_lo)
    seq.dma("sync", trowB, trow_in[0, :].partition_broadcast(128))
    for ct in range(2):
        for ch in range(16):
            seq.comp("tensor", lambda t, ct=ct, ch=ch: t.matmul(
                pA[2 + ch % 2][:, :], w3T[:, ct * 128:(ct + 1) * 128],
                hga_lo[:, ch * 512:(ch + 1) * 512], start=True, stop=True))
            seq.comp("scalar", lambda s, ch=ch: s.activation(
                out=kpre[:, ch * 512:(ch + 1) * 512], in_=pA[2 + ch % 2][:, :], func=AF.Identity))
        seq.comp("scalar", lambda s, ct=ct: s.activation(
            out=T1b[:, :], in_=trowB, func=AF.Exp, scale=nds[:, 2 * ct:2 * ct + 1]))
        seq.comp("vector", lambda v: v.tensor_tensor(T2b[:, :], kpre, T1b[:, :], Op.mult))
        seq.dma("scalar", kscr.ap()[ct * 128:(ct + 1) * 128, :], T2b[:, :])
        if DBGON:
            seq.dma("scalar", kdbg_out[ct * 128:(ct + 1) * 128, :], T2b[:, :])

    def cplx_mul_bcast(S4, re_b, im_b):
        Sre, Sim = S4[:, :, 0, :], S4[:, :, 1, :]
        T13 = T1b[:].rearrange("p (c k) -> p c k", k=128)
        T23 = T2b[:].rearrange("p (c k) -> p c k", k=128)
        T33 = T3b[:].rearrange("p (c k) -> p c k", k=128)
        seq.comp("gpsimd", lambda g: g.tensor_tensor(T13, Sre, re_b, Op.mult))
        seq.comp("gpsimd", lambda g: g.tensor_tensor(T23, Sim, im_b, Op.mult))
        seq.comp("gpsimd", lambda g: g.tensor_tensor(T33, Sre, im_b, Op.mult))
        seq.comp("gpsimd", lambda g: g.tensor_tensor(Sim, Sim, re_b, Op.mult))
        seq.comp("vector", lambda v: v.tensor_tensor(Sre, T13, T23, Op.subtract))
        seq.comp("vector", lambda v: v.tensor_tensor(Sim, Sim, T33, Op.add))

    def fwd_fft(src_rows, kf_dma):
        ldX = seq.dma("sync", X2[0:64, 0:8192].rearrange("p (c m) -> p c m", m=128), src_rows)
        evsA = {}
        for c0 in range(0, QC, 2):
            def sa(t, c0=c0):
                t.matmul(pA[(c0 // 2) % 2][:, 0:256], X2[0:64, c0 * 128:(c0 + 1) * 128],
                         wc64[:, :], start=True, stop=True)
                return t.matmul(pA[(c0 // 2) % 2][:, 256:512], X2[0:64, (c0 + 1) * 128:(c0 + 2) * 128],
                                wc64[:, :], start=True, stop=True)
            said = seq.comp("tensor", sa, deps=[ldX, evsA.get(c0 - 4)])
            evsA[c0] = seq.comp("scalar", lambda s, c0=c0: s.activation(
                out=S[:, c0 * 256:(c0 + 2) * 256], in_=pA[(c0 // 2) % 2][:, :], func=AF.Identity),
                deps=[said])
        S4 = S[:].rearrange("p (c two k) -> p c two k", two=2, k=128)
        cplx_mul_bcast(S4, twfr[:].unsqueeze(1).to_broadcast((128, QC, 128)),
                       twfi[:].unsqueeze(1).to_broadcast((128, QC, 128)))
        prevC = len(seq.steps) - 1
        evsC = {}
        for cc in range(0, QC, 2):
            rre = S4[:, cc:cc + 2, 0, :]
            rim = S4[:, cc:cc + 2, 1, :]
            bp = (2, 3) if (cc // 2) % 2 == 0 else (0, 1)

            def sc2(t, rre=rre, rim=rim, bp=bp):
                t.matmul(pA[bp[0]][:, 0:256], wre[:, :], rre, start=True, stop=False)
                t.matmul(pA[bp[0]][:, 0:256], wimn[:, :], rim, start=False, stop=True)
                t.matmul(pA[bp[1]][:, 0:256], wim[:, :], rre, start=True, stop=False)
                return t.matmul(pA[bp[1]][:, 0:256], wre[:, :], rim, start=False, stop=True)
            scid = seq.comp("tensor", sc2, deps=[evsC.get(cc - 4, prevC)])
            Z4 = Z[:].rearrange("p (c two k) -> p c two k", two=2, k=128)

            def ev(s, cc=cc, Z4=Z4, bp=bp):
                s.activation(out=Z4[:, cc:cc + 2, 0, :],
                             in_=pA[bp[0]][:, 0:256].rearrange("p (a k) -> p a k", a=2), func=AF.Identity)
                return s.activation(out=Z4[:, cc:cc + 2, 1, :],
                                    in_=pA[bp[1]][:, 0:256].rearrange("p (a k) -> p a k", a=2), func=AF.Identity)
            evsC[cc] = seq.comp("scalar", ev, deps=[scid])
        if kf_dma is not None:
            seq.comp("scalar", lambda s: s.activation(out=Z[:, :], in_=Z[:, :], func=AF.Identity,
                                                      scale=1.0 / 16384.0))
            seq.dma("scalar", kf_dma, Z[:, :])
            if DBGON and kf_dma is KFDBG[0]:
                seq.dma("scalar", kfdbg_out, Z[:, :])

    def product_with_kf(q):
        seq.dma("sync", KF[:, :], kfscr.ap()[q, :, :])
        Z4 = Z[:].rearrange("p (c two k) -> p c two k", two=2, k=128)
        K4 = KF[:].rearrange("p (c two k) -> p c two k", two=2, k=128)
        Zre, Zim = Z4[:, :, 0, :], Z4[:, :, 1, :]
        Kre, Kim = K4[:, :, 0, :], K4[:, :, 1, :]
        T13 = T1b[:].rearrange("p (c k) -> p c k", k=128)
        T23 = T2b[:].rearrange("p (c k) -> p c k", k=128)
        T33 = T3b[:].rearrange("p (c k) -> p c k", k=128)
        seq.comp("vector", lambda v: v.tensor_tensor(T13, Zre, Kre, Op.mult))
        seq.comp("vector", lambda v: v.tensor_tensor(T23, Zim, Kim, Op.mult))
        seq.comp("vector", lambda v: v.tensor_tensor(T33, Zre, Kim, Op.mult))
        seq.comp("vector", lambda v: v.tensor_tensor(Zim, Zim, Kre, Op.mult))
        seq.comp("vector", lambda v: v.tensor_tensor(Zre, T13, T23, Op.subtract))
        seq.comp("vector", lambda v: v.tensor_tensor(Zim, Zim, T33, Op.add))

    def inv_fft(dst_rows, x0_rows):
        prevD = len(seq.steps) - 1
        evsD = {}
        for c0 in range(0, QC, 2):
            bp = (2, 3) if (c0 // 2) % 2 == 0 else (0, 1)

            def sd(t, c0=c0, bp=bp):
                for j in (0, 1):
                    cc = c0 + j
                    t.matmul(pA[bp[j]][:, 0:256], Z[:, cc * 256:cc * 256 + 128], wpd[:, :],
                             start=True, stop=False)
                    inst = t.matmul(pA[bp[j]][:, 0:256], Z[:, cc * 256 + 128:cc * 256 + 256],
                                    wpd2[:, :], start=False, stop=True)
                return inst
            sdid = seq.comp("tensor", sd, deps=[evsD.get(c0 - 4, prevD)])

            def ev(s, c0=c0, bp=bp):
                s.activation(out=S[:, c0 * 256:(c0 + 1) * 256], in_=pA[bp[0]][:, 0:256], func=AF.Identity)
                return s.activation(out=S[:, (c0 + 1) * 256:(c0 + 2) * 256], in_=pA[bp[1]][:, 0:256],
                                    func=AF.Identity)
            evsD[c0] = seq.comp("scalar", ev, deps=[sdid])
        S4 = S[:].rearrange("p (c two k) -> p c two k", two=2, k=128)
        cplx_mul_bcast(S4, twir[:].unsqueeze(1).to_broadcast((128, QC, 128)),
                       twii[:].unsqueeze(1).to_broadcast((128, QC, 128)))
        seq.dma("sync", X0v[0:64, 8192:16384].rearrange("p (c m) -> p c m", m=128), x0_rows)
        prevE = len(seq.steps) - 1
        evsE = {}
        for cc in range(0, QC, 2):
            rre = S4[:, cc:cc + 2, 0, :]
            rim = S4[:, cc:cc + 2, 1, :]
            pe = pE if (cc // 2) % 2 == 0 else pE2

            def se(t, rre=rre, rim=rim, pe=pe):
                t.matmul(pe[:, 0:256], wre[:, 0:64], rre, start=True, stop=False)
                return t.matmul(pe[:, 0:256], wim[:, 0:64], rim, start=False, stop=True)
            seid = seq.comp("tensor", se, deps=[evsE.get(cc - 4, prevE)])
            evsE[cc] = seq.comp("scalar", lambda s, cc=cc, pe=pe: s.activation(
                out=Y2[0:64, cc * 128:(cc + 2) * 128], in_=pe[:, 0:256], func=AF.Identity),
                deps=[seid])
        seq.comp("vector", lambda v: v.tensor_tensor(
            Y2[0:64, 0:8192], Y2[0:64, 0:8192], X0v[0:64, 8192:16384], Op.mult))
        seq.dma("scalar", dst_rows, Y2[0:64, 0:8192].rearrange("p (c m) -> p c m", m=128))

    def q_rows(base_ap, q, b):
        return base_ap.rearrange("c (bb n1 n2) -> n1 c bb n2", bb=B, n2=128)[
            :, q * QC:(q + 1) * QC, b, :]

    KFDBG = [None]
    for q in range(NQ):
        src = kscr.ap().rearrange("c (n1 n2) -> n1 c n2", n2=128)[
            :, q * QC:(q + 1) * QC, :]
        dmaq = kfscr.ap()[q, :, :]
        if q == 0:
            KFDBG[0] = dmaq
        fwd_fft(src, dmaq)
    for b in range(B):
        for q in range(NQ):
            fwd_fft(q_rows(v2_in, q, b), None)
            product_with_kf(q)
            inv_fft(q_rows(yscr.ap(), q, b), q_rows(x0_in, q, b))

    for ct in range(2):
        rows = slice(ct * 128, (ct + 1) * 128)
        for b in range(B):
            cols = slice(b * L, (b + 1) * L)
            seq.dma("sync", T1b[:, :], v2_in[rows, cols])
            seq.dma("sync", T2b[:, :], x0_in[rows, cols])
            seq.dma("sync", T3b[:, :], yscr.ap()[rows, cols])
            seq.comp("vector", lambda v: v.tensor_tensor(T1b[:, :], T1b[:, :], T2b[:, :], Op.mult))
            seq.comp("vector", lambda v, ct=ct: v.scalar_tensor_tensor(
                T3b[:, :], T1b[:, :], fbv[:, 2 * ct:2 * ct + 1], T3b[:, :], Op.mult, Op.add))
            seq.dma("scalar", y_out[rows, cols], T3b[:, :])

    es.close()


def build_a2():
    nc = bass.Bass("TRN2", target_bir_lowering=False, debug=False)
    es = ExitStack()
    CS = D // NCORES
    NQ = 4
    QC = 64
    L = T
    v2_in = nc.dram_tensor("v2c", [CS, B * L], BF16, kind="ExternalInput").ap()
    x0_in = nc.dram_tensor("x0c", [CS, B * L], BF16, kind="ExternalInput").ap()
    zT_in = nc.dram_tensor("zT", [3, L], F32, kind="ExternalInput").ap()
    w0T_in = nc.dram_tensor("w0T", [3, 64], F32, kind="ExternalInput").ap()
    w1T_in = nc.dram_tensor("w1T", [64, 64], F32, kind="ExternalInput").ap()
    w2T_in = nc.dram_tensor("w2T", [64, 64], F32, kind="ExternalInput").ap()
    w3T_in = nc.dram_tensor("w3T", [64, CS], F32, kind="ExternalInput").ap()
    fvec_in = nc.dram_tensor("fvec", [64, 8], F32, kind="ExternalInput").ap()
    trow_in = nc.dram_tensor("trow", [1, L], F32, kind="ExternalInput").ap()
    nds_in = nc.dram_tensor("nds", [CS, 2], F32, kind="ExternalInput").ap()
    fb_in = nc.dram_tensor("fbv", [CS, 2], F32, kind="ExternalInput").ap()
    wre_in = nc.dram_tensor("wre", [128, 128], BF16, kind="ExternalInput").ap()
    wim_in = nc.dram_tensor("wim", [128, 128], BF16, kind="ExternalInput").ap()
    wimn_in = nc.dram_tensor("wimn", [128, 128], BF16, kind="ExternalInput").ap()
    wc64_in = nc.dram_tensor("wc64", [64, 256], BF16, kind="ExternalInput").ap()
    wpd_in = nc.dram_tensor("wpd", [128, 256], BF16, kind="ExternalInput").ap()
    wpd2_in = nc.dram_tensor("wpd2", [128, 256], BF16, kind="ExternalInput").ap()
    twfr_in = nc.dram_tensor("twfr", [128, 128], F32, kind="ExternalInput").ap()
    twfi_in = nc.dram_tensor("twfi", [128, 128], F32, kind="ExternalInput").ap()
    twir_in = nc.dram_tensor("twir", [128, 128], F32, kind="ExternalInput").ap()
    twii_in = nc.dram_tensor("twii", [128, 128], F32, kind="ExternalInput").ap()
    y_out = nc.dram_tensor("y_o", [CS, B * L], BF16, kind="ExternalOutput").ap()
    kdbg_out = nc.dram_tensor("kdbg_o", [CS, L], BF16, kind="ExternalOutput").ap()
    h1dbg_out = nc.dram_tensor("h1dbg_o", [64, L], F32, kind="ExternalOutput").ap()
    h3dbg_out = nc.dram_tensor("h3dbg_o", [64, L], F32, kind="ExternalOutput").ap()
    kfdbg_out = nc.dram_tensor("kfdbg_o", [128, 64 * 256], BF16, kind="ExternalOutput").ap()
    kscr = nc.dram_tensor("kscr", [CS, L], BF16)
    kfscr = nc.dram_tensor("kfscr", [NQ, 128, QC * 256], BF16)
    yscr = nc.dram_tensor("yscr", [CS, B * L], BF16)


    seq = Seq(nc, es, "a2seq")
    io = dict(v2_in=v2_in, x0_in=x0_in, zT_in=zT_in, w0T_in=w0T_in, w1T_in=w1T_in, w2T_in=w2T_in, w3T_in=w3T_in, fvec_in=fvec_in, trow_in=trow_in, nds_in=nds_in, fb_in=fb_in, wre_in=wre_in, wim_in=wim_in, wimn_in=wimn_in, wc64_in=wc64_in, wpd_in=wpd_in, wpd2_in=wpd2_in, twfr_in=twfr_in, twfi_in=twfi_in, twir_in=twir_in, twii_in=twii_in, y_out=y_out, kdbg_out=kdbg_out, h1dbg_out=h1dbg_out, h3dbg_out=h3dbg_out, kfdbg_out=kfdbg_out, kscr=kscr, kfscr=kfscr, yscr=yscr)
    emit_a2(nc, seq, io)
    seq.finish()
    block = es.enter_context(nc.Block())
    for en in ["sync", "vector", "scalar", "gpsimd", "tensor"]:
        getattr(block, en)(lambda e, en=en: seq.emit(en, e))
    es.close()
    return nc


def a2_host_inputs(inp, v2_full, x0_full):
    import math
    N1 = 128
    Wd = np.exp(-2j * np.pi * np.arange(N1)[:, None] * np.arange(N1)[None, :] / N1)
    TwF = np.exp(-2j * np.pi * np.outer(np.arange(N1), np.arange(N1)) / 16384.0)
    wre = Wd.real.astype(bf16)
    wim = Wd.imag.astype(bf16)
    wimn = (-Wd.imag).astype(bf16)
    wc64 = np.concatenate([Wd.real[:64, :], Wd.imag[:64, :]], 1).astype(bf16)
    wpd = np.concatenate([Wd.real, -Wd.imag], 1).astype(bf16)
    wpd2 = np.concatenate([Wd.imag, Wd.real], 1).astype(bf16)
    twfr = TwF.real.astype(np.float32)
    twfi = TwF.imag.astype(np.float32)
    twir = TwF.real.astype(np.float32)
    twii = (-TwF.imag).astype(np.float32)
    Lq = T
    t = np.linspace(0.0, 1.0, Lq)[:, None]
    t_resc = np.linspace(0.0, Lq - 1.0, Lq)
    wv = 2.0 * math.pi * t_resc / Lq
    f = np.linspace(0.0001, 0.0, 1)
    ang = f[None, :] * wv[:, None]
    z = np.concatenate([t, np.cos(ang), -np.sin(ang)], axis=-1)
    zT = np.ascontiguousarray(z.T).astype(np.float32)
    trow = np.linspace(0.0, 1.0, Lq).reshape(1, Lq).astype(np.float32)
    max_decay = math.log(0.01) / 0.3
    min_decay = math.log(0.01) / 1.5
    deltas = np.linspace(min_decay, max_decay, D)
    nds_full = np.zeros((D, 2), np.float32)
    nds_full[:, 0] = -np.abs(deltas)
    fb_full = np.zeros((D, 2), np.float32)
    fb_full[:, 0] = np.asarray(inp["L0_attn_filter_bias"], np.float32)
    fvec = np.zeros((64, 8), np.float32)
    fvec[:, 0] = np.asarray(inp["L0_attn_sin_freq"], np.float32).reshape(64)
    fvec[:, 2] = np.asarray(inp["L0_attn_mlp_b0"], np.float32)
    fvec[:, 4] = np.asarray(inp["L0_attn_mlp_b1"], np.float32)
    fvec[:, 6] = np.asarray(inp["L0_attn_mlp_b2"], np.float32)
    w0T = np.ascontiguousarray(np.asarray(inp["L0_attn_mlp_w0"], np.float32).T)
    w1T = np.ascontiguousarray(np.asarray(inp["L0_attn_mlp_w1"], np.float32).T)
    w2T = np.ascontiguousarray(np.asarray(inp["L0_attn_mlp_w2"], np.float32).T)
    w3 = np.asarray(inp["L0_attn_mlp_w3"], np.float32)
    maps = []
    for c in range(NCORES):
        rs = slice(c * 256, (c + 1) * 256)
        maps.append(dict(
            v2c=np.ascontiguousarray(v2_full[rs]), x0c=np.ascontiguousarray(x0_full[rs]),
            zT=zT, w0T=w0T, w1T=w1T, w2T=w2T,
            w3T=np.ascontiguousarray(w3[rs].T), fvec=fvec, trow=trow,
            nds=np.ascontiguousarray(nds_full[rs]), fbv=np.ascontiguousarray(fb_full[rs]),
            wre=wre, wim=wim, wimn=wimn, wc64=wc64, wpd=wpd, wpd2=wpd2,
            twfr=twfr, twfi=twfi, twir=twir, twii=twii))
    return maps


def emit_a3(nc, seq, io):
    es = ExitStack()
    W = TSH
    H0 = 8192
    xfm_in = io["xfm_in"]
    xn_in = io["xn_in"]
    y_in = io["y_in"]
    opw_in = io["opw_in"]
    opb_in = io["opb_in"]
    wgu_in = io.get("wgu_in")
    wpr_in = io.get("wpr_in")
    o1_in = io["o1_in"]
    xo_out = io["xo_out"]
    hscr = io["hscr"]
    wgu_src = io.get("wgu_src") or (
        lambda m0: wgu_in.rearrange("(k p) m -> p k m", p=128)[:, :, m0:m0 + 128])
    wpr_srcs = io.get("wpr_srcs") or (
        lambda k, g: [(0, 1024, wpr_in[k * 128:(k + 1) * 128, g * 1024:(g + 1) * 1024])])
    sbuf = nc.sbuf_tensor
    xnr = es.enter_context(sbuf("xnr", [128, 16 * W], BF16))
    wg = es.enter_context(sbuf("wg", [128, 16 * 128], BF16))
    wu = es.enter_context(sbuf("wu", [128, 16 * 128], BF16))
    gsb = es.enter_context(sbuf("gsb", [128, 512], F32))
    hsb = es.enter_context(sbuf("hsb", [128, 2048], BF16))
    hq = es.enter_context(sbuf("hq", [128, 2048], F32))
    rstdB = es.enter_context(sbuf("rstdB", [128, 2048], F32))
    strow = es.enter_context(sbuf("strow", [1, 2048], F32))
    ones = es.enter_context(sbuf("ones", [128, 2], F32))
    ones1 = es.enter_context(sbuf("ones1_s", [1, 128], F32))
    cst = es.enter_context(sbuf("cst", [1, 2], F32))
    opb = es.enter_context(sbuf("opb_s", [128, 32], F32))
    kch = es.enter_context(sbuf("kch", [128, 1024], BF16))
    wch = es.enter_context(sbuf("wch", [128, 8 * 128], BF16))
    xres = es.enter_context(sbuf("xres", [128, 512], F32))
    ob = es.enter_context(sbuf("ob", [128, 512], F32))
    P = [es.enter_context(nc.psum_tensor(f"pp{i}", [128, 512], F32)) for i in range(8)]

    seq.dma("sync", ones1[:], o1_in[:])
    seq.dma("sync", opb[:].rearrange("p (h two) -> p h two", two=2),
            opb_in.rearrange("(h p) two -> p h two", p=128))
    seq.dma("sync", xnr[:].rearrange("p (k t) -> p k t", t=W),
            xn_in.rearrange("(k p) t -> p k t", p=128))
    seq.comp("vector", lambda v: v.memset(ones[:], 1.0))
    seq.comp("vector", lambda v: v.memset(cst[:, 0:1], 1.1920929e-7))

    NHT = H0 // 128
    for ht in range(NHT):
        seq.dma("sync", wg[:].rearrange("p (k m) -> p k m", m=128), wgu_src(ht * 128))
        seq.dma("sync", wu[:].rearrange("p (k m) -> p k m", m=128), wgu_src(H0 + ht * 128))
        for tc in range(4):
            def mmg(t, tc=tc):
                for k in range(16):
                    t.matmul(P[0][:, :], wg[:, k * 128:(k + 1) * 128],
                             xnr[:, k * W + tc * 512:k * W + (tc + 1) * 512],
                             start=(k == 0), stop=(k == 15))
                for k in range(16):
                    inst = t.matmul(P[1][:, :], wu[:, k * 128:(k + 1) * 128],
                                    xnr[:, k * W + tc * 512:k * W + (tc + 1) * 512],
                                    start=(k == 0), stop=(k == 15))
                return inst
            seq.comp("tensor", mmg)
            seq.comp("scalar", lambda s: s.activation(out=gsb[:, :], in_=P[0][:, :], func=AF.Silu))
            seq.comp("vector", lambda v, tc=tc: v.tensor_tensor(
                hsb[:, tc * 512:(tc + 1) * 512], gsb[:, :], P[1][:, :], Op.mult))
        seq.comp("scalar", lambda s: s.activation(out=hq[:, :], in_=hsb[:, :], func=AF.Square))

        def stm(t, ht=ht):
            for tc in range(4):
                inst = t.matmul(P[2 + tc][0:1, :], ones[:, 0:1], hq[:, tc * 512:(tc + 1) * 512],
                                start=(ht == 0), stop=(ht == NHT - 1))
            return inst
        seq.comp("tensor", stm)
        seq.dma("scalar", hscr.ap()[ht * 128:(ht + 1) * 128, :], hsb[:, :])

    def rstd_fn(s):
        for tc in range(4):
            s.activation(out=strow[0:1, tc * 512:(tc + 1) * 512], in_=P[2 + tc][0:1, :],
                         func=AF.Ln, bias=cst[0:1, 0:1], scale=1.0 / H0)
            inst = s.activation(out=strow[0:1, tc * 512:(tc + 1) * 512],
                                in_=strow[0:1, tc * 512:(tc + 1) * 512], func=AF.Exp, scale=-0.5)
        return inst
    seq.comp("scalar", rstd_fn)

    def bc_fn(t):
        for tc in range(4):
            inst = t.matmul(P[tc][:, :], ones1[:], strow[0:1, tc * 512:(tc + 1) * 512],
                            start=True, stop=True)
        return inst
    seq.comp("tensor", bc_fn)

    def bc_ev(v):
        for tc in range(4):
            inst = v.tensor_copy(rstdB[:, tc * 512:(tc + 1) * 512], P[tc][:, :])
        return inst
    seq.comp("vector", bc_ev)

    for tc in range(4):
        cs = slice(tc * 512, (tc + 1) * 512)
        for g in range(2):
            for k in range(NHT):
                seq.dma("sync", kch[:, 0:512], hscr.ap()[k * 128:(k + 1) * 128, cs])
                for c0w, wdw, srcw in wpr_srcs(k, g):
                    seq.dma("sync", wch[:, c0w:c0w + wdw], srcw)
                seq.comp("vector", lambda v, tc=tc: v.tensor_tensor(
                    kch[:, 512:1024], kch[:, 0:512], rstdB[:, tc * 512:(tc + 1) * 512], Op.mult))

                def mmd(t, k=k):
                    for j in range(8):
                        inst = t.matmul(P[j][:, :], wch[:, j * 128:(j + 1) * 128], kch[:, 512:1024],
                                        start=(k == 0), stop=False)
                    return inst
                seq.comp("tensor", mmd)
            for k in range(16):
                seq.dma("sync", kch[:, 0:512], y_in.rearrange("(kk p) t -> p kk t", p=128)[:, k, cs])
                seq.dma("sync", wch[:], opw_in[k * 128:(k + 1) * 128, g * 1024:(g + 1) * 1024])

                def mma(t, k=k):
                    for j in range(8):
                        inst = t.matmul(P[j][:, :], wch[:, j * 128:(j + 1) * 128], kch[:, 0:512],
                                        start=False, stop=(k == 15))
                    return inst
                seq.comp("tensor", mma)
            for j in range(8):
                dout = g * 8 + j
                seq.dma("sync", xres[:, :], xfm_in.rearrange("(kk p) t -> p kk t", p=128)[:, dout, cs])
                seq.comp("scalar", lambda s, j=j, dout=dout: s.activation(
                    out=ob[:, :], in_=P[j][:, :], func=AF.Identity, bias=opb[:, 2 * dout:2 * dout + 1]))
                seq.comp("vector", lambda v: v.tensor_tensor(ob[:, :], ob[:, :], xres[:, :], Op.add))
                seq.dma("scalar", xo_out.rearrange("(kk p) t -> p kk t", p=128)[:, dout, cs], ob[:, :])

    es.close()


def build_a3():
    nc = bass.Bass("TRN2", target_bir_lowering=False, debug=False)
    es = ExitStack()
    W = TSH
    H0 = 8192
    xfm_in = nc.dram_tensor("xfm", [D, W], F32, kind="ExternalInput").ap()
    xn_in = nc.dram_tensor("xn", [D, W], BF16, kind="ExternalInput").ap()
    y_in = nc.dram_tensor("yv", [D, W], BF16, kind="ExternalInput").ap()
    opw_in = nc.dram_tensor("opwT", [D, D], BF16, kind="ExternalInput").ap()
    opb_in = nc.dram_tensor("opb", [D, 2], F32, kind="ExternalInput").ap()
    wgu_in = nc.dram_tensor("wguT", [D, 2 * H0], BF16, kind="ExternalInput").ap()
    wpr_in = nc.dram_tensor("wprT", [H0, D], BF16, kind="ExternalInput").ap()
    o1_in = nc.dram_tensor("ones1", [1, 128], F32, kind="ExternalInput").ap()
    xo_out = nc.dram_tensor("xo_o", [D, W], F32, kind="ExternalOutput").ap()
    hscr = nc.dram_tensor("hscr", [H0, W], BF16)


    seq = Seq(nc, es, "a3seq")
    io = dict(xfm_in=xfm_in, xn_in=xn_in, y_in=y_in, opw_in=opw_in, opb_in=opb_in, wgu_in=wgu_in, wpr_in=wpr_in, o1_in=o1_in, xo_out=xo_out, hscr=hscr)
    emit_a3(nc, seq, io)
    seq.finish()
    block = es.enter_context(nc.Block())
    for en in ["sync", "vector", "scalar", "gpsimd", "tensor"]:
        getattr(block, en)(lambda e, en=en: seq.emit(en, e))
    es.close()
    return nc


def emit_b1(nc, seq, io):
    es = ExitStack()
    W = TSH
    xfm_in = io["xfm_in"]
    wq_in = io["wq_in"]
    wk_in = io["wk_in"]
    wv_in = io["wv_in"]
    o1_in = io["o1_in"]
    xn_out = io["xn_out"]
    q_out = io["q_out"]
    k_out = io["k_out"]
    v_out = io["v_out"]
    lf_out = io["lf_out"]
    sbuf = nc.sbuf_tensor
    xsb = es.enter_context(sbuf("xsb", [128, 16 * 512], F32))
    sq = es.enter_context(sbuf("sq", [128, 16 * 512], F32))
    xnr = es.enter_context(sbuf("xnr", [128, 16 * W], BF16))
    strow = es.enter_context(sbuf("strow", [1, 3 * 512], F32))
    ones = es.enter_context(sbuf("ones", [128, 2], F32))
    ones1 = es.enter_context(sbuf("ones1_s", [1, 128], F32))
    cst = es.enter_context(sbuf("cst", [1, 2], F32))
    wsb2 = [es.enter_context(sbuf(f"wsb{i}", [128, 16 * 128], BF16)) for i in range(2)]
    fsb = es.enter_context(sbuf("fsb", [128, W], F32))
    s2sb = es.enter_context(sbuf("s2sb", [128, W], F32))
    obuf2 = [es.enter_context(sbuf(f"obuf{i}", [128, W], BF16)) for i in range(2)]
    ofp16 = es.enter_context(sbuf("ofp16", [128, W], FP16))
    pst0 = es.enter_context(nc.psum_tensor("pst0", [1, 512], F32))
    pst1 = es.enter_context(nc.psum_tensor("pst1", [1, 512], F32))
    pbc = es.enter_context(nc.psum_tensor("pbc", [128, 1024], F32))
    pu = [es.enter_context(nc.psum_tensor(f"pu{i}", [128, 512], F32)) for i in range(4)]

    seq.dma("sync", ones1[:], o1_in[:])
    seq.comp("vector", lambda v: v.memset(ones[:], 1.0))
    seq.comp("vector", lambda v: v.memset(cst[:, 0:1], 1e-5))

    for bi in range(4):
        t0 = bi * 512
        seq.dma("sync", xsb[:].rearrange("p (k x) -> p k x", x=512),
                xfm_in.rearrange("(k p) t -> p k t", p=128)[:, :, t0:t0 + 512])
        seq.comp("scalar", lambda s: s.activation(out=sq[:], in_=xsb[:], func=AF.Square))

        def st_fn(t):
            for k in range(16):
                t.matmul(pst0[:, :], ones[:, 0:1], xsb[:, k * 512:(k + 1) * 512],
                         start=(k == 0), stop=(k == 15))
            for k in range(16):
                inst = t.matmul(pst1[:, :], ones[:, 0:1], sq[:, k * 512:(k + 1) * 512],
                                start=(k == 0), stop=(k == 15))
            return inst
        seq.comp("tensor", st_fn)

        def row_fn(v):
            m = strow[0:1, 0:512]
            var = strow[0:1, 512:1024]
            v.tensor_scalar(m, pst0[:, :], 1.0 / D, None, Op.mult)
            v.tensor_tensor(var, m, m, Op.mult)
            return v.scalar_tensor_tensor(var, pst1[:, :], 1.0 / D, var, Op.mult, Op.subtract)
        seq.comp("vector", row_fn)

        def rstd_fn(s):
            vv = strow[0:1, 512:1024]
            s.activation(out=vv, in_=vv, func=AF.Ln, bias=cst[0:1, 0:1])
            return s.activation(out=strow[0:1, 1024:1536], in_=vv, func=AF.Exp, scale=-0.5)
        seq.comp("scalar", rstd_fn)
        seq.comp("vector", lambda v: v.tensor_tensor(
            strow[0:1, 0:512], strow[0:1, 0:512], strow[0:1, 1024:1536], Op.mult))

        def bc_fn(t):
            t.matmul(pbc[:, 0:512], ones1[:], strow[0:1, 1024:1536], start=True, stop=True)
            return t.matmul(pbc[:, 512:1024], ones1[:], strow[0:1, 0:512], start=True, stop=True)
        seq.comp("tensor", bc_fn)

        def ap_fn(v, t0=t0):
            for k in range(16):
                v.tensor_tensor(sq[:, k * 512:(k + 1) * 512], xsb[:, k * 512:(k + 1) * 512],
                                pbc[:, 0:512], Op.mult)
                inst = v.tensor_tensor(xnr[:, k * W + t0:k * W + t0 + 512],
                                       sq[:, k * 512:(k + 1) * 512], pbc[:, 512:1024], Op.subtract)
            return inst
        seq.comp("vector", ap_fn)
    seq.dma("scalar", xn_out.rearrange("(k p) t -> p k t", p=128),
            xnr[:].rearrange("p (k t) -> p k t", t=W))

    xn_done = len(seq.steps) - 1
    B1 = {"mm": [None, None], "evp": xn_done, "ob": [xn_done, xn_done],
          "st": [None, None], "lf": None, "ts": xn_done, "ti": 0}

    def proj(w_in_ap, dtile, emit):
        ti = B1["ti"]
        p = ti % 2
        wsbp = wsb2[p]
        ldid = seq.dma("sync", wsbp[:].rearrange("p (k m) -> p k m", m=128),
                       w_in_ap.rearrange("(k p) m -> p k m", p=128)[:, :, dtile * 128:(dtile + 1) * 128],
                       deps=[B1["mm"][p]], lane=f"ws{p}")

        def mm(t, wsbp=wsbp):
            for c in range(4):
                for k in range(16):
                    inst = t.matmul(pu[c][:, :], wsbp[:, k * 128:(k + 1) * 128],
                                    xnr[:, k * W + c * 512:k * W + (c + 1) * 512],
                                    start=(k == 0), stop=(k == 15))
            return inst
        mmid = seq.comp("tensor", mm, deps=[ldid, B1["evp"], xn_done])
        B1["mm"][p] = mmid
        emit(dtile, mmid, ti)
        B1["ti"] += 1

    for dt_ in range(16):
        def q_emit(dtile, mmid, ti):
            ob = obuf2[ti % 2]

            def ev(s, ob=ob):
                for c in range(4):
                    inst = s.activation(out=ob[:, c * 512:(c + 1) * 512], in_=pu[c][:, :], func=AF.Silu)
                return inst
            evid = seq.comp("scalar", ev, deps=[mmid, B1["st"][ti % 2]])
            B1["evp"] = evid
            B1["st"][ti % 2] = seq.dma("scalar", q_out[dtile * 128:(dtile + 1) * 128, :], ob[:],
                                       deps=[evid], lane=f"ob{ti % 2}")
        proj(wq_in, dt_, q_emit)

        def k_emit(dtile, mmid, ti):
            ob = obuf2[ti % 2]

            def ev(s):
                for c in range(4):
                    inst = s.activation(out=fsb[:, c * 512:(c + 1) * 512], in_=pu[c][:, :], func=AF.Sigmoid)
                return inst
            evid = seq.comp("scalar", ev, deps=[mmid, B1["ts"]])
            B1["evp"] = evid
            s2 = seq.comp("scalar", lambda s: s.activation(out=s2sb[:], in_=fsb[:], func=AF.Sigmoid),
                          deps=[evid])
            ts = seq.comp("vector", lambda v, ob=ob: v.tensor_scalar(
                ob[:], s2sb[:], -1.0, 1.0, Op.mult, Op.add), deps=[s2, B1["st"][ti % 2]])
            B1["ts"] = ts
            B1["st"][ti % 2] = seq.dma("scalar", k_out[dtile * 128:(dtile + 1) * 128, :], ob[:],
                                       deps=[ts], lane=f"ob{ti % 2}")
            ln = seq.comp("scalar", lambda s: s.activation(out=ofp16[:], in_=s2sb[:], func=AF.Ln),
                          deps=[s2, B1["lf"]])
            B1["lf"] = seq.dma("scalar", lf_out[dtile * 128:(dtile + 1) * 128, :], ofp16[:],
                               deps=[ln], lane="lf")
        proj(wk_in, dt_, k_emit)

        def v_emit(dtile, mmid, ti):
            ob = obuf2[ti % 2]

            def ev(s, ob=ob):
                for c in range(4):
                    inst = s.activation(out=ob[:, c * 512:(c + 1) * 512], in_=pu[c][:, :], func=AF.Identity)
                return inst
            evid = seq.comp("scalar", ev, deps=[mmid, B1["st"][ti % 2]])
            B1["evp"] = evid
            B1["st"][ti % 2] = seq.dma("scalar", v_out[dtile * 128:(dtile + 1) * 128, :], ob[:],
                                       deps=[evid], lane=f"ob{ti % 2}")
        proj(wv_in, dt_, v_emit)
    seq.barrier()

    es.close()


def build_b1():
    nc = bass.Bass("TRN2", target_bir_lowering=False, debug=False)
    es = ExitStack()
    W = TSH
    xfm_in = nc.dram_tensor("xfm", [D, W], F32, kind="ExternalInput").ap()
    wq_in = nc.dram_tensor("wqT", [D, D], BF16, kind="ExternalInput").ap()
    wk_in = nc.dram_tensor("wkT", [D, D], BF16, kind="ExternalInput").ap()
    wv_in = nc.dram_tensor("wvT", [D, D], BF16, kind="ExternalInput").ap()
    o1_in = nc.dram_tensor("ones1", [1, 128], F32, kind="ExternalInput").ap()
    xn_out = nc.dram_tensor("xn_o", [D, W], BF16, kind="ExternalOutput").ap()
    q_out = nc.dram_tensor("q_o", [D, W], BF16, kind="ExternalOutput").ap()
    k_out = nc.dram_tensor("k_o", [D, W], BF16, kind="ExternalOutput").ap()
    v_out = nc.dram_tensor("v_o", [D, W], BF16, kind="ExternalOutput").ap()
    lf_out = nc.dram_tensor("lf_o", [D, W], FP16, kind="ExternalOutput").ap()


    seq = Seq(nc, es, "b1seq")
    io = dict(xfm_in=xfm_in, wq_in=wq_in, wk_in=wk_in, wv_in=wv_in, o1_in=o1_in, xn_out=xn_out, q_out=q_out, k_out=k_out, v_out=v_out, lf_out=lf_out)
    emit_b1(nc, seq, io)
    seq.finish()
    block = es.enter_context(nc.Block())
    for en in ["sync", "vector", "scalar", "gpsimd", "tensor"]:
        getattr(block, en)(lambda e, en=en: seq.emit(en, e))
    es.close()
    return nc


def emit_b2(nc, seq, io):
    es = ExitStack()
    CH = 64
    NCH = T // CH
    E = 128
    q_in = io["q_in"]
    k_in = io["k_in"]
    ktm_in = io["ktm_in"]
    vtm_in = io["vtm_in"]
    lftm_in = io["lftm_in"]
    utri_in = io["utri_in"]
    mtri_in = io["mtri_in"]
    o_out = io["o_out"]
    sbuf = nc.sbuf_tensor
    qe = es.enter_context(sbuf("qe", [128, T], BF16))
    ke = es.enter_context(sbuf("ke", [128, T], BF16))
    ktm = es.enter_context(sbuf("ktm_s", [64, 8 * E], BF16))
    vtm = es.enter_context(sbuf("vtm_s", [64, 8 * E], BF16))
    lftm = es.enter_context(sbuf("lftm_s", [64, 8 * E], FP16))
    utri = es.enter_context(sbuf("utri_s", [CH, CH], FP16))
    mtri = es.enter_context(sbuf("mtri_s", [CH, CH], BF16))
    expp = es.enter_context(sbuf("expp", [128, CH], BF16))
    expn = es.enter_context(sbuf("expn", [128, CH], BF16))
    expntm = es.enter_context(sbuf("expntm", [64, E], BF16))
    qdec = es.enter_context(sbuf("qdec", [128, CH], BF16))
    kdec = es.enter_context(sbuf("kdec", [128, CH], BF16))
    kdtm = es.enter_context(sbuf("kdtm", [64, E], BF16))
    scst = es.enter_context(sbuf("scst", [64, CH], BF16))
    Ssb = es.enter_context(sbuf("Ssb", [128, E], BF16))
    Stmp = es.enter_context(sbuf("Stmp", [128, E], F32))
    eal = es.enter_context(sbuf("eal", [128, 2], F32))
    obuf = es.enter_context(sbuf("obuf", [64, 8 * E], BF16))
    pAc1 = es.enter_context(nc.psum_tensor("pAc1", [64, E], F32))
    pAc2 = es.enter_context(nc.psum_tensor("pAc2", [128, CH], F32))
    psc = es.enter_context(nc.psum_tensor("psc", [64, CH], F32))
    po = es.enter_context(nc.psum_tensor("po", [64, E], F32))
    pS = es.enter_context(nc.psum_tensor("pS", [128, E], F32))

    seq.dma("sync", utri[:], utri_in[:])
    seq.dma("sync", mtri[:], mtri_in[:])

    for pi in range(4):
        def pair_of(core_pi):
            return core_pi
        pass

    for pi in range(4):
        seq.dma("sync", qe[:], q_in[pi * 128:(pi + 1) * 128, 0:T])
        seq.dma("sync", ke[:], k_in[pi * 128:(pi + 1) * 128, 0:T])
        seq.comp("vector", lambda v: v.memset(Ssb[:], 0.0))
        for c8 in range(NCH // 8):
            rows = slice(pi * T + c8 * 8 * CH, pi * T + (c8 + 1) * 8 * CH)
            seq.dma("sync", ktm[:].rearrange("p (c e) -> p c e", e=E),
                    ktm_in[rows, :].rearrange("(c p) e -> p c e", p=64))
            seq.dma("sync", vtm[:].rearrange("p (c e) -> p c e", e=E),
                    vtm_in[rows, :].rearrange("(c p) e -> p c e", p=64))
            seq.dma("sync", lftm[:].rearrange("p (c e) -> p c e", e=E),
                    lftm_in[rows, :].rearrange("(c p) e -> p c e", p=64))
            for cs_ in range(8):
                c = c8 * 8 + cs_
                lfc = lftm[:, cs_ * E:(cs_ + 1) * E]

                def mmA(t, lfc=lfc):
                    t.matmul(pAc1[:, :], utri[:], lfc, start=True, stop=True)
                    return t.matmul(pAc2[:, :], lfc, utri[:], start=True, stop=True)
                seq.comp("tensor", mmA)

                def evA(s):
                    s.activation(out=expp[:, :], in_=pAc2[:, :], func=AF.Exp)
                    s.activation(out=expn[:, :], in_=pAc2[:, :], func=AF.Exp, scale=-1.0)
                    return s.activation(out=expntm[:, :], in_=pAc1[:, :], func=AF.Exp, scale=-1.0)
                seq.comp("scalar", evA)

                def dec(v, c=c, cs_=cs_):
                    v.tensor_copy(eal[:, 0:1], expp[:, CH - 1:CH])
                    v.tensor_tensor(qdec[:], qe[:, c * CH:(c + 1) * CH], expp[:], Op.mult)
                    v.tensor_tensor(kdec[:], ke[:, c * CH:(c + 1) * CH], expn[:], Op.mult)
                    return v.tensor_tensor(kdtm[:], ktm[:, cs_ * E:(cs_ + 1) * E], expntm[:], Op.mult)
                seq.comp("vector", dec)
                seq.comp("tensor", lambda t: t.matmul(psc[:, :], kdec[:], qdec[:], start=True, stop=True))
                seq.comp("vector", lambda v: v.tensor_tensor(scst[:], psc[:, :], mtri[:], Op.mult))

                def mmo(t, cs_=cs_):
                    t.matmul(po[:, :], scst[:], vtm[:, cs_ * E:(cs_ + 1) * E], start=True, stop=False)
                    t.matmul(po[:, :], qdec[:], Ssb[:], start=False, stop=True)
                    return t.matmul(pS[:, :], kdtm[:], vtm[:, cs_ * E:(cs_ + 1) * E], start=True, stop=True)
                seq.comp("tensor", mmo)

                def upd(v, cs_=cs_):
                    v.tensor_tensor(Stmp[:], Ssb[:], pS[:, :], Op.add)
                    v.tensor_scalar(Ssb[:], Stmp[:], eal[:, 0:1], None, Op.mult)
                    return v.tensor_copy(obuf[:, cs_ * E:(cs_ + 1) * E], po[:, :])
                seq.comp("vector", upd)
            seq.dma("scalar", o_out[pi, c8 * 8 * CH:(c8 + 1) * 8 * CH, :].rearrange("(c p) e -> p c e", p=64),
                    obuf[:].rearrange("p (c e) -> p c e", e=E))

    es.close()


def build_b2():
    nc = bass.Bass("TRN2", target_bir_lowering=False, debug=False)
    es = ExitStack()
    CH = 64
    NCH = T // CH
    E = 128
    q_in = nc.dram_tensor("qfm", [512, T], BF16, kind="ExternalInput").ap()
    k_in = nc.dram_tensor("kfm", [512, T], BF16, kind="ExternalInput").ap()
    ktm_in = nc.dram_tensor("ktm", [4 * T, 128], BF16, kind="ExternalInput").ap()
    vtm_in = nc.dram_tensor("vtm", [4 * T, 128], BF16, kind="ExternalInput").ap()
    lftm_in = nc.dram_tensor("lftm", [4 * T, 128], FP16, kind="ExternalInput").ap()
    utri_in = nc.dram_tensor("utri", [CH, CH], FP16, kind="ExternalInput").ap()
    mtri_in = nc.dram_tensor("mtri", [CH, CH], BF16, kind="ExternalInput").ap()
    o_out = nc.dram_tensor("o_o", [4, T, E], BF16, kind="ExternalOutput").ap()


    seq = Seq(nc, es, "b2seq")
    io = dict(q_in=q_in, k_in=k_in, ktm_in=ktm_in, vtm_in=vtm_in, lftm_in=lftm_in, utri_in=utri_in, mtri_in=mtri_in, o_out=o_out)
    emit_b2(nc, seq, io)
    seq.finish()
    block = es.enter_context(nc.Block())
    for en in ["sync", "vector", "scalar", "gpsimd", "tensor"]:
        getattr(block, en)(lambda e, en=en: seq.emit(en, e))
    es.close()
    return nc


def b2_host_inputs(q_fm, k_fm, k_tm, v_tm, lf_tm):
    utri = np.triu(np.ones((64, 64))).astype(np.float16)
    mtri = np.triu(np.ones((64, 64))).astype(bf16)
    maps = []
    for c in range(NCORES):
        qs, ks = [], []
        ktms, vtms, lftms = [], [], []
        for pi in range(4):
            p = 4 * c + pi
            b, h = p // 16, p % 16
            rows = slice(h * 128, (h + 1) * 128)
            cols = slice(b * T, (b + 1) * T)
            trows = slice(b * T, (b + 1) * T)
            tcols = slice(h * 128, (h + 1) * 128)
            qs.append(q_fm[rows, cols])
            ks.append(k_fm[rows, cols])
            ktms.append(k_tm[trows, tcols])
            vtms.append(v_tm[trows, tcols])
            lftms.append(lf_tm[trows, tcols])
        maps.append(dict(
            qfm=np.ascontiguousarray(np.concatenate(qs, 0)),
            kfm=np.ascontiguousarray(np.concatenate(ks, 0)),
            ktm=np.ascontiguousarray(np.concatenate(ktms, 0)),
            vtm=np.ascontiguousarray(np.concatenate(vtms, 0)),
            lftm=np.ascontiguousarray(np.concatenate(lftms, 0)),
            utri=utri, mtri=mtri))
    return maps


def emit_b2f(nc, seq, io):
    es = ExitStack()
    CH = 64
    NCH = T // CH
    E = 128
    W = TSH
    q_all = io["q_all"]
    k_all = io["k_all"]
    v_all = io["v_all"]
    lf_all = io["lf_all"]
    utri_in = io["utri_in"]
    mtri_in = io["mtri_in"]
    i128_in = io["i128_in"]
    o_out = io["o_out"]
    sbuf = nc.sbuf_tensor
    qe = es.enter_context(sbuf("qe", [128, T], BF16))
    ke = es.enter_context(sbuf("ke", [128, T], BF16))
    ve = es.enter_context(sbuf("ve", [128, T], BF16))
    lfe = es.enter_context(sbuf("lfe", [128, T], FP16))
    ofm = es.enter_context(sbuf("ofm", [128, T], BF16))
    lsc = es.enter_context(sbuf("lsc", [128, 512], F32))
    iden = es.enter_context(sbuf("iden_s", [128, 128], BF16))
    idf = es.enter_context(sbuf("idf_s", [128, 128], F32))
    ktm2 = [es.enter_context(sbuf(f"ktm_s{i}", [64, 8 * E], BF16)) for i in range(2)]
    vtm2 = [es.enter_context(sbuf(f"vtm_s{i}", [64, 8 * E], BF16)) for i in range(2)]
    lftm2 = [es.enter_context(sbuf(f"lftm_s{i}", [64, 8 * E], FP16)) for i in range(2)]
    utri = es.enter_context(sbuf("utri_s", [CH, CH], FP16))
    mtri = es.enter_context(sbuf("mtri_s", [CH, CH], BF16))
    expp = es.enter_context(sbuf("expp", [128, CH], BF16))
    expn = es.enter_context(sbuf("expn", [128, CH], BF16))
    expntm = es.enter_context(sbuf("expntm", [64, E], BF16))
    qdec = es.enter_context(sbuf("qdec", [128, CH], BF16))
    kdec = es.enter_context(sbuf("kdec", [128, CH], BF16))
    kdtm = es.enter_context(sbuf("kdtm", [64, E], BF16))
    scst = es.enter_context(sbuf("scst", [64, CH], BF16))
    Ssb = es.enter_context(sbuf("Ssb", [128, E], BF16))
    Stmp = es.enter_context(sbuf("Stmp", [128, E], F32))
    eal = es.enter_context(sbuf("eal", [128, 2], F32))
    obuf = es.enter_context(sbuf("obuf", [64, 8 * E], BF16))
    pAc1 = es.enter_context(nc.psum_tensor("pAc1", [64, E], F32))
    pAc2 = es.enter_context(nc.psum_tensor("pAc2", [128, CH], F32))
    psc = es.enter_context(nc.psum_tensor("psc", [64, CH], F32))
    po = es.enter_context(nc.psum_tensor("po", [64, E], F32))
    pS = es.enter_context(nc.psum_tensor("pS", [128, E], F32))
    ptK = es.enter_context(nc.psum_tensor("ptK", [64, 1024], BF16))
    ptL = es.enter_context(nc.psum_tensor("ptL", [64, 512], F32))
    ptO = es.enter_context(nc.psum_tensor("ptO", [128, 1024], BF16))

    seq.dma("sync", utri[:], utri_in[:])
    seq.dma("sync", mtri[:], mtri_in[:])
    seq.dma("sync", iden[:], i128_in[:])
    seq.comp("vector", lambda v: v.tensor_copy(idf[:], iden[:]))

    for pi in range(4):
        def dmas(c, pi=pi):
            h = (4 * c + pi) % 16
            L = []
            for jb in range(4):
                r0 = jb * D + h * 128
                cw = slice(jb * W, (jb + 1) * W)
                L.append((qe[:, cw], q_all[r0:r0 + 128, :]))
                L.append((ke[:, cw], k_all[r0:r0 + 128, :]))
                L.append((ve[:, cw], v_all[r0:r0 + 128, :]))
                L.append((lfe[:, cw], lf_all[r0:r0 + 128, :]))
            return L
        ldid = _sw_dmas(seq, "sync", 16, dmas)
        mz = seq.comp("vector", lambda v: v.memset(Ssb[:], 0.0), deps=[ldid])
        PR = {"ktmrd": mz, "vtmrd": mz, "lftmrd": mz, "ptl": mz,
              "kcp": mz, "vcp": mz, "lev": [mz, mz], "pto": mz, "ofmc": mz,
              "evA": mz, "dec": mz, "scst": mz, "mmo": mz, "upd": mz, "psc": mz,
              "mmA": mz}
        for c8 in range(NCH // 8):
            b0 = c8 * 512
            cb = c8 % 2
            ktm, vtm, lftm = ktm2[cb], vtm2[cb], lftm2[cb]

            def trk(t, b0=b0):
                for i in range(8):
                    inst = t.transpose(ptK[:, i * 128:(i + 1) * 128],
                                       ke[:, b0 + i * 64:b0 + (i + 1) * 64], iden[:])
                return inst
            tk = seq.comp("tensor", trk, deps=[ldid, PR["vcp"]])
            kc = seq.comp("vector", lambda v, ktm=ktm: v.tensor_copy(ktm[:], ptK[:]),
                          deps=[tk, PR["ktmrd"]])
            PR["kcp"] = kc

            def trv(t, b0=b0):
                for i in range(8):
                    inst = t.transpose(ptK[:, i * 128:(i + 1) * 128],
                                       ve[:, b0 + i * 64:b0 + (i + 1) * 64], iden[:])
                return inst
            tv = seq.comp("tensor", trv, deps=[kc])
            vc = seq.comp("vector", lambda v, vtm=vtm: v.tensor_copy(vtm[:], ptK[:]),
                          deps=[tv, PR["vtmrd"]])
            PR["vcp"] = vc
            lc = seq.comp("vector", lambda v, b0=b0: v.tensor_copy(lsc[:], lfe[:, b0:b0 + 512]),
                          deps=[ldid, PR["lev"][0], PR["lev"][1]])
            levs = []
            for hf in range(2):
                def trl(t, hf=hf):
                    for i in range(4):
                        ii = hf * 4 + i
                        inst = t.transpose(ptL[:, i * 128:(i + 1) * 128],
                                           lsc[:, ii * 64:(ii + 1) * 64], idf[:])
                    return inst
                tl = seq.comp("tensor", trl, deps=[lc, levs[0] if hf else PR["lev"][1]])
                lev = seq.comp("scalar", lambda s, hf=hf, lftm=lftm: s.activation(
                    out=lftm[:, hf * 512:(hf + 1) * 512], in_=ptL[:], func=AF.Identity),
                    deps=[tl, PR["lftmrd"]])
                levs.append(lev)
            PR["lev"] = levs
            tm_ready = [kc, vc] + levs
            for cs_ in range(8):
                c = c8 * 8 + cs_
                lfc = lftm[:, cs_ * E:(cs_ + 1) * E]

                def mmA(t, lfc=lfc):
                    t.matmul(pAc1[:, :], utri[:], lfc, start=True, stop=True)
                    return t.matmul(pAc2[:, :], lfc, utri[:], start=True, stop=True)
                ma = seq.comp("tensor", mmA, deps=tm_ready + [PR["evA"]])
                PR["mmA"] = ma

                def evA(s):
                    s.activation(out=expp[:, :], in_=pAc2[:, :], func=AF.Exp)
                    s.activation(out=expn[:, :], in_=pAc2[:, :], func=AF.Exp, scale=-1.0)
                    return s.activation(out=expntm[:, :], in_=pAc1[:, :], func=AF.Exp, scale=-1.0)
                ea = seq.comp("scalar", evA, deps=[ma, PR["dec"]])
                PR["evA"] = ea

                def dec(v, c=c, cs_=cs_, ktm=ktm):
                    v.tensor_copy(eal[:, 0:1], expp[:, CH - 1:CH])
                    v.tensor_tensor(qdec[:], qe[:, c * CH:(c + 1) * CH], expp[:], Op.mult)
                    v.tensor_tensor(kdec[:], ke[:, c * CH:(c + 1) * CH], expn[:], Op.mult)
                    return v.tensor_tensor(kdtm[:], ktm[:, cs_ * E:(cs_ + 1) * E], expntm[:], Op.mult)
                dc = seq.comp("vector", dec, deps=[ea, PR["mmo"]])
                PR["dec"] = dc
                ps = seq.comp("tensor", lambda t: t.matmul(psc[:, :], kdec[:], qdec[:],
                                                           start=True, stop=True),
                              deps=[dc, PR["scst"]])
                PR["psc"] = ps
                st = seq.comp("vector", lambda v: v.tensor_tensor(scst[:], psc[:, :], mtri[:], Op.mult),
                              deps=[ps, PR["mmo"]])
                PR["scst"] = st

                def mmo(t, cs_=cs_, vtm=vtm):
                    t.matmul(po[:, :], scst[:], vtm[:, cs_ * E:(cs_ + 1) * E], start=True, stop=False)
                    t.matmul(po[:, :], qdec[:], Ssb[:], start=False, stop=True)
                    return t.matmul(pS[:, :], kdtm[:], vtm[:, cs_ * E:(cs_ + 1) * E], start=True, stop=True)
                mo = seq.comp("tensor", mmo, deps=[st, PR["upd"]])
                PR["mmo"] = mo

                def upd(v, cs_=cs_):
                    v.tensor_tensor(Stmp[:], Ssb[:], pS[:, :], Op.add)
                    v.tensor_scalar(Ssb[:], Stmp[:], eal[:, 0:1], None, Op.mult)
                    return v.tensor_copy(obuf[:, cs_ * E:(cs_ + 1) * E], po[:, :])
                up = seq.comp("vector", upd, deps=[mo, PR["pto"]])
                PR["upd"] = up
            PR["ktmrd"] = PR["dec"]
            PR["vtmrd"] = PR["mmo"]
            PR["lftmrd"] = PR["mmA"]

            def tro(t):
                for i in range(8):
                    inst = t.transpose(ptO[:, i * 128:i * 128 + 64],
                                       obuf[:, i * 128:(i + 1) * 128], iden[0:64, 0:64])
                return inst
            to = seq.comp("tensor", tro, deps=[PR["upd"], PR["ofmc"]])
            PR["pto"] = to
            oc = seq.comp("vector", lambda v, b0=b0: v.tensor_copy(
                ofm[:, b0:b0 + 512].rearrange("p (i z) -> p i z", z=64),
                ptO[:].rearrange("p (i z) -> p i z", z=128)[:, :, 0:64]), deps=[to])
            PR["ofmc"] = oc
        seq.dma("scalar", o_out[pi * 128:(pi + 1) * 128, :], ofm[:], deps=[PR["ofmc"]], lane="b2o")
        seq.barrier()

    es.close()


def emit_b3(nc, seq, io):
    es = ExitStack()
    W = TSH
    H1 = 8192
    xfm_in = io["xfm_in"]
    xn_in = io["xn_in"]
    o_in = io.get("o_in")
    wo_in = io["wo_in"]
    nw_in = io["nw_in"]
    wfc_in = io.get("wfc_in")
    wpr_in = io.get("wpr_in")
    o1_in = io["o1_in"]
    xo_out = io["xo_out"]
    hscr = io["hscr"]
    wfc_src = io.get("wfc_src") or (
        lambda m0: wfc_in.rearrange("(k p) m -> p k m", p=128)[:, :, m0:m0 + 128])
    wpr_srcs = io.get("wpr_srcs") or (
        lambda k, g: [(0, 1024, wpr_in[k * 128:(k + 1) * 128, g * 1024:(g + 1) * 1024])])
    onr_fill = io.get("onr_fill")
    sbuf = nc.sbuf_tensor
    xnr = es.enter_context(sbuf("xnr", [128, 16 * W], BF16))
    onr = es.enter_context(sbuf("onr", [128, 16 * W], BF16))
    wg = es.enter_context(sbuf("wg", [128, 16 * 128], BF16))
    gsb = es.enter_context(sbuf("gsb", [128, 512], F32))
    hsb = es.enter_context(sbuf("hsb", [128, 2048], BF16))
    hq = es.enter_context(sbuf("hq", [128, 2048], F32))
    rstdB = es.enter_context(sbuf("rstdB", [128, 2048], F32))
    strow = es.enter_context(sbuf("strow", [1, 2048], F32))
    ones = es.enter_context(sbuf("ones", [128, 2], F32))
    ones1 = es.enter_context(sbuf("ones1_s", [1, 128], F32))
    cst = es.enter_context(sbuf("cst", [1, 4], F32))
    nw = es.enter_context(sbuf("nw_s", [128, 32], F32))
    kch = es.enter_context(sbuf("kch", [128, 1024], BF16))
    wch = es.enter_context(sbuf("wch", [128, 8 * 128], BF16))
    xres = es.enter_context(sbuf("xres", [128, 512], F32))
    ob = es.enter_context(sbuf("ob", [128, 512], F32))
    P = [es.enter_context(nc.psum_tensor(f"pp{i}", [128, 512], F32)) for i in range(8)]

    seq.dma("sync", ones1[:], o1_in[:])
    seq.dma("sync", nw[:].rearrange("p (h two) -> p h two", two=2),
            nw_in.rearrange("(h p) two -> p h two", p=128))
    seq.dma("sync", xnr[:].rearrange("p (k t) -> p k t", t=W),
            xn_in.rearrange("(k p) t -> p k t", p=128))
    if onr_fill is None:
        seq.dma("sync", onr[:].rearrange("p (k t) -> p k t", t=W),
                o_in.rearrange("(k p) t -> p k t", p=128))
    else:
        onr_fill(seq, onr)
    seq.comp("vector", lambda v: v.memset(ones[:], 1.0))
    seq.comp("vector", lambda v: v.memset(cst[:, 0:1], 1e-6))
    seq.comp("vector", lambda v: v.memset(cst[:, 2:3], 1.1920929e-7))

    for k in range(16):
        seq.comp("scalar", lambda s, k=k: s.activation(out=hq[:, :], in_=onr[:, k * W:(k + 1) * W],
                                                       func=AF.Square))
        def stm(t, k=k):
            for tcc in range(4):
                inst = t.matmul(P[2 + tcc][0:1, :], ones[:, 0:1], hq[:, tcc * 512:(tcc + 1) * 512],
                                start=(k == 0), stop=(k == 15))
            return inst
        seq.comp("tensor", stm)

    def rstd_fn(s):
        for tcc in range(4):
            s.activation(out=strow[0:1, tcc * 512:(tcc + 1) * 512], in_=P[2 + tcc][0:1, :],
                         func=AF.Ln, bias=cst[0:1, 0:1], scale=1.0 / D)
            inst = s.activation(out=strow[0:1, tcc * 512:(tcc + 1) * 512],
                                in_=strow[0:1, tcc * 512:(tcc + 1) * 512], func=AF.Exp, scale=-0.5)
        return inst
    seq.comp("scalar", rstd_fn)

    def bc_fn(t):
        for tcc in range(4):
            inst = t.matmul(P[tcc][:, :], ones1[:], strow[0:1, tcc * 512:(tcc + 1) * 512],
                            start=True, stop=True)
        return inst
    seq.comp("tensor", bc_fn)

    def bc_ev(v):
        for tcc in range(4):
            inst = v.tensor_copy(rstdB[:, tcc * 512:(tcc + 1) * 512], P[tcc][:, :])
        return inst
    seq.comp("vector", bc_ev)

    def apply_on(v):
        for k in range(16):
            v.tensor_tensor(onr[:, k * W:(k + 1) * W], onr[:, k * W:(k + 1) * W], rstdB[:, :], Op.mult)
            inst = v.tensor_scalar(onr[:, k * W:(k + 1) * W], onr[:, k * W:(k + 1) * W],
                                   nw[:, 2 * k:2 * k + 1], None, Op.mult)
        return inst
    seq.comp("vector", apply_on)

    NHT = H1 // 128
    for ht in range(NHT):
        seq.dma("sync", wg[:].rearrange("p (k m) -> p k m", m=128), wfc_src(ht * 128))
        for tc in range(4):
            def mmg(t, tc=tc):
                for k in range(16):
                    inst = t.matmul(P[0][:, :], wg[:, k * 128:(k + 1) * 128],
                                    xnr[:, k * W + tc * 512:k * W + (tc + 1) * 512],
                                    start=(k == 0), stop=(k == 15))
                return inst
            seq.comp("tensor", mmg)
            def ev(s, tc=tc):
                s.activation(out=gsb[:, :], in_=P[0][:, :], func=AF.Relu)
                return s.activation(out=hsb[:, tc * 512:(tc + 1) * 512], in_=gsb[:, :], func=AF.Square)
            seq.comp("scalar", ev)
        seq.comp("scalar", lambda s: s.activation(out=hq[:, :], in_=hsb[:, :], func=AF.Square))

        def stm(t, ht=ht):
            for tcc in range(4):
                inst = t.matmul(P[2 + tcc][0:1, :], ones[:, 0:1], hq[:, tcc * 512:(tcc + 1) * 512],
                                start=(ht == 0), stop=(ht == NHT - 1))
            return inst
        seq.comp("tensor", stm)
        seq.dma("scalar", hscr.ap()[ht * 128:(ht + 1) * 128, :], hsb[:, :])

    def rstd2_fn(s):
        for tcc in range(4):
            s.activation(out=strow[0:1, tcc * 512:(tcc + 1) * 512], in_=P[2 + tcc][0:1, :],
                         func=AF.Ln, bias=cst[0:1, 2:3], scale=1.0 / H1)
            inst = s.activation(out=strow[0:1, tcc * 512:(tcc + 1) * 512],
                                in_=strow[0:1, tcc * 512:(tcc + 1) * 512], func=AF.Exp, scale=-0.5)
        return inst
    seq.comp("scalar", rstd2_fn)
    seq.comp("tensor", bc_fn)
    seq.comp("vector", bc_ev)

    for tc in range(4):
        cs = slice(tc * 512, (tc + 1) * 512)
        for g in range(2):
            for k in range(NHT):
                seq.dma("sync", kch[:, 0:512], hscr.ap()[k * 128:(k + 1) * 128, cs])
                for c0w, wdw, srcw in wpr_srcs(k, g):
                    seq.dma("sync", wch[:, c0w:c0w + wdw], srcw)
                seq.comp("vector", lambda v, tc=tc: v.tensor_tensor(
                    kch[:, 512:1024], kch[:, 0:512], rstdB[:, tc * 512:(tc + 1) * 512], Op.mult))

                def mmd(t, k=k):
                    for j in range(8):
                        inst = t.matmul(P[j][:, :], wch[:, j * 128:(j + 1) * 128], kch[:, 512:1024],
                                        start=(k == 0), stop=False)
                    return inst
                seq.comp("tensor", mmd)
            for k in range(16):
                seq.dma("sync", wch[:], wo_in[k * 128:(k + 1) * 128, g * 1024:(g + 1) * 1024])

                def mma(t, k=k, tc=tc):
                    for j in range(8):
                        inst = t.matmul(P[j][:, :], wch[:, j * 128:(j + 1) * 128],
                                        onr[:, k * W + tc * 512:k * W + (tc + 1) * 512],
                                        start=False, stop=(k == 15))
                    return inst
                seq.comp("tensor", mma)
            for j in range(8):
                dout = g * 8 + j
                seq.dma("sync", xres[:, :], xfm_in.rearrange("(kk p) t -> p kk t", p=128)[:, dout, cs])
                seq.comp("vector", lambda v, j=j: v.tensor_tensor(ob[:, :], P[j][:, :], xres[:, :], Op.add))
                seq.dma("scalar", xo_out.rearrange("(kk p) t -> p kk t", p=128)[:, dout, cs], ob[:, :])

    es.close()


def build_b3():
    nc = bass.Bass("TRN2", target_bir_lowering=False, debug=False)
    es = ExitStack()
    W = TSH
    H1 = 8192
    xfm_in = nc.dram_tensor("xfm", [D, W], F32, kind="ExternalInput").ap()
    xn_in = nc.dram_tensor("xn", [D, W], BF16, kind="ExternalInput").ap()
    o_in = nc.dram_tensor("ofm", [D, W], BF16, kind="ExternalInput").ap()
    wo_in = nc.dram_tensor("woT", [D, D], BF16, kind="ExternalInput").ap()
    nw_in = nc.dram_tensor("nw", [D, 2], F32, kind="ExternalInput").ap()
    wfc_in = nc.dram_tensor("wfcT", [D, H1], BF16, kind="ExternalInput").ap()
    wpr_in = nc.dram_tensor("wprT", [H1, D], BF16, kind="ExternalInput").ap()
    o1_in = nc.dram_tensor("ones1", [1, 128], F32, kind="ExternalInput").ap()
    xo_out = nc.dram_tensor("xo_o", [D, W], F32, kind="ExternalOutput").ap()
    hscr = nc.dram_tensor("hscr", [H1, W], BF16)


    seq = Seq(nc, es, "b3seq")
    io = dict(xfm_in=xfm_in, xn_in=xn_in, o_in=o_in, wo_in=wo_in, nw_in=nw_in, wfc_in=wfc_in, wpr_in=wpr_in, o1_in=o1_in, xo_out=xo_out, hscr=hscr)
    emit_b3(nc, seq, io)
    seq.finish()
    block = es.enter_context(nc.Block())
    for en in ["sync", "vector", "scalar", "gpsimd", "tensor"]:
        getattr(block, en)(lambda e, en=en: seq.emit(en, e))
    es.close()
    return nc


def emit_c1(nc, seq, io):
    es = ExitStack()
    W = TSH
    DX = 2080
    xfm_in = io["xfm_in"]
    wx_in = io["wx_in"]
    o1_in = io["o1_in"]
    xn_out = io["xn_out"]
    xp_out = io["xp_out"]
    sbuf = nc.sbuf_tensor
    xsb = es.enter_context(sbuf("xsb", [128, 16 * 512], F32))
    sq = es.enter_context(sbuf("sq", [128, 16 * 512], F32))
    xnr = es.enter_context(sbuf("xnr", [128, 16 * W], BF16))
    strow = es.enter_context(sbuf("strow", [1, 3 * 512], F32))
    ones = es.enter_context(sbuf("ones", [128, 2], F32))
    ones1 = es.enter_context(sbuf("ones1_s", [1, 128], F32))
    cst = es.enter_context(sbuf("cst", [1, 2], F32))
    wsb = es.enter_context(sbuf("wsb", [128, 16 * 128], BF16))
    obuf = es.enter_context(sbuf("obuf", [128, W], BF16))
    pst0 = es.enter_context(nc.psum_tensor("pst0", [1, 512], F32))
    pst1 = es.enter_context(nc.psum_tensor("pst1", [1, 512], F32))
    pbc = es.enter_context(nc.psum_tensor("pbc", [128, 1024], F32))
    pu = [es.enter_context(nc.psum_tensor(f"pu{i}", [128, 512], F32)) for i in range(4)]

    seq.dma("sync", ones1[:], o1_in[:])
    seq.comp("vector", lambda v: v.memset(ones[:], 1.0))
    seq.comp("vector", lambda v: v.memset(cst[:, 0:1], 1e-5))
    for bi in range(4):
        t0 = bi * 512
        seq.dma("sync", xsb[:].rearrange("p (k x) -> p k x", x=512),
                xfm_in.rearrange("(k p) t -> p k t", p=128)[:, :, t0:t0 + 512])
        seq.comp("scalar", lambda s: s.activation(out=sq[:], in_=xsb[:], func=AF.Square))

        def st_fn(t):
            for k in range(16):
                t.matmul(pst0[:, :], ones[:, 0:1], xsb[:, k * 512:(k + 1) * 512],
                         start=(k == 0), stop=(k == 15))
            for k in range(16):
                inst = t.matmul(pst1[:, :], ones[:, 0:1], sq[:, k * 512:(k + 1) * 512],
                                start=(k == 0), stop=(k == 15))
            return inst
        seq.comp("tensor", st_fn)

        def row_fn(v):
            m = strow[0:1, 0:512]
            var = strow[0:1, 512:1024]
            v.tensor_scalar(m, pst0[:, :], 1.0 / D, None, Op.mult)
            v.tensor_tensor(var, m, m, Op.mult)
            return v.scalar_tensor_tensor(var, pst1[:, :], 1.0 / D, var, Op.mult, Op.subtract)
        seq.comp("vector", row_fn)

        def rstd_fn(s):
            vv = strow[0:1, 512:1024]
            s.activation(out=vv, in_=vv, func=AF.Ln, bias=cst[0:1, 0:1])
            return s.activation(out=strow[0:1, 1024:1536], in_=vv, func=AF.Exp, scale=-0.5)
        seq.comp("scalar", rstd_fn)
        seq.comp("vector", lambda v: v.tensor_tensor(
            strow[0:1, 0:512], strow[0:1, 0:512], strow[0:1, 1024:1536], Op.mult))

        def bc_fn(t):
            t.matmul(pbc[:, 0:512], ones1[:], strow[0:1, 1024:1536], start=True, stop=True)
            return t.matmul(pbc[:, 512:1024], ones1[:], strow[0:1, 0:512], start=True, stop=True)
        seq.comp("tensor", bc_fn)

        def ap_fn(v, t0=t0):
            for k in range(16):
                v.tensor_tensor(sq[:, k * 512:(k + 1) * 512], xsb[:, k * 512:(k + 1) * 512],
                                pbc[:, 0:512], Op.mult)
                inst = v.tensor_tensor(xnr[:, k * W + t0:k * W + t0 + 512],
                                       sq[:, k * 512:(k + 1) * 512], pbc[:, 512:1024], Op.subtract)
            return inst
        seq.comp("vector", ap_fn)
    seq.dma("scalar", xn_out.rearrange("(k p) t -> p k t", p=128),
            xnr[:].rearrange("p (k t) -> p k t", t=W))

    for dt_ in range(17):
        p = 128 if dt_ < 16 else 32
        seq.dma("sync", wsb[:, :16 * p].rearrange("p2 (k m) -> p2 k m", m=p),
                wx_in.rearrange("(k p2) m -> p2 k m", p2=128)[:, :, dt_ * 128:dt_ * 128 + p])

        def mm(t, p=p):
            for c in range(4):
                for k in range(16):
                    inst = t.matmul(pu[c][:p, :], wsb[:, k * p:(k + 1) * p],
                                    xnr[:, k * W + c * 512:k * W + (c + 1) * 512],
                                    start=(k == 0), stop=(k == 15))
            return inst
        seq.comp("tensor", mm)

        def ev(s, p=p):
            for c in range(4):
                inst = s.activation(out=obuf[:p, c * 512:(c + 1) * 512], in_=pu[c][:p, :], func=AF.Identity)
            return inst
        seq.comp("scalar", ev)
        seq.dma("scalar", xp_out[dt_ * 128:dt_ * 128 + p, :], obuf[:p, :])

    es.close()


def build_c1():
    nc = bass.Bass("TRN2", target_bir_lowering=False, debug=False)
    es = ExitStack()
    W = TSH
    DX = 2080
    xfm_in = nc.dram_tensor("xfm", [D, W], F32, kind="ExternalInput").ap()
    wx_in = nc.dram_tensor("wxT", [D, DX], BF16, kind="ExternalInput").ap()
    o1_in = nc.dram_tensor("ones1", [1, 128], F32, kind="ExternalInput").ap()
    xn_out = nc.dram_tensor("xn_o", [D, W], BF16, kind="ExternalOutput").ap()
    xp_out = nc.dram_tensor("xp_o", [DX, W], BF16, kind="ExternalOutput").ap()


    seq = Seq(nc, es, "c1seq")
    io = dict(xfm_in=xfm_in, wx_in=wx_in, o1_in=o1_in, xn_out=xn_out, xp_out=xp_out)
    emit_c1(nc, seq, io)
    seq.finish()
    block = es.enter_context(nc.Block())
    for en in ["sync", "vector", "scalar", "gpsimd", "tensor"]:
        getattr(block, en)(lambda e, en=en: seq.emit(en, e))
    es.close()
    return nc


def emit_c2(nc, seq, io):
    es = ExitStack()
    CS = 256
    NST = 16
    xpd_in = io["xpd_in"]
    xnb_in = io["xnb_in"]
    bc_in = io["bc_in"]
    alog_in = io["alog_in"]
    o1_in = io["o1_in"]
    y_out = io["y_out"]
    sbuf = nc.sbuf_tensor
    dt_ = es.enter_context(sbuf("dt_s", [128, T], F32))
    dtx = es.enter_context(sbuf("dtx", [128, T], F32))
    asb = es.enter_context(sbuf("asb", [128, T], BF16))
    bsb = es.enter_context(sbuf("bsb", [128, T], BF16))
    hsb = es.enter_context(sbuf("hsb", [128, T], BF16))
    ysb = es.enter_context(sbuf("ysb", [128, T], F32))
    Bb = es.enter_context(sbuf("Bb", [128, T], BF16))
    Cb = Bb
    brow1 = es.enter_context(sbuf("brow1", [1, T], BF16))
    crow1 = brow1
    amat = es.enter_context(sbuf("amat", [128, 64], F32))
    ones1 = es.enter_context(sbuf("ones1_s", [1, 128], BF16))
    o1f = es.enter_context(sbuf("o1f", [1, 128], F32))
    pb = [es.enter_context(nc.psum_tensor(f"pb{i}", [128, 512], F32)) for i in range(4)]

    seq.dma("sync", o1f[:], o1_in[:])
    seq.comp("vector", lambda v: v.tensor_copy(ones1[:], o1f[:]))
    seq.dma("sync", amat[:], alog_in[:])
    seq.comp("scalar", lambda s: s.activation(out=amat[:], in_=amat[:], func=AF.Exp))
    seq.comp("vector", lambda v: v.tensor_scalar(amat[:], amat[:], -1.0, None, Op.mult))

    def bcast_row(row_idx, dst):
        rsrc = brow1 if row_idx < 16 else crow1
        seq.dma("sync", rsrc[:], bc_in[row_idx:row_idx + 1, CUR_COLS[0]])
        for half in range(4):
            def mm(t, rsrc=rsrc, half=half):
                for c in range(4):
                    c0 = half * 2048 + c * 512
                    inst = t.matmul(pb[c][:, :], ones1[:], rsrc[0:1, c0:c0 + 512],
                                    start=True, stop=True)
                return inst
            seq.comp("tensor", mm)

            def ev(s, dst=dst, half=half):
                for c in range(4):
                    c0 = half * 2048 + c * 512
                    inst = s.activation(out=dst[:, c0:c0 + 512], in_=pb[c][:, :], func=AF.Identity)
                return inst
            seq.comp("scalar", ev)

    CUR_COLS = [None]
    for b in range(B):
        cols = slice(b * T, (b + 1) * T)
        CUR_COLS[0] = cols
        for ct in range(2):
            rows = slice(ct * 128, (ct + 1) * 128)
            seq.dma("sync", bsb[:], xpd_in[rows, cols])
            seq.dma("sync", hsb[:], xnb_in[rows, cols])

            def sp(v):
                v.tensor_copy(dt_[:], bsb[:])
                v.tensor_scalar(ysb[:], dt_[:], -1.0, None, Op.mult)
                v.tensor_tensor(ysb[:], ysb[:], dt_[:], Op.max)
                return v.tensor_scalar(dtx[:], dt_[:], 0.0, None, Op.max)
            seq.comp("vector", sp)

            def sp2(s):
                s.activation(out=ysb[:], in_=ysb[:], func=AF.Exp, scale=-1.0)
                return s.activation(out=ysb[:], in_=ysb[:], func=AF.Ln, bias=1.0)
            seq.comp("scalar", sp2)
            seq.comp("vector", lambda v: v.tensor_tensor(dt_[:], dtx[:], ysb[:], Op.add))
            seq.comp("vector", lambda v: v.tensor_copy(ysb[:], hsb[:]))
            seq.comp("vector", lambda v: v.tensor_tensor(dtx[:], dt_[:], ysb[:], Op.mult))
            seq.comp("vector", lambda v: v.memset(ysb[:], 0.0))
            for n in range(NST):
                bcast_row(n, Bb)
                seq.comp("scalar", lambda s, ct=ct, n=n: s.activation(
                    out=asb[:], in_=dt_[:], func=AF.Exp, scale=amat[:, ct * 32 + 2 * n:ct * 32 + 2 * n + 1]))
                seq.comp("vector", lambda v: v.tensor_tensor(bsb[:], dtx[:], Bb[:], Op.mult))
                bcast_row(16 + n, Cb)
                seq.comp("vector", lambda v: v.tensor_tensor_scan(hsb[:], asb[:], bsb[:], 0.0, Op.mult, Op.add))
                seq.comp("gpsimd", lambda g: g.tensor_tensor(bsb[:], hsb[:], Cb[:], Op.mult))
                seq.comp("vector", lambda v: v.tensor_tensor(ysb[:], ysb[:], bsb[:], Op.add))
            seq.dma("scalar", y_out[rows, cols], ysb[:])

    es.close()


def build_c2():
    nc = bass.Bass("TRN2", target_bir_lowering=False, debug=False)
    es = ExitStack()
    CS = 256
    NST = 16
    xpd_in = nc.dram_tensor("xpd", [CS, TOK], BF16, kind="ExternalInput").ap()
    xnb_in = nc.dram_tensor("xnb", [CS, TOK], BF16, kind="ExternalInput").ap()
    bc_in = nc.dram_tensor("bcr", [32, TOK], BF16, kind="ExternalInput").ap()
    alog_in = nc.dram_tensor("alog", [128, 64], F32, kind="ExternalInput").ap()
    o1_in = nc.dram_tensor("ones1", [1, 128], F32, kind="ExternalInput").ap()
    y_out = nc.dram_tensor("y_o", [CS, TOK], F32, kind="ExternalOutput").ap()


    seq = Seq(nc, es, "c2seq")
    io = dict(xpd_in=xpd_in, xnb_in=xnb_in, bc_in=bc_in, alog_in=alog_in, o1_in=o1_in, y_out=y_out)
    emit_c2(nc, seq, io)
    seq.finish()
    block = es.enter_context(nc.Block())
    for en in ["sync", "vector", "scalar", "gpsimd", "tensor"]:
        getattr(block, en)(lambda e, en=en: seq.emit(en, e))
    es.close()
    return nc


def emit_c2p(nc, seq, io):
    es = ExitStack()
    NST = 16
    xpd_in = io["xpd_in"]
    xnb_in = io["xnb_in"]
    bc_in = io["bc_in"]
    alog_in = io["alog_in"]
    y_out = io["y_out"]
    sbuf = nc.sbuf_tensor
    dt_ = es.enter_context(sbuf("dt_s", [128, T], FP16))
    dtx = es.enter_context(sbuf("dtx", [128, T], BF16))
    ysb = es.enter_context(sbuf("ysb", [128, T], F32))
    asb = es.enter_context(sbuf("asb", [128, T], BF16))
    bsb = es.enter_context(sbuf("bsb", [128, T], BF16))
    hsb = es.enter_context(sbuf("hsb", [128, T], BF16))
    prod = es.enter_context(sbuf("prod", [128, T], BF16))
    BC4 = es.enter_context(sbuf("BC4", [128, 4 * T], BF16))
    Bb = [BC4[:, 0:T], BC4[:, T:2 * T]]
    Cb = [BC4[:, 2 * T:3 * T], BC4[:, 3 * T:4 * T]]
    scr = BC4[:].bitcast(F32)[:, 0:T]
    amat = es.enter_context(sbuf("amat", [128, 64], F32))

    seq.dma("sync", amat[:], alog_in[:])
    seq.comp("scalar", lambda s: s.activation(out=amat[:], in_=amat[:], func=AF.Exp))
    seq.comp("vector", lambda v: v.tensor_scalar(amat[:], amat[:], -1.0, None, Op.mult))

    for b in range(B):
        cols = slice(b * T, (b + 1) * T)
        for ct in range(2):
            seq.barrier()
            rows = slice(ct * 128, (ct + 1) * 128)
            ld1 = seq.dma("sync", bsb[:], xpd_in[rows, cols], lane="c2a")
            ld2 = seq.dma("sync", hsb[:], xnb_in[rows, cols], lane="c2b")

            def sp(v):
                v.tensor_copy(dt_[:], bsb[:])
                v.tensor_scalar(scr[:], dt_[:], -1.0, None, Op.mult)
                v.tensor_tensor(scr[:], scr[:], dt_[:], Op.max)
                return v.tensor_scalar(ysb[:], dt_[:], 0.0, None, Op.max)
            i1 = seq.comp("vector", sp, deps=[ld1])

            def sp2(s):
                s.activation(out=scr[:], in_=scr[:], func=AF.Exp, scale=-1.0)
                return s.activation(out=scr[:], in_=scr[:], func=AF.Ln, bias=1.0)
            i2 = seq.comp("scalar", sp2, deps=[i1])
            i3 = seq.comp("vector", lambda v: v.tensor_tensor(dt_[:], ysb[:], scr[:], Op.add),
                          deps=[i2])
            i4 = seq.comp("vector", lambda v: v.tensor_copy(scr[:], hsb[:]), deps=[ld2, i3])
            i5 = seq.comp("vector", lambda v: v.tensor_tensor(dtx[:], dt_[:], scr[:], Op.mult),
                          deps=[i4])
            sp_done = seq.comp("vector", lambda v: v.memset(ysb[:], 0.0), deps=[i5])

            bs_k = {}
            sc_k = {}
            pr_k = {}
            ad_k = {}
            ex_k = {}
            for n in range(NST):
                bi = n % 2
                lB = seq.dma("sync", Bb[bi], bc_in[n, cols].partition_broadcast(128),
                             deps=[bs_k.get(n - 2), sp_done if n < 2 else None], lane=f"bb{bi}")
                lC = seq.dma("sync", Cb[bi], bc_in[16 + n, cols].partition_broadcast(128),
                             deps=[pr_k.get(n - 2), sp_done if n < 2 else None], lane=f"cc{bi}")
                ex_k[n] = seq.comp("scalar", lambda s, ct=ct, n=n: s.activation(
                    out=asb[:], in_=dt_[:], func=AF.Exp,
                    scale=amat[:, ct * 32 + 2 * n:ct * 32 + 2 * n + 1]),
                    deps=[sc_k.get(n - 1), sp_done])
                bs_k[n] = seq.comp("vector", lambda v, bi=bi: v.tensor_tensor(
                    bsb[:], dtx[:], Bb[bi], Op.mult), deps=[lB, sp_done])
                sc_k[n] = seq.comp("vector", lambda v: v.tensor_tensor_scan(
                    hsb[:], asb[:], bsb[:], 0.0, Op.mult, Op.add),
                    deps=[ex_k[n], pr_k.get(n - 1)])
                pr_k[n] = seq.comp("gpsimd", lambda g, bi=bi: g.tensor_tensor(
                    prod[:], hsb[:], Cb[bi], Op.mult), deps=[sc_k[n], lC])
                ad_k[n] = seq.comp("gpsimd", lambda g: g.tensor_tensor(
                    ysb[:], ysb[:], prod[:], Op.add), deps=[pr_k[n]])
            seq.dma("scalar", y_out[rows, cols], ysb[:], deps=[ad_k[NST - 1]], lane="c2y")
    seq.barrier()
    es.close()


def emit_c3(nc, seq, io):
    es = ExitStack()
    W = TSH
    H2 = 5120
    xfm_in = io["xfm_in"]
    xn_in = io["xn_in"]
    y_in = io["y_in"]
    dv_in = io["dv_in"]
    wfc_in = io.get("wfc_in")
    wpr_in = io.get("wpr_in")
    xo_out = io["xo_out"]
    hscr = io["hscr"]
    wfc_src = io.get("wfc_src") or (
        lambda m0: wfc_in.rearrange("(k p) m -> p k m", p=128)[:, :, m0:m0 + 128])
    wpr_srcs = io.get("wpr_srcs") or (
        lambda k, g: [(0, 1024, wpr_in[k * 128:(k + 1) * 128, g * 1024:(g + 1) * 1024])])
    sbuf = nc.sbuf_tensor
    xnr = es.enter_context(sbuf("xnr", [128, 16 * W], BF16))
    wg = es.enter_context(sbuf("wg", [128, 16 * 128], BF16))
    gsb = es.enter_context(sbuf("gsb", [128, 512], F32))
    hsb = es.enter_context(sbuf("hsb", [128, 2048], BF16))
    dv = es.enter_context(sbuf("dv_s", [128, 32], F32))
    kch = es.enter_context(sbuf("kch", [128, 512], BF16))
    wch = es.enter_context(sbuf("wch", [128, 8 * 128], BF16))
    xres = es.enter_context(sbuf("xres", [128, 512], F32))
    ych = es.enter_context(sbuf("ych", [128, 512], F32))
    ob = es.enter_context(sbuf("ob", [128, 512], F32))
    P = [es.enter_context(nc.psum_tensor(f"pp{i}", [128, 512], F32)) for i in range(8)]

    seq.dma("sync", dv[:].rearrange("p (h two) -> p h two", two=2),
            dv_in.rearrange("(h p) two -> p h two", p=128))
    seq.dma("sync", xnr[:].rearrange("p (k t) -> p k t", t=W),
            xn_in.rearrange("(k p) t -> p k t", p=128))

    NHT = H2 // 128
    for ht in range(NHT):
        seq.dma("sync", wg[:].rearrange("p (k m) -> p k m", m=128), wfc_src(ht * 128))
        for tc in range(4):
            def mmg(t, tc=tc):
                for k in range(16):
                    inst = t.matmul(P[tc % 2][:, :], wg[:, k * 128:(k + 1) * 128],
                                    xnr[:, k * W + tc * 512:k * W + (tc + 1) * 512],
                                    start=(k == 0), stop=(k == 15))
                return inst
            seq.comp("tensor", mmg)

            def ev(s, tc=tc):
                s.activation(out=gsb[:, :], in_=P[tc % 2][:, :], func=AF.Relu)
                return s.activation(out=hsb[:, tc * 512:(tc + 1) * 512], in_=gsb[:, :], func=AF.Square)
            seq.comp("scalar", ev)
        seq.dma("scalar", hscr.ap()[ht * 128:(ht + 1) * 128, :], hsb[:, :])

    for tc in range(4):
        cs = slice(tc * 512, (tc + 1) * 512)
        for g in range(2):
            for k in range(NHT):
                seq.dma("sync", kch[:, :], hscr.ap()[k * 128:(k + 1) * 128, cs])
                for c0w, wdw, srcw in wpr_srcs(k, g):
                    seq.dma("sync", wch[:, c0w:c0w + wdw], srcw)

                def mmd(t, k=k):
                    for j in range(8):
                        inst = t.matmul(P[j][:, :], wch[:, j * 128:(j + 1) * 128], kch[:, :],
                                        start=(k == 0), stop=(k == NHT - 1))
                    return inst
                seq.comp("tensor", mmd)
            for j in range(8):
                dout = g * 8 + j
                seq.dma("sync", xres[:, :], xfm_in.rearrange("(kk p) t -> p kk t", p=128)[:, dout, cs])
                seq.dma("sync", ych[:, :], y_in.rearrange("(kk p) t -> p kk t", p=128)[:, dout, cs])

                def evd(v, j=j, dout=dout, tc=tc):
                    v.scalar_tensor_tensor(ych[:, :], xnr[:, dout * W + tc * 512:dout * W + (tc + 1) * 512],
                                           dv[:, 2 * dout:2 * dout + 1], ych[:, :], Op.mult, Op.add)
                    v.tensor_tensor(ob[:, :], P[j][:, :], xres[:, :], Op.add)
                    return v.tensor_tensor(ob[:, :], ob[:, :], ych[:, :], Op.add)
                seq.comp("vector", evd)
                seq.dma("scalar", xo_out.rearrange("(kk p) t -> p kk t", p=128)[:, dout, cs], ob[:, :])

    es.close()


def build_c3():
    nc = bass.Bass("TRN2", target_bir_lowering=False, debug=False)
    es = ExitStack()
    W = TSH
    H2 = 5120
    xfm_in = nc.dram_tensor("xfm", [D, W], F32, kind="ExternalInput").ap()
    xn_in = nc.dram_tensor("xn", [D, W], BF16, kind="ExternalInput").ap()
    y_in = nc.dram_tensor("yssm", [D, W], F32, kind="ExternalInput").ap()
    dv_in = nc.dram_tensor("dv", [D, 2], F32, kind="ExternalInput").ap()
    wfc_in = nc.dram_tensor("wfcT", [D, H2], BF16, kind="ExternalInput").ap()
    wpr_in = nc.dram_tensor("wprT", [H2, D], BF16, kind="ExternalInput").ap()
    xo_out = nc.dram_tensor("xo_o", [D, W], F32, kind="ExternalOutput").ap()
    hscr = nc.dram_tensor("hscr", [H2, W], BF16)


    seq = Seq(nc, es, "c3seq")
    io = dict(xfm_in=xfm_in, xn_in=xn_in, y_in=y_in, dv_in=dv_in, wfc_in=wfc_in, wpr_in=wpr_in, xo_out=xo_out, hscr=hscr)
    emit_c3(nc, seq, io)
    seq.finish()
    block = es.enter_context(nc.Block())
    for en in ["sync", "vector", "scalar", "gpsimd", "tensor"]:
        getattr(block, en)(lambda e, en=en: seq.emit(en, e))
    es.close()
    return nc


def emit_d1(nc, seq, io):
    es = ExitStack()
    W = TSH
    xfm_in = io["xfm_in"]
    wq_in = io["wq_in"]
    wk_in = io["wk_in"]
    wv_in = io["wv_in"]
    cosr_in = io["cosr_in"]
    sinr_in = io["sinr_in"]
    o1_in = io["o1_in"]
    xn_out = io["xn_out"]
    q_out = io["q_out"]
    k_out = io["k_out"]
    v_out = io["v_out"]
    sbuf = nc.sbuf_tensor
    xsb = es.enter_context(sbuf("xsb", [128, 16 * 512], F32))
    sq = es.enter_context(sbuf("sq", [128, 16 * 512], F32))
    xnr = es.enter_context(sbuf("xnr", [128, 16 * W], BF16))
    strow = es.enter_context(sbuf("strow", [1, 3 * 512], F32))
    ones = es.enter_context(sbuf("ones", [128, 2], F32))
    ones1 = es.enter_context(sbuf("ones1_s", [1, 128], F32))
    cst = es.enter_context(sbuf("cst", [1, 2], F32))
    wsb = es.enter_context(sbuf("wsb", [128, 8192], BF16))
    qtm = es.enter_context(sbuf("qtm", [128, 2048], BF16))
    qrl = es.enter_context(sbuf("qrl", [128, 2048], BF16))
    tcos = es.enter_context(sbuf("tcos", [128, 2048], BF16))
    tsin = es.enter_context(sbuf("tsin", [128, 2048], BF16))
    t1s = es.enter_context(sbuf("t1s", [128, 2048], BF16))
    pst0 = es.enter_context(nc.psum_tensor("pst0", [1, 512], F32))
    pst1 = es.enter_context(nc.psum_tensor("pst1", [1, 512], F32))
    pbc = es.enter_context(nc.psum_tensor("pbc", [128, 1024], F32))
    pu = [es.enter_context(nc.psum_tensor(f"pu{i}", [128, 512], F32)) for i in range(4)]

    seq.dma("sync", ones1[:], o1_in[:])
    seq.comp("vector", lambda v: v.memset(ones[:], 1.0))
    seq.comp("vector", lambda v: v.memset(cst[:, 0:1], 1e-5))
    for bi in range(4):
        t0 = bi * 512
        seq.dma("sync", xsb[:].rearrange("p (k x) -> p k x", x=512),
                xfm_in.rearrange("(k p) t -> p k t", p=128)[:, :, t0:t0 + 512])
        seq.comp("scalar", lambda s: s.activation(out=sq[:], in_=xsb[:], func=AF.Square))

        def st_fn(t):
            for k in range(16):
                t.matmul(pst0[:, :], ones[:, 0:1], xsb[:, k * 512:(k + 1) * 512],
                         start=(k == 0), stop=(k == 15))
            for k in range(16):
                inst = t.matmul(pst1[:, :], ones[:, 0:1], sq[:, k * 512:(k + 1) * 512],
                                start=(k == 0), stop=(k == 15))
            return inst
        seq.comp("tensor", st_fn)

        def row_fn(v):
            m = strow[0:1, 0:512]
            var = strow[0:1, 512:1024]
            v.tensor_scalar(m, pst0[:, :], 1.0 / D, None, Op.mult)
            v.tensor_tensor(var, m, m, Op.mult)
            return v.scalar_tensor_tensor(var, pst1[:, :], 1.0 / D, var, Op.mult, Op.subtract)
        seq.comp("vector", row_fn)

        def rstd_fn(s):
            vv = strow[0:1, 512:1024]
            s.activation(out=vv, in_=vv, func=AF.Ln, bias=cst[0:1, 0:1])
            return s.activation(out=strow[0:1, 1024:1536], in_=vv, func=AF.Exp, scale=-0.5)
        seq.comp("scalar", rstd_fn)
        seq.comp("vector", lambda v: v.tensor_tensor(
            strow[0:1, 0:512], strow[0:1, 0:512], strow[0:1, 1024:1536], Op.mult))

        def bc_fn(t):
            t.matmul(pbc[:, 0:512], ones1[:], strow[0:1, 1024:1536], start=True, stop=True)
            return t.matmul(pbc[:, 512:1024], ones1[:], strow[0:1, 0:512], start=True, stop=True)
        seq.comp("tensor", bc_fn)

        def ap_fn(v, t0=t0):
            for k in range(16):
                v.tensor_tensor(sq[:, k * 512:(k + 1) * 512], xsb[:, k * 512:(k + 1) * 512],
                                pbc[:, 0:512], Op.mult)
                inst = v.tensor_tensor(xnr[:, k * W + t0:k * W + t0 + 512],
                                       sq[:, k * 512:(k + 1) * 512], pbc[:, 512:1024], Op.subtract)
            return inst
        seq.comp("vector", ap_fn)
    seq.dma("scalar", xn_out.rearrange("(k p) t -> p k t", p=128),
            xnr[:].rearrange("p (k t) -> p k t", t=W))

    def qkv3(w_ap, OD, out_base_col, out_ap, rope):
        NOC = OD // 512
        for oc in range(NOC):
            seq.dma("sync", wsb[:].rearrange("p (k m) -> p k m", m=512),
                    w_ap.rearrange("(k p) m -> p k m", p=128)[:, :, oc * 512:(oc + 1) * 512])
            for tt in range(16):
                def mm(t, tt=tt):
                    for kk in range(16):
                        inst = t.matmul(pu[0][:, :], xnr[:, kk * W + tt * 128:kk * W + (tt + 1) * 128],
                                        wsb[:, kk * 512:(kk + 1) * 512],
                                        start=(kk == 0), stop=(kk == 15))
                    return inst
                seq.comp("tensor", mm)
                seq.comp("scalar", lambda s: s.activation(out=qtm[:, 0:512], in_=pu[0][:, :], func=AF.Identity))
                if rope:
                    seq.dma("sync", tcos[:, 0:512], cosr_in[tt * 128:(tt + 1) * 128, out_base_col + oc * 512:out_base_col + (oc + 1) * 512])
                    seq.dma("sync", tsin[:, 0:512], sinr_in[tt * 128:(tt + 1) * 128, out_base_col + oc * 512:out_base_col + (oc + 1) * 512])

                    def rope_fn(v):
                        q3 = qtm[:, 0:512].rearrange("p (h e) -> p h e", e=128)
                        r3 = qrl[:, 0:512].rearrange("p (h e) -> p h e", e=128)
                        v.tensor_copy(r3[:, :, 1:128], q3[:, :, 0:127])
                        v.tensor_copy(r3[:, :, 0:1], q3[:, :, 127:128])
                        v.tensor_tensor(t1s[:, 0:512], qtm[:, 0:512], tcos[:, 0:512], Op.mult)
                        v.tensor_tensor(qrl[:, 0:512], qrl[:, 0:512], tsin[:, 0:512], Op.mult)
                        return v.tensor_tensor(qtm[:, 0:512], t1s[:, 0:512], qrl[:, 0:512], Op.add)
                    seq.comp("vector", rope_fn)
                seq.dma("scalar", out_ap[tt * 128:(tt + 1) * 128, oc * 512:(oc + 1) * 512], qtm[:, 0:512])

    qkv3(wq_in, D, 0, q_out, True)
    qkv3(wk_in, 512, 0, k_out, True)
    qkv3(wv_in, 512, 0, v_out, False)

    es.close()


def build_d1():
    nc = bass.Bass("TRN2", target_bir_lowering=False, debug=False)
    es = ExitStack()
    W = TSH
    xfm_in = nc.dram_tensor("xfm", [D, W], F32, kind="ExternalInput").ap()
    wq_in = nc.dram_tensor("wq", [D, D], BF16, kind="ExternalInput").ap()
    wk_in = nc.dram_tensor("wk", [D, 512], BF16, kind="ExternalInput").ap()
    wv_in = nc.dram_tensor("wv", [D, 512], BF16, kind="ExternalInput").ap()
    cosr_in = nc.dram_tensor("cosr", [W, D], BF16, kind="ExternalInput").ap()
    sinr_in = nc.dram_tensor("sinr", [W, D], BF16, kind="ExternalInput").ap()
    o1_in = nc.dram_tensor("ones1", [1, 128], F32, kind="ExternalInput").ap()
    xn_out = nc.dram_tensor("xn_o", [D, W], BF16, kind="ExternalOutput").ap()
    q_out = nc.dram_tensor("q_o", [W, D], BF16, kind="ExternalOutput").ap()
    k_out = nc.dram_tensor("k_o", [W, 512], BF16, kind="ExternalOutput").ap()
    v_out = nc.dram_tensor("v_o", [W, 512], BF16, kind="ExternalOutput").ap()


    seq = Seq(nc, es, "d1seq")
    io = dict(xfm_in=xfm_in, wq_in=wq_in, wk_in=wk_in, wv_in=wv_in, cosr_in=cosr_in, sinr_in=sinr_in, o1_in=o1_in, xn_out=xn_out, q_out=q_out, k_out=k_out, v_out=v_out)
    emit_d1(nc, seq, io)
    seq.finish()
    block = es.enter_context(nc.Block())
    for en in ["sync", "vector", "scalar", "gpsimd", "tensor"]:
        getattr(block, en)(lambda e, en=en: seq.emit(en, e))
    es.close()
    return nc


def emit_d2(nc, seq, io):
    es = ExitStack()
    E = 128
    q_in = io["q_in"]
    k_in = io["k_in"]
    v_in = io["v_in"]
    i128_in = io["i128_in"]
    o1_in = io["o1_in"]
    o_out = io["o_out"]
    DBGON = io.get("dbg", True)
    dbg_out = io.get("dbg_out")
    dbg2_out = io.get("dbg2_out")
    dbg3_out = io.get("dbg3_out")
    bscr = io["bscr"]
    sbuf = nc.sbuf_tensor
    qe = es.enter_context(sbuf("qe", [128, T], BF16))
    ke = es.enter_context(sbuf("ke", [128, T], BF16))
    vt = es.enter_context(sbuf("vt", [128, T], BF16))
    pT = es.enter_context(sbuf("pT", [128, 64 * 512], BF16))
    obuf = es.enter_context(sbuf("obuf", [128, T], BF16))
    iden = es.enter_context(sbuf("iden_s", [128, 128], BF16))
    idf = es.enter_context(sbuf("idf_s", [128, 128], F32))
    ones1 = es.enter_context(sbuf("ones1_s", [1, 128], F32))
    onesc = es.enter_context(sbuf("onesc", [128, 2], BF16))
    onesf = es.enter_context(sbuf("onesf", [128, 2], F32))
    red = es.enter_context(sbuf("red", [128, 2], F32))
    redr = es.enter_context(sbuf("redr", [1, 128], F32))
    brow = es.enter_context(sbuf("brow", [1, 16], F32))
    bcol = es.enter_context(sbuf("bcol", [128, 2], F32))
    srow = es.enter_context(sbuf("srow", [1, 512], F32))
    ob2 = es.enter_context(sbuf("ob2", [128, 512], F32))
    pqk = [es.enter_context(nc.psum_tensor(f"pqk{i}", [128, 512], F32)) for i in range(4)]
    po = es.enter_context(nc.psum_tensor("po", [128, 512], F32))
    psum_s = es.enter_context(nc.psum_tensor("psum_s", [1, 512], F32))
    prd = es.enter_context(nc.psum_tensor("prd", [1, 128], F32))
    pbc2 = es.enter_context(nc.psum_tensor("pbc2", [128, 512], F32))

    seq.dma("sync", iden[:], i128_in[:])
    seq.dma("sync", ones1[:], o1_in[:])
    seq.comp("vector", lambda v: v.memset(onesc[:], 1.0))
    seq.comp("vector", lambda v: v.memset(onesf[:], 1.0))
    seq.comp("vector", lambda v: v.tensor_copy(idf[:], iden[:]))

    for pi in range(4):
        rows = slice(pi * T, (pi + 1) * T)
        seq.dma_t("sync", qe[:], q_in[rows, :])
        seq.dma_t("sync", ke[:], k_in[rows, :])
        seq.dma("sync", vt[:].rearrange("p (c e) -> p c e", e=E),
                v_in[rows, :].rearrange("(c p) e -> p c e", p=128))
        qqv = pT[:].bitcast(F32)
        for which, src_t, dst_col in ((0, qe, 0), (1, ke, 2)):
            seq.comp("scalar", lambda s, src_t=src_t: s.activation(
                out=qqv[:, 0:T], in_=src_t[:], func=AF.Square))
            seq.comp("vector", lambda v, dst_col=dst_col: v.memset(brow[0:1, dst_col:dst_col + 1], 0.0))
            for ch in range(16):
                seq.comp("tensor", lambda t, ch=ch: t.matmul(
                    psum_s[:, :], onesf[:, 0:1], qqv[:, ch * 512:(ch + 1) * 512],
                    start=True, stop=True))

                def mxr(v, dst_col=dst_col):
                    v.tensor_reduce(brow[0:1, 8:9], psum_s[0:1, :], axis=AX.X, op=Op.max)
                    return v.tensor_tensor(brow[0:1, dst_col:dst_col + 1],
                                           brow[0:1, dst_col:dst_col + 1], brow[0:1, 8:9], Op.max)
                seq.comp("vector", mxr)

        def bias_fn(v):
            return v.tensor_tensor(brow[0:1, 4:5], brow[0:1, 0:1], brow[0:1, 2:3], Op.mult)
        seq.comp("vector", bias_fn)

        def bias_fn2(s):
            s.activation(out=brow[0:1, 4:5], in_=brow[0:1, 4:5], func=AF.Ln)
            return s.activation(out=brow[0:1, 4:5], in_=brow[0:1, 4:5], func=AF.Exp, scale=0.5)
        seq.comp("scalar", bias_fn2)

        seq.comp("vector", lambda v: v.tensor_scalar(
            brow[0:1, 6:7], brow[0:1, 4:5], 0.08838834764831845, -30.0, Op.mult, Op.add))
        seq.comp("vector", lambda v: v.tensor_scalar(
            brow[0:1, 4:5], brow[0:1, 6:7], 0.0, -1.0, Op.max, Op.mult))
        seq.dma("scalar", bscr.ap(), brow[:])
        if DBGON:
            seq.dma("scalar", dbg_out[pi:pi + 1, :], brow[:])
        seq.dma("sync", bcol[:, 0:2], bscr.ap()[0, 4:6].partition_broadcast(128))

        for tc in range(16):
            tcs = slice(tc * 512, (tc + 1) * 512)
            for scg in range(16):
                def mqk(t, scg=scg, tc=tc):
                    for j in range(4):
                        sc = scg * 4 + j
                        inst = t.matmul(pqk[j][:, :], ke[:, sc * 128:(sc + 1) * 128],
                                        qe[:, tcs], start=True, stop=True)
                    return inst
                seq.comp("tensor", mqk)
                seq.comp("scalar", lambda s, scg=scg: s.activation(
                    out=pT[:, scg * 2048:(scg + 1) * 2048].rearrange("p (j t) -> p j t", j=4),
                    in_=pqk[0][:, :].to_broadcast((128, 4, 512)) if False else pqk[0][:, :],
                    func=AF.Exp, scale=0.08838834764831845, bias=bcol[:, 0:1])) if False else None
                def ex(s, scg=scg):
                    for j in range(4):
                        inst = s.activation(out=pT[:, (scg * 4 + j) * 512:(scg * 4 + j + 1) * 512],
                                            in_=pqk[j][:, :], func=AF.Exp,
                                            scale=0.08838834764831845, bias=bcol[:, 0:1])
                    return inst
                seq.comp("scalar", ex)
            if DBGON and tc == 0:
                seq.comp("vector", lambda v: v.tensor_copy(ob2[0:1, 0:512], pT[0:1, 0:512]))
                seq.dma("scalar", dbg3_out[pi:pi + 1, :], ob2[0:1, 0:512])
            for scg in range(16):
                def mav(t, scg=scg):
                    for j in range(4):
                        sc = scg * 4 + j
                        t.matmul(po[:, :], vt[:, sc * 128:(sc + 1) * 128],
                                 pT[:, sc * 512:(sc + 1) * 512],
                                 start=(sc == 0), stop=(sc == 63))
                        inst = t.matmul(psum_s[:, :], onesc[:, 0:1],
                                        pT[:, sc * 512:(sc + 1) * 512],
                                        start=(sc == 0), stop=(sc == 63))
                    return inst
                seq.comp("tensor", mav)
            seq.comp("vector", lambda v: v.reciprocal(srow[0:1, :], psum_s[0:1, :]))
            if DBGON:
                seq.comp("vector", lambda v: v.tensor_copy(ob2[0:1, 0:512], psum_s[0:1, :]))
                seq.dma("scalar", dbg2_out[pi:pi + 1, :], ob2[0:1, 0:512])
            seq.comp("tensor", lambda t: t.matmul(pbc2[:, :], ones1[:], srow[0:1, :], start=True, stop=True))
            seq.comp("scalar", lambda s: s.activation(out=ob2[:, :], in_=po[:, :], func=AF.Identity))
            seq.comp("vector", lambda v, tc=tc: v.tensor_tensor(obuf[:, tcs], ob2[:, :], pbc2[:, :], Op.mult))
        seq.dma("scalar", o_out[pi, :, :], obuf[:])

    es.close()


def build_d2():
    nc = bass.Bass("TRN2", target_bir_lowering=False, debug=False)
    es = ExitStack()
    E = 128
    q_in = nc.dram_tensor("qtm", [4 * T, E], BF16, kind="ExternalInput").ap()
    k_in = nc.dram_tensor("ktm", [4 * T, E], BF16, kind="ExternalInput").ap()
    v_in = nc.dram_tensor("vtm", [4 * T, E], BF16, kind="ExternalInput").ap()
    i128_in = nc.dram_tensor("iden", [128, 128], BF16, kind="ExternalInput").ap()
    o1_in = nc.dram_tensor("ones1", [1, 128], F32, kind="ExternalInput").ap()
    o_out = nc.dram_tensor("o_o", [4, E, T], BF16, kind="ExternalOutput").ap()
    dbg_out = nc.dram_tensor("dbg_o", [4, 16], F32, kind="ExternalOutput").ap()
    dbg2_out = nc.dram_tensor("dbg2_o", [4, 512], F32, kind="ExternalOutput").ap()
    dbg3_out = nc.dram_tensor("dbg3_o", [4, 512], F32, kind="ExternalOutput").ap()
    bscr = nc.dram_tensor("bscr", [1, 16], F32)


    seq = Seq(nc, es, "d2seq")
    io = dict(q_in=q_in, k_in=k_in, v_in=v_in, i128_in=i128_in, o1_in=o1_in, o_out=o_out, dbg_out=dbg_out, dbg2_out=dbg2_out, dbg3_out=dbg3_out, bscr=bscr)
    emit_d2(nc, seq, io)
    seq.finish()
    block = es.enter_context(nc.Block())
    for en in ["sync", "vector", "scalar", "gpsimd", "tensor"]:
        getattr(block, en)(lambda e, en=en: seq.emit(en, e))
    es.close()
    return nc


def emit_d2p(nc, seq, io):
    es = ExitStack()
    E = 128
    q_in = io["q_in"]
    k_in = io["k_in"]
    v_in = io["v_in"]
    o1_in = io["o1_in"]
    o_out = io["o_out"]
    bscr = io["bscr"]
    sbuf = nc.sbuf_tensor
    qe = es.enter_context(sbuf("qe", [128, T], BF16))
    ke = es.enter_context(sbuf("ke", [128, T], BF16))
    vt = es.enter_context(sbuf("vt", [128, T], BF16))
    pT = es.enter_context(sbuf("pT", [128, 64 * 512], BF16))
    obuf = es.enter_context(sbuf("obuf", [128, T], BF16))
    ones1 = es.enter_context(sbuf("ones1_s", [1, 128], F32))
    onesc = es.enter_context(sbuf("onesc", [128, 2], BF16))
    onesf = es.enter_context(sbuf("onesf", [128, 2], F32))
    brow = es.enter_context(sbuf("brow", [1, 16], F32))
    bcol = es.enter_context(sbuf("bcol", [128, 2], F32))
    srow = es.enter_context(sbuf("srow", [1, 512], F32))
    ob2 = es.enter_context(sbuf("ob2", [128, 512], F32))
    pqk = [es.enter_context(nc.psum_tensor(f"pqk{i}", [128, 1024], F32)) for i in range(3)]
    po = es.enter_context(nc.psum_tensor("po", [128, 512], F32))
    psum_s = es.enter_context(nc.psum_tensor("psum_s", [1, 512], F32))
    pbc2 = pqk[2][:, 0:512]

    seq.dma("sync", ones1[:], o1_in[:])
    seq.comp("vector", lambda v: v.memset(onesc[:], 1.0))
    seq.comp("vector", lambda v: v.memset(onesf[:], 1.0))

    SCALE = 0.08838834764831845
    for pi in range(4):
        seq.barrier()
        rows = slice(pi * T, (pi + 1) * T)
        ldq = seq.dma_t("sync", qe[:], q_in[rows, :], lane="dq")
        ldk = seq.dma_t("sync", ke[:], k_in[rows, :], lane="dk")
        ldv = seq.dma("sync", vt[:].rearrange("p (c e) -> p c e", e=E),
                      v_in[rows, :].rearrange("(c p) e -> p c e", p=128), lane="dv")
        qqv = pT[:].bitcast(F32)
        for which, src_t, dst_col, ld in ((0, qe, 0, ldq), (1, ke, 2, ldk)):
            seq.comp("scalar", lambda s, src_t=src_t: s.activation(
                out=qqv[:, 0:T], in_=src_t[:], func=AF.Square), deps=[ld, seq.last_id])
            seq.comp("vector", lambda v, dst_col=dst_col: v.memset(brow[0:1, dst_col:dst_col + 1], 0.0))
            for ch in range(16):
                seq.comp("tensor", lambda t, ch=ch: t.matmul(
                    psum_s[:, :], onesf[:, 0:1], qqv[:, ch * 512:(ch + 1) * 512],
                    start=True, stop=True))

                def mxr(v, dst_col=dst_col):
                    v.tensor_reduce(brow[0:1, 8:9], psum_s[0:1, :], axis=AX.X, op=Op.max)
                    return v.tensor_tensor(brow[0:1, dst_col:dst_col + 1],
                                           brow[0:1, dst_col:dst_col + 1], brow[0:1, 8:9], Op.max)
                seq.comp("vector", mxr)
        seq.comp("vector", lambda v: v.tensor_tensor(
            brow[0:1, 4:5], brow[0:1, 0:1], brow[0:1, 2:3], Op.mult))

        def bias_fn2(s):
            s.activation(out=brow[0:1, 4:5], in_=brow[0:1, 4:5], func=AF.Ln)
            return s.activation(out=brow[0:1, 4:5], in_=brow[0:1, 4:5], func=AF.Exp, scale=0.5)
        seq.comp("scalar", bias_fn2)
        seq.comp("vector", lambda v: v.tensor_scalar(
            brow[0:1, 6:7], brow[0:1, 4:5], SCALE, -30.0, Op.mult, Op.add))
        seq.comp("vector", lambda v: v.tensor_scalar(
            brow[0:1, 4:5], brow[0:1, 6:7], 0.0, -1.0, Op.max, Op.mult))
        seq.dma("scalar", bscr.ap(), brow[:])
        bias_done = seq.dma("sync", bcol[:, 0:2], bscr.ap()[0, 4:6].partition_broadcast(128))

        av_prev = {}
        norm_prev = []
        exp_lastS = {}
        store_id = None
        for tc in range(16):
            tcs = slice(tc * 512, (tc + 1) * 512)
            av_cur = {}
            for scg in range(32):
                S = scg % 3

                def mqk(t, scg=scg, tcs=tcs, S=S):
                    for j in range(2):
                        sc = scg * 2 + j
                        inst = t.matmul(pqk[S][:, j * 512:(j + 1) * 512],
                                        ke[:, sc * 128:(sc + 1) * 128], qe[:, tcs],
                                        start=True, stop=True)
                    return inst
                m_id = seq.comp("tensor", mqk, deps=[exp_lastS.get(S), bias_done,
                                                     av_prev.get(scg), ldq, ldk]
                                + (norm_prev if scg < 3 else []))

                def ex(s, scg=scg, S=S):
                    return s.activation(out=pT[:, scg * 1024:(scg + 1) * 1024],
                                        in_=pqk[S][:, :], func=AF.Exp,
                                        scale=SCALE, bias=bcol[:, 0:1])
                e_id = seq.comp("scalar", ex, deps=[m_id, av_prev.get(scg), bias_done])
                exp_lastS[S] = e_id

                def mav(t, scg=scg, tc=tc):
                    for j in range(2):
                        sc = scg * 2 + j
                        t.matmul(po[:, :], vt[:, sc * 128:(sc + 1) * 128],
                                 pT[:, sc * 512:(sc + 1) * 512],
                                 start=(sc == 0), stop=(sc == 63))
                        inst = t.matmul(psum_s[:, :], onesc[:, 0:1],
                                        pT[:, sc * 512:(sc + 1) * 512],
                                        start=(sc == 0), stop=(sc == 63))
                    return inst
                a_id = seq.comp("tensor", mav, deps=[e_id, ldv] + (norm_prev if scg == 0 else []))
                av_cur[scg] = a_id
            rec = seq.comp("vector", lambda v: v.reciprocal(srow[0:1, :], psum_s[0:1, :]),
                           deps=[av_cur[31]])
            bcm = seq.comp("tensor", lambda t: t.matmul(pbc2, ones1[:], srow[0:1, :],
                                                        start=True, stop=True),
                           deps=[rec, exp_lastS.get(2)] + norm_prev)
            oba = seq.comp("scalar", lambda s: s.activation(out=ob2[:, :], in_=po[:, :],
                                                            func=AF.Identity),
                           deps=[av_cur[31]] + norm_prev)
            obm = seq.comp("vector", lambda v, tcs=tcs: v.tensor_tensor(
                obuf[:, tcs], ob2[:, :], pbc2, Op.mult), deps=[oba, bcm, store_id])
            norm_prev = [obm]
            av_prev = av_cur
        store_id = seq.dma("scalar", o_out[pi, :, :], obuf[:], deps=[norm_prev[0]], lane="os")
    seq.barrier()
    es.close()


def emit_d3(nc, seq, io):
    es = ExitStack()
    W = TSH
    H3 = 16384
    xfm_in = io["xfm_in"]
    xn_in = io["xn_in"]
    o_in = io.get("o_in")
    wo_in = io["wo_in"]
    wg_in = io.get("wg_in")
    wu_in = io.get("wu_in")
    wd_in = io.get("wd_in")
    xo_out = io["xo_out"]
    hscr = io["hscr"]
    wg_src = io.get("wg_src") or (
        lambda m0: wg_in.rearrange("(k p) m -> p k m", p=128)[:, :, m0:m0 + 128])
    wu_src = io.get("wu_src") or (
        lambda m0: wu_in.rearrange("(k p) m -> p k m", p=128)[:, :, m0:m0 + 128])
    wd_srcs = io.get("wd_srcs") or (
        lambda k, g: [(0, 1024, wd_in[k * 128:(k + 1) * 128, g * 1024:(g + 1) * 1024])])
    onr_fill = io.get("onr_fill")
    sbuf = nc.sbuf_tensor
    xnr = es.enter_context(sbuf("xnr", [128, 16 * W], BF16))
    onr = es.enter_context(sbuf("onr", [128, 16 * W], BF16))
    wg = es.enter_context(sbuf("wg", [128, 16 * 128], BF16))
    wu = es.enter_context(sbuf("wu", [128, 16 * 128], BF16))
    gsb = es.enter_context(sbuf("gsb", [128, 512], F32))
    g2 = es.enter_context(sbuf("g2", [128, 512], F32))
    g3 = es.enter_context(sbuf("g3", [128, 512], F32))
    hsb = es.enter_context(sbuf("hsb", [128, 2048], BF16))
    kch = es.enter_context(sbuf("kch", [128, 512], BF16))
    wch = es.enter_context(sbuf("wch", [128, 8 * 128], BF16))
    xres = es.enter_context(sbuf("xres", [128, 512], F32))
    ob = es.enter_context(sbuf("ob", [128, 512], F32))
    P = [es.enter_context(nc.psum_tensor(f"pp{i}", [128, 512], F32)) for i in range(8)]

    seq.dma("sync", xnr[:].rearrange("p (k t) -> p k t", t=W),
            xn_in.rearrange("(k p) t -> p k t", p=128))
    if onr_fill is None:
        seq.dma("sync", onr[:].rearrange("p (k t) -> p k t", t=W),
                o_in.rearrange("(k p) t -> p k t", p=128))
    else:
        onr_fill(seq, onr)

    NHT = H3 // 128
    C1 = 0.044715
    C2 = 0.7978845608028654
    for ht in range(NHT):
        seq.dma("sync", wg[:].rearrange("p (k m) -> p k m", m=128), wg_src(ht * 128))
        seq.dma("sync", wu[:].rearrange("p (k m) -> p k m", m=128), wu_src(ht * 128))
        for tc in range(4):
            def mmg(t, tc=tc):
                for k in range(16):
                    t.matmul(P[0][:, :], wg[:, k * 128:(k + 1) * 128],
                             xnr[:, k * W + tc * 512:k * W + (tc + 1) * 512],
                             start=(k == 0), stop=(k == 15))
                for k in range(16):
                    inst = t.matmul(P[1][:, :], wu[:, k * 128:(k + 1) * 128],
                                    xnr[:, k * W + tc * 512:k * W + (tc + 1) * 512],
                                    start=(k == 0), stop=(k == 15))
                return inst
            seq.comp("tensor", mmg)

            def ge1(s):
                s.activation(out=gsb[:, :], in_=P[0][:, :], func=AF.Identity)
                return s.activation(out=g2[:, :], in_=P[0][:, :], func=AF.Square)
            seq.comp("scalar", ge1)

            def ge2(v):
                v.tensor_tensor(g3[:, :], g2[:, :], gsb[:, :], Op.mult)
                return v.scalar_tensor_tensor(g3[:, :], g3[:, :], C1, gsb[:, :], Op.mult, Op.add)
            seq.comp("vector", ge2)
            seq.comp("scalar", lambda s: s.activation(out=g3[:, :], in_=g3[:, :], func=AF.Tanh, scale=C2))

            def ge3(v, tc=tc):
                v.scalar_tensor_tensor(g3[:, :], g3[:, :], 1.0, gsb[:, :], Op.add, Op.mult)
                return v.scalar_tensor_tensor(hsb[:, tc * 512:(tc + 1) * 512], g3[:, :], 0.5,
                                              P[1][:, :], Op.mult, Op.mult)
            seq.comp("vector", ge3)
        seq.dma("scalar", hscr.ap()[ht * 128:(ht + 1) * 128, :], hsb[:, :])

    for tc in range(4):
        cs = slice(tc * 512, (tc + 1) * 512)
        for g in range(2):
            for k in range(NHT):
                seq.dma("sync", kch[:, :], hscr.ap()[k * 128:(k + 1) * 128, cs])
                for c0w, wdw, srcw in wd_srcs(k, g):
                    seq.dma("sync", wch[:, c0w:c0w + wdw], srcw)

                def mmd(t, k=k):
                    for j in range(8):
                        inst = t.matmul(P[j][:, :], wch[:, j * 128:(j + 1) * 128], kch[:, :],
                                        start=(k == 0), stop=False)
                    return inst
                seq.comp("tensor", mmd)
            for k in range(16):
                seq.dma("sync", wch[:], wo_in[k * 128:(k + 1) * 128, g * 1024:(g + 1) * 1024])

                def mma(t, k=k, tc=tc):
                    for j in range(8):
                        inst = t.matmul(P[j][:, :], wch[:, j * 128:(j + 1) * 128],
                                        onr[:, k * W + tc * 512:k * W + (tc + 1) * 512],
                                        start=False, stop=(k == 15))
                    return inst
                seq.comp("tensor", mma)
            for j in range(8):
                dout = g * 8 + j
                seq.dma("sync", xres[:, :], xfm_in.rearrange("(kk p) t -> p kk t", p=128)[:, dout, cs])
                seq.comp("vector", lambda v, j=j: v.tensor_tensor(ob[:, :], P[j][:, :], xres[:, :], Op.add))
                seq.dma("scalar", xo_out.rearrange("(kk p) t -> p kk t", p=128)[:, dout, cs], ob[:, :])

    es.close()


def build_d3():
    nc = bass.Bass("TRN2", target_bir_lowering=False, debug=False)
    es = ExitStack()
    W = TSH
    H3 = 16384
    xfm_in = nc.dram_tensor("xfm", [D, W], F32, kind="ExternalInput").ap()
    xn_in = nc.dram_tensor("xn", [D, W], BF16, kind="ExternalInput").ap()
    o_in = nc.dram_tensor("ofm", [D, W], BF16, kind="ExternalInput").ap()
    wo_in = nc.dram_tensor("wo", [D, D], BF16, kind="ExternalInput").ap()
    wg_in = nc.dram_tensor("wgT", [D, H3], BF16, kind="ExternalInput").ap()
    wu_in = nc.dram_tensor("wuT", [D, H3], BF16, kind="ExternalInput").ap()
    wd_in = nc.dram_tensor("wdT", [H3, D], BF16, kind="ExternalInput").ap()
    xo_out = nc.dram_tensor("xo_o", [D, W], F32, kind="ExternalOutput").ap()
    hscr = nc.dram_tensor("hscr", [H3, W], BF16)


    seq = Seq(nc, es, "d3seq")
    io = dict(xfm_in=xfm_in, xn_in=xn_in, o_in=o_in, wo_in=wo_in, wg_in=wg_in, wu_in=wu_in, wd_in=wd_in, xo_out=xo_out, hscr=hscr)
    emit_d3(nc, seq, io)
    seq.finish()
    block = es.enter_context(nc.Block())
    for en in ["sync", "vector", "scalar", "gpsimd", "tensor"]:
        getattr(block, en)(lambda e, en=en: seq.emit(en, e))
    es.close()
    return nc


FFN_SER1 = False
FFN_SER1K = set()
FFN_P1X = set()
FFN_SER2 = False
FFN_SER2A = False
FFN_SER2E = False


def emit_ffn2(nc, seq, cfg):
    es = ExitStack()
    kind = cfg["kind"]

    def D1(l):
        return None if (FFN_SER1 or kind in FFN_SER1K) else l

    def D2(l):
        return None if FFN_SER2 else l

    def D2a(l):
        return None if (FFN_SER2 or FFN_SER2A) else l

    def D2e(l):
        return None if (FFN_SER2 or FFN_SER2E) else l
    W = TSH
    Hdim = cfg["Hdim"]
    NHT = Hdim // 128
    gated = kind in ("a3", "d3")
    has_rms = kind in ("a3", "b3")
    has_attn = kind in ("a3", "b3", "d3")
    xfm_in = cfg["xfm_in"]
    xn_in = cfg["xn_in"]
    xo_out = cfg["xo_out"]
    hscr = cfg["hscr"]
    gate_src = cfg["gate_src"]
    up_src = cfg.get("up_src")
    down_srcs = cfg["down_srcs"]
    sbuf = nc.sbuf_tensor
    xnr = es.enter_context(sbuf("xnr", [128, 16 * W], BF16))
    wgb = [es.enter_context(sbuf(f"wgb{i}", [128, 16 * 128], BF16)) for i in range(2)]
    wub = [es.enter_context(sbuf(f"wub{i}", [128, 16 * 128], BF16)) for i in range(2)] if gated else None
    hsb = [es.enter_context(sbuf(f"hsb{i}", [128, 2048], BF16)) for i in range(2)]
    gsb = [es.enter_context(sbuf(f"gsb{i}", [128, 512], F32)) for i in range(2)]
    g2 = [es.enter_context(sbuf(f"g2_{i}", [128, 512], F32)) for i in range(2)] if kind == "d3" else None
    g3 = [es.enter_context(sbuf(f"g3_{i}", [128, 512], F32)) for i in range(2)] if kind == "d3" else None
    hq = es.enter_context(sbuf("hq", [128, 2048], F32)) if has_rms or kind == "b3" else None
    rstdB = es.enter_context(sbuf("rstdB", [128, 2048], F32)) if has_rms else None
    strow = es.enter_context(sbuf("strow", [1, 2048], F32)) if has_rms else None
    ones = es.enter_context(sbuf("ones", [128, 2], F32))
    ones1 = es.enter_context(sbuf("ones1_s", [1, 128], F32)) if has_rms else None
    cst = es.enter_context(sbuf("cst", [1, 4], F32))
    onr = es.enter_context(sbuf("onr", [128, 16 * W], BF16)) if kind in ("b3", "d3") else None
    nw = es.enter_context(sbuf("nw_s", [128, 32], F32)) if kind == "b3" else None
    opb = es.enter_context(sbuf("opb_s", [128, 32], F32)) if kind == "a3" else None
    dv = es.enter_context(sbuf("dv_s", [128, 32], F32)) if kind == "c3" else None
    NB = 4
    kch = [es.enter_context(sbuf(f"kch{i}", [128, 512], BF16)) for i in range(NB)]
    kch2 = [es.enter_context(sbuf(f"kch2_{i}", [128, 512], BF16)) for i in range(NB)] if has_rms else None
    kchy = [es.enter_context(sbuf(f"kchy{i}", [128, 512], BF16)) for i in range(NB)] if kind == "a3" else None
    wch = [es.enter_context(sbuf(f"wch{i}", [128, 1024], BF16)) for i in range(NB)]
    xres = [es.enter_context(sbuf(f"xres{i}", [128, 512], F32)) for i in range(2)]
    ych = [es.enter_context(sbuf(f"ych{i}", [128, 512], F32)) for i in range(2)] if kind == "c3" else None
    ob = [es.enter_context(sbuf(f"ob{i}", [128, 512], F32)) for i in range(2)]
    P = [es.enter_context(nc.psum_tensor(f"pp{i}", [128, 512], F32)) for i in range(8)]

    id_xnr = seq.dma("sync", xnr[:].rearrange("p (k t) -> p k t", t=W),
                     xn_in.rearrange("(k p) t -> p k t", p=128))
    seq.comp("vector", lambda v: v.memset(ones[:], 1.0))
    if kind == "a3":
        seq.dma("sync", opb[:].rearrange("p (h two) -> p h two", two=2),
                cfg["opb_in"].rearrange("(h p) two -> p h two", p=128))
        seq.comp("vector", lambda v: v.memset(cst[:, 0:1], 1.1920929e-7))
    elif kind == "b3":
        seq.dma("sync", nw[:].rearrange("p (h two) -> p h two", two=2),
                cfg["nw_in"].rearrange("(h p) two -> p h two", p=128))
        seq.comp("vector", lambda v: v.memset(cst[:, 0:1], 1e-6))
        seq.comp("vector", lambda v: v.memset(cst[:, 2:3], 1.1920929e-7))
    elif kind == "c3":
        seq.dma("sync", dv[:].rearrange("p (h two) -> p h two", two=2),
                cfg["dv_in"].rearrange("(h p) two -> p h two", p=128))
    if has_rms:
        seq.dma("sync", ones1[:], cfg["o1_in"][:])
    id_onr = None
    if kind in ("b3", "d3"):
        cfg["onr_fill"](seq, onr)
        id_onr = len(seq.steps) - 1

    def rstd_chain(cstcol, scl, deps0=None, depsT=None):
        sid = seq.comp("scalar", lambda s: _rstd_fn(s, cstcol, scl), deps=deps0)
        tid = seq.comp("tensor", lambda t: _bc_fn(t), deps=[sid] + (depsT or []))
        vid = seq.comp("vector", lambda v: _bc_ev(v), deps=[tid])
        return vid

    def _rstd_fn(s, cstcol, scl):
        for tcc in range(4):
            s.activation(out=strow[0:1, tcc * 512:(tcc + 1) * 512], in_=P[4 + tcc][0:1, :],
                         func=AF.Ln, bias=cst[0:1, cstcol:cstcol + 1], scale=scl)
            inst = s.activation(out=strow[0:1, tcc * 512:(tcc + 1) * 512],
                                in_=strow[0:1, tcc * 512:(tcc + 1) * 512], func=AF.Exp, scale=-0.5)
        return inst

    def _bc_fn(t):
        for tcc in range(4):
            inst = t.matmul(P[tcc][:, :], ones1[:], strow[0:1, tcc * 512:(tcc + 1) * 512],
                            start=True, stop=True)
        return inst

    def _bc_ev(v):
        for tcc in range(4):
            inst = v.tensor_copy(rstdB[:, tcc * 512:(tcc + 1) * 512], P[tcc][:, :])
        return inst

    pass0_dep = [None]
    if kind == "b3":
        seq.barrier()
        for k in range(16):
            seq.comp("scalar", lambda s, k=k: s.activation(
                out=hq[:, :], in_=onr[:, k * W:(k + 1) * W], func=AF.Square))

            def stm0(t, k=k):
                for tcc in range(4):
                    inst = t.matmul(P[4 + tcc][0:1, :], ones[:, 0:1], hq[:, tcc * 512:(tcc + 1) * 512],
                                    start=(k == 0), stop=(k == 15))
                return inst
            seq.comp("tensor", stm0)
        rstd_chain(0, 1.0 / D)

        def apply_on(v):
            for k in range(16):
                v.tensor_tensor(onr[:, k * W:(k + 1) * W], onr[:, k * W:(k + 1) * W], rstdB[:, :], Op.mult)
                inst = v.tensor_scalar(onr[:, k * W:(k + 1) * W], onr[:, k * W:(k + 1) * W],
                                       nw[:, 2 * k:2 * k + 1], None, Op.mult)
            return inst
        p0v = seq.comp("vector", apply_on)
        pass0_dep[0] = p0v
        seq.barrier()

    ld_prev = {}
    mm_last_on_buf = {}
    ev_last_on_bank = {}
    hs_store = {}
    ev_final = {}
    stat_prev = [None]
    for ht in range(NHT):
        bi = ht % 2
        prs = [(wgb[bi][:].rearrange("p (k m) -> p k m", m=128), gate_src(ht * 128))]
        if gated:
            prs.append((wub[bi][:].rearrange("p (k m) -> p k m", m=128), up_src(ht * 128)))
        idL = seq.dmas("sync", prs,
                       deps=(None if "ld" in FFN_P1X else D1([mm_last_on_buf.get(bi), id_xnr, id_onr])),
                       lane=(None if "ld" in FFN_P1X else f"wg{bi}"))
        ld_prev[bi] = idL
        for tc in range(4):
            bk = tc % 2
            g0, g1 = 2 * bk, 2 * bk + 1

            def mm1(t, ht=ht, tc=tc, bi=bi, g0=g0, g1=g1):
                for k in range(16):
                    inst = t.matmul(P[g0][:, :], wgb[bi][:, k * 128:(k + 1) * 128],
                                    xnr[:, k * W + tc * 512:k * W + (tc + 1) * 512],
                                    start=(k == 0), stop=(k == 15))
                if gated:
                    for k in range(16):
                        inst = t.matmul(P[g1][:, :], wub[bi][:, k * 128:(k + 1) * 128],
                                        xnr[:, k * W + tc * 512:k * W + (tc + 1) * 512],
                                        start=(k == 0), stop=(k == 15))
                return inst
            idM = seq.comp("tensor", mm1, deps=D1([idL, ev_last_on_bank.get(bk), id_xnr, pass0_dep[0]]))
            mm_last_on_buf[bi] = idM
            hdst = hsb[bi][:, tc * 512:(tc + 1) * 512]
            dep0 = [idM, hs_store.get(ht - 2)]
            if kind == "a3":
                s1 = seq.comp("scalar", lambda s, g0=g0, bk=bk: s.activation(
                    out=gsb[bk][:, :], in_=P[g0][:, :], func=AF.Silu), deps=D1(dep0))
                v1 = seq.comp("vector", lambda v, g1=g1, bk=bk, hdst=hdst: v.tensor_tensor(
                    hdst, gsb[bk][:, :], P[g1][:, :], Op.mult), deps=D1([s1]))
                evl = v1
            elif kind in ("b3", "c3"):
                s1 = seq.comp("scalar", lambda s, g0=g0, bk=bk: s.activation(
                    out=gsb[bk][:, :], in_=P[g0][:, :], func=AF.Relu), deps=D1(dep0))
                evl = seq.comp("vector", lambda v, bk=bk, hdst=hdst: v.tensor_tensor(
                    hdst, gsb[bk][:, :], gsb[bk][:, :], Op.mult), deps=D1([s1]))
            else:
                C1 = 0.044715
                C2 = 0.7978845608028654

                def s1f(s, g0=g0, bk=bk):
                    s.activation(out=gsb[bk][:, :], in_=P[g0][:, :], func=AF.Identity)
                    return s.activation(out=g2[bk][:, :], in_=P[g0][:, :], func=AF.Square)
                s1 = seq.comp("scalar", s1f, deps=D1(dep0))

                def v1f(v, bk=bk, C1=C1):
                    v.tensor_tensor(g3[bk][:, :], g2[bk][:, :], gsb[bk][:, :], Op.mult)
                    return v.scalar_tensor_tensor(g3[bk][:, :], g3[bk][:, :], C1, gsb[bk][:, :],
                                                  Op.mult, Op.add)
                v1 = seq.comp("vector", v1f, deps=D1([s1]))
                s2 = seq.comp("scalar", lambda s, bk=bk, C2=C2: s.activation(
                    out=g3[bk][:, :], in_=g3[bk][:, :], func=AF.Tanh, scale=C2), deps=D1([v1]))

                def v2f(v, bk=bk, g1=g1, hdst=hdst):
                    v.scalar_tensor_tensor(g3[bk][:, :], g3[bk][:, :], 1.0, gsb[bk][:, :],
                                           Op.add, Op.mult)
                    return v.scalar_tensor_tensor(hdst, g3[bk][:, :], 0.5, P[g1][:, :],
                                                  Op.mult, Op.mult)
                evl = seq.comp("vector", v2f, deps=D1([s2]))
            ev_last_on_bank[bk] = evl
            ev_final[(ht, tc)] = evl
        evs = [ev_final[(ht, tc)] for tc in range(4)]
        if has_rms:
            sq = seq.comp("scalar", lambda s, bi=bi: s.activation(
                out=hq[:, :], in_=hsb[bi][:, :], func=AF.Square),
                deps=(None if "sq" in FFN_P1X else D1(evs + [stat_prev[0]])))

            def stm(t, ht=ht):
                for tcc in range(4):
                    inst = t.matmul(P[4 + tcc][0:1, :], ones[:, 0:1], hq[:, tcc * 512:(tcc + 1) * 512],
                                    start=(ht == 0), stop=(ht == NHT - 1))
                return inst
            stat_prev[0] = seq.comp("tensor", stm, deps=D1([sq]))
        hs_store[ht] = seq.dma("scalar", hscr.ap()[ht * 128:(ht + 1) * 128, :], hsb[bi][:, :],
                               deps=(None if "st" in FFN_P1X else D1(evs)),
                               lane=(None if "st" in FFN_P1X else f"hs{bi}"))

    if has_rms:
        cstcol = 0 if kind == "a3" else 2
        rstd_chain(cstcol, 1.0 / Hdim, deps0=[stat_prev[0]],
                   depsT=[ev_final[(NHT - 1, tcx)] for tcx in range(4)])
    seq.barrier()

    NDK = NHT
    prev_group_ev = list(seq._bar)
    ld_k = {}
    ld_w = {}
    sc_k = {}
    mm_k = {}
    st_ob = {}
    for tc in range(4):
        cs = slice(tc * 512, (tc + 1) * 512)
        for g in range(2):
            first_deps = list(prev_group_ev)
            for k in range(NDK):
                bi = k % NB
                extra = first_deps if k < NB else []
                dsl = down_srcs(k, g)
                kprs = [(kch[bi][:, :], hscr.ap()[k * 128:(k + 1) * 128, cs])]
                if len(dsl) > 1:
                    c0w, wdw, srcw = dsl[-1]
                    kprs.append((wch[bi][:, c0w:c0w + wdw], srcw))
                    dsl = dsl[:-1]
                idk = seq.dmas("sync", kprs,
                               deps=D2([(sc_k if has_rms else mm_k).get(k - NB),
                                        mm_k.get(k - NB)] + extra), lane=f"kc{bi}")
                idw = seq.dmas("gpsimd", [(wch[bi][:, c0w:c0w + wdw], srcw)
                                          for c0w, wdw, srcw in dsl],
                               deps=D2([mm_k.get(k - NB)] + extra), lane=f"wd{bi}")
                idw2 = idk
                if has_rms:
                    ids = seq.comp("vector", lambda v, bi=bi, tc=tc: v.tensor_tensor(
                        kch2[bi][:, :], kch[bi][:, :], rstdB[:, tc * 512:(tc + 1) * 512], Op.mult),
                        deps=D2([idk, mm_k.get(k - NB)] + extra))
                    sc_k[k] = ids
                    rhs = kch2[bi]
                    mmdeps = [ids, idw, idw2]
                else:
                    rhs = kch[bi]
                    mmdeps = [idk, idw, idw2]
                if k == 0:
                    mmdeps += first_deps

                def mmd(t, k=k, rhs=rhs, last=(k == NDK - 1 and not has_attn)):
                    for j in range(8):
                        inst = t.matmul(P[j][:, :], wch[k % NB][:, j * 128:(j + 1) * 128], rhs[:, :],
                                        start=(k == 0), stop=last)
                    return inst
                mm_k[k] = seq.comp("tensor", mmd, deps=D2(mmdeps))
            last_mm = mm_k[NDK - 1]
            if has_attn:
                wsrc = cfg["attn_w_in"]
                for k in range(16):
                    bi = k % NB
                    idw = seq.dmas("gpsimd", [(wch[bi][:, 0:512], wsrc[k * 128:(k + 1) * 128, g * 1024:g * 1024 + 512])],
                                   deps=D2a([mm_k.get(NDK + k - NB, last_mm if k < NB else None)]), lane=f"wd{bi}")
                    idw2 = seq.dmas("sync", [(wch[bi][:, 512:1024], wsrc[k * 128:(k + 1) * 128, g * 1024 + 512:(g + 1) * 1024])],
                                    deps=D2a([mm_k.get(NDK + k - NB, last_mm if k < NB else None)]), lane=f"ww{bi}")
                    if kind == "a3":
                        idk = seq.dma("sync", kchy[bi][:, :],
                                      cfg["y_in"].rearrange("(kk p) t -> p kk t", p=128)[:, k, cs],
                                      deps=D2a([mm_k.get(NDK + k - NB)]), lane=f"ky{bi}")
                        rhs2 = kchy[bi]
                        mmdeps = [idw, idw2, idk]
                    else:
                        rhs2 = None
                        mmdeps = [idw, idw2, id_onr]

                    def mma(t, k=k, bi=bi, rhs2=rhs2, tc=tc):
                        for j in range(8):
                            if rhs2 is not None:
                                inst = t.matmul(P[j][:, :], wch[bi][:, j * 128:(j + 1) * 128],
                                                rhs2[:, :], start=False, stop=(k == 15))
                            else:
                                inst = t.matmul(P[j][:, :], wch[bi][:, j * 128:(j + 1) * 128],
                                                onr[:, k * W + tc * 512:k * W + (tc + 1) * 512],
                                                start=False, stop=(k == 15))
                        return inst
                    mm_k[NDK + k] = seq.comp("tensor", mma, deps=D2a(mmdeps))
                last_mm = mm_k[NDK + 15]
            group_ev = []
            for j in range(8):
                dout = g * 8 + j
                bj = j % 2
                idx = seq.dma("sync", xres[bj][:, :],
                              xfm_in.rearrange("(kk p) t -> p kk t", p=128)[:, dout, cs],
                              deps=D2e([st_ob.get((tc, g, j - 2))]), lane=f"xr{bj}")
                if kind == "c3":
                    idy = seq.dma("sync", ych[bj][:, :],
                                  cfg["y_in"].rearrange("(kk p) t -> p kk t", p=128)[:, dout, cs],
                                  deps=D2e([st_ob.get((tc, g, j - 2))]), lane=f"yc{bj}")

                    def evd(v, j=j, bj=bj, dout=dout, tc=tc):
                        v.scalar_tensor_tensor(ych[bj][:, :],
                                               xnr[:, dout * W + tc * 512:dout * W + (tc + 1) * 512],
                                               dv[:, 2 * dout:2 * dout + 1], ych[bj][:, :], Op.mult, Op.add)
                        v.tensor_tensor(ob[bj][:, :], P[j][:, :], xres[bj][:, :], Op.add)
                        return v.tensor_tensor(ob[bj][:, :], ob[bj][:, :], ych[bj][:, :], Op.add)
                    ev = seq.comp("vector", evd, deps=D2e([last_mm, idx, idy, st_ob.get((tc, g, j - 2))]))
                elif kind == "a3":
                    sev = seq.comp("scalar", lambda s, j=j, bj=bj, dout=dout: s.activation(
                        out=ob[bj][:, :], in_=P[j][:, :], func=AF.Identity,
                        bias=opb[:, 2 * dout:2 * dout + 1]),
                        deps=D2e([last_mm, st_ob.get((tc, g, j - 2))]))
                    ev = seq.comp("vector", lambda v, bj=bj: v.tensor_tensor(
                        ob[bj][:, :], ob[bj][:, :], xres[bj][:, :], Op.add), deps=D2e([sev, idx]))
                else:
                    ev = seq.comp("vector", lambda v, j=j, bj=bj: v.tensor_tensor(
                        ob[bj][:, :], P[j][:, :], xres[bj][:, :], Op.add),
                        deps=D2e([last_mm, idx, st_ob.get((tc, g, j - 2))]))
                group_ev.append(ev)
                st_ob[(tc, g, j)] = seq.dma("scalar",
                                            xo_out.rearrange("(kk p) t -> p kk t", p=128)[:, dout, cs],
                                            ob[bj][:, :], deps=D2e([ev]), lane=f"so{bj}")
            prev_group_ev = group_ev + [st_ob[(tc, g, j)] for j in range(8)]
            ld_k.clear()
            ld_w.clear()
            sc_k.clear()
            mm_k.clear()
    seq.barrier()
    es.close()


DBG = {}


def kernel_unfused(**inp):
    import math
    ones1 = np.ones((1, 128), np.float32)
    iden = np.eye(128, dtype=bf16)
    cores = list(range(NCORES))

    res = run_bass_kernel_spmd(build_p0(), p0_host_inputs(inp), core_ids=cores)
    prep = p0_collect(res.results)

    x = np.ascontiguousarray(np.asarray(inp["L0_attn_x"], np.float32).reshape(TOK, D).T)
    res = run_bass_kernel_spmd(build_a1(), a1_host_inputs(inp, prep), core_ids=cores)
    xn0 = np.concatenate([r["xn_o"] for r in res.results], 1)
    x0f = np.concatenate([r["x0_o"] for r in res.results], 1)
    v2f = np.concatenate([r["v2_o"] for r in res.results], 1)

    res = run_bass_kernel_spmd(build_a2(), a2_host_inputs(inp, v2f, x0f), core_ids=cores)
    yf = np.concatenate([r["y_o"] for r in res.results], 0)

    opb = np.zeros((D, 2), np.float32)
    opb[:, 0] = np.asarray(inp["L0_attn_out_proj_b"], np.float32)
    maps = [dict(xfm=np.ascontiguousarray(x[:, c * TSH:(c + 1) * TSH]),
                 xn=np.ascontiguousarray(xn0[:, c * TSH:(c + 1) * TSH]),
                 yv=np.ascontiguousarray(yf[:, c * TSH:(c + 1) * TSH]),
                 opwT=prep["opwT"], opb=opb, wguT=prep["l0wgu"], wprT=prep["l0wpr"],
                 ones1=ones1) for c in range(NCORES)]
    res = run_bass_kernel_spmd(build_a3(), maps, core_ids=cores)
    x1 = np.concatenate([r["xo_o"] for r in res.results], 1)
    DBG["x1"] = x1

    maps = [dict(xfm=np.ascontiguousarray(x1[:, c * TSH:(c + 1) * TSH]), wqT=prep["wqT"],
                 wkT=prep["wkT"], wvT=prep["wvT"], ones1=ones1) for c in range(NCORES)]
    res = run_bass_kernel_spmd(build_b1(), maps, core_ids=cores)
    xn1 = np.concatenate([r["xn_o"] for r in res.results], 1)
    qf = np.concatenate([r["q_o"] for r in res.results], 1)
    kf = np.concatenate([r["k_o"] for r in res.results], 1)
    vf = np.concatenate([r["v_o"] for r in res.results], 1)
    lff = np.concatenate([r["lf_o"] for r in res.results], 1)

    maps = b2_host_inputs(qf, kf, np.ascontiguousarray(kf.T), np.ascontiguousarray(vf.T),
                          np.ascontiguousarray(lff.T))
    res = run_bass_kernel_spmd(build_b2(), maps, core_ids=cores)
    o_tm = np.zeros((TOK, D), np.float32)
    for c in range(NCORES):
        for pi in range(4):
            p = 4 * c + pi
            b, h = p // 16, p % 16
            o_tm[b * T:(b + 1) * T, h * 128:(h + 1) * 128] = res.results[c]["o_o"][pi].astype(np.float32)
    o_fm = np.ascontiguousarray(o_tm.T.astype(bf16))

    nw2 = np.zeros((D, 2), np.float32)
    nw2[:, 0] = np.asarray(inp["L1_attn_norm_w"], np.float32)
    maps = [dict(xfm=np.ascontiguousarray(x1[:, c * TSH:(c + 1) * TSH]),
                 xn=np.ascontiguousarray(xn1[:, c * TSH:(c + 1) * TSH]),
                 ofm=np.ascontiguousarray(o_fm[:, c * TSH:(c + 1) * TSH]),
                 woT=prep["woT"], nw=nw2, wfcT=prep["l1wfc"], wprT=prep["l1wpr"],
                 ones1=ones1) for c in range(NCORES)]
    res = run_bass_kernel_spmd(build_b3(), maps, core_ids=cores)
    x2 = np.concatenate([r["xo_o"] for r in res.results], 1)
    DBG["x2"] = x2

    maps = [dict(xfm=np.ascontiguousarray(x2[:, c * TSH:(c + 1) * TSH]), wxT=prep["wxT"],
                 ones1=ones1) for c in range(NCORES)]
    res = run_bass_kernel_spmd(build_c1(), maps, core_ids=cores)
    xn2 = np.concatenate([r["xn_o"] for r in res.results], 1)
    xpf = np.concatenate([r["xp_o"] for r in res.results], 1)

    alog = np.asarray(inp["L2_attn_A_log"], np.float32)
    maps = []
    for c in range(NCORES):
        rs = slice(c * 256, (c + 1) * 256)
        am = np.zeros((128, 64), np.float32)
        al = alog[rs]
        for ct in range(2):
            am[:, ct * 32:ct * 32 + 32:2] = al[ct * 128:(ct + 1) * 128]
        maps.append(dict(xpd=np.ascontiguousarray(xpf[rs]), xnb=np.ascontiguousarray(xn2[rs]),
                         bcr=np.ascontiguousarray(xpf[2048:2080]), alog=am, ones1=ones1))
    res = run_bass_kernel_spmd(build_c2(), maps, core_ids=cores)
    yssm = np.concatenate([r["y_o"] for r in res.results], 0)

    dv2 = np.zeros((D, 2), np.float32)
    dv2[:, 0] = np.asarray(inp["L2_attn_D"], np.float32)
    maps = [dict(xfm=np.ascontiguousarray(x2[:, c * TSH:(c + 1) * TSH]),
                 xn=np.ascontiguousarray(xn2[:, c * TSH:(c + 1) * TSH]),
                 yssm=np.ascontiguousarray(yssm[:, c * TSH:(c + 1) * TSH]),
                 dv=dv2, wfcT=prep["l2wfc"], wprT=prep["l2wpr"]) for c in range(NCORES)]
    res = run_bass_kernel_spmd(build_c3(), maps, core_ids=cores)
    x3 = np.concatenate([r["xo_o"] for r in res.results], 1)
    DBG["x3"] = x3

    HD = 128
    inv_freq = 1.0 / 10000.0 ** (np.arange(0, HD, 2) / HD)
    pos = np.arange(float(T))
    freqs = pos[:, None] * inv_freq[None, :]
    emb = np.concatenate([freqs, freqs], -1)
    cos_rep = np.tile(np.cos(emb).astype(bf16), (1, 16))
    sin_rep = np.tile(np.sin(emb).astype(bf16), (1, 16))
    maps = []
    for c in range(NCORES):
        pos0 = (c * TSH) % T
        maps.append(dict(xfm=np.ascontiguousarray(x3[:, c * TSH:(c + 1) * TSH]),
                         wq=prep["l3wq"], wk=prep["l3wk"], wv=prep["l3wv"],
                         cosr=np.ascontiguousarray(cos_rep[pos0:pos0 + TSH]),
                         sinr=np.ascontiguousarray(sin_rep[pos0:pos0 + TSH]), ones1=ones1))
    res = run_bass_kernel_spmd(build_d1(), maps, core_ids=cores)
    xn3 = np.concatenate([r["xn_o"] for r in res.results], 1)
    qtm = np.concatenate([r["q_o"] for r in res.results], 0)
    ktm = np.concatenate([r["k_o"] for r in res.results], 0)
    vtm = np.concatenate([r["v_o"] for r in res.results], 0)

    maps = []
    for c in range(NCORES):
        qs, ks, vs = [], [], []
        for pi in range(4):
            p = 4 * c + pi
            b, h = p // 16, p % 16
            hk = h // 4
            qs.append(qtm[b * T:(b + 1) * T, h * 128:(h + 1) * 128])
            ks.append(ktm[b * T:(b + 1) * T, hk * 128:(hk + 1) * 128])
            vs.append(vtm[b * T:(b + 1) * T, hk * 128:(hk + 1) * 128])
        maps.append(dict(qtm=np.ascontiguousarray(np.concatenate(qs, 0)),
                         ktm=np.ascontiguousarray(np.concatenate(ks, 0)),
                         vtm=np.ascontiguousarray(np.concatenate(vs, 0)),
                         iden=iden, ones1=ones1))
    res = run_bass_kernel_spmd(build_d2(), maps, core_ids=cores)
    o3 = np.zeros((D, TOK), np.float32)
    for c in range(NCORES):
        for pi in range(4):
            p = 4 * c + pi
            b, h = p // 16, p % 16
            o3[h * 128:(h + 1) * 128, b * T:(b + 1) * T] = res.results[c]["o_o"][pi].astype(np.float32)
    o3b = o3.astype(bf16)

    maps = [dict(xfm=np.ascontiguousarray(x3[:, c * TSH:(c + 1) * TSH]),
                 xn=np.ascontiguousarray(xn3[:, c * TSH:(c + 1) * TSH]),
                 ofm=np.ascontiguousarray(o3b[:, c * TSH:(c + 1) * TSH]),
                 wo=prep["l3wo"], wgT=prep["l3g"], wuT=prep["l3u"], wdT=prep["l3d"])
            for c in range(NCORES)]
    res = run_bass_kernel_spmd(build_d3(), maps, core_ids=cores)
    x4 = np.concatenate([r["xo_o"] for r in res.results], 1)
    DBG["x4"] = x4
    return np.ascontiguousarray(x4.T).reshape(B, T, D).astype(np.float32)


def _sw_dmas(seq, eng, n, dmas_fn, deps=None):
    def wrap(e, h, tgt):
        for arm in e.switch_core_id(n=128):
            c = arm.logical % NCORES
            for dst, src in dmas_fn(c):
                e.dma_start(out=dst, in_=src).then_inc(h, 16)
    return seq.step(eng, 16 * n, wrap, deps=deps, async_=True)


def _ag_step(seq, nc, src_t, dst_t, name, grouped=False):
    rg = [[0, 1, 2, 3], [4, 5, 6, 7]] if grouped else [list(range(NCORES))]

    def fn(g, h, tgt):
        g.collective_compute("AllGather", Op.bypass, replica_groups=rg,
                             ins=[src_t.ap().opt()], outs=[dst_t.ap().opt()]).then_inc(h, 1)
    return seq.step("gpsimd", 1, fn, async_=True)


class _NcProxy:

    def __init__(self, nc, pfx):
        self._nc = nc
        self._pfx = pfx

    def sbuf_tensor(self, name, shape, dt):
        return self._nc.sbuf_tensor(self._pfx + name, shape, dt)

    def psum_tensor(self, name, shape, dt):
        return self._nc.psum_tensor(self._pfx + name, shape, dt)

    def __getattr__(self, k):
        return getattr(self._nc, k)


def _mk_wcol_src(t, Rs):
    r = t.ap().rearrange("(blk k p) m -> p blk k m", blk=NCORES, p=128)
    return lambda m0: r[:, m0 // Rs, :, m0 % Rs:m0 % Rs + 128]


def _mk_wpr_srcs(t, Ktot):
    ap = t.ap()

    def f(k, g):
        return [(jj * 256, 256,
                 ap[(g * 4 + jj) * Ktot + k * 128:(g * 4 + jj) * Ktot + (k + 1) * 128, 0:256])
                for jj in range(4)]
    return f


def build_full(upto="d"):
    nc = bass.Bass("TRN2", target_bir_lowering=False, debug=False)
    oes = ExitStack()
    seq = Seq(nc, oes, "fseq")
    H = {}

    def ein(name, shape, dt):
        H[name] = nc.dram_tensor(name, shape, dt, kind="ExternalInput")
        return H[name]

    def itn(name, shape, dt, shared=False):
        H[name] = nc.dram_tensor(name, shape, dt,
                                 addr_space="Shared" if shared else "Local")
        return H[name]

    def eout(name, shape, dt):
        H[name] = nc.dram_tensor(name, shape, dt, kind="ExternalOutput")
        return H[name]

    W = TSH
    do_b = upto in ("b", "c", "d")
    do_c = upto in ("c", "d")
    do_d = upto == "d"
    for name, R, K, kind, tr in P0_SPECS:
        Rs = R // NCORES
        ein(name, [Rs, K], F32)
        oshape = [K, Rs] if tr else [Rs, K]
        itn(name + "_o", oshape, BF16)
        itn(name + "_all", [NCORES * oshape[0], oshape[1]], BF16, shared=True)
    for name, R, Kp, Kd in P0_NV:
        Rs = R // NCORES
        ein(name + "_pk", [Rs, Kp], I32)
        ein(name + "_sc", [Rs, Kd // 32], F32)
        itn(name + "_o", [Kd, Rs], BF16)
        itn(name + "_all", [NCORES * Kd, Rs], BF16, shared=True)
    ein("iden", [128, 128], BF16)
    ein("icst", [128, 2], I32)
    itn("gpart", [1, 16], F32)
    itn("gsum", [1, 16], F32)
    ein("a1_xfm", [D, W + 2], F32)
    ein("a1_ipb", [3 * D, 2], F32)
    ein("a1_shw", [3 * D, 8], F32)
    ein("a1_shb", [3 * D, 8], F32)
    ein("a1_hsc", [128, 2], F32)
    ein("ones1", [1, 128], F32)
    ein("a_xres", [D, W], F32)
    itn("xn0", [D, W], BF16)
    itn("v2_loc", [D, W], BF16)
    itn("x0_loc", [D, W], BF16)
    itn("v2_all", [NCORES * D, W], BF16, shared=True)
    itn("x0_all", [NCORES * D, W], BF16, shared=True)
    itn("v2_cs", [256, TOK], BF16)
    itn("x0_cs", [256, TOK], BF16)
    for nm, shape, dt in [("zT", [3, T], F32), ("w0T", [3, 64], F32), ("w1T", [64, 64], F32),
                          ("w2T", [64, 64], F32), ("w3T", [64, 256], F32), ("fvec", [64, 8], F32),
                          ("trow", [1, T], F32), ("nds", [256, 2], F32), ("fbv", [256, 2], F32),
                          ("wre", [128, 128], BF16), ("wim", [128, 128], BF16), ("wimn", [128, 128], BF16),
                          ("wc64", [64, 256], BF16), ("wpd", [128, 256], BF16), ("wpd2", [128, 256], BF16),
                          ("twfr", [128, 128], F32), ("twfi", [128, 128], F32),
                          ("twir", [128, 128], F32), ("twii", [128, 128], F32)]:
        ein("a2_" + nm, shape, dt)
    itn("a2_kscr", [256, T], BF16)
    itn("a2_kfscr", [4, 128, 64 * 256], BF16)
    itn("a2_yscr", [256, B * T], BF16)
    itn("y0_loc", [256, TOK], BF16)
    itn("y0_all", [NCORES * 256, TOK], BF16, shared=True)
    itn("y0_cs", [D, W], BF16)
    itn("a3_hscr", [8192, W], BF16)
    itn("xr1", [D, W], F32)
    ein("a3_opb", [D, 2], F32)
    itn("xn1", [D, W], BF16)
    itn("q1_loc", [D, W], BF16)
    itn("k1_loc", [D, W], BF16)
    itn("v1_loc", [D, W], BF16)
    itn("lf1_loc", [D, W], FP16)
    itn("q1_all", [4 * D, W], BF16)
    itn("k1_all", [4 * D, W], BF16)
    itn("v1_all", [4 * D, W], BF16)
    itn("lf1_all", [4 * D, W], FP16)
    ein("b2_utri", [64, 64], FP16)
    ein("b2_mtri", [64, 64], BF16)
    itn("o1_loc", [512, T], BF16)
    itn("o1_all", [4 * 512, T], BF16)
    ein("b3_nw", [D, 2], F32)
    itn("b3_hscr", [8192, W], BF16)
    itn("xr2", [D, W], F32)
    itn("xn2", [D, W], BF16)
    itn("xp_loc", [2080, W], BF16)
    itn("xn2_all", [NCORES * D, W], BF16, shared=True)
    itn("xp_all", [NCORES * 2080, W], BF16, shared=True)
    itn("c2_xpd", [256, TOK], BF16)
    itn("c2_xnb", [256, TOK], BF16)
    itn("c2_bcr", [32, TOK], BF16)
    ein("c2_alog", [128, 64], F32)
    itn("y2_loc", [256, TOK], F32)
    itn("y2_all", [NCORES * 256, TOK], F32, shared=True)
    itn("y2_cs", [D, W], F32)
    ein("c3_dv", [D, 2], F32)
    itn("c3_hscr", [5120, W], BF16)
    itn("xr3", [D, W], F32)
    itn("xn3", [D, W], BF16)
    ein("d1_cosr", [W, D], BF16)
    ein("d1_sinr", [W, D], BF16)
    itn("q3_loc", [W, D], BF16)
    itn("k3_loc", [W, 512], BF16)
    itn("v3_loc", [W, 512], BF16)
    itn("q3_all", [4 * W, D], BF16)
    itn("k3_all", [4 * W, 512], BF16)
    itn("v3_all", [4 * W, 512], BF16)
    itn("d2_q", [4 * T, 128], BF16)
    itn("d2_k", [4 * T, 128], BF16)
    itn("d2_v", [4 * T, 128], BF16)
    itn("d2_bscr", [1, 16], F32)
    itn("o3_loc", [4, 128, T], BF16)
    itn("o3_all", [4 * 4, 128, T], BF16)
    itn("d3_hscr", [16384, W], BF16)
    if upto == "a":
        eout("xr1_dbg", [D, W], F32)
    elif upto == "b":
        eout("xr2_dbg", [D, W], F32)
    elif upto == "c":
        eout("xr3_dbg", [D, W], F32)
    else:
        eout("xo_f", [D, W], F32)

    ins, outs = {}, {}
    for name, R, K, kind, tr in P0_SPECS:
        ins[name] = H[name].ap()
        outs[name] = H[name + "_o"].ap()
    for name, R, Kp, Kd in P0_NV:
        ins[name + "_pk"] = H[name + "_pk"].ap()
        ins[name + "_sc"] = H[name + "_sc"].ap()
        outs[name] = H[name + "_o"].ap()
    seq.barrier()
    seq.label("p0")
    emit_p0p(_NcProxy(nc, "p0_"), seq, dict(ins=ins, outs=outs, iden_in=H["iden"].ap(), icst_in=H["icst"].ap(),
                            gpart=H["gpart"], gsum=H["gsum"]))
    seq.barrier()
    seq.label("ag_w")
    for name, R, K, kind, tr in P0_SPECS:
        _ag_step(seq, nc, H[name + "_o"], H[name + "_all"], name)
    for name, R, Kp, Kd in P0_NV:
        _ag_step(seq, nc, H[name + "_o"], H[name + "_all"], name)

    seq.barrier()
    seq.label("a1")
    emit_a1(_NcProxy(nc, "a1_"), seq, dict(
        xfm_in=H["a1_xfm"].ap(), ipw_in=H["ipwT_all"].ap(), ipb_in=H["a1_ipb"].ap(),
        shw_in=H["a1_shw"].ap(), shb_in=H["a1_shb"].ap(), hsc_in=H["a1_hsc"].ap(),
        o1_in=H["ones1"].ap(), xn_out=H["xn0"].ap(), x0_out=H["x0_loc"].ap(),
        v2_out=H["v2_loc"].ap()))
    seq.barrier()
    seq.label("ag_a")
    _ag_step(seq, nc, H["v2_loc"], H["v2_all"], "agv2")
    _ag_step(seq, nc, H["x0_loc"], H["x0_all"], "agx0")

    def loc_chan2(dst_t, src_t, blk_rows, r_of_c, RB):
        def dmas(c, dst_t=dst_t, src_t=src_t, blk_rows=blk_rows, r_of_c=r_of_c, RB=RB):
            r0 = r_of_c(c)
            return [(dst_t.ap()[:, j * W:(j + 1) * W],
                     src_t.ap()[j * blk_rows + r0:j * blk_rows + r0 + RB, :])
                    for j in range(NCORES)]
        _sw_dmas(seq, "sync", NCORES, dmas)

    def loc_tok(dst_t, src_t, RB):
        def dmas(c, dst_t=dst_t, src_t=src_t, RB=RB):
            return [(dst_t.ap()[s * RB:(s + 1) * RB, :],
                     src_t.ap()[s * RB:(s + 1) * RB, c * W:(c + 1) * W])
                    for s in range(NCORES)]
        _sw_dmas(seq, "sync", NCORES, dmas)

    loc_chan2(H["v2_cs"], H["v2_all"], D, lambda c: c * 256, 256)
    loc_chan2(H["x0_cs"], H["x0_all"], D, lambda c: c * 256, 256)
    a2io = dict(v2_in=H["v2_cs"].ap(), x0_in=H["x0_cs"].ap(), y_out=H["y0_loc"].ap(),
                kscr=H["a2_kscr"], kfscr=H["a2_kfscr"], yscr=H["a2_yscr"], dbg=False)
    for nm in ["zT", "w0T", "w1T", "w2T", "w3T", "fvec", "trow", "nds", "wre", "wim",
               "wimn", "wc64", "wpd", "wpd2", "twfr", "twfi", "twir", "twii"]:
        a2io[nm + "_in"] = H["a2_" + nm].ap()
    a2io["fb_in"] = H["a2_fbv"].ap()
    seq.barrier()
    seq.label("a2")
    emit_a2(_NcProxy(nc, "a2_"), seq, a2io)
    seq.barrier()
    seq.label("ag_y0")
    _ag_step(seq, nc, H["y0_loc"], H["y0_all"], "agy0")
    loc_tok(H["y0_cs"], H["y0_all"], 256)
    seq.barrier()
    seq.label("a3")
    _gs0 = _mk_wcol_src(H["l0wgu_all"], 2048)
    emit_ffn2(_NcProxy(nc, "a3_"), seq, dict(
        kind="a3", Hdim=8192,
        xfm_in=H["a_xres"].ap(), xn_in=H["xn0"].ap(), y_in=H["y0_cs"].ap(),
        attn_w_in=H["opwT_all"].ap(), opb_in=H["a3_opb"].ap(), o1_in=H["ones1"].ap(),
        xo_out=H["xr1"].ap(), hscr=H["a3_hscr"],
        gate_src=_gs0, up_src=lambda m0: _gs0(8192 + m0),
        down_srcs=_mk_wpr_srcs(H["l0wpr_all"], 8192)))
    if upto == "a":
        seq.dma("scalar", H["xr1_dbg"].ap(), H["xr1"].ap())

    if do_b:
        seq.barrier()
        seq.label("b1")
        emit_b1(_NcProxy(nc, "b1_"), seq, dict(
            xfm_in=H["xr1"].ap(), wq_in=H["wqT_all"].ap(), wk_in=H["wkT_all"].ap(),
            wv_in=H["wvT_all"].ap(), o1_in=H["ones1"].ap(), xn_out=H["xn1"].ap(),
            q_out=H["q1_loc"].ap(), k_out=H["k1_loc"].ap(), v_out=H["v1_loc"].ap(),
            lf_out=H["lf1_loc"].ap()))
        seq.barrier()
        seq.label("ag_b1")
        for nm in ["q1", "k1", "v1", "lf1"]:
            _ag_step(seq, nc, H[nm + "_loc"], H[nm + "_all"], nm, grouped=True)
        seq.barrier()
        seq.label("b2")
        emit_b2f(_NcProxy(nc, "b2_"), seq, dict(
            q_all=H["q1_all"].ap(), k_all=H["k1_all"].ap(), v_all=H["v1_all"].ap(),
            lf_all=H["lf1_all"].ap(), utri_in=H["b2_utri"].ap(), mtri_in=H["b2_mtri"].ap(),
            i128_in=H["iden"].ap(), o_out=H["o1_loc"].ap()))
        seq.barrier()
        seq.label("ag_o1")
        _ag_step(seq, nc, H["o1_loc"], H["o1_all"], "ago1", grouped=True)

        def onr_fill_b(seq2, onr):
            o1a = H["o1_all"].ap()

            def dmas(c):
                jb = c % 4
                L = []
                for kk in range(16):
                    sl, pi = kk // 4, kk % 4
                    L.append((onr[:, kk * W:(kk + 1) * W],
                              o1a[sl * 512 + pi * 128:sl * 512 + pi * 128 + 128,
                                  jb * W:(jb + 1) * W]))
                return L
            _sw_dmas(seq2, "sync", 16, dmas)
        seq.barrier()
        seq.label("b3")
        emit_ffn2(_NcProxy(nc, "b3_"), seq, dict(
            kind="b3", Hdim=8192,
            xfm_in=H["xr1"].ap(), xn_in=H["xn1"].ap(), attn_w_in=H["woT_all"].ap(),
            nw_in=H["b3_nw"].ap(), o1_in=H["ones1"].ap(), xo_out=H["xr2"].ap(),
            hscr=H["b3_hscr"], onr_fill=onr_fill_b,
            gate_src=_mk_wcol_src(H["l1wfc_all"], 1024),
            down_srcs=_mk_wpr_srcs(H["l1wpr_all"], 8192)))
        if upto == "b":
            seq.dma("scalar", H["xr2_dbg"].ap(), H["xr2"].ap())

    if do_c:
        seq.barrier()
        seq.label("c1")
        emit_c1(_NcProxy(nc, "c1_"), seq, dict(
            xfm_in=H["xr2"].ap(), wx_in=H["wxT_all"].ap(), o1_in=H["ones1"].ap(),
            xn_out=H["xn2"].ap(), xp_out=H["xp_loc"].ap()))
        seq.barrier()
        seq.label("ag_c1")
        _ag_step(seq, nc, H["xn2"], H["xn2_all"], "agxn2")
        _ag_step(seq, nc, H["xp_loc"], H["xp_all"], "agxp")
        loc_chan2(H["c2_xpd"], H["xp_all"], 2080, lambda c: c * 256, 256)
        loc_chan2(H["c2_xnb"], H["xn2_all"], D, lambda c: c * 256, 256)
        for j in range(NCORES):
            seq.dma("sync", H["c2_bcr"].ap()[:, j * W:(j + 1) * W],
                    H["xp_all"].ap()[j * 2080 + 2048:j * 2080 + 2080, :])
        seq.barrier()
        seq.label("c2")
        emit_c2p(_NcProxy(nc, "c2_"), seq, dict(
            xpd_in=H["c2_xpd"].ap(), xnb_in=H["c2_xnb"].ap(), bc_in=H["c2_bcr"].ap(),
            alog_in=H["c2_alog"].ap(), y_out=H["y2_loc"].ap()))
        seq.barrier()
        seq.label("ag_y2")
        _ag_step(seq, nc, H["y2_loc"], H["y2_all"], "agy2")
        loc_tok(H["y2_cs"], H["y2_all"], 256)
        seq.barrier()
        seq.label("c3")
        emit_ffn2(_NcProxy(nc, "c3_"), seq, dict(
            kind="c3", Hdim=5120,
            xfm_in=H["xr2"].ap(), xn_in=H["xn2"].ap(), y_in=H["y2_cs"].ap(),
            dv_in=H["c3_dv"].ap(), xo_out=H["xr3"].ap(), hscr=H["c3_hscr"],
            gate_src=_mk_wcol_src(H["l2wfc_all"], 640),
            down_srcs=_mk_wpr_srcs(H["l2wpr_all"], 5120)))
        if upto == "c":
            seq.dma("scalar", H["xr3_dbg"].ap(), H["xr3"].ap())

    if do_d:
        seq.barrier()
        seq.label("d1")
        emit_d1(_NcProxy(nc, "d1_"), seq, dict(
            xfm_in=H["xr3"].ap(), wq_in=H["l3wq_all"].ap(), wk_in=H["l3wk_all"].ap(),
            wv_in=H["l3wv_all"].ap(), cosr_in=H["d1_cosr"].ap(), sinr_in=H["d1_sinr"].ap(),
            o1_in=H["ones1"].ap(), xn_out=H["xn3"].ap(), q_out=H["q3_loc"].ap(),
            k_out=H["k3_loc"].ap(), v_out=H["v3_loc"].ap()))
        seq.barrier()
        seq.label("ag_d1")
        _ag_step(seq, nc, H["q3_loc"], H["q3_all"], "agq3", grouped=True)
        _ag_step(seq, nc, H["k3_loc"], H["k3_all"], "agk3", grouped=True)
        _ag_step(seq, nc, H["v3_loc"], H["v3_all"], "agv3", grouped=True)

        def dmas_d2(c):
            L = []
            for pi in range(4):
                h = (4 * c + pi) % 16
                hk = h // 4
                L.append((H["d2_q"].ap()[pi * T:(pi + 1) * T, :],
                          H["q3_all"].ap()[0:T, h * 128:(h + 1) * 128]))
                L.append((H["d2_k"].ap()[pi * T:(pi + 1) * T, :],
                          H["k3_all"].ap()[0:T, hk * 128:(hk + 1) * 128]))
                L.append((H["d2_v"].ap()[pi * T:(pi + 1) * T, :],
                          H["v3_all"].ap()[0:T, hk * 128:(hk + 1) * 128]))
            return L
        _sw_dmas(seq, "sync", 12, dmas_d2)
        seq.barrier()
        seq.label("d2")
        emit_d2p(_NcProxy(nc, "d2_"), seq, dict(
            q_in=H["d2_q"].ap(), k_in=H["d2_k"].ap(), v_in=H["d2_v"].ap(),
            o1_in=H["ones1"].ap(), o_out=H["o3_loc"].ap(),
            bscr=H["d2_bscr"]))
        seq.barrier()
        seq.label("ag_o3")
        _ag_step(seq, nc, H["o3_loc"], H["o3_all"], "ago3", grouped=True)

        def onr_fill_d(seq2, onr):
            o3a = H["o3_all"].ap()

            def dmas(c):
                jb = c % 4
                L = []
                for kk in range(16):
                    L.append((onr[:, kk * W:(kk + 1) * W],
                              o3a[kk, :, jb * W:(jb + 1) * W]))
                return L
            _sw_dmas(seq2, "sync", 16, dmas)
        seq.barrier()
        seq.label("d3")
        emit_ffn2(_NcProxy(nc, "d3_"), seq, dict(
            kind="d3", Hdim=16384,
            xfm_in=H["xr3"].ap(), xn_in=H["xn3"].ap(), attn_w_in=H["l3wo_all"].ap(),
            xo_out=H["xo_f"].ap(), hscr=H["d3_hscr"], onr_fill=onr_fill_d,
            gate_src=_mk_wcol_src(H["l3g_all"], 2048),
            up_src=_mk_wcol_src(H["l3u_all"], 2048),
            down_srcs=_mk_wpr_srcs(H["l3d_all"], 16384)))

    seq.finish()
    block = oes.enter_context(nc.Block())
    for en in ["sync", "vector", "scalar", "gpsimd", "tensor"]:
        getattr(block, en)(lambda e, en=en: seq.emit(en, e))
    oes.close()
    return nc


def full_host_inputs(inp):
    import math
    maps = p0_host_inputs(inp)
    x = np.asarray(inp["L0_attn_x"], np.float32).reshape(TOK, D)
    xT = x.T
    ones1 = np.ones((1, 128), np.float32)
    ipb = np.zeros((3 * D, 2), np.float32)
    ipb[:, 0] = np.asarray(inp["L0_attn_in_proj_b"], np.float32)
    shw_r = np.asarray(inp["L0_attn_short_w"], np.float32).reshape(3 * D, 3)
    shw = np.zeros((3 * D, 8), np.float32)
    shw[:, 0] = shw_r[:, 0]
    shw[:, 2] = shw_r[:, 1]
    shw[:, 4] = shw_r[:, 2]
    shb = np.zeros((3 * D, 8), np.float32)
    shb[:, 0] = np.asarray(inp["L0_attn_short_b"], np.float32)
    N1 = 128
    Wd = np.exp(-2j * np.pi * np.arange(N1)[:, None] * np.arange(N1)[None, :] / N1)
    TwF = np.exp(-2j * np.pi * np.outer(np.arange(N1), np.arange(N1)) / 16384.0)
    a2sh = dict(
        a2_wre=Wd.real.astype(bf16), a2_wim=Wd.imag.astype(bf16),
        a2_wimn=(-Wd.imag).astype(bf16),
        a2_wc64=np.concatenate([Wd.real[:64, :], Wd.imag[:64, :]], 1).astype(bf16),
        a2_wpd=np.concatenate([Wd.real, -Wd.imag], 1).astype(bf16),
        a2_wpd2=np.concatenate([Wd.imag, Wd.real], 1).astype(bf16),
        a2_twfr=TwF.real.astype(np.float32), a2_twfi=TwF.imag.astype(np.float32),
        a2_twir=TwF.real.astype(np.float32), a2_twii=(-TwF.imag).astype(np.float32))
    Lq = T
    t = np.linspace(0.0, 1.0, Lq)[:, None]
    t_resc = np.linspace(0.0, Lq - 1.0, Lq)
    wv = 2.0 * math.pi * t_resc / Lq
    f = np.linspace(0.0001, 0.0, 1)
    ang = f[None, :] * wv[:, None]
    z = np.concatenate([t, np.cos(ang), -np.sin(ang)], axis=-1)
    a2sh["a2_zT"] = np.ascontiguousarray(z.T).astype(np.float32)
    a2sh["a2_trow"] = np.linspace(0.0, 1.0, Lq).reshape(1, Lq).astype(np.float32)
    max_decay = math.log(0.01) / 0.3
    min_decay = math.log(0.01) / 1.5
    deltas = np.linspace(min_decay, max_decay, D)
    nds_full = np.zeros((D, 2), np.float32)
    nds_full[:, 0] = -np.abs(deltas)
    fb_full = np.zeros((D, 2), np.float32)
    fb_full[:, 0] = np.asarray(inp["L0_attn_filter_bias"], np.float32)
    fvec = np.zeros((64, 8), np.float32)
    fvec[:, 0] = np.asarray(inp["L0_attn_sin_freq"], np.float32).reshape(64)
    fvec[:, 2] = np.asarray(inp["L0_attn_mlp_b0"], np.float32)
    fvec[:, 4] = np.asarray(inp["L0_attn_mlp_b1"], np.float32)
    fvec[:, 6] = np.asarray(inp["L0_attn_mlp_b2"], np.float32)
    a2sh["a2_fvec"] = fvec
    a2sh["a2_w0T"] = np.ascontiguousarray(np.asarray(inp["L0_attn_mlp_w0"], np.float32).T)
    a2sh["a2_w1T"] = np.ascontiguousarray(np.asarray(inp["L0_attn_mlp_w1"], np.float32).T)
    a2sh["a2_w2T"] = np.ascontiguousarray(np.asarray(inp["L0_attn_mlp_w2"], np.float32).T)
    w3 = np.asarray(inp["L0_attn_mlp_w3"], np.float32)
    opb = np.zeros((D, 2), np.float32)
    opb[:, 0] = np.asarray(inp["L0_attn_out_proj_b"], np.float32)
    nw2 = np.zeros((D, 2), np.float32)
    nw2[:, 0] = np.asarray(inp["L1_attn_norm_w"], np.float32)
    utri = np.triu(np.ones((64, 64))).astype(np.float16)
    mtri = np.triu(np.ones((64, 64))).astype(bf16)
    alog = np.asarray(inp["L2_attn_A_log"], np.float32)
    dv2 = np.zeros((D, 2), np.float32)
    dv2[:, 0] = np.asarray(inp["L2_attn_D"], np.float32)
    HD = 128
    inv_freq = 1.0 / 10000.0 ** (np.arange(0, HD, 2) / HD)
    pos = np.arange(float(T))
    freqs = pos[:, None] * inv_freq[None, :]
    emb = np.concatenate([freqs, freqs], -1)
    cos_rep = np.tile(np.cos(emb).astype(bf16), (1, 16))
    sin_rep = np.tile(np.sin(emb).astype(bf16), (1, 16))
    for c in range(NCORES):
        m = maps[c]
        t0 = c * TSH
        xfm = np.zeros((D, TSH + 2), np.float32)
        xfm[:, 2:] = x[t0:t0 + TSH].T
        hs = np.zeros((128, 2), np.float32)
        if c % 4 != 0:
            xfm[:, 0:2] = x[t0 - 2:t0].T
            hs[:] = 1.0
        m["ones1"] = ones1
        m["a1_xfm"] = np.ascontiguousarray(xfm)
        m["a1_ipb"] = ipb
        m["a1_shw"] = shw
        m["a1_shb"] = shb
        m["a1_hsc"] = hs
        m["a_xres"] = np.ascontiguousarray(xT[:, t0:t0 + TSH])
        m.update(a2sh)
        rs = slice(c * 256, (c + 1) * 256)
        m["a2_w3T"] = np.ascontiguousarray(w3[rs].T)
        m["a2_nds"] = np.ascontiguousarray(nds_full[rs])
        m["a2_fbv"] = np.ascontiguousarray(fb_full[rs])
        m["a3_opb"] = opb
        m["b2_utri"] = utri
        m["b2_mtri"] = mtri
        m["b3_nw"] = nw2
        am = np.zeros((128, 64), np.float32)
        al = alog[rs]
        for ct in range(2):
            am[:, ct * 32:ct * 32 + 32:2] = al[ct * 128:(ct + 1) * 128]
        m["c2_alog"] = am
        m["c3_dv"] = dv2
        pos0 = t0 % T
        m["d1_cosr"] = np.ascontiguousarray(cos_rep[pos0:pos0 + TSH])
        m["d1_sinr"] = np.ascontiguousarray(sin_rep[pos0:pos0 + TSH])
    return maps


def kernel(**inp):
    maps = full_host_inputs(inp)
    res = run_bass_kernel_spmd(build_full("d"), maps, core_ids=list(range(NCORES)))
    x4 = np.concatenate([r["xo_f"] for r in res.results], 1)
    return np.ascontiguousarray(x4.T).reshape(B, T, D).astype(np.float32)


# version 18 (final)
# speedup vs baseline: 1.1389x; 1.1389x over previous
import numpy as np
import ml_dtypes
from contextlib import ExitStack

import concourse.bass as bass
import concourse.mybir as mybir
from concourse.bass_utils import run_bass_kernel_spmd
from concourse.alu_op_type import AluOpType as Op

F32 = mybir.dt.float32
BF16 = mybir.dt.bfloat16
FP16 = mybir.dt.float16
I32 = mybir.dt.int32
U8 = mybir.dt.uint8
AF = mybir.ActivationFunctionType
AX = mybir.AxisListType
BIG = float(2 ** 23)
LN2 = float(np.log(2.0))

NCORES = 8
B = 2
T = 8192
D = 2048
TOK = B * T
TSH = TOK // NCORES
bf16 = ml_dtypes.bfloat16


def cdiv(a, b):
    return (a + b - 1) // b


class Seq:

    ENGS = ("sync", "vector", "scalar", "gpsimd", "tensor")

    def __init__(self, nc, es, name="seq"):
        self.nc = nc
        self.es = es
        self.name = name
        self.hs = {}
        for e in self.ENGS:
            self._lane(e)
        self.steps = []
        self.info = []
        self.cum = {}
        self.cur_label = None
        self.last_id = None
        self.last_per_lane = {}
        self._bar = []

    def _lane(self, lane):
        if lane not in self.hs:
            self.hs[lane] = self.es.enter_context(self.nc.semaphore(self.name + "_" + lane))
        return self.hs[lane]

    def label(self, name):
        self.cur_label = name

    def step(self, eng, inc, fn, deps=None, async_=False, lane=None):
        lane = lane or eng
        self._lane(lane)
        if deps is None:
            deps = [self.last_id] if self.last_id is not None else list(self._bar)
        sid = len(self.steps)
        self.cum[lane] = self.cum.get(lane, 0) + inc
        self.steps.append((eng, inc, fn, self.cur_label, tuple(d for d in deps if d is not None), lane))
        self.info.append((lane, self.cum[lane], async_))
        self.last_id = sid
        self.last_per_lane[lane] = sid
        return sid

    def barrier(self):
        self._bar = list(self.last_per_lane.values())
        self.last_id = None

    def dma(self, eng, dst, src, deps=None, lane=None):
        def fn(e, h, tgt):
            e.dma_start(out=dst, in_=src).then_inc(h, 16)
        return self.step(eng, 16, fn, deps=deps, async_=True, lane=lane)

    def dma_t(self, eng, dst, src, deps=None, lane=None):
        def fn(e, h, tgt):
            e.dma_start_transpose(dst, src).then_inc(h, 16)
        return self.step(eng, 16, fn, deps=deps, async_=True, lane=lane)

    def dmas(self, eng, pairs, deps=None, lane=None):
        def fn(e, h, tgt):
            for dst, src in pairs:
                e.dma_start(out=dst, in_=src).then_inc(h, 16)
        return self.step(eng, 16 * len(pairs), fn, deps=deps, async_=True, lane=lane)

    def comp(self, eng, fn, deps=None):
        def wrap(e, h, tgt):
            inst = fn(e)
            d = e.drain()
            (d if d is not None else inst).then_inc(h, 1)
        return self.step(eng, 1, wrap, deps=deps, async_=False)

    def finish(self):
        tot = dict(self.cum)

        def fn(e, h, tgt):
            for ln, t in tot.items():
                if t > 0:
                    e.wait_ge(self.hs[ln], t)
        self.step("scalar", 0, fn, deps=[])

    def emit(self, eng_name, eng):
        waited = {}
        acc = 0
        open_lab = None
        ctx = None
        for e, inc, fn, lab, deps, lane in self.steps:
            if e == eng_name:
                if lab != open_lab:
                    if ctx is not None:
                        ctx.__exit__(None, None, None)
                    ctx = self.nc.named_scope(lab) if lab else None
                    if ctx is not None:
                        ctx.__enter__()
                    open_lab = lab
                for d in deps:
                    dl, dc, dasync = self.info[d]
                    if dl == eng_name and not dasync:
                        continue
                    if dc > waited.get(dl, 0):
                        eng.wait_ge(self.hs[dl], dc)
                        waited[dl] = dc
                fn(eng, self.hs[lane], acc + inc)
                acc += inc
        if ctx is not None:
            ctx.__exit__(None, None, None)


P0_SPECS = [
    ("l0wgu", 16384, 2048, "tern_ste", True),
    ("l0wpr", 2048, 8192, "tern_ste", True),
    ("l1wfc", 8192, 2048, "tern", True),
    ("l1wpr", 2048, 8192, "tern", True),
    ("l2wfc", 5120, 2048, "tern_g", True),
    ("l2wpr", 2048, 5120, "tern_g", True),
    ("ipwT", 2048, 6144, "cast", False),
    ("opwT", 2048, 2048, "cast", False),
    ("wqT", 2048, 2048, "cast", False),
    ("wkT", 2048, 2048, "cast", False),
    ("wvT", 2048, 2048, "cast", False),
    ("woT", 2048, 2048, "cast", False),
    ("wxT", 2048, 2080, "cast", False),
    ("l3wq", 2048, 2048, "cast", False),
    ("l3wk", 2048, 512, "cast", False),
    ("l3wv", 2048, 512, "cast", False),
    ("l3wo", 2048, 2048, "cast", False),
]
P0_NV = [
    ("l3g", 16384, 1024, 2048),
    ("l3u", 16384, 1024, 2048),
    ("l3d", 2048, 8192, 16384),
]
CW0 = 4096


def emit_p0(nc, seq, io):
    es = ExitStack()
    ins, outs = io["ins"], io["outs"]
    iden_in, icst_in = io["iden_in"], io["icst_in"]
    gpart, gsum = io["gpart"], io["gsum"]
    CW = CW0
    sbuf = nc.sbuf_tensor
    w0 = es.enter_context(sbuf("w0", [128, CW], F32))
    wbf2 = es.enter_context(sbuf("wbf2", [128, CW], BF16))
    wbf = es.enter_context(sbuf("wbf", [128, CW], BF16))
    sc1 = es.enter_context(sbuf("sc1", [128, CW // 64], F32))
    wn = es.enter_context(sbuf("wn", [128, CW], F32))
    sg = es.enter_context(sbuf("sg", [128, CW], F32))
    t1 = es.enter_context(sbuf("t1", [128, CW], F32))
    t2 = es.enter_context(sbuf("t2", [128, CW], F32))
    qb = es.enter_context(sbuf("qb", [128, CW], BF16))
    qb2 = es.enter_context(sbuf("qb2", [128, CW], BF16))
    scB = es.enter_context(sbuf("scB", [128, CW // 64], BF16))
    onesw = es.enter_context(sbuf("onesw", [128, CW], F32))
    sA = es.enter_context(sbuf("sA", [128, CW], F32))
    wnG = es.enter_context(sbuf("wnG", [128, CW], F32))
    gq = es.enter_context(sbuf("gq", [128, CW], BF16))
    iden = es.enter_context(sbuf("iden_s", [128, 128], BF16))
    icst = es.enter_context(sbuf("icst_s", [128, 2], I32))
    gsc = es.enter_context(sbuf("gsc", [128, 4], F32))
    ones = es.enter_context(sbuf("ones", [128, 1], F32))
    red = es.enter_context(sbuf("red", [128, 1], F32))
    red2 = es.enter_context(sbuf("red2", [1, 16], F32))
    pk = es.enter_context(sbuf("pk", [128, CW // 2], I32))
    cod = es.enter_context(sbuf("cod", [128, CW], I32))
    nsc = es.enter_context(sbuf("nsc", [128, 512], F32))
    tT = es.enter_context(sbuf("tT", [128, 1024], BF16))
    ptr = es.enter_context(nc.psum_tensor("ptr", [128, 1024], BF16))
    pred = es.enter_context(nc.psum_tensor("pred", [1, 2], F32))

    seq.dma("sync", iden[:], iden_in[:])
    seq.dma("sync", icst[:], icst_in[:])
    seq.comp("vector", lambda v: v.memset(ones[:], 1.0))
    seq.comp("vector", lambda v: v.memset(onesw[:], 1.0))

    for j, (name, R, K) in enumerate([("l2wfc", 5120, 2048), ("l2wpr", 2048, 5120)]):
        Rs = R // NCORES
        for rt in range(Rs // 128):
            for kc in range(0, K, CW):
                w = min(CW, K - kc)
                seq.dma("sync", w0[:, :w], ins[name][rt * 128:(rt + 1) * 128, kc:kc + w])

                def red_fn(v, w=w):
                    if w < CW:
                        v.memset(w0[:, w:], 0.0)
                    return v.tensor_reduce(red[:, :], w0[:, :], axis=AX.X, op=Op.add,
                                           apply_absolute_value=True)
                seq.comp("vector", red_fn)
                first = (rt == 0 and kc == 0)
                seq.comp("tensor", lambda t, j=j, first=first: t.matmul(
                    pred[:, j:j + 1], red[:], ones[:], start=first, stop=True))
    def red2_fn(v):
        v.memset(red2[:], 0.0)
        return v.tensor_copy(red2[:, 0:2], pred[:])
    seq.comp("vector", red2_fn)
    seq.dma("scalar", gpart.ap(), red2[:])

    def cc_fn(g, h, tgt):
        g.collective_compute("AllReduce", Op.add, replica_groups=[list(range(NCORES))],
                             ins=[gpart.ap().opt()], outs=[gsum.ap().opt()]).then_inc(h, 1)
    seq.step("gpsimd", 1, cc_fn, async_=True)
    seq.dma("sync", gsc[:, 0:2], gsum.ap()[0, 0:2].partition_broadcast(128))

    def gsc_fn(v):
        NE = float(5120 * 2048)
        v.tensor_scalar(gsc[:, 2:3], gsc[:, 1:2], 1.0 / NE, 1e-5, Op.mult, Op.max)
        return v.tensor_scalar(gsc[:, 0:1], gsc[:, 0:1], 1.0 / NE, 1e-5, Op.mult, Op.max)
    seq.comp("vector", gsc_fn)

    def store_chunk(name, rt, kc, w, tr, wb):
        if not tr:
            seq.dma("scalar", outs[name][rt * 128:(rt + 1) * 128, kc:kc + w], wb[:, :w])
            return
        for b0 in range(0, w, 1024):
            nb = min(8, (w - b0) // 128)

            def tr_fn(t, b0=b0, nb=nb, wb=wb):
                for i in range(nb):
                    inst = t.transpose(ptr[:, i * 128:(i + 1) * 128],
                                       wb[:, b0 + i * 128: b0 + (i + 1) * 128], iden[:])
                return inst
            seq.comp("tensor", tr_fn)
            seq.comp("vector", lambda v, nb=nb: v.tensor_copy(tT[:, :nb * 128], ptr[:, :nb * 128]))
            for i in range(nb):
                seq.dma("scalar", outs[name][kc + b0 + i * 128: kc + b0 + (i + 1) * 128,
                                             rt * 128:(rt + 1) * 128], tT[:, i * 128:(i + 1) * 128])

    tctr = [0]

    def wbuf():
        return wbf if tctr[0] % 2 == 0 else wbf2

    for name, R, K, kind, tr in P0_SPECS:
        Rs = R // NCORES
        gcol = 0 if name == "l2wfc" else 2
        for rt in range(Rs // 128):
            for kc in range(0, K, CW):
                tctr[0] += 1
                wb = wbuf()
                w = min(CW, K - kc)
                G = w // 64
                seq.dma("sync", w0[:, :w], ins[name][rt * 128:(rt + 1) * 128, kc:kc + w])
                if kind == "cast":
                    seq.comp("vector", lambda v, w=w, wb=wb: v.tensor_copy(wb[:, :w], w0[:, :w]))
                elif kind in ("tern", "tern_ste"):
                    def t_fn1(v, w=w, G=G):
                        v.tensor_copy(qb[:, :w], w0[:, :w])
                        v.tensor_copy(t2[:, :w], qb[:, :w])
                        if w < CW:
                            v.memset(t2[:, w:], 0.0)
                        v.tensor_reduce(sc1[:, :], t2[:, :].rearrange("p (g x) -> p g x", x=64),
                                        axis=AX.X, op=Op.add, apply_absolute_value=True)
                        v.tensor_scalar(sc1[:, :], sc1[:, :], 1.0 / 64, 1e-8, Op.mult, Op.max)
                        v.scalar_tensor_tensor(scB[:, :], sc1[:, :], 1.0, onesw[:, 0:64], Op.mult, Op.mult)
                        return v.scalar_tensor_tensor(sc1[:, :], scB[:, :], 1.0, onesw[:, 0:64], Op.mult, Op.mult)
                    seq.comp("vector", t_fn1)

                    def t_sg(s, w=w):
                        s.activation(out=sA[:, :w], in_=t2[:, :w], func=AF.Abs,
                                     scale=1.0 / 0.5009765625)
                        return s.activation(out=sg[:, :w], in_=t2[:, :w], func=AF.Sign)
                    seq.comp("scalar", t_sg)
                    seq.comp("gpsimd", lambda g, w=w, G=G: g.tensor_tensor(
                        wnG[:, :w].rearrange("p (g x) -> p g x", x=64),
                        onesw[:, :w].rearrange("p (g x) -> p g x", x=64),
                        sc1[:, :G].unsqueeze(2).to_broadcast((128, G, 64)), Op.mult))

                    def cmp_fn(v, w=w):
                        v.tensor_tensor(t2[:, :w], sA[:, :w], wnG[:, :w], Op.is_ge)
                        return v.tensor_tensor(wn[:, :w], sg[:, :w], t2[:, :w], Op.mult)
                    seq.comp("vector", cmp_fn)
                    seq.comp("gpsimd", lambda g, w=w, G=G: g.tensor_tensor(
                        gq[:, :w].rearrange("p (g x) -> p g x", x=64),
                        wn[:, :w].rearrange("p (g x) -> p g x", x=64),
                        scB[:, :G].unsqueeze(2).to_broadcast((128, G, 64)), Op.mult))
                    if kind == "tern_ste":
                        def ste(v, w=w, wb=wb):
                            v.tensor_tensor(qb2[:, :w], gq[:, :w], qb[:, :w], Op.subtract)
                            return v.tensor_tensor(wb[:, :w], qb[:, :w], qb2[:, :w], Op.add)
                        seq.comp("vector", ste)
                    else:
                        seq.comp("vector", lambda v, w=w, wb=wb: v.tensor_copy(wb[:, :w], gq[:, :w]))
                else:
                    def tg_sg(s, w=w, gcol=gcol):
                        s.activation(out=sg[:, :w], in_=w0[:, :w], func=AF.Sign)
                        s.activation(out=sg[:, :w], in_=sg[:, :w], func=AF.Copy,
                                     scale=gsc[:, gcol:gcol + 1])
                        return s.activation(out=sA[:, :w], in_=w0[:, :w], func=AF.Abs, scale=1.0 / 0.7)
                    seq.comp("scalar", tg_sg)
                    seq.comp("vector", lambda v, w=w, gcol=gcol, wb=wb: v.scalar_tensor_tensor(
                        wb[:, :w], sA[:, :w], gsc[:, gcol:gcol + 1], sg[:, :w], Op.is_gt, Op.mult))
                store_chunk(name, rt, kc, w, tr, wb)

    for name, R, Kp, Kd in P0_NV:
        Rs = R // NCORES
        for rt in range(Rs // 128):
            seq.dma("sync", nsc[:, :Kd // 32], ins[name + "_sc"][rt * 128:(rt + 1) * 128, :])
            for kc in range(0, Kd, CW):
                tctr[0] += 1
                wb = wbuf()
                w = min(CW, Kd - kc)
                GS = w // 32
                seq.dma("sync", pk[:, :w // 2], ins[name + "_pk"][rt * 128:(rt + 1) * 128, kc // 2:(kc + w) // 2])

                def nv1(v, w=w):
                    c3 = cod[:, :w].rearrange("p (k two) -> p k two", two=2)
                    v.tensor_scalar(c3[:, :, 0:1], pk[:, :w // 2].unsqueeze(2), icst[:, 0:1], None, Op.bitwise_and)
                    v.tensor_scalar(c3[:, :, 1:2], pk[:, :w // 2].unsqueeze(2), icst[:, 1:2], icst[:, 0:1],
                                    Op.arith_shift_right, Op.bitwise_and)
                    v.tensor_copy(t2[:, :w], cod[:, :w])
                    v.tensor_scalar(wn[:, :w], t2[:, :w], 8.0, None, Op.is_ge)
                    v.scalar_tensor_tensor(t1[:, :w], wn[:, :w], -8.0, t2[:, :w], Op.mult, Op.add)
                    v.tensor_scalar(wn[:, :w], wn[:, :w], -2.0, 1.0, Op.mult, Op.add)
                    v.tensor_scalar(t2[:, :w], t1[:, :w], 0.5, -0.25, Op.mult, Op.add)
                    return v.tensor_scalar(t2[:, :w], t2[:, :w], BIG, BIG, Op.add, Op.subtract)
                seq.comp("vector", nv1)
                seq.comp("scalar", lambda s, w=w: s.activation(out=sg[:, :w], in_=t2[:, :w], func=AF.Exp, scale=LN2))

                def nv2(v, w=w):
                    lsb = cod[:, :w].bitcast(F32)
                    v.scalar_tensor_tensor(lsb, t2[:, :w], -2.0, t1[:, :w], Op.mult, Op.add)
                    v.tensor_scalar(t2[:, :w], lsb, 2.0, 0.25, Op.add, Op.mult)
                    v.tensor_tensor(t2[:, :w], t2[:, :w], sg[:, :w], Op.mult)
                    v.tensor_scalar(sg[:, :w], t1[:, :w], 2.0, None, Op.is_lt)
                    v.tensor_scalar(t1[:, :w], t1[:, :w], 0.5, None, Op.mult)
                    v.tensor_tensor(t1[:, :w], t1[:, :w], t2[:, :w], Op.subtract)
                    v.tensor_tensor(t1[:, :w], t1[:, :w], sg[:, :w], Op.mult)
                    v.tensor_tensor(t2[:, :w], t2[:, :w], t1[:, :w], Op.add)
                    return v.tensor_tensor(t2[:, :w], t2[:, :w], wn[:, :w], Op.mult)
                seq.comp("vector", nv2)
                seq.comp("gpsimd", lambda g, w=w, GS=GS, kc=kc, wb=wb: g.tensor_tensor(
                    wb[:, :w].rearrange("p (g x) -> p g x", x=32),
                    t2[:, :w].rearrange("p (g x) -> p g x", x=32),
                    nsc[:, kc // 32: kc // 32 + GS].unsqueeze(2).to_broadcast((128, GS, 32)), Op.mult))
                store_chunk(name, rt, kc, w, True, wb)


    es.close()


def emit_p0p(nc, seq, io):
    es = ExitStack()
    ins, outs = io["ins"], io["outs"]
    iden_in, icst_in = io["iden_in"], io["icst_in"]
    gpart, gsum = io["gpart"], io["gsum"]
    CW = CW0
    sbuf = nc.sbuf_tensor
    w02 = [es.enter_context(sbuf(f"w0{i}", [128, CW], F32)) for i in range(2)]
    wbfs = [es.enter_context(sbuf(f"wbf{i}", [128, CW], BF16)) for i in range(2)]
    sc1 = es.enter_context(sbuf("sc1", [128, CW // 64], F32))
    wn = es.enter_context(sbuf("wn", [128, CW], F32))
    sg = es.enter_context(sbuf("sg", [128, CW], F32))
    t2 = es.enter_context(sbuf("t2", [128, CW], F32))
    qb = es.enter_context(sbuf("qb", [128, CW], BF16))
    qb2 = es.enter_context(sbuf("qb2", [128, CW], BF16))
    scB = es.enter_context(sbuf("scB", [128, CW // 64], BF16))
    onesw = es.enter_context(sbuf("onesw", [128, CW], F32))
    sA = es.enter_context(sbuf("sA", [128, CW], F32))
    wnG = es.enter_context(sbuf("wnG", [128, CW], F32))
    gq = es.enter_context(sbuf("gq", [128, CW], BF16))
    t1 = wnG
    cod = sA[:].bitcast(I32)
    iden = es.enter_context(sbuf("iden_s", [128, 128], BF16))
    icst = es.enter_context(sbuf("icst_s", [128, 2], I32))
    gsc = es.enter_context(sbuf("gsc", [128, 4], F32))
    ones = es.enter_context(sbuf("ones", [128, 1], F32))
    red = es.enter_context(sbuf("red", [128, 1], F32))
    red2 = es.enter_context(sbuf("red2", [1, 16], F32))
    pk2 = [es.enter_context(sbuf(f"pk{i}", [128, CW // 2], I32)) for i in range(2)]
    nsc = es.enter_context(sbuf("nsc", [128, 512], F32))
    tT2 = [es.enter_context(sbuf(f"tT{i}", [128, 1024], BF16)) for i in range(2)]
    ptr2 = [es.enter_context(nc.psum_tensor(f"ptr{i}", [128, 1024], BF16)) for i in range(2)]
    pred = es.enter_context(nc.psum_tensor("pred", [1, 2], F32))

    seq.dma("sync", iden[:], iden_in[:])
    seq.dma("sync", icst[:], icst_in[:])
    seq.comp("vector", lambda v: v.memset(ones[:], 1.0))
    seq.comp("vector", lambda v: v.memset(onesw[:], 1.0))

    for j, (name, R, K) in enumerate([("l2wfc", 5120, 2048), ("l2wpr", 2048, 5120)]):
        Rs = R // NCORES
        for rt in range(Rs // 128):
            for kc in range(0, K, CW):
                w = min(CW, K - kc)
                seq.dma("sync", w02[0][:, :w], ins[name][rt * 128:(rt + 1) * 128, kc:kc + w])

                def red_fn(v, w=w):
                    if w < CW:
                        v.memset(w02[0][:, w:], 0.0)
                    return v.tensor_reduce(red[:, :], w02[0][:, :], axis=AX.X, op=Op.add,
                                           apply_absolute_value=True)
                seq.comp("vector", red_fn)
                first = (rt == 0 and kc == 0)
                seq.comp("tensor", lambda t, j=j, first=first: t.matmul(
                    pred[:, j:j + 1], red[:], ones[:], start=first, stop=True))

    def red2_fn(v):
        v.memset(red2[:], 0.0)
        return v.tensor_copy(red2[:, 0:2], pred[:])
    seq.comp("vector", red2_fn)
    seq.dma("scalar", gpart.ap(), red2[:])

    def cc_fn(g, h, tgt):
        g.collective_compute("AllReduce", Op.add, replica_groups=[list(range(NCORES))],
                             ins=[gpart.ap().opt()], outs=[gsum.ap().opt()]).then_inc(h, 1)
    seq.step("gpsimd", 1, cc_fn, async_=True)
    seq.dma("sync", gsc[:, 0:2], gsum.ap()[0, 0:2].partition_broadcast(128))

    def gsc_fn(v):
        NE = float(5120 * 2048)
        v.tensor_scalar(gsc[:, 2:3], gsc[:, 1:2], 1.0 / NE, 1e-5, Op.mult, Op.max)
        return v.tensor_scalar(gsc[:, 0:1], gsc[:, 0:1], 1.0 / NE, 1e-5, Op.mult, Op.max)
    gid = seq.comp("vector", gsc_fn)
    seq.barrier()

    LR = {"t2rd": None, "sgrd": None, "wnrd": None, "wnGrd": None, "scrd": [],
          "gqrd": None, "w0rd": [gid, gid], "pkrd": [None, None],
          "wbfree": [None, None], "ptr_ev": {}, "tst": {}, "nscrd": None}
    ci = [0]

    def store_chunk(name, rt, kc, w, tr, wb, wb_ready):
        p = ci[0] % 2
        if not tr:
            sid = seq.dma("gpsimd", outs[name][rt * 128:(rt + 1) * 128, kc:kc + w], wb[:, :w],
                          deps=[wb_ready], lane=f"st{p}")
            LR["wbfree"][p] = sid
            return
        last_tr = None
        for gi, b0 in enumerate(range(0, w, 1024)):
            nb = min(8, (w - b0) // 128)
            pb = gi % 2

            def tr_fn(t, b0=b0, nb=nb, wb=wb, pb=pb):
                for i in range(nb):
                    inst = t.transpose(ptr2[pb][:, i * 128:(i + 1) * 128],
                                       wb[:, b0 + i * 128:b0 + (i + 1) * 128], iden[:])
                return inst
            ttr = seq.comp("tensor", tr_fn, deps=[wb_ready, LR["ptr_ev"].get(pb)])
            ev = seq.comp("scalar", lambda s, nb=nb, pb=pb: s.activation(
                out=tT2[pb][:, :nb * 128], in_=ptr2[pb][:, :nb * 128], func=AF.Identity),
                deps=[ttr, LR["tst"].get(pb)])
            LR["ptr_ev"][pb] = ev
            prs = [(outs[name][kc + b0 + i * 128:kc + b0 + (i + 1) * 128,
                               rt * 128:(rt + 1) * 128], tT2[pb][:, i * 128:(i + 1) * 128])
                   for i in range(nb)]
            st = seq.dmas("gpsimd", prs, deps=[ev], lane=f"so{pb}")
            LR["tst"][pb] = st
            last_tr = ttr
        LR["wbfree"][p] = last_tr

    for name, R, K, kind, tr in P0_SPECS:
        Rs = R // NCORES
        gcol = 0 if name == "l2wfc" else 2
        for rt in range(Rs // 128):
            for kc in range(0, K, CW):
                p = ci[0] % 2
                w0 = w02[p]
                wb = wbfs[p]
                w = min(CW, K - kc)
                G = w // 64
                ld = seq.dma("sync", w0[:, :w], ins[name][rt * 128:(rt + 1) * 128, kc:kc + w],
                             deps=[LR["w0rd"][p]], lane=f"w0{p}")
                if kind == "cast":
                    cp = seq.comp("vector", lambda v, w=w, wb=wb, w0=w0: v.tensor_copy(
                        wb[:, :w], w0[:, :w]), deps=[ld, LR["wbfree"][p]])
                    LR["w0rd"][p] = cp
                    store_chunk(name, rt, kc, w, tr, wb, cp)
                elif kind in ("tern", "tern_ste"):
                    def t_fn1(v, w=w, G=G, w0=w0):
                        v.tensor_copy(qb[:, :w], w0[:, :w])
                        v.tensor_copy(t2[:, :w], qb[:, :w])
                        if w < CW:
                            v.memset(t2[:, w:], 0.0)
                        v.tensor_reduce(sc1[:, :], t2[:, :].rearrange("p (g x) -> p g x", x=64),
                                        axis=AX.X, op=Op.add, apply_absolute_value=True)
                        v.tensor_scalar(sc1[:, :], sc1[:, :], 1.0 / 64, 1e-8, Op.mult, Op.max)
                        v.scalar_tensor_tensor(scB[:, :], sc1[:, :], 1.0, onesw[:, 0:64], Op.mult, Op.mult)
                        return v.scalar_tensor_tensor(sc1[:, :], scB[:, :], 1.0, onesw[:, 0:64], Op.mult, Op.mult)
                    t1id = seq.comp("vector", t_fn1,
                                    deps=[ld, LR["t2rd"]] + LR["scrd"])
                    LR["w0rd"][p] = t1id

                    def t_sg(s, w=w):
                        s.activation(out=sA[:, :w], in_=t2[:, :w], func=AF.Abs,
                                     scale=1.0 / 0.5009765625)
                        return s.activation(out=sg[:, :w], in_=t2[:, :w], func=AF.Sign)
                    tsg = seq.comp("scalar", t_sg, deps=[t1id, LR["sgrd"]])
                    gp1 = seq.comp("gpsimd", lambda g, w=w, G=G: g.tensor_tensor(
                        wnG[:, :w].rearrange("p (g x) -> p g x", x=64),
                        onesw[:, :w].rearrange("p (g x) -> p g x", x=64),
                        sc1[:, :G].unsqueeze(2).to_broadcast((128, G, 64)), Op.mult),
                        deps=[t1id, LR["wnGrd"]])

                    def cmp_fn(v, w=w):
                        v.tensor_tensor(t2[:, :w], sA[:, :w], wnG[:, :w], Op.is_ge)
                        return v.tensor_tensor(wn[:, :w], sg[:, :w], t2[:, :w], Op.mult)
                    cmpid = seq.comp("vector", cmp_fn, deps=[tsg, gp1, LR["wnrd"]])
                    LR["t2rd"] = tsg
                    LR["sgrd"] = cmpid
                    LR["wnGrd"] = cmpid
                    gp2 = seq.comp("gpsimd", lambda g, w=w, G=G: g.tensor_tensor(
                        gq[:, :w].rearrange("p (g x) -> p g x", x=64),
                        wn[:, :w].rearrange("p (g x) -> p g x", x=64),
                        scB[:, :G].unsqueeze(2).to_broadcast((128, G, 64)), Op.mult),
                        deps=[cmpid, LR["gqrd"]])
                    LR["wnrd"] = gp2
                    LR["scrd"] = [gp1, gp2]
                    if kind == "tern_ste":
                        def ste(v, w=w, wb=wb):
                            v.tensor_tensor(qb2[:, :w], gq[:, :w], qb[:, :w], Op.subtract)
                            return v.tensor_tensor(wb[:, :w], qb[:, :w], qb2[:, :w], Op.add)
                        steid = seq.comp("vector", ste, deps=[gp2, LR["wbfree"][p]])
                    else:
                        steid = seq.comp("vector", lambda v, w=w, wb=wb: v.tensor_copy(
                            wb[:, :w], gq[:, :w]), deps=[gp2, LR["wbfree"][p]])
                    LR["gqrd"] = steid
                    store_chunk(name, rt, kc, w, tr, wb, steid)
                else:
                    def tg_sg(s, w=w, gcol=gcol, w0=w0):
                        s.activation(out=sg[:, :w], in_=w0[:, :w], func=AF.Sign)
                        s.activation(out=sg[:, :w], in_=sg[:, :w], func=AF.Copy,
                                     scale=gsc[:, gcol:gcol + 1])
                        return s.activation(out=sA[:, :w], in_=w0[:, :w], func=AF.Abs, scale=1.0 / 0.7)
                    tgid = seq.comp("scalar", tg_sg, deps=[ld, LR["sgrd"]])
                    LR["w0rd"][p] = tgid
                    wbstt = seq.comp("vector", lambda v, w=w, gcol=gcol, wb=wb: v.scalar_tensor_tensor(
                        wb[:, :w], sA[:, :w], gsc[:, gcol:gcol + 1], sg[:, :w], Op.is_gt, Op.mult),
                        deps=[tgid, LR["wbfree"][p]])
                    LR["sgrd"] = wbstt
                    store_chunk(name, rt, kc, w, tr, wb, wbstt)
                ci[0] += 1

    for name, R, Kp, Kd in P0_NV:
        Rs = R // NCORES
        for rt in range(Rs // 128):
            ldn = seq.dma("sync", nsc[:, :Kd // 32], ins[name + "_sc"][rt * 128:(rt + 1) * 128, :],
                          deps=[LR["nscrd"]], lane="nsc")
            for kc in range(0, Kd, CW):
                p = ci[0] % 2
                wb = wbfs[p]
                pk = pk2[p]
                w = min(CW, Kd - kc)
                GS = w // 32
                ldp = seq.dma("sync", pk[:, :w // 2],
                              ins[name + "_pk"][rt * 128:(rt + 1) * 128, kc // 2:(kc + w) // 2],
                              deps=[LR["pkrd"][p]], lane=f"pk{p}")

                def nv1(v, w=w, pk=pk):
                    c3 = cod[:, :w].rearrange("p (k two) -> p k two", two=2)
                    v.tensor_scalar(c3[:, :, 0:1], pk[:, :w // 2].unsqueeze(2), icst[:, 0:1], None, Op.bitwise_and)
                    v.tensor_scalar(c3[:, :, 1:2], pk[:, :w // 2].unsqueeze(2), icst[:, 1:2], icst[:, 0:1],
                                    Op.arith_shift_right, Op.bitwise_and)
                    v.tensor_copy(t2[:, :w], cod[:, :w])
                    v.tensor_scalar(wn[:, :w], t2[:, :w], 8.0, None, Op.is_ge)
                    v.scalar_tensor_tensor(t1[:, :w], wn[:, :w], -8.0, t2[:, :w], Op.mult, Op.add)
                    v.tensor_scalar(wn[:, :w], wn[:, :w], -2.0, 1.0, Op.mult, Op.add)
                    v.tensor_scalar(t2[:, :w], t1[:, :w], 0.5, -0.25, Op.mult, Op.add)
                    return v.tensor_scalar(t2[:, :w], t2[:, :w], BIG, BIG, Op.add, Op.subtract)
                n1 = seq.comp("vector", nv1, deps=[ldp, LR["t2rd"], LR["wnrd"], LR["sgrd"],
                                                   LR["wnGrd"]])
                LR["pkrd"][p] = n1
                sexp = seq.comp("scalar", lambda s, w=w: s.activation(
                    out=sg[:, :w], in_=t2[:, :w], func=AF.Exp, scale=LN2),
                    deps=[n1, LR["sgrd"]])

                def nv2(v, w=w):
                    lsb = cod[:, :w].bitcast(F32)
                    v.scalar_tensor_tensor(lsb, t2[:, :w], -2.0, t1[:, :w], Op.mult, Op.add)
                    v.tensor_scalar(t2[:, :w], lsb, 2.0, 0.25, Op.add, Op.mult)
                    v.tensor_tensor(t2[:, :w], t2[:, :w], sg[:, :w], Op.mult)
                    v.tensor_scalar(sg[:, :w], t1[:, :w], 2.0, None, Op.is_lt)
                    v.tensor_scalar(t1[:, :w], t1[:, :w], 0.5, None, Op.mult)
                    v.tensor_tensor(t1[:, :w], t1[:, :w], t2[:, :w], Op.subtract)
                    v.tensor_tensor(t1[:, :w], t1[:, :w], sg[:, :w], Op.mult)
                    v.tensor_tensor(t2[:, :w], t2[:, :w], t1[:, :w], Op.add)
                    return v.tensor_tensor(t2[:, :w], t2[:, :w], wn[:, :w], Op.mult)
                n2 = seq.comp("vector", nv2, deps=[sexp])
                LR["sgrd"] = n2
                LR["wnrd"] = n2
                gp3 = seq.comp("gpsimd", lambda g, w=w, GS=GS, kc=kc, wb=wb: g.tensor_tensor(
                    wb[:, :w].rearrange("p (g x) -> p g x", x=32),
                    t2[:, :w].rearrange("p (g x) -> p g x", x=32),
                    nsc[:, kc // 32:kc // 32 + GS].unsqueeze(2).to_broadcast((128, GS, 32)), Op.mult),
                    deps=[n2, ldn, LR["wbfree"][p]])
                LR["t2rd"] = gp3
                LR["nscrd"] = gp3 if kc + w >= Kd else LR["nscrd"]
                store_chunk(name, rt, kc, w, True, wb, gp3)
                ci[0] += 1
    seq.barrier()
    es.close()


def build_p0():
    nc = bass.Bass("TRN2", target_bir_lowering=False, debug=False)
    es = ExitStack()
    ins, outs = {}, {}
    for name, R, K, kind, tr in P0_SPECS:
        Rs = R // NCORES
        ins[name] = nc.dram_tensor(name, [Rs, K], F32, kind="ExternalInput").ap()
        oshape = [K, Rs] if tr else [Rs, K]
        outs[name] = nc.dram_tensor(name + "_o", oshape, BF16, kind="ExternalOutput").ap()
    for name, R, Kp, Kd in P0_NV:
        Rs = R // NCORES
        ins[name + "_pk"] = nc.dram_tensor(name + "_pk", [Rs, Kp], I32, kind="ExternalInput").ap()
        ins[name + "_sc"] = nc.dram_tensor(name + "_sc", [Rs, Kd // 32], F32, kind="ExternalInput").ap()
        outs[name] = nc.dram_tensor(name + "_o", [Kd, Rs], BF16, kind="ExternalOutput").ap()
    dbg_out = nc.dram_tensor("dbg_o", [128, 4], F32, kind="ExternalOutput").ap()

    dbg2_out = nc.dram_tensor("dbg2_o", [128, 2048], F32, kind="ExternalOutput").ap()
    dbg3_out = nc.dram_tensor("dbg3_o", [128, 2048], F32, kind="ExternalOutput").ap()
    iden_in = nc.dram_tensor("iden", [128, 128], BF16, kind="ExternalInput").ap()
    icst_in = nc.dram_tensor("icst", [128, 2], I32, kind="ExternalInput").ap()
    gpart = nc.dram_tensor("gpart", [1, 16], F32)
    gsum = nc.dram_tensor("gsum", [1, 16], F32)


    seq = Seq(nc, es, "p0seq")
    io = dict(ins=ins, outs=outs, iden_in=iden_in, icst_in=icst_in, gpart=gpart, gsum=gsum)
    emit_p0(nc, seq, io)
    seq.finish()
    block = es.enter_context(nc.Block())

    @block.sync
    def _(sync):
        seq.emit("sync", sync)

    @block.vector
    def _(vector):
        seq.emit("vector", vector)

    @block.scalar
    def _(scalar):
        seq.emit("scalar", scalar)

    @block.gpsimd
    def _(gpsimd):
        seq.emit("gpsimd", gpsimd)

    @block.tensor
    def _(tensor):
        seq.emit("tensor", tensor)

    es.close()
    return nc


def p0_host_inputs(inp):
    tr = lambda a: np.ascontiguousarray(np.asarray(a).T)
    asc = lambda a: np.ascontiguousarray(np.asarray(a))
    full = {
        "l0wgu": asc(inp["L0_ffn_w_gate_up"]),
        "l0wpr": asc(inp["L0_ffn_w_proj"]),
        "l1wfc": asc(inp["L1_ffn_W_fc"]),
        "l1wpr": asc(inp["L1_ffn_W_proj"]),
        "l2wfc": asc(inp["L2_ffn_w_fc"]),
        "l2wpr": asc(inp["L2_ffn_w_proj"]),
        "ipwT": tr(inp["L0_attn_in_proj_w"]),
        "opwT": tr(inp["L0_attn_out_proj_w"]),
        "wqT": tr(inp["L1_attn_Wq"]),
        "wkT": tr(inp["L1_attn_Wk"]),
        "wvT": tr(inp["L1_attn_Wv"]),
        "woT": tr(inp["L1_attn_Wo"]),
        "wxT": tr(inp["L2_attn_W_xproj"]),
        "l3wq": asc(inp["L3_attn_Wq"]),
        "l3wk": asc(inp["L3_attn_Wk"]),
        "l3wv": asc(inp["L3_attn_Wv"]),
        "l3wo": asc(inp["L3_attn_Wo"]),
        "l3g_pk": asc(inp["L3_ffn_gate_W_packed"]),
        "l3g_sc": asc(inp["L3_ffn_gate_scales"]),
        "l3u_pk": asc(inp["L3_ffn_up_W_packed"]),
        "l3u_sc": asc(inp["L3_ffn_up_scales"]),
        "l3d_pk": asc(inp["L3_ffn_down_W_packed"]),
        "l3d_sc": asc(inp["L3_ffn_down_scales"]),
    }
    iden = np.eye(128, dtype=bf16)
    icst = np.tile(np.array([[15, 4]], np.int32), (128, 1))
    maps = []
    for c in range(NCORES):
        m = {"iden": iden, "icst": icst}
        for k, v in full.items():
            R = v.shape[0]
            Rs = R // NCORES
            m[k] = np.ascontiguousarray(v[c * Rs:(c + 1) * Rs])
        maps.append(m)
    return maps


def p0_collect(results):
    out = {}
    for name, R, K, kind, tr in P0_SPECS:
        out[name] = np.concatenate([r[name + "_o"] for r in results], axis=1 if tr else 0)
    for name, R, Kp, Kd in P0_NV:
        out[name] = np.concatenate([r[name + "_o"] for r in results], axis=1)
    return out


if __name__ == "__main__":
    print("kernel library")


def emit_a1(nc, seq, io):
    es = ExitStack()
    W = TSH
    WH = W + 2
    xfm_in = io["xfm_in"]
    ipw_in = io["ipw_in"]
    ipb_in = io["ipb_in"]
    shw_in = io["shw_in"]
    shb_in = io["shb_in"]
    hsc_in = io["hsc_in"]
    o1_in = io["o1_in"]
    xn_out = io["xn_out"]
    x0_out = io["x0_out"]
    v2_out = io["v2_out"]
    sbuf = nc.sbuf_tensor
    xsb = es.enter_context(sbuf("xsb", [128, 16 * 512], F32))
    sq = es.enter_context(sbuf("sq", [128, 16 * 512], F32))
    xnr = es.enter_context(sbuf("xnr", [128, 16 * WH], BF16))
    strow = es.enter_context(sbuf("strow", [1, 6 * 512], F32))
    cst = es.enter_context(sbuf("cst", [1, 2], F32))
    ones = es.enter_context(sbuf("ones", [128, 2], F32))
    ones1 = es.enter_context(sbuf("ones1_s", [1, 128], F32))
    hsc = es.enter_context(sbuf("hsc_s", [128, 2], F32))
    wsb2 = [es.enter_context(sbuf(f"wsb{i}", [128, 16 * 128], BF16)) for i in range(2)]
    usb = es.enter_context(sbuf("usb", [128, WH], F32))
    ucv = es.enter_context(sbuf("ucv", [128, W], F32))
    x1sb = es.enter_context(sbuf("x1sb", [128, W], F32))
    obuf2 = [es.enter_context(sbuf(f"obuf{i}", [128, W], BF16)) for i in range(2)]
    ipb = es.enter_context(sbuf("ipb_s", [128, 96], F32))
    shw = es.enter_context(sbuf("shw_s", [128, 48 * 8], F32))
    shb = es.enter_context(sbuf("shb_s", [128, 48 * 8], F32))
    pst0 = es.enter_context(nc.psum_tensor("pst0", [1, 512], F32))
    pst1 = es.enter_context(nc.psum_tensor("pst1", [1, 512], F32))
    pbc = es.enter_context(nc.psum_tensor("pbc", [128, 1024], F32))
    pu = [es.enter_context(nc.psum_tensor(f"pu{i}", [128, 512], F32)) for i in range(4)]

    seq.dma("sync", ones1[:], o1_in[:])
    seq.dma("sync", hsc[:], hsc_in[:])
    seq.dma("sync", ipb[:].rearrange("p (h two) -> p h two", two=2),
            ipb_in.rearrange("(h p) two -> p h two", p=128))
    seq.dma("sync", shw[:].rearrange("p (h x) -> p h x", x=8),
            shw_in.rearrange("(h p) x -> p h x", p=128))
    seq.dma("sync", shb[:].rearrange("p (h x) -> p h x", x=8),
            shb_in.rearrange("(h p) x -> p h x", p=128))
    seq.comp("vector", lambda v: v.memset(ones[:], 1.0))
    seq.comp("vector", lambda v: v.memset(cst[:, 0:1], 1e-5))

    ln_blocks = [(i * 512, 512, 0) for i in range(4)] + [(WH - 512, 512, 0)]
    for t0, bw, _ in ln_blocks:
        seq.dma("sync", xsb[:].rearrange("p (k x) -> p k x", x=512)[:, :, :bw],
                xfm_in.rearrange("(k p) t -> p k t", p=128)[:, :, t0:t0 + bw])

        def sq_fn(s, bw=bw):
            return s.activation(out=sq[:].rearrange("p (k x) -> p k x", x=512)[:, :, :bw],
                                in_=xsb[:].rearrange("p (k x) -> p k x", x=512)[:, :, :bw],
                                func=AF.Square)
        seq.comp("scalar", sq_fn)

        def st_fn(t, bw=bw):
            for k in range(16):
                t.matmul(pst0[:, :bw], ones[:, 0:1], xsb[:, k * 512:k * 512 + bw],
                         start=(k == 0), stop=(k == 15))
            for k in range(16):
                inst = t.matmul(pst1[:, :bw], ones[:, 0:1], sq[:, k * 512:k * 512 + bw],
                                start=(k == 0), stop=(k == 15))
            return inst
        seq.comp("tensor", st_fn)

        def row_fn(v, bw=bw):
            m = strow[0:1, 0:bw]
            var = strow[0:1, 512:512 + bw]
            v.tensor_scalar(m, pst0[:, :bw], 1.0 / 2048, None, Op.mult)
            v.tensor_tensor(var, m, m, Op.mult)
            v.scalar_tensor_tensor(var, pst1[:, :bw], 1.0 / 2048, var, Op.mult, Op.subtract)
            return v.tensor_copy(strow[0:1, 2048:2048 + bw], var)
        seq.comp("vector", row_fn)

        def rstd_fn(s, bw=bw):
            v = strow[0:1, 512:512 + bw]
            s.activation(out=v, in_=v, func=AF.Ln, bias=cst[0:1, 0:1])
            return s.activation(out=strow[0:1, 1024:1024 + bw], in_=v, func=AF.Exp, scale=-0.5)
        seq.comp("scalar", rstd_fn)
        seq.comp("vector", lambda v, bw=bw: v.tensor_tensor(
            strow[0:1, 1536:1536 + bw], strow[0:1, 0:bw], strow[0:1, 1024:1024 + bw], Op.mult))

        def bc_fn(t, bw=bw):
            t.matmul(pbc[:, 0:bw], ones1[:], strow[0:1, 1024:1024 + bw], start=True, stop=True)
            return t.matmul(pbc[:, 512:512 + bw], ones1[:], strow[0:1, 1536:1536 + bw], start=True, stop=True)
        seq.comp("tensor", bc_fn)

        def ap_fn(v, t0=t0, bw=bw):
            for k in range(16):
                v.tensor_tensor(sq[:, k * 512:k * 512 + bw], xsb[:, k * 512:k * 512 + bw],
                                pbc[:, 0:bw], Op.mult)
                inst = v.tensor_tensor(xnr[:, k * WH + t0:k * WH + t0 + bw],
                                       sq[:, k * 512:k * 512 + bw], pbc[:, 512:512 + bw],
                                       Op.subtract)
            return inst
        seq.comp("vector", ap_fn)
    seq.dma("scalar", xn_out.rearrange("(k p) t -> p k t", p=128),
            xnr[:].rearrange("p (k t) -> p k t", t=WH)[:, :, 2:])

    xn_store = len(seq.steps) - 1
    TR = {"ld": [None, None], "mm": [None, None], "ev": xn_store, "evh": xn_store,
          "conv": xn_store, "st": [None, None], "ti": 0}

    def do_htile(h, kind, i):
        ti = TR["ti"]
        p = ti % 2
        wsbp = wsb2[p]
        ldid = seq.dma("sync", wsbp[:].rearrange("p (k m) -> p k m", m=128),
                       ipw_in.rearrange("(k p) m -> p k m", p=128)[:, :, h * 128:(h + 1) * 128],
                       deps=[TR["mm"][p]], lane=f"ws{p}")

        def mm_fn(t, h=h, wsbp=wsbp):
            for c in range(4):
                c0 = c * 512
                for k in range(16):
                    inst = t.matmul(pu[c][:, :512], wsbp[:, k * 128:(k + 1) * 128],
                                    xnr[:, k * WH + c0:k * WH + c0 + 512],
                                    start=(k == 0), stop=(k == 15))
            return inst
        mmid = seq.comp("tensor", mm_fn, deps=[ldid, TR["ev"], xn_store])

        def ev_fn(s, h=h):
            for c in range(4):
                inst = s.activation(out=usb[:, c * 512:(c + 1) * 512], in_=pu[c][:, :512],
                                    func=AF.Identity, bias=ipb[:, 2 * h:2 * h + 1])
            return inst
        evid = seq.comp("scalar", ev_fn, deps=[mmid, TR["conv"]])

        def mmh_fn(t, h=h, wsbp=wsbp):
            for k in range(16):
                inst = t.matmul(pu[0][:, :2], wsbp[:, k * 128:(k + 1) * 128],
                                xnr[:, k * WH + 2048:k * WH + 2050],
                                start=(k == 0), stop=(k == 15))
            return inst
        mmhid = seq.comp("tensor", mmh_fn, deps=[evid])
        TR["mm"][p] = mmhid
        evhid = seq.comp("scalar", lambda s, h=h: s.activation(
            out=usb[:, 2048:2050], in_=pu[0][:, :2], func=AF.Identity,
            bias=ipb[:, 2 * h:2 * h + 1]), deps=[mmhid])
        TR["ev"] = evhid
        ob = obuf2[ti % 2]

        def cv_fn(v, h=h, kind=kind, ob=ob):
            v.tensor_tensor(usb[:, 0:2], usb[:, 0:2], hsc[:, 0:2], Op.mult)
            v.tensor_scalar(ucv[:], usb[:, 0:W], shw[:, h * 8:h * 8 + 1], None, Op.mult)
            v.scalar_tensor_tensor(ucv[:], usb[:, 1:1 + W], shw[:, h * 8 + 2:h * 8 + 3], ucv[:], Op.mult, Op.add)
            v.scalar_tensor_tensor(ucv[:], usb[:, 2:2 + W], shw[:, h * 8 + 4:h * 8 + 5], ucv[:], Op.mult, Op.add)
            if kind == "x1":
                return v.tensor_scalar(x1sb[:], ucv[:], shb[:, h * 8:h * 8 + 1], None, Op.add)
            elif kind == "v":
                v.tensor_scalar(ucv[:], ucv[:], shb[:, h * 8:h * 8 + 1], None, Op.add)
                return v.tensor_tensor(ob[:], ucv[:], x1sb[:], Op.mult)
            else:
                return v.tensor_scalar(ob[:], ucv[:], shb[:, h * 8:h * 8 + 1], None, Op.add)
        cvid = seq.comp("vector", cv_fn, deps=[evhid, TR["st"][ti % 2]])
        TR["conv"] = cvid
        if kind == "v":
            TR["st"][ti % 2] = seq.dma("scalar", v2_out[i * 128:(i + 1) * 128, :], ob[:],
                                       deps=[cvid], lane=f"ob{ti % 2}")
        elif kind == "x0":
            TR["st"][ti % 2] = seq.dma("scalar", x0_out[i * 128:(i + 1) * 128, :], ob[:],
                                       deps=[cvid], lane=f"ob{ti % 2}")
        TR["ti"] += 1

    for i in range(16):
        do_htile(16 + i, "x1", i)
        do_htile(32 + i, "v", i)
        do_htile(0 + i, "x0", i)
    seq.barrier()


    es.close()


def build_a1():
    nc = bass.Bass("TRN2", target_bir_lowering=False, debug=False)
    es = ExitStack()
    W = TSH
    WH = W + 2
    xfm_in = nc.dram_tensor("xfm", [D, WH], F32, kind="ExternalInput").ap()
    ipw_in = nc.dram_tensor("ipwT", [D, 3 * D], BF16, kind="ExternalInput").ap()
    ipb_in = nc.dram_tensor("ipb", [3 * D, 2], F32, kind="ExternalInput").ap()
    shw_in = nc.dram_tensor("shw", [3 * D, 8], F32, kind="ExternalInput").ap()
    shb_in = nc.dram_tensor("shb", [3 * D, 8], F32, kind="ExternalInput").ap()
    hsc_in = nc.dram_tensor("hsc", [128, 2], F32, kind="ExternalInput").ap()
    o1_in = nc.dram_tensor("ones1", [1, 128], F32, kind="ExternalInput").ap()
    xn_out = nc.dram_tensor("xn_o", [D, W], BF16, kind="ExternalOutput").ap()
    x0_out = nc.dram_tensor("x0_o", [D, W], BF16, kind="ExternalOutput").ap()
    v2_out = nc.dram_tensor("v2_o", [D, W], BF16, kind="ExternalOutput").ap()


    seq = Seq(nc, es, "a1seq")
    io = dict(xfm_in=xfm_in, ipw_in=ipw_in, ipb_in=ipb_in, shw_in=shw_in, shb_in=shb_in, hsc_in=hsc_in, o1_in=o1_in, xn_out=xn_out, x0_out=x0_out, v2_out=v2_out)
    emit_a1(nc, seq, io)
    seq.finish()
    block = es.enter_context(nc.Block())
    for en in ["sync", "vector", "scalar", "gpsimd", "tensor"]:
        getattr(block, en)(lambda e, en=en: seq.emit(en, e))
    es.close()
    return nc


def a1_host_inputs(inp, prep):
    x = np.asarray(inp["L0_attn_x"], np.float32).reshape(TOK, D)
    ipb = np.zeros((3 * D, 2), np.float32)
    ipb[:, 0] = np.asarray(inp["L0_attn_in_proj_b"], np.float32)
    shw_r = np.asarray(inp["L0_attn_short_w"], np.float32).reshape(3 * D, 3)
    shw = np.zeros((3 * D, 8), np.float32)
    shw[:, 0] = shw_r[:, 0]
    shw[:, 2] = shw_r[:, 1]
    shw[:, 4] = shw_r[:, 2]
    shb = np.zeros((3 * D, 8), np.float32)
    shb[:, 0] = np.asarray(inp["L0_attn_short_b"], np.float32)
    ones1 = np.ones((1, 128), np.float32)
    maps = []
    for c in range(NCORES):
        t0 = c * TSH
        xfm = np.zeros((D, TSH + 2), np.float32)
        xfm[:, 2:] = x[t0:t0 + TSH].T
        hs = np.zeros((128, 2), np.float32)
        if c % 4 != 0:
            xfm[:, 0:2] = x[t0 - 2:t0].T
            hs[:] = 1.0
        maps.append(dict(xfm=np.ascontiguousarray(xfm), ipwT=np.ascontiguousarray(prep["ipwT"]),
                         ipb=ipb, shw=shw, shb=shb, hsc=hs, ones1=ones1))
    return maps


def emit_a2(nc, seq, io):
    es = ExitStack()
    CS = D // NCORES
    NQ = 4
    QC = 64
    L = T
    v2_in = io["v2_in"]
    x0_in = io["x0_in"]
    zT_in = io["zT_in"]
    w0T_in = io["w0T_in"]
    w1T_in = io["w1T_in"]
    w2T_in = io["w2T_in"]
    w3T_in = io["w3T_in"]
    fvec_in = io["fvec_in"]
    trow_in = io["trow_in"]
    nds_in = io["nds_in"]
    fb_in = io["fb_in"]
    wre_in = io["wre_in"]
    wim_in = io["wim_in"]
    wimn_in = io["wimn_in"]
    wc64_in = io["wc64_in"]
    wpd_in = io["wpd_in"]
    wpd2_in = io["wpd2_in"]
    twfr_in = io["twfr_in"]
    twfi_in = io["twfi_in"]
    twir_in = io["twir_in"]
    twii_in = io["twii_in"]
    y_out = io["y_out"]
    DBGON = io.get("dbg", True)
    kdbg_out = io.get("kdbg_out")
    h1dbg_out = io.get("h1dbg_out")
    h3dbg_out = io.get("h3dbg_out")
    kfdbg_out = io.get("kfdbg_out")
    kscr = io["kscr"]
    kfscr = io["kfscr"]
    yscr = io["yscr"]
    sbuf = nc.sbuf_tensor
    S = es.enter_context(sbuf("Sb", [128, 16384], BF16))
    Z = es.enter_context(sbuf("Zb", [128, 16384], BF16))
    T1b = es.enter_context(sbuf("T1b", [128, 8192], BF16))
    T2b = es.enter_context(sbuf("T2b", [128, 8192], BF16))
    T3b = es.enter_context(sbuf("T3b", [128, 8192], BF16))
    KF = es.enter_context(sbuf("KFb", [128, 16384], BF16))
    wre = es.enter_context(sbuf("wre_s", [128, 128], BF16))
    wim = es.enter_context(sbuf("wim_s", [128, 128], BF16))
    wimn = es.enter_context(sbuf("wimn_s", [128, 128], BF16))
    wc64 = es.enter_context(sbuf("wc64_s", [64, 256], BF16))
    wpd = es.enter_context(sbuf("wpd_s", [128, 256], BF16))
    wpd2 = es.enter_context(sbuf("wpd2_s", [128, 256], BF16))
    twfr = es.enter_context(sbuf("twfr_s", [128, 128], F32))
    twfi = es.enter_context(sbuf("twfi_s", [128, 128], F32))
    twir = es.enter_context(sbuf("twir_s", [128, 128], F32))
    twii = es.enter_context(sbuf("twii_s", [128, 128], F32))
    w0T = es.enter_context(sbuf("w0T_s", [3, 64], F32))
    w1T = es.enter_context(sbuf("w1T_s", [64, 64], F32))
    w2T = es.enter_context(sbuf("w2T_s", [64, 64], F32))
    w3T = es.enter_context(sbuf("w3T_s", [64, CS], F32))
    fvec = es.enter_context(sbuf("fvec_s", [64, 8], F32))
    nds = es.enter_context(sbuf("nds_s", [128, 4], F32))
    fbv = es.enter_context(sbuf("fbv_s", [128, 4], F32))
    pA = [es.enter_context(nc.psum_tensor(f"pA{i}", [128, 512], F32)) for i in range(4)]
    pE = es.enter_context(nc.psum_tensor("pE", [64, 512], F32))
    pE2 = es.enter_context(nc.psum_tensor("pE2", [64, 512], F32))

    hga_lo = S[0:64, :].bitcast(F32)
    hga_hi = S[64:128, :].bitcast(F32)
    hgb_hi = Z[64:128, :].bitcast(F32)
    zTv = Z[0:3, :].bitcast(F32)
    kpre = Z[0:128, :].bitcast(F32)
    trowB = KF[0:128, :].bitcast(F32)
    X2 = Z
    Y2 = KF
    X0v = KF

    for dst, src2 in [(wre, wre_in), (wim, wim_in), (wimn, wimn_in), (wc64, wc64_in),
                      (wpd, wpd_in), (wpd2, wpd2_in), (twfr, twfr_in), (twfi, twfi_in),
                      (twir, twir_in), (twii, twii_in), (w0T, w0T_in),
                      (w1T, w1T_in), (w2T, w2T_in), (w3T, w3T_in), (fvec, fvec_in),
                      ]:
        seq.dma("sync", dst[:], src2[:])
    seq.dma("sync", nds[:].rearrange("p (ct two) -> p ct two", two=2),
            nds_in.rearrange("(ct p) two -> p ct two", p=128))
    seq.dma("sync", fbv[:].rearrange("p (ct two) -> p ct two", two=2),
            fb_in.rearrange("(ct p) two -> p ct two", p=128))
    seq.dma("sync", zTv, zT_in[:])

    TWO_PI = float(2 * np.pi)

    def fv_fn(v):
        v.tensor_tensor(fvec[:, 2:3], fvec[:, 0:1], fvec[:, 2:3], Op.mult)
        v.tensor_tensor(fvec[:, 4:5], fvec[:, 0:1], fvec[:, 4:5], Op.mult)
        return v.tensor_tensor(fvec[:, 6:7], fvec[:, 0:1], fvec[:, 6:7], Op.mult)
    seq.comp("vector", fv_fn)

    def mlp_layer(lhsT, rhs, bcol, last=False):
        prev_tail = len(seq.steps) - 1
        acts = {}
        for ch in range(16):
            mid = seq.comp("tensor", lambda t, ch=ch: t.matmul(
                pA[ch % 2][:64, :], lhsT, rhs[:, ch * 512:(ch + 1) * 512],
                start=True, stop=True), deps=[acts.get(ch - 2, prev_tail)])
            acts[ch] = seq.comp("scalar", lambda s, ch=ch: s.activation(
                out=hga_hi[:, ch * 512:(ch + 1) * 512], in_=pA[ch % 2][:64, :],
                func=AF.Identity, scale=fvec[:, 0:1], bias=fvec[:, bcol:bcol + 1]),
                deps=[mid])

        def rr(v):
            MAGIC = 1.5 * BIG
            v.tensor_scalar(hgb_hi, hga_hi, 1.0 / TWO_PI, MAGIC, Op.mult, Op.add)
            v.tensor_scalar(hgb_hi, hgb_hi, MAGIC, None, Op.subtract)
            return v.scalar_tensor_tensor(hga_hi, hgb_hi, -TWO_PI, hga_hi, Op.mult, Op.add)
        seq.comp("vector", rr)
        seq.comp("scalar", lambda s: s.activation(out=hga_lo, in_=hga_hi, func=AF.Sin))

    mlp_layer(w0T[:, :], zTv, 2)
    if DBGON:
        seq.dma("scalar", h1dbg_out, hga_lo)
    mlp_layer(w1T[:, :], hga_lo, 4)
    mlp_layer(w2T[:, :], hga_lo, 6)
    if DBGON:
        seq.dma("scalar", h3dbg_out, hga_lo)
    seq.dma("sync", trowB, trow_in[0, :].partition_broadcast(128))
    for ct in range(2):
        for ch in range(16):
            seq.comp("tensor", lambda t, ct=ct, ch=ch: t.matmul(
                pA[2 + ch % 2][:, :], w3T[:, ct * 128:(ct + 1) * 128],
                hga_lo[:, ch * 512:(ch + 1) * 512], start=True, stop=True))
            seq.comp("scalar", lambda s, ch=ch: s.activation(
                out=kpre[:, ch * 512:(ch + 1) * 512], in_=pA[2 + ch % 2][:, :], func=AF.Identity))
        seq.comp("scalar", lambda s, ct=ct: s.activation(
            out=T1b[:, :], in_=trowB, func=AF.Exp, scale=nds[:, 2 * ct:2 * ct + 1]))
        seq.comp("vector", lambda v: v.tensor_tensor(T2b[:, :], kpre, T1b[:, :], Op.mult))
        seq.dma("scalar", kscr.ap()[ct * 128:(ct + 1) * 128, :], T2b[:, :])
        if DBGON:
            seq.dma("scalar", kdbg_out[ct * 128:(ct + 1) * 128, :], T2b[:, :])

    def cplx_mul_bcast(S4, re_b, im_b):
        Sre, Sim = S4[:, :, 0, :], S4[:, :, 1, :]
        T13 = T1b[:].rearrange("p (c k) -> p c k", k=128)
        T23 = T2b[:].rearrange("p (c k) -> p c k", k=128)
        T33 = T3b[:].rearrange("p (c k) -> p c k", k=128)
        seq.comp("gpsimd", lambda g: g.tensor_tensor(T13, Sre, re_b, Op.mult))
        seq.comp("gpsimd", lambda g: g.tensor_tensor(T23, Sim, im_b, Op.mult))
        seq.comp("gpsimd", lambda g: g.tensor_tensor(T33, Sre, im_b, Op.mult))
        seq.comp("gpsimd", lambda g: g.tensor_tensor(Sim, Sim, re_b, Op.mult))
        seq.comp("vector", lambda v: v.tensor_tensor(Sre, T13, T23, Op.subtract))
        seq.comp("vector", lambda v: v.tensor_tensor(Sim, Sim, T33, Op.add))

    def fwd_fft(src_rows, kf_dma):
        ldX = seq.dma("sync", X2[0:64, 0:8192].rearrange("p (c m) -> p c m", m=128), src_rows)
        evsA = {}
        for c0 in range(0, QC, 2):
            def sa(t, c0=c0):
                t.matmul(pA[(c0 // 2) % 2][:, 0:256], X2[0:64, c0 * 128:(c0 + 1) * 128],
                         wc64[:, :], start=True, stop=True)
                return t.matmul(pA[(c0 // 2) % 2][:, 256:512], X2[0:64, (c0 + 1) * 128:(c0 + 2) * 128],
                                wc64[:, :], start=True, stop=True)
            said = seq.comp("tensor", sa, deps=[ldX, evsA.get(c0 - 4)])
            evsA[c0] = seq.comp("scalar", lambda s, c0=c0: s.activation(
                out=S[:, c0 * 256:(c0 + 2) * 256], in_=pA[(c0 // 2) % 2][:, :], func=AF.Identity),
                deps=[said])
        S4 = S[:].rearrange("p (c two k) -> p c two k", two=2, k=128)
        cplx_mul_bcast(S4, twfr[:].unsqueeze(1).to_broadcast((128, QC, 128)),
                       twfi[:].unsqueeze(1).to_broadcast((128, QC, 128)))
        prevC = len(seq.steps) - 1
        evsC = {}
        for cc in range(0, QC, 2):
            rre = S4[:, cc:cc + 2, 0, :]
            rim = S4[:, cc:cc + 2, 1, :]
            bp = (2, 3) if (cc // 2) % 2 == 0 else (0, 1)

            def sc2(t, rre=rre, rim=rim, bp=bp):
                t.matmul(pA[bp[0]][:, 0:256], wre[:, :], rre, start=True, stop=False)
                t.matmul(pA[bp[0]][:, 0:256], wimn[:, :], rim, start=False, stop=True)
                t.matmul(pA[bp[1]][:, 0:256], wim[:, :], rre, start=True, stop=False)
                return t.matmul(pA[bp[1]][:, 0:256], wre[:, :], rim, start=False, stop=True)
            scid = seq.comp("tensor", sc2, deps=[evsC.get(cc - 4, prevC)])
            Z4 = Z[:].rearrange("p (c two k) -> p c two k", two=2, k=128)

            def ev(s, cc=cc, Z4=Z4, bp=bp):
                s.activation(out=Z4[:, cc:cc + 2, 0, :],
                             in_=pA[bp[0]][:, 0:256].rearrange("p (a k) -> p a k", a=2), func=AF.Identity)
                return s.activation(out=Z4[:, cc:cc + 2, 1, :],
                                    in_=pA[bp[1]][:, 0:256].rearrange("p (a k) -> p a k", a=2), func=AF.Identity)
            evsC[cc] = seq.comp("scalar", ev, deps=[scid])
        if kf_dma is not None:
            seq.comp("scalar", lambda s: s.activation(out=Z[:, :], in_=Z[:, :], func=AF.Identity,
                                                      scale=1.0 / 16384.0))
            seq.dma("scalar", kf_dma, Z[:, :])
            if DBGON and kf_dma is KFDBG[0]:
                seq.dma("scalar", kfdbg_out, Z[:, :])

    def product_with_kf(q):
        seq.dma("sync", KF[:, :], kfscr.ap()[q, :, :])
        Z4 = Z[:].rearrange("p (c two k) -> p c two k", two=2, k=128)
        K4 = KF[:].rearrange("p (c two k) -> p c two k", two=2, k=128)
        Zre, Zim = Z4[:, :, 0, :], Z4[:, :, 1, :]
        Kre, Kim = K4[:, :, 0, :], K4[:, :, 1, :]
        T13 = T1b[:].rearrange("p (c k) -> p c k", k=128)
        T23 = T2b[:].rearrange("p (c k) -> p c k", k=128)
        T33 = T3b[:].rearrange("p (c k) -> p c k", k=128)
        seq.comp("vector", lambda v: v.tensor_tensor(T13, Zre, Kre, Op.mult))
        seq.comp("vector", lambda v: v.tensor_tensor(T23, Zim, Kim, Op.mult))
        seq.comp("vector", lambda v: v.tensor_tensor(T33, Zre, Kim, Op.mult))
        seq.comp("vector", lambda v: v.tensor_tensor(Zim, Zim, Kre, Op.mult))
        seq.comp("vector", lambda v: v.tensor_tensor(Zre, T13, T23, Op.subtract))
        seq.comp("vector", lambda v: v.tensor_tensor(Zim, Zim, T33, Op.add))

    def inv_fft(dst_rows, x0_rows):
        prevD = len(seq.steps) - 1
        evsD = {}
        for c0 in range(0, QC, 2):
            bp = (2, 3) if (c0 // 2) % 2 == 0 else (0, 1)

            def sd(t, c0=c0, bp=bp):
                for j in (0, 1):
                    cc = c0 + j
                    t.matmul(pA[bp[j]][:, 0:256], Z[:, cc * 256:cc * 256 + 128], wpd[:, :],
                             start=True, stop=False)
                    inst = t.matmul(pA[bp[j]][:, 0:256], Z[:, cc * 256 + 128:cc * 256 + 256],
                                    wpd2[:, :], start=False, stop=True)
                return inst
            sdid = seq.comp("tensor", sd, deps=[evsD.get(c0 - 4, prevD)])

            def ev(s, c0=c0, bp=bp):
                s.activation(out=S[:, c0 * 256:(c0 + 1) * 256], in_=pA[bp[0]][:, 0:256], func=AF.Identity)
                return s.activation(out=S[:, (c0 + 1) * 256:(c0 + 2) * 256], in_=pA[bp[1]][:, 0:256],
                                    func=AF.Identity)
            evsD[c0] = seq.comp("scalar", ev, deps=[sdid])
        S4 = S[:].rearrange("p (c two k) -> p c two k", two=2, k=128)
        cplx_mul_bcast(S4, twir[:].unsqueeze(1).to_broadcast((128, QC, 128)),
                       twii[:].unsqueeze(1).to_broadcast((128, QC, 128)))
        seq.dma("sync", X0v[0:64, 8192:16384].rearrange("p (c m) -> p c m", m=128), x0_rows)
        prevE = len(seq.steps) - 1
        evsE = {}
        for cc in range(0, QC, 2):
            rre = S4[:, cc:cc + 2, 0, :]
            rim = S4[:, cc:cc + 2, 1, :]
            pe = pE if (cc // 2) % 2 == 0 else pE2

            def se(t, rre=rre, rim=rim, pe=pe):
                t.matmul(pe[:, 0:256], wre[:, 0:64], rre, start=True, stop=False)
                return t.matmul(pe[:, 0:256], wim[:, 0:64], rim, start=False, stop=True)
            seid = seq.comp("tensor", se, deps=[evsE.get(cc - 4, prevE)])
            evsE[cc] = seq.comp("scalar", lambda s, cc=cc, pe=pe: s.activation(
                out=Y2[0:64, cc * 128:(cc + 2) * 128], in_=pe[:, 0:256], func=AF.Identity),
                deps=[seid])
        seq.comp("vector", lambda v: v.tensor_tensor(
            Y2[0:64, 0:8192], Y2[0:64, 0:8192], X0v[0:64, 8192:16384], Op.mult))
        seq.dma("scalar", dst_rows, Y2[0:64, 0:8192].rearrange("p (c m) -> p c m", m=128))

    def q_rows(base_ap, q, b):
        return base_ap.rearrange("c (bb n1 n2) -> n1 c bb n2", bb=B, n2=128)[
            :, q * QC:(q + 1) * QC, b, :]

    KFDBG = [None]
    for q in range(NQ):
        src = kscr.ap().rearrange("c (n1 n2) -> n1 c n2", n2=128)[
            :, q * QC:(q + 1) * QC, :]
        dmaq = kfscr.ap()[q, :, :]
        if q == 0:
            KFDBG[0] = dmaq
        fwd_fft(src, dmaq)
    for b in range(B):
        for q in range(NQ):
            fwd_fft(q_rows(v2_in, q, b), None)
            product_with_kf(q)
            inv_fft(q_rows(yscr.ap(), q, b), q_rows(x0_in, q, b))

    for ct in range(2):
        rows = slice(ct * 128, (ct + 1) * 128)
        for b in range(B):
            cols = slice(b * L, (b + 1) * L)
            seq.dma("sync", T1b[:, :], v2_in[rows, cols])
            seq.dma("sync", T2b[:, :], x0_in[rows, cols])
            seq.dma("sync", T3b[:, :], yscr.ap()[rows, cols])
            seq.comp("vector", lambda v: v.tensor_tensor(T1b[:, :], T1b[:, :], T2b[:, :], Op.mult))
            seq.comp("vector", lambda v, ct=ct: v.scalar_tensor_tensor(
                T3b[:, :], T1b[:, :], fbv[:, 2 * ct:2 * ct + 1], T3b[:, :], Op.mult, Op.add))
            seq.dma("scalar", y_out[rows, cols], T3b[:, :])

    es.close()


def build_a2():
    nc = bass.Bass("TRN2", target_bir_lowering=False, debug=False)
    es = ExitStack()
    CS = D // NCORES
    NQ = 4
    QC = 64
    L = T
    v2_in = nc.dram_tensor("v2c", [CS, B * L], BF16, kind="ExternalInput").ap()
    x0_in = nc.dram_tensor("x0c", [CS, B * L], BF16, kind="ExternalInput").ap()
    zT_in = nc.dram_tensor("zT", [3, L], F32, kind="ExternalInput").ap()
    w0T_in = nc.dram_tensor("w0T", [3, 64], F32, kind="ExternalInput").ap()
    w1T_in = nc.dram_tensor("w1T", [64, 64], F32, kind="ExternalInput").ap()
    w2T_in = nc.dram_tensor("w2T", [64, 64], F32, kind="ExternalInput").ap()
    w3T_in = nc.dram_tensor("w3T", [64, CS], F32, kind="ExternalInput").ap()
    fvec_in = nc.dram_tensor("fvec", [64, 8], F32, kind="ExternalInput").ap()
    trow_in = nc.dram_tensor("trow", [1, L], F32, kind="ExternalInput").ap()
    nds_in = nc.dram_tensor("nds", [CS, 2], F32, kind="ExternalInput").ap()
    fb_in = nc.dram_tensor("fbv", [CS, 2], F32, kind="ExternalInput").ap()
    wre_in = nc.dram_tensor("wre", [128, 128], BF16, kind="ExternalInput").ap()
    wim_in = nc.dram_tensor("wim", [128, 128], BF16, kind="ExternalInput").ap()
    wimn_in = nc.dram_tensor("wimn", [128, 128], BF16, kind="ExternalInput").ap()
    wc64_in = nc.dram_tensor("wc64", [64, 256], BF16, kind="ExternalInput").ap()
    wpd_in = nc.dram_tensor("wpd", [128, 256], BF16, kind="ExternalInput").ap()
    wpd2_in = nc.dram_tensor("wpd2", [128, 256], BF16, kind="ExternalInput").ap()
    twfr_in = nc.dram_tensor("twfr", [128, 128], F32, kind="ExternalInput").ap()
    twfi_in = nc.dram_tensor("twfi", [128, 128], F32, kind="ExternalInput").ap()
    twir_in = nc.dram_tensor("twir", [128, 128], F32, kind="ExternalInput").ap()
    twii_in = nc.dram_tensor("twii", [128, 128], F32, kind="ExternalInput").ap()
    y_out = nc.dram_tensor("y_o", [CS, B * L], BF16, kind="ExternalOutput").ap()
    kdbg_out = nc.dram_tensor("kdbg_o", [CS, L], BF16, kind="ExternalOutput").ap()
    h1dbg_out = nc.dram_tensor("h1dbg_o", [64, L], F32, kind="ExternalOutput").ap()
    h3dbg_out = nc.dram_tensor("h3dbg_o", [64, L], F32, kind="ExternalOutput").ap()
    kfdbg_out = nc.dram_tensor("kfdbg_o", [128, 64 * 256], BF16, kind="ExternalOutput").ap()
    kscr = nc.dram_tensor("kscr", [CS, L], BF16)
    kfscr = nc.dram_tensor("kfscr", [NQ, 128, QC * 256], BF16)
    yscr = nc.dram_tensor("yscr", [CS, B * L], BF16)


    seq = Seq(nc, es, "a2seq")
    io = dict(v2_in=v2_in, x0_in=x0_in, zT_in=zT_in, w0T_in=w0T_in, w1T_in=w1T_in, w2T_in=w2T_in, w3T_in=w3T_in, fvec_in=fvec_in, trow_in=trow_in, nds_in=nds_in, fb_in=fb_in, wre_in=wre_in, wim_in=wim_in, wimn_in=wimn_in, wc64_in=wc64_in, wpd_in=wpd_in, wpd2_in=wpd2_in, twfr_in=twfr_in, twfi_in=twfi_in, twir_in=twir_in, twii_in=twii_in, y_out=y_out, kdbg_out=kdbg_out, h1dbg_out=h1dbg_out, h3dbg_out=h3dbg_out, kfdbg_out=kfdbg_out, kscr=kscr, kfscr=kfscr, yscr=yscr)
    emit_a2(nc, seq, io)
    seq.finish()
    block = es.enter_context(nc.Block())
    for en in ["sync", "vector", "scalar", "gpsimd", "tensor"]:
        getattr(block, en)(lambda e, en=en: seq.emit(en, e))
    es.close()
    return nc


def a2_host_inputs(inp, v2_full, x0_full):
    import math
    N1 = 128
    Wd = np.exp(-2j * np.pi * np.arange(N1)[:, None] * np.arange(N1)[None, :] / N1)
    TwF = np.exp(-2j * np.pi * np.outer(np.arange(N1), np.arange(N1)) / 16384.0)
    wre = Wd.real.astype(bf16)
    wim = Wd.imag.astype(bf16)
    wimn = (-Wd.imag).astype(bf16)
    wc64 = np.concatenate([Wd.real[:64, :], Wd.imag[:64, :]], 1).astype(bf16)
    wpd = np.concatenate([Wd.real, -Wd.imag], 1).astype(bf16)
    wpd2 = np.concatenate([Wd.imag, Wd.real], 1).astype(bf16)
    twfr = TwF.real.astype(np.float32)
    twfi = TwF.imag.astype(np.float32)
    twir = TwF.real.astype(np.float32)
    twii = (-TwF.imag).astype(np.float32)
    Lq = T
    t = np.linspace(0.0, 1.0, Lq)[:, None]
    t_resc = np.linspace(0.0, Lq - 1.0, Lq)
    wv = 2.0 * math.pi * t_resc / Lq
    f = np.linspace(0.0001, 0.0, 1)
    ang = f[None, :] * wv[:, None]
    z = np.concatenate([t, np.cos(ang), -np.sin(ang)], axis=-1)
    zT = np.ascontiguousarray(z.T).astype(np.float32)
    trow = np.linspace(0.0, 1.0, Lq).reshape(1, Lq).astype(np.float32)
    max_decay = math.log(0.01) / 0.3
    min_decay = math.log(0.01) / 1.5
    deltas = np.linspace(min_decay, max_decay, D)
    nds_full = np.zeros((D, 2), np.float32)
    nds_full[:, 0] = -np.abs(deltas)
    fb_full = np.zeros((D, 2), np.float32)
    fb_full[:, 0] = np.asarray(inp["L0_attn_filter_bias"], np.float32)
    fvec = np.zeros((64, 8), np.float32)
    fvec[:, 0] = np.asarray(inp["L0_attn_sin_freq"], np.float32).reshape(64)
    fvec[:, 2] = np.asarray(inp["L0_attn_mlp_b0"], np.float32)
    fvec[:, 4] = np.asarray(inp["L0_attn_mlp_b1"], np.float32)
    fvec[:, 6] = np.asarray(inp["L0_attn_mlp_b2"], np.float32)
    w0T = np.ascontiguousarray(np.asarray(inp["L0_attn_mlp_w0"], np.float32).T)
    w1T = np.ascontiguousarray(np.asarray(inp["L0_attn_mlp_w1"], np.float32).T)
    w2T = np.ascontiguousarray(np.asarray(inp["L0_attn_mlp_w2"], np.float32).T)
    w3 = np.asarray(inp["L0_attn_mlp_w3"], np.float32)
    maps = []
    for c in range(NCORES):
        rs = slice(c * 256, (c + 1) * 256)
        maps.append(dict(
            v2c=np.ascontiguousarray(v2_full[rs]), x0c=np.ascontiguousarray(x0_full[rs]),
            zT=zT, w0T=w0T, w1T=w1T, w2T=w2T,
            w3T=np.ascontiguousarray(w3[rs].T), fvec=fvec, trow=trow,
            nds=np.ascontiguousarray(nds_full[rs]), fbv=np.ascontiguousarray(fb_full[rs]),
            wre=wre, wim=wim, wimn=wimn, wc64=wc64, wpd=wpd, wpd2=wpd2,
            twfr=twfr, twfi=twfi, twir=twir, twii=twii))
    return maps


def emit_a3(nc, seq, io):
    es = ExitStack()
    W = TSH
    H0 = 8192
    xfm_in = io["xfm_in"]
    xn_in = io["xn_in"]
    y_in = io["y_in"]
    opw_in = io["opw_in"]
    opb_in = io["opb_in"]
    wgu_in = io.get("wgu_in")
    wpr_in = io.get("wpr_in")
    o1_in = io["o1_in"]
    xo_out = io["xo_out"]
    hscr = io["hscr"]
    wgu_src = io.get("wgu_src") or (
        lambda m0: wgu_in.rearrange("(k p) m -> p k m", p=128)[:, :, m0:m0 + 128])
    wpr_srcs = io.get("wpr_srcs") or (
        lambda k, g: [(0, 1024, wpr_in[k * 128:(k + 1) * 128, g * 1024:(g + 1) * 1024])])
    sbuf = nc.sbuf_tensor
    xnr = es.enter_context(sbuf("xnr", [128, 16 * W], BF16))
    wg = es.enter_context(sbuf("wg", [128, 16 * 128], BF16))
    wu = es.enter_context(sbuf("wu", [128, 16 * 128], BF16))
    gsb = es.enter_context(sbuf("gsb", [128, 512], F32))
    hsb = es.enter_context(sbuf("hsb", [128, 2048], BF16))
    hq = es.enter_context(sbuf("hq", [128, 2048], F32))
    rstdB = es.enter_context(sbuf("rstdB", [128, 2048], F32))
    strow = es.enter_context(sbuf("strow", [1, 2048], F32))
    ones = es.enter_context(sbuf("ones", [128, 2], F32))
    ones1 = es.enter_context(sbuf("ones1_s", [1, 128], F32))
    cst = es.enter_context(sbuf("cst", [1, 2], F32))
    opb = es.enter_context(sbuf("opb_s", [128, 32], F32))
    kch = es.enter_context(sbuf("kch", [128, 1024], BF16))
    wch = es.enter_context(sbuf("wch", [128, 8 * 128], BF16))
    xres = es.enter_context(sbuf("xres", [128, 512], F32))
    ob = es.enter_context(sbuf("ob", [128, 512], F32))
    P = [es.enter_context(nc.psum_tensor(f"pp{i}", [128, 512], F32)) for i in range(8)]

    seq.dma("sync", ones1[:], o1_in[:])
    seq.dma("sync", opb[:].rearrange("p (h two) -> p h two", two=2),
            opb_in.rearrange("(h p) two -> p h two", p=128))
    seq.dma("sync", xnr[:].rearrange("p (k t) -> p k t", t=W),
            xn_in.rearrange("(k p) t -> p k t", p=128))
    seq.comp("vector", lambda v: v.memset(ones[:], 1.0))
    seq.comp("vector", lambda v: v.memset(cst[:, 0:1], 1.1920929e-7))

    NHT = H0 // 128
    for ht in range(NHT):
        seq.dma("sync", wg[:].rearrange("p (k m) -> p k m", m=128), wgu_src(ht * 128))
        seq.dma("sync", wu[:].rearrange("p (k m) -> p k m", m=128), wgu_src(H0 + ht * 128))
        for tc in range(4):
            def mmg(t, tc=tc):
                for k in range(16):
                    t.matmul(P[0][:, :], wg[:, k * 128:(k + 1) * 128],
                             xnr[:, k * W + tc * 512:k * W + (tc + 1) * 512],
                             start=(k == 0), stop=(k == 15))
                for k in range(16):
                    inst = t.matmul(P[1][:, :], wu[:, k * 128:(k + 1) * 128],
                                    xnr[:, k * W + tc * 512:k * W + (tc + 1) * 512],
                                    start=(k == 0), stop=(k == 15))
                return inst
            seq.comp("tensor", mmg)
            seq.comp("scalar", lambda s: s.activation(out=gsb[:, :], in_=P[0][:, :], func=AF.Silu))
            seq.comp("vector", lambda v, tc=tc: v.tensor_tensor(
                hsb[:, tc * 512:(tc + 1) * 512], gsb[:, :], P[1][:, :], Op.mult))
        seq.comp("scalar", lambda s: s.activation(out=hq[:, :], in_=hsb[:, :], func=AF.Square))

        def stm(t, ht=ht):
            for tc in range(4):
                inst = t.matmul(P[2 + tc][0:1, :], ones[:, 0:1], hq[:, tc * 512:(tc + 1) * 512],
                                start=(ht == 0), stop=(ht == NHT - 1))
            return inst
        seq.comp("tensor", stm)
        seq.dma("scalar", hscr.ap()[ht * 128:(ht + 1) * 128, :], hsb[:, :])

    def rstd_fn(s):
        for tc in range(4):
            s.activation(out=strow[0:1, tc * 512:(tc + 1) * 512], in_=P[2 + tc][0:1, :],
                         func=AF.Ln, bias=cst[0:1, 0:1], scale=1.0 / H0)
            inst = s.activation(out=strow[0:1, tc * 512:(tc + 1) * 512],
                                in_=strow[0:1, tc * 512:(tc + 1) * 512], func=AF.Exp, scale=-0.5)
        return inst
    seq.comp("scalar", rstd_fn)

    def bc_fn(t):
        for tc in range(4):
            inst = t.matmul(P[tc][:, :], ones1[:], strow[0:1, tc * 512:(tc + 1) * 512],
                            start=True, stop=True)
        return inst
    seq.comp("tensor", bc_fn)

    def bc_ev(v):
        for tc in range(4):
            inst = v.tensor_copy(rstdB[:, tc * 512:(tc + 1) * 512], P[tc][:, :])
        return inst
    seq.comp("vector", bc_ev)

    for tc in range(4):
        cs = slice(tc * 512, (tc + 1) * 512)
        for g in range(2):
            for k in range(NHT):
                seq.dma("sync", kch[:, 0:512], hscr.ap()[k * 128:(k + 1) * 128, cs])
                for c0w, wdw, srcw in wpr_srcs(k, g):
                    seq.dma("sync", wch[:, c0w:c0w + wdw], srcw)
                seq.comp("vector", lambda v, tc=tc: v.tensor_tensor(
                    kch[:, 512:1024], kch[:, 0:512], rstdB[:, tc * 512:(tc + 1) * 512], Op.mult))

                def mmd(t, k=k):
                    for j in range(8):
                        inst = t.matmul(P[j][:, :], wch[:, j * 128:(j + 1) * 128], kch[:, 512:1024],
                                        start=(k == 0), stop=False)
                    return inst
                seq.comp("tensor", mmd)
            for k in range(16):
                seq.dma("sync", kch[:, 0:512], y_in.rearrange("(kk p) t -> p kk t", p=128)[:, k, cs])
                seq.dma("sync", wch[:], opw_in[k * 128:(k + 1) * 128, g * 1024:(g + 1) * 1024])

                def mma(t, k=k):
                    for j in range(8):
                        inst = t.matmul(P[j][:, :], wch[:, j * 128:(j + 1) * 128], kch[:, 0:512],
                                        start=False, stop=(k == 15))
                    return inst
                seq.comp("tensor", mma)
            for j in range(8):
                dout = g * 8 + j
                seq.dma("sync", xres[:, :], xfm_in.rearrange("(kk p) t -> p kk t", p=128)[:, dout, cs])
                seq.comp("scalar", lambda s, j=j, dout=dout: s.activation(
                    out=ob[:, :], in_=P[j][:, :], func=AF.Identity, bias=opb[:, 2 * dout:2 * dout + 1]))
                seq.comp("vector", lambda v: v.tensor_tensor(ob[:, :], ob[:, :], xres[:, :], Op.add))
                seq.dma("scalar", xo_out.rearrange("(kk p) t -> p kk t", p=128)[:, dout, cs], ob[:, :])

    es.close()


def build_a3():
    nc = bass.Bass("TRN2", target_bir_lowering=False, debug=False)
    es = ExitStack()
    W = TSH
    H0 = 8192
    xfm_in = nc.dram_tensor("xfm", [D, W], F32, kind="ExternalInput").ap()
    xn_in = nc.dram_tensor("xn", [D, W], BF16, kind="ExternalInput").ap()
    y_in = nc.dram_tensor("yv", [D, W], BF16, kind="ExternalInput").ap()
    opw_in = nc.dram_tensor("opwT", [D, D], BF16, kind="ExternalInput").ap()
    opb_in = nc.dram_tensor("opb", [D, 2], F32, kind="ExternalInput").ap()
    wgu_in = nc.dram_tensor("wguT", [D, 2 * H0], BF16, kind="ExternalInput").ap()
    wpr_in = nc.dram_tensor("wprT", [H0, D], BF16, kind="ExternalInput").ap()
    o1_in = nc.dram_tensor("ones1", [1, 128], F32, kind="ExternalInput").ap()
    xo_out = nc.dram_tensor("xo_o", [D, W], F32, kind="ExternalOutput").ap()
    hscr = nc.dram_tensor("hscr", [H0, W], BF16)


    seq = Seq(nc, es, "a3seq")
    io = dict(xfm_in=xfm_in, xn_in=xn_in, y_in=y_in, opw_in=opw_in, opb_in=opb_in, wgu_in=wgu_in, wpr_in=wpr_in, o1_in=o1_in, xo_out=xo_out, hscr=hscr)
    emit_a3(nc, seq, io)
    seq.finish()
    block = es.enter_context(nc.Block())
    for en in ["sync", "vector", "scalar", "gpsimd", "tensor"]:
        getattr(block, en)(lambda e, en=en: seq.emit(en, e))
    es.close()
    return nc


def emit_b1(nc, seq, io):
    es = ExitStack()
    W = TSH
    xfm_in = io["xfm_in"]
    wq_in = io["wq_in"]
    wk_in = io["wk_in"]
    wv_in = io["wv_in"]
    o1_in = io["o1_in"]
    xn_out = io["xn_out"]
    q_out = io["q_out"]
    k_out = io["k_out"]
    v_out = io["v_out"]
    lf_out = io["lf_out"]
    sbuf = nc.sbuf_tensor
    xsb = es.enter_context(sbuf("xsb", [128, 16 * 512], F32))
    sq = es.enter_context(sbuf("sq", [128, 16 * 512], F32))
    xnr = es.enter_context(sbuf("xnr", [128, 16 * W], BF16))
    strow = es.enter_context(sbuf("strow", [1, 3 * 512], F32))
    ones = es.enter_context(sbuf("ones", [128, 2], F32))
    ones1 = es.enter_context(sbuf("ones1_s", [1, 128], F32))
    cst = es.enter_context(sbuf("cst", [1, 2], F32))
    wsb2 = [es.enter_context(sbuf(f"wsb{i}", [128, 16 * 128], BF16)) for i in range(2)]
    fsb = es.enter_context(sbuf("fsb", [128, W], F32))
    s2sb = es.enter_context(sbuf("s2sb", [128, W], F32))
    obuf2 = [es.enter_context(sbuf(f"obuf{i}", [128, W], BF16)) for i in range(2)]
    ofp16 = es.enter_context(sbuf("ofp16", [128, W], FP16))
    pst0 = es.enter_context(nc.psum_tensor("pst0", [1, 512], F32))
    pst1 = es.enter_context(nc.psum_tensor("pst1", [1, 512], F32))
    pbc = es.enter_context(nc.psum_tensor("pbc", [128, 1024], F32))
    pu = [es.enter_context(nc.psum_tensor(f"pu{i}", [128, 512], F32)) for i in range(4)]

    seq.dma("sync", ones1[:], o1_in[:])
    seq.comp("vector", lambda v: v.memset(ones[:], 1.0))
    seq.comp("vector", lambda v: v.memset(cst[:, 0:1], 1e-5))

    for bi in range(4):
        t0 = bi * 512
        seq.dma("sync", xsb[:].rearrange("p (k x) -> p k x", x=512),
                xfm_in.rearrange("(k p) t -> p k t", p=128)[:, :, t0:t0 + 512])
        seq.comp("scalar", lambda s: s.activation(out=sq[:], in_=xsb[:], func=AF.Square))

        def st_fn(t):
            for k in range(16):
                t.matmul(pst0[:, :], ones[:, 0:1], xsb[:, k * 512:(k + 1) * 512],
                         start=(k == 0), stop=(k == 15))
            for k in range(16):
                inst = t.matmul(pst1[:, :], ones[:, 0:1], sq[:, k * 512:(k + 1) * 512],
                                start=(k == 0), stop=(k == 15))
            return inst
        seq.comp("tensor", st_fn)

        def row_fn(v):
            m = strow[0:1, 0:512]
            var = strow[0:1, 512:1024]
            v.tensor_scalar(m, pst0[:, :], 1.0 / D, None, Op.mult)
            v.tensor_tensor(var, m, m, Op.mult)
            return v.scalar_tensor_tensor(var, pst1[:, :], 1.0 / D, var, Op.mult, Op.subtract)
        seq.comp("vector", row_fn)

        def rstd_fn(s):
            vv = strow[0:1, 512:1024]
            s.activation(out=vv, in_=vv, func=AF.Ln, bias=cst[0:1, 0:1])
            return s.activation(out=strow[0:1, 1024:1536], in_=vv, func=AF.Exp, scale=-0.5)
        seq.comp("scalar", rstd_fn)
        seq.comp("vector", lambda v: v.tensor_tensor(
            strow[0:1, 0:512], strow[0:1, 0:512], strow[0:1, 1024:1536], Op.mult))

        def bc_fn(t):
            t.matmul(pbc[:, 0:512], ones1[:], strow[0:1, 1024:1536], start=True, stop=True)
            return t.matmul(pbc[:, 512:1024], ones1[:], strow[0:1, 0:512], start=True, stop=True)
        seq.comp("tensor", bc_fn)

        def ap_fn(v, t0=t0):
            for k in range(16):
                v.tensor_tensor(sq[:, k * 512:(k + 1) * 512], xsb[:, k * 512:(k + 1) * 512],
                                pbc[:, 0:512], Op.mult)
                inst = v.tensor_tensor(xnr[:, k * W + t0:k * W + t0 + 512],
                                       sq[:, k * 512:(k + 1) * 512], pbc[:, 512:1024], Op.subtract)
            return inst
        seq.comp("vector", ap_fn)
    seq.dma("scalar", xn_out.rearrange("(k p) t -> p k t", p=128),
            xnr[:].rearrange("p (k t) -> p k t", t=W))

    xn_done = len(seq.steps) - 1
    B1 = {"mm": [None, None], "evp": xn_done, "ob": [xn_done, xn_done],
          "st": [None, None], "lf": None, "ts": xn_done, "ti": 0}

    def proj(w_in_ap, dtile, emit):
        ti = B1["ti"]
        p = ti % 2
        wsbp = wsb2[p]
        ldid = seq.dma("sync", wsbp[:].rearrange("p (k m) -> p k m", m=128),
                       w_in_ap.rearrange("(k p) m -> p k m", p=128)[:, :, dtile * 128:(dtile + 1) * 128],
                       deps=[B1["mm"][p]], lane=f"ws{p}")

        def mm(t, wsbp=wsbp):
            for c in range(4):
                for k in range(16):
                    inst = t.matmul(pu[c][:, :], wsbp[:, k * 128:(k + 1) * 128],
                                    xnr[:, k * W + c * 512:k * W + (c + 1) * 512],
                                    start=(k == 0), stop=(k == 15))
            return inst
        mmid = seq.comp("tensor", mm, deps=[ldid, B1["evp"], xn_done])
        B1["mm"][p] = mmid
        emit(dtile, mmid, ti)
        B1["ti"] += 1

    for dt_ in range(16):
        def q_emit(dtile, mmid, ti):
            ob = obuf2[ti % 2]

            def ev(s, ob=ob):
                for c in range(4):
                    inst = s.activation(out=ob[:, c * 512:(c + 1) * 512], in_=pu[c][:, :], func=AF.Silu)
                return inst
            evid = seq.comp("scalar", ev, deps=[mmid, B1["st"][ti % 2]])
            B1["evp"] = evid
            B1["st"][ti % 2] = seq.dma("scalar", q_out[dtile * 128:(dtile + 1) * 128, :], ob[:],
                                       deps=[evid], lane=f"ob{ti % 2}")
        proj(wq_in, dt_, q_emit)

        def k_emit(dtile, mmid, ti):
            ob = obuf2[ti % 2]

            def ev(s):
                for c in range(4):
                    inst = s.activation(out=fsb[:, c * 512:(c + 1) * 512], in_=pu[c][:, :], func=AF.Sigmoid)
                return inst
            evid = seq.comp("scalar", ev, deps=[mmid, B1["ts"]])
            B1["evp"] = evid
            s2 = seq.comp("scalar", lambda s: s.activation(out=s2sb[:], in_=fsb[:], func=AF.Sigmoid),
                          deps=[evid])
            ts = seq.comp("vector", lambda v, ob=ob: v.tensor_scalar(
                ob[:], s2sb[:], -1.0, 1.0, Op.mult, Op.add), deps=[s2, B1["st"][ti % 2]])
            B1["ts"] = ts
            B1["st"][ti % 2] = seq.dma("scalar", k_out[dtile * 128:(dtile + 1) * 128, :], ob[:],
                                       deps=[ts], lane=f"ob{ti % 2}")
            ln = seq.comp("scalar", lambda s: s.activation(out=ofp16[:], in_=s2sb[:], func=AF.Ln),
                          deps=[s2, B1["lf"]])
            B1["lf"] = seq.dma("scalar", lf_out[dtile * 128:(dtile + 1) * 128, :], ofp16[:],
                               deps=[ln], lane="lf")
        proj(wk_in, dt_, k_emit)

        def v_emit(dtile, mmid, ti):
            ob = obuf2[ti % 2]

            def ev(s, ob=ob):
                for c in range(4):
                    inst = s.activation(out=ob[:, c * 512:(c + 1) * 512], in_=pu[c][:, :], func=AF.Identity)
                return inst
            evid = seq.comp("scalar", ev, deps=[mmid, B1["st"][ti % 2]])
            B1["evp"] = evid
            B1["st"][ti % 2] = seq.dma("scalar", v_out[dtile * 128:(dtile + 1) * 128, :], ob[:],
                                       deps=[evid], lane=f"ob{ti % 2}")
        proj(wv_in, dt_, v_emit)
    seq.barrier()

    es.close()


def build_b1():
    nc = bass.Bass("TRN2", target_bir_lowering=False, debug=False)
    es = ExitStack()
    W = TSH
    xfm_in = nc.dram_tensor("xfm", [D, W], F32, kind="ExternalInput").ap()
    wq_in = nc.dram_tensor("wqT", [D, D], BF16, kind="ExternalInput").ap()
    wk_in = nc.dram_tensor("wkT", [D, D], BF16, kind="ExternalInput").ap()
    wv_in = nc.dram_tensor("wvT", [D, D], BF16, kind="ExternalInput").ap()
    o1_in = nc.dram_tensor("ones1", [1, 128], F32, kind="ExternalInput").ap()
    xn_out = nc.dram_tensor("xn_o", [D, W], BF16, kind="ExternalOutput").ap()
    q_out = nc.dram_tensor("q_o", [D, W], BF16, kind="ExternalOutput").ap()
    k_out = nc.dram_tensor("k_o", [D, W], BF16, kind="ExternalOutput").ap()
    v_out = nc.dram_tensor("v_o", [D, W], BF16, kind="ExternalOutput").ap()
    lf_out = nc.dram_tensor("lf_o", [D, W], FP16, kind="ExternalOutput").ap()


    seq = Seq(nc, es, "b1seq")
    io = dict(xfm_in=xfm_in, wq_in=wq_in, wk_in=wk_in, wv_in=wv_in, o1_in=o1_in, xn_out=xn_out, q_out=q_out, k_out=k_out, v_out=v_out, lf_out=lf_out)
    emit_b1(nc, seq, io)
    seq.finish()
    block = es.enter_context(nc.Block())
    for en in ["sync", "vector", "scalar", "gpsimd", "tensor"]:
        getattr(block, en)(lambda e, en=en: seq.emit(en, e))
    es.close()
    return nc


def emit_b2(nc, seq, io):
    es = ExitStack()
    CH = 64
    NCH = T // CH
    E = 128
    q_in = io["q_in"]
    k_in = io["k_in"]
    ktm_in = io["ktm_in"]
    vtm_in = io["vtm_in"]
    lftm_in = io["lftm_in"]
    utri_in = io["utri_in"]
    mtri_in = io["mtri_in"]
    o_out = io["o_out"]
    sbuf = nc.sbuf_tensor
    qe = es.enter_context(sbuf("qe", [128, T], BF16))
    ke = es.enter_context(sbuf("ke", [128, T], BF16))
    ktm = es.enter_context(sbuf("ktm_s", [64, 8 * E], BF16))
    vtm = es.enter_context(sbuf("vtm_s", [64, 8 * E], BF16))
    lftm = es.enter_context(sbuf("lftm_s", [64, 8 * E], FP16))
    utri = es.enter_context(sbuf("utri_s", [CH, CH], FP16))
    mtri = es.enter_context(sbuf("mtri_s", [CH, CH], BF16))
    expp = es.enter_context(sbuf("expp", [128, CH], BF16))
    expn = es.enter_context(sbuf("expn", [128, CH], BF16))
    expntm = es.enter_context(sbuf("expntm", [64, E], BF16))
    qdec = es.enter_context(sbuf("qdec", [128, CH], BF16))
    kdec = es.enter_context(sbuf("kdec", [128, CH], BF16))
    kdtm = es.enter_context(sbuf("kdtm", [64, E], BF16))
    scst = es.enter_context(sbuf("scst", [64, CH], BF16))
    Ssb = es.enter_context(sbuf("Ssb", [128, E], BF16))
    Stmp = es.enter_context(sbuf("Stmp", [128, E], F32))
    eal = es.enter_context(sbuf("eal", [128, 2], F32))
    obuf = es.enter_context(sbuf("obuf", [64, 8 * E], BF16))
    pAc1 = es.enter_context(nc.psum_tensor("pAc1", [64, E], F32))
    pAc2 = es.enter_context(nc.psum_tensor("pAc2", [128, CH], F32))
    psc = es.enter_context(nc.psum_tensor("psc", [64, CH], F32))
    po = es.enter_context(nc.psum_tensor("po", [64, E], F32))
    pS = es.enter_context(nc.psum_tensor("pS", [128, E], F32))

    seq.dma("sync", utri[:], utri_in[:])
    seq.dma("sync", mtri[:], mtri_in[:])

    for pi in range(4):
        def pair_of(core_pi):
            return core_pi
        pass

    for pi in range(4):
        seq.dma("sync", qe[:], q_in[pi * 128:(pi + 1) * 128, 0:T])
        seq.dma("sync", ke[:], k_in[pi * 128:(pi + 1) * 128, 0:T])
        seq.comp("vector", lambda v: v.memset(Ssb[:], 0.0))
        for c8 in range(NCH // 8):
            rows = slice(pi * T + c8 * 8 * CH, pi * T + (c8 + 1) * 8 * CH)
            seq.dma("sync", ktm[:].rearrange("p (c e) -> p c e", e=E),
                    ktm_in[rows, :].rearrange("(c p) e -> p c e", p=64))
            seq.dma("sync", vtm[:].rearrange("p (c e) -> p c e", e=E),
                    vtm_in[rows, :].rearrange("(c p) e -> p c e", p=64))
            seq.dma("sync", lftm[:].rearrange("p (c e) -> p c e", e=E),
                    lftm_in[rows, :].rearrange("(c p) e -> p c e", p=64))
            for cs_ in range(8):
                c = c8 * 8 + cs_
                lfc = lftm[:, cs_ * E:(cs_ + 1) * E]

                def mmA(t, lfc=lfc):
                    t.matmul(pAc1[:, :], utri[:], lfc, start=True, stop=True)
                    return t.matmul(pAc2[:, :], lfc, utri[:], start=True, stop=True)
                seq.comp("tensor", mmA)

                def evA(s):
                    s.activation(out=expp[:, :], in_=pAc2[:, :], func=AF.Exp)
                    s.activation(out=expn[:, :], in_=pAc2[:, :], func=AF.Exp, scale=-1.0)
                    return s.activation(out=expntm[:, :], in_=pAc1[:, :], func=AF.Exp, scale=-1.0)
                seq.comp("scalar", evA)

                def dec(v, c=c, cs_=cs_):
                    v.tensor_copy(eal[:, 0:1], expp[:, CH - 1:CH])
                    v.tensor_tensor(qdec[:], qe[:, c * CH:(c + 1) * CH], expp[:], Op.mult)
                    v.tensor_tensor(kdec[:], ke[:, c * CH:(c + 1) * CH], expn[:], Op.mult)
                    return v.tensor_tensor(kdtm[:], ktm[:, cs_ * E:(cs_ + 1) * E], expntm[:], Op.mult)
                seq.comp("vector", dec)
                seq.comp("tensor", lambda t: t.matmul(psc[:, :], kdec[:], qdec[:], start=True, stop=True))
                seq.comp("vector", lambda v: v.tensor_tensor(scst[:], psc[:, :], mtri[:], Op.mult))

                def mmo(t, cs_=cs_):
                    t.matmul(po[:, :], scst[:], vtm[:, cs_ * E:(cs_ + 1) * E], start=True, stop=False)
                    t.matmul(po[:, :], qdec[:], Ssb[:], start=False, stop=True)
                    return t.matmul(pS[:, :], kdtm[:], vtm[:, cs_ * E:(cs_ + 1) * E], start=True, stop=True)
                seq.comp("tensor", mmo)

                def upd(v, cs_=cs_):
                    v.tensor_tensor(Stmp[:], Ssb[:], pS[:, :], Op.add)
                    v.tensor_scalar(Ssb[:], Stmp[:], eal[:, 0:1], None, Op.mult)
                    return v.tensor_copy(obuf[:, cs_ * E:(cs_ + 1) * E], po[:, :])
                seq.comp("vector", upd)
            seq.dma("scalar", o_out[pi, c8 * 8 * CH:(c8 + 1) * 8 * CH, :].rearrange("(c p) e -> p c e", p=64),
                    obuf[:].rearrange("p (c e) -> p c e", e=E))

    es.close()


def build_b2():
    nc = bass.Bass("TRN2", target_bir_lowering=False, debug=False)
    es = ExitStack()
    CH = 64
    NCH = T // CH
    E = 128
    q_in = nc.dram_tensor("qfm", [512, T], BF16, kind="ExternalInput").ap()
    k_in = nc.dram_tensor("kfm", [512, T], BF16, kind="ExternalInput").ap()
    ktm_in = nc.dram_tensor("ktm", [4 * T, 128], BF16, kind="ExternalInput").ap()
    vtm_in = nc.dram_tensor("vtm", [4 * T, 128], BF16, kind="ExternalInput").ap()
    lftm_in = nc.dram_tensor("lftm", [4 * T, 128], FP16, kind="ExternalInput").ap()
    utri_in = nc.dram_tensor("utri", [CH, CH], FP16, kind="ExternalInput").ap()
    mtri_in = nc.dram_tensor("mtri", [CH, CH], BF16, kind="ExternalInput").ap()
    o_out = nc.dram_tensor("o_o", [4, T, E], BF16, kind="ExternalOutput").ap()


    seq = Seq(nc, es, "b2seq")
    io = dict(q_in=q_in, k_in=k_in, ktm_in=ktm_in, vtm_in=vtm_in, lftm_in=lftm_in, utri_in=utri_in, mtri_in=mtri_in, o_out=o_out)
    emit_b2(nc, seq, io)
    seq.finish()
    block = es.enter_context(nc.Block())
    for en in ["sync", "vector", "scalar", "gpsimd", "tensor"]:
        getattr(block, en)(lambda e, en=en: seq.emit(en, e))
    es.close()
    return nc


def b2_host_inputs(q_fm, k_fm, k_tm, v_tm, lf_tm):
    utri = np.triu(np.ones((64, 64))).astype(np.float16)
    mtri = np.triu(np.ones((64, 64))).astype(bf16)
    maps = []
    for c in range(NCORES):
        qs, ks = [], []
        ktms, vtms, lftms = [], [], []
        for pi in range(4):
            p = 4 * c + pi
            b, h = p // 16, p % 16
            rows = slice(h * 128, (h + 1) * 128)
            cols = slice(b * T, (b + 1) * T)
            trows = slice(b * T, (b + 1) * T)
            tcols = slice(h * 128, (h + 1) * 128)
            qs.append(q_fm[rows, cols])
            ks.append(k_fm[rows, cols])
            ktms.append(k_tm[trows, tcols])
            vtms.append(v_tm[trows, tcols])
            lftms.append(lf_tm[trows, tcols])
        maps.append(dict(
            qfm=np.ascontiguousarray(np.concatenate(qs, 0)),
            kfm=np.ascontiguousarray(np.concatenate(ks, 0)),
            ktm=np.ascontiguousarray(np.concatenate(ktms, 0)),
            vtm=np.ascontiguousarray(np.concatenate(vtms, 0)),
            lftm=np.ascontiguousarray(np.concatenate(lftms, 0)),
            utri=utri, mtri=mtri))
    return maps


def emit_b2f(nc, seq, io):
    es = ExitStack()
    CH = 64
    NCH = T // CH
    E = 128
    W = TSH
    q_all = io["q_all"]
    k_all = io["k_all"]
    v_all = io["v_all"]
    lf_all = io["lf_all"]
    utri_in = io["utri_in"]
    mtri_in = io["mtri_in"]
    i128_in = io["i128_in"]
    o_out = io["o_out"]
    sbuf = nc.sbuf_tensor
    qe = es.enter_context(sbuf("qe", [128, T], BF16))
    ke = es.enter_context(sbuf("ke", [128, T], BF16))
    ve = es.enter_context(sbuf("ve", [128, T], BF16))
    lfe = es.enter_context(sbuf("lfe", [128, T], FP16))
    ofm = es.enter_context(sbuf("ofm", [128, T], BF16))
    lsc = es.enter_context(sbuf("lsc", [128, 512], F32))
    iden = es.enter_context(sbuf("iden_s", [128, 128], BF16))
    idf = es.enter_context(sbuf("idf_s", [128, 128], F32))
    ktm2 = [es.enter_context(sbuf(f"ktm_s{i}", [64, 8 * E], BF16)) for i in range(2)]
    vtm2 = [es.enter_context(sbuf(f"vtm_s{i}", [64, 8 * E], BF16)) for i in range(2)]
    lftm2 = [es.enter_context(sbuf(f"lftm_s{i}", [64, 8 * E], FP16)) for i in range(2)]
    utri = es.enter_context(sbuf("utri_s", [CH, CH], FP16))
    mtri = es.enter_context(sbuf("mtri_s", [CH, CH], BF16))
    expp = es.enter_context(sbuf("expp", [128, CH], BF16))
    expn = es.enter_context(sbuf("expn", [128, CH], BF16))
    expntm = es.enter_context(sbuf("expntm", [64, E], BF16))
    qdec = es.enter_context(sbuf("qdec", [128, CH], BF16))
    kdec = es.enter_context(sbuf("kdec", [128, CH], BF16))
    kdtm = es.enter_context(sbuf("kdtm", [64, E], BF16))
    scst = es.enter_context(sbuf("scst", [64, CH], BF16))
    Ssb = es.enter_context(sbuf("Ssb", [128, E], BF16))
    Stmp = es.enter_context(sbuf("Stmp", [128, E], F32))
    eal = es.enter_context(sbuf("eal", [128, 2], F32))
    obuf = es.enter_context(sbuf("obuf", [64, 8 * E], BF16))
    pAc1 = es.enter_context(nc.psum_tensor("pAc1", [64, E], F32))
    pAc2 = es.enter_context(nc.psum_tensor("pAc2", [128, CH], F32))
    psc = es.enter_context(nc.psum_tensor("psc", [64, CH], F32))
    po = es.enter_context(nc.psum_tensor("po", [64, E], F32))
    pS = es.enter_context(nc.psum_tensor("pS", [128, E], F32))
    ptK = es.enter_context(nc.psum_tensor("ptK", [64, 1024], BF16))
    ptL = es.enter_context(nc.psum_tensor("ptL", [64, 512], F32))
    ptO = es.enter_context(nc.psum_tensor("ptO", [128, 1024], BF16))

    seq.dma("sync", utri[:], utri_in[:])
    seq.dma("sync", mtri[:], mtri_in[:])
    seq.dma("sync", iden[:], i128_in[:])
    seq.comp("vector", lambda v: v.tensor_copy(idf[:], iden[:]))

    for pi in range(4):
        def dmas(c, pi=pi):
            h = (4 * c + pi) % 16
            L = []
            for jb in range(4):
                r0 = jb * D + h * 128
                cw = slice(jb * W, (jb + 1) * W)
                L.append((qe[:, cw], q_all[r0:r0 + 128, :]))
                L.append((ke[:, cw], k_all[r0:r0 + 128, :]))
                L.append((ve[:, cw], v_all[r0:r0 + 128, :]))
                L.append((lfe[:, cw], lf_all[r0:r0 + 128, :]))
            return L
        ldid = _sw_dmas(seq, "sync", 16, dmas)
        mz = seq.comp("vector", lambda v: v.memset(Ssb[:], 0.0), deps=[ldid])
        PR = {"ktmrd": mz, "vtmrd": mz, "lftmrd": mz, "ptl": mz,
              "kcp": mz, "vcp": mz, "lev": [mz, mz], "pto": mz, "ofmc": mz,
              "evA": mz, "dec": mz, "scst": mz, "mmo": mz, "upd": mz, "psc": mz,
              "mmA": mz}
        for c8 in range(NCH // 8):
            b0 = c8 * 512
            cb = c8 % 2
            ktm, vtm, lftm = ktm2[cb], vtm2[cb], lftm2[cb]

            def trk(t, b0=b0):
                for i in range(8):
                    inst = t.transpose(ptK[:, i * 128:(i + 1) * 128],
                                       ke[:, b0 + i * 64:b0 + (i + 1) * 64], iden[:])
                return inst
            tk = seq.comp("tensor", trk, deps=[ldid, PR["vcp"]])
            kc = seq.comp("vector", lambda v, ktm=ktm: v.tensor_copy(ktm[:], ptK[:]),
                          deps=[tk, PR["ktmrd"]])
            PR["kcp"] = kc

            def trv(t, b0=b0):
                for i in range(8):
                    inst = t.transpose(ptK[:, i * 128:(i + 1) * 128],
                                       ve[:, b0 + i * 64:b0 + (i + 1) * 64], iden[:])
                return inst
            tv = seq.comp("tensor", trv, deps=[kc])
            vc = seq.comp("vector", lambda v, vtm=vtm: v.tensor_copy(vtm[:], ptK[:]),
                          deps=[tv, PR["vtmrd"]])
            PR["vcp"] = vc
            lc = seq.comp("vector", lambda v, b0=b0: v.tensor_copy(lsc[:], lfe[:, b0:b0 + 512]),
                          deps=[ldid, PR["lev"][0], PR["lev"][1]])
            levs = []
            for hf in range(2):
                def trl(t, hf=hf):
                    for i in range(4):
                        ii = hf * 4 + i
                        inst = t.transpose(ptL[:, i * 128:(i + 1) * 128],
                                           lsc[:, ii * 64:(ii + 1) * 64], idf[:])
                    return inst
                tl = seq.comp("tensor", trl, deps=[lc, levs[0] if hf else PR["lev"][1]])
                lev = seq.comp("scalar", lambda s, hf=hf, lftm=lftm: s.activation(
                    out=lftm[:, hf * 512:(hf + 1) * 512], in_=ptL[:], func=AF.Identity),
                    deps=[tl, PR["lftmrd"]])
                levs.append(lev)
            PR["lev"] = levs
            tm_ready = [kc, vc] + levs
            for cs_ in range(8):
                c = c8 * 8 + cs_
                lfc = lftm[:, cs_ * E:(cs_ + 1) * E]

                def mmA(t, lfc=lfc):
                    t.matmul(pAc1[:, :], utri[:], lfc, start=True, stop=True)
                    return t.matmul(pAc2[:, :], lfc, utri[:], start=True, stop=True)
                ma = seq.comp("tensor", mmA, deps=tm_ready + [PR["evA"]])
                PR["mmA"] = ma

                def evA(s):
                    s.activation(out=expp[:, :], in_=pAc2[:, :], func=AF.Exp)
                    s.activation(out=expn[:, :], in_=pAc2[:, :], func=AF.Exp, scale=-1.0)
                    return s.activation(out=expntm[:, :], in_=pAc1[:, :], func=AF.Exp, scale=-1.0)
                ea = seq.comp("scalar", evA, deps=[ma, PR["dec"]])
                PR["evA"] = ea

                def dec(v, c=c, cs_=cs_, ktm=ktm):
                    v.tensor_copy(eal[:, 0:1], expp[:, CH - 1:CH])
                    v.tensor_tensor(qdec[:], qe[:, c * CH:(c + 1) * CH], expp[:], Op.mult)
                    v.tensor_tensor(kdec[:], ke[:, c * CH:(c + 1) * CH], expn[:], Op.mult)
                    return v.tensor_tensor(kdtm[:], ktm[:, cs_ * E:(cs_ + 1) * E], expntm[:], Op.mult)
                dc = seq.comp("vector", dec, deps=[ea, PR["mmo"]])
                PR["dec"] = dc
                ps = seq.comp("tensor", lambda t: t.matmul(psc[:, :], kdec[:], qdec[:],
                                                           start=True, stop=True),
                              deps=[dc, PR["scst"]])
                PR["psc"] = ps
                st = seq.comp("vector", lambda v: v.tensor_tensor(scst[:], psc[:, :], mtri[:], Op.mult),
                              deps=[ps, PR["mmo"]])
                PR["scst"] = st

                def mmo(t, cs_=cs_, vtm=vtm):
                    t.matmul(po[:, :], scst[:], vtm[:, cs_ * E:(cs_ + 1) * E], start=True, stop=False)
                    t.matmul(po[:, :], qdec[:], Ssb[:], start=False, stop=True)
                    return t.matmul(pS[:, :], kdtm[:], vtm[:, cs_ * E:(cs_ + 1) * E], start=True, stop=True)
                mo = seq.comp("tensor", mmo, deps=[st, PR["upd"]])
                PR["mmo"] = mo

                def upd(v, cs_=cs_):
                    v.tensor_tensor(Stmp[:], Ssb[:], pS[:, :], Op.add)
                    v.tensor_scalar(Ssb[:], Stmp[:], eal[:, 0:1], None, Op.mult)
                    return v.tensor_copy(obuf[:, cs_ * E:(cs_ + 1) * E], po[:, :])
                up = seq.comp("vector", upd, deps=[mo, PR["pto"]])
                PR["upd"] = up
            PR["ktmrd"] = PR["dec"]
            PR["vtmrd"] = PR["mmo"]
            PR["lftmrd"] = PR["mmA"]

            def tro(t):
                for i in range(8):
                    inst = t.transpose(ptO[:, i * 128:i * 128 + 64],
                                       obuf[:, i * 128:(i + 1) * 128], iden[0:64, 0:64])
                return inst
            to = seq.comp("tensor", tro, deps=[PR["upd"], PR["ofmc"]])
            PR["pto"] = to
            oc = seq.comp("vector", lambda v, b0=b0: v.tensor_copy(
                ofm[:, b0:b0 + 512].rearrange("p (i z) -> p i z", z=64),
                ptO[:].rearrange("p (i z) -> p i z", z=128)[:, :, 0:64]), deps=[to])
            PR["ofmc"] = oc
        seq.dma("scalar", o_out[pi * 128:(pi + 1) * 128, :], ofm[:], deps=[PR["ofmc"]], lane="b2o")
        seq.barrier()

    es.close()


def emit_b3(nc, seq, io):
    es = ExitStack()
    W = TSH
    H1 = 8192
    xfm_in = io["xfm_in"]
    xn_in = io["xn_in"]
    o_in = io.get("o_in")
    wo_in = io["wo_in"]
    nw_in = io["nw_in"]
    wfc_in = io.get("wfc_in")
    wpr_in = io.get("wpr_in")
    o1_in = io["o1_in"]
    xo_out = io["xo_out"]
    hscr = io["hscr"]
    wfc_src = io.get("wfc_src") or (
        lambda m0: wfc_in.rearrange("(k p) m -> p k m", p=128)[:, :, m0:m0 + 128])
    wpr_srcs = io.get("wpr_srcs") or (
        lambda k, g: [(0, 1024, wpr_in[k * 128:(k + 1) * 128, g * 1024:(g + 1) * 1024])])
    onr_fill = io.get("onr_fill")
    sbuf = nc.sbuf_tensor
    xnr = es.enter_context(sbuf("xnr", [128, 16 * W], BF16))
    onr = es.enter_context(sbuf("onr", [128, 16 * W], BF16))
    wg = es.enter_context(sbuf("wg", [128, 16 * 128], BF16))
    gsb = es.enter_context(sbuf("gsb", [128, 512], F32))
    hsb = es.enter_context(sbuf("hsb", [128, 2048], BF16))
    hq = es.enter_context(sbuf("hq", [128, 2048], F32))
    rstdB = es.enter_context(sbuf("rstdB", [128, 2048], F32))
    strow = es.enter_context(sbuf("strow", [1, 2048], F32))
    ones = es.enter_context(sbuf("ones", [128, 2], F32))
    ones1 = es.enter_context(sbuf("ones1_s", [1, 128], F32))
    cst = es.enter_context(sbuf("cst", [1, 4], F32))
    nw = es.enter_context(sbuf("nw_s", [128, 32], F32))
    kch = es.enter_context(sbuf("kch", [128, 1024], BF16))
    wch = es.enter_context(sbuf("wch", [128, 8 * 128], BF16))
    xres = es.enter_context(sbuf("xres", [128, 512], F32))
    ob = es.enter_context(sbuf("ob", [128, 512], F32))
    P = [es.enter_context(nc.psum_tensor(f"pp{i}", [128, 512], F32)) for i in range(8)]

    seq.dma("sync", ones1[:], o1_in[:])
    seq.dma("sync", nw[:].rearrange("p (h two) -> p h two", two=2),
            nw_in.rearrange("(h p) two -> p h two", p=128))
    seq.dma("sync", xnr[:].rearrange("p (k t) -> p k t", t=W),
            xn_in.rearrange("(k p) t -> p k t", p=128))
    if onr_fill is None:
        seq.dma("sync", onr[:].rearrange("p (k t) -> p k t", t=W),
                o_in.rearrange("(k p) t -> p k t", p=128))
    else:
        onr_fill(seq, onr)
    seq.comp("vector", lambda v: v.memset(ones[:], 1.0))
    seq.comp("vector", lambda v: v.memset(cst[:, 0:1], 1e-6))
    seq.comp("vector", lambda v: v.memset(cst[:, 2:3], 1.1920929e-7))

    for k in range(16):
        seq.comp("scalar", lambda s, k=k: s.activation(out=hq[:, :], in_=onr[:, k * W:(k + 1) * W],
                                                       func=AF.Square))
        def stm(t, k=k):
            for tcc in range(4):
                inst = t.matmul(P[2 + tcc][0:1, :], ones[:, 0:1], hq[:, tcc * 512:(tcc + 1) * 512],
                                start=(k == 0), stop=(k == 15))
            return inst
        seq.comp("tensor", stm)

    def rstd_fn(s):
        for tcc in range(4):
            s.activation(out=strow[0:1, tcc * 512:(tcc + 1) * 512], in_=P[2 + tcc][0:1, :],
                         func=AF.Ln, bias=cst[0:1, 0:1], scale=1.0 / D)
            inst = s.activation(out=strow[0:1, tcc * 512:(tcc + 1) * 512],
                                in_=strow[0:1, tcc * 512:(tcc + 1) * 512], func=AF.Exp, scale=-0.5)
        return inst
    seq.comp("scalar", rstd_fn)

    def bc_fn(t):
        for tcc in range(4):
            inst = t.matmul(P[tcc][:, :], ones1[:], strow[0:1, tcc * 512:(tcc + 1) * 512],
                            start=True, stop=True)
        return inst
    seq.comp("tensor", bc_fn)

    def bc_ev(v):
        for tcc in range(4):
            inst = v.tensor_copy(rstdB[:, tcc * 512:(tcc + 1) * 512], P[tcc][:, :])
        return inst
    seq.comp("vector", bc_ev)

    def apply_on(v):
        for k in range(16):
            v.tensor_tensor(onr[:, k * W:(k + 1) * W], onr[:, k * W:(k + 1) * W], rstdB[:, :], Op.mult)
            inst = v.tensor_scalar(onr[:, k * W:(k + 1) * W], onr[:, k * W:(k + 1) * W],
                                   nw[:, 2 * k:2 * k + 1], None, Op.mult)
        return inst
    seq.comp("vector", apply_on)

    NHT = H1 // 128
    for ht in range(NHT):
        seq.dma("sync", wg[:].rearrange("p (k m) -> p k m", m=128), wfc_src(ht * 128))
        for tc in range(4):
            def mmg(t, tc=tc):
                for k in range(16):
                    inst = t.matmul(P[0][:, :], wg[:, k * 128:(k + 1) * 128],
                                    xnr[:, k * W + tc * 512:k * W + (tc + 1) * 512],
                                    start=(k == 0), stop=(k == 15))
                return inst
            seq.comp("tensor", mmg)
            def ev(s, tc=tc):
                s.activation(out=gsb[:, :], in_=P[0][:, :], func=AF.Relu)
                return s.activation(out=hsb[:, tc * 512:(tc + 1) * 512], in_=gsb[:, :], func=AF.Square)
            seq.comp("scalar", ev)
        seq.comp("scalar", lambda s: s.activation(out=hq[:, :], in_=hsb[:, :], func=AF.Square))

        def stm(t, ht=ht):
            for tcc in range(4):
                inst = t.matmul(P[2 + tcc][0:1, :], ones[:, 0:1], hq[:, tcc * 512:(tcc + 1) * 512],
                                start=(ht == 0), stop=(ht == NHT - 1))
            return inst
        seq.comp("tensor", stm)
        seq.dma("scalar", hscr.ap()[ht * 128:(ht + 1) * 128, :], hsb[:, :])

    def rstd2_fn(s):
        for tcc in range(4):
            s.activation(out=strow[0:1, tcc * 512:(tcc + 1) * 512], in_=P[2 + tcc][0:1, :],
                         func=AF.Ln, bias=cst[0:1, 2:3], scale=1.0 / H1)
            inst = s.activation(out=strow[0:1, tcc * 512:(tcc + 1) * 512],
                                in_=strow[0:1, tcc * 512:(tcc + 1) * 512], func=AF.Exp, scale=-0.5)
        return inst
    seq.comp("scalar", rstd2_fn)
    seq.comp("tensor", bc_fn)
    seq.comp("vector", bc_ev)

    for tc in range(4):
        cs = slice(tc * 512, (tc + 1) * 512)
        for g in range(2):
            for k in range(NHT):
                seq.dma("sync", kch[:, 0:512], hscr.ap()[k * 128:(k + 1) * 128, cs])
                for c0w, wdw, srcw in wpr_srcs(k, g):
                    seq.dma("sync", wch[:, c0w:c0w + wdw], srcw)
                seq.comp("vector", lambda v, tc=tc: v.tensor_tensor(
                    kch[:, 512:1024], kch[:, 0:512], rstdB[:, tc * 512:(tc + 1) * 512], Op.mult))

                def mmd(t, k=k):
                    for j in range(8):
                        inst = t.matmul(P[j][:, :], wch[:, j * 128:(j + 1) * 128], kch[:, 512:1024],
                                        start=(k == 0), stop=False)
                    return inst
                seq.comp("tensor", mmd)
            for k in range(16):
                seq.dma("sync", wch[:], wo_in[k * 128:(k + 1) * 128, g * 1024:(g + 1) * 1024])

                def mma(t, k=k, tc=tc):
                    for j in range(8):
                        inst = t.matmul(P[j][:, :], wch[:, j * 128:(j + 1) * 128],
                                        onr[:, k * W + tc * 512:k * W + (tc + 1) * 512],
                                        start=False, stop=(k == 15))
                    return inst
                seq.comp("tensor", mma)
            for j in range(8):
                dout = g * 8 + j
                seq.dma("sync", xres[:, :], xfm_in.rearrange("(kk p) t -> p kk t", p=128)[:, dout, cs])
                seq.comp("vector", lambda v, j=j: v.tensor_tensor(ob[:, :], P[j][:, :], xres[:, :], Op.add))
                seq.dma("scalar", xo_out.rearrange("(kk p) t -> p kk t", p=128)[:, dout, cs], ob[:, :])

    es.close()


def build_b3():
    nc = bass.Bass("TRN2", target_bir_lowering=False, debug=False)
    es = ExitStack()
    W = TSH
    H1 = 8192
    xfm_in = nc.dram_tensor("xfm", [D, W], F32, kind="ExternalInput").ap()
    xn_in = nc.dram_tensor("xn", [D, W], BF16, kind="ExternalInput").ap()
    o_in = nc.dram_tensor("ofm", [D, W], BF16, kind="ExternalInput").ap()
    wo_in = nc.dram_tensor("woT", [D, D], BF16, kind="ExternalInput").ap()
    nw_in = nc.dram_tensor("nw", [D, 2], F32, kind="ExternalInput").ap()
    wfc_in = nc.dram_tensor("wfcT", [D, H1], BF16, kind="ExternalInput").ap()
    wpr_in = nc.dram_tensor("wprT", [H1, D], BF16, kind="ExternalInput").ap()
    o1_in = nc.dram_tensor("ones1", [1, 128], F32, kind="ExternalInput").ap()
    xo_out = nc.dram_tensor("xo_o", [D, W], F32, kind="ExternalOutput").ap()
    hscr = nc.dram_tensor("hscr", [H1, W], BF16)


    seq = Seq(nc, es, "b3seq")
    io = dict(xfm_in=xfm_in, xn_in=xn_in, o_in=o_in, wo_in=wo_in, nw_in=nw_in, wfc_in=wfc_in, wpr_in=wpr_in, o1_in=o1_in, xo_out=xo_out, hscr=hscr)
    emit_b3(nc, seq, io)
    seq.finish()
    block = es.enter_context(nc.Block())
    for en in ["sync", "vector", "scalar", "gpsimd", "tensor"]:
        getattr(block, en)(lambda e, en=en: seq.emit(en, e))
    es.close()
    return nc


def emit_c1(nc, seq, io):
    es = ExitStack()
    W = TSH
    DX = 2080
    xfm_in = io["xfm_in"]
    wx_in = io["wx_in"]
    o1_in = io["o1_in"]
    xn_out = io["xn_out"]
    xp_out = io["xp_out"]
    sbuf = nc.sbuf_tensor
    xsb = es.enter_context(sbuf("xsb", [128, 16 * 512], F32))
    sq = es.enter_context(sbuf("sq", [128, 16 * 512], F32))
    xnr = es.enter_context(sbuf("xnr", [128, 16 * W], BF16))
    strow = es.enter_context(sbuf("strow", [1, 3 * 512], F32))
    ones = es.enter_context(sbuf("ones", [128, 2], F32))
    ones1 = es.enter_context(sbuf("ones1_s", [1, 128], F32))
    cst = es.enter_context(sbuf("cst", [1, 2], F32))
    wsb = es.enter_context(sbuf("wsb", [128, 16 * 128], BF16))
    obuf = es.enter_context(sbuf("obuf", [128, W], BF16))
    pst0 = es.enter_context(nc.psum_tensor("pst0", [1, 512], F32))
    pst1 = es.enter_context(nc.psum_tensor("pst1", [1, 512], F32))
    pbc = es.enter_context(nc.psum_tensor("pbc", [128, 1024], F32))
    pu = [es.enter_context(nc.psum_tensor(f"pu{i}", [128, 512], F32)) for i in range(4)]

    seq.dma("sync", ones1[:], o1_in[:])
    seq.comp("vector", lambda v: v.memset(ones[:], 1.0))
    seq.comp("vector", lambda v: v.memset(cst[:, 0:1], 1e-5))
    for bi in range(4):
        t0 = bi * 512
        seq.dma("sync", xsb[:].rearrange("p (k x) -> p k x", x=512),
                xfm_in.rearrange("(k p) t -> p k t", p=128)[:, :, t0:t0 + 512])
        seq.comp("scalar", lambda s: s.activation(out=sq[:], in_=xsb[:], func=AF.Square))

        def st_fn(t):
            for k in range(16):
                t.matmul(pst0[:, :], ones[:, 0:1], xsb[:, k * 512:(k + 1) * 512],
                         start=(k == 0), stop=(k == 15))
            for k in range(16):
                inst = t.matmul(pst1[:, :], ones[:, 0:1], sq[:, k * 512:(k + 1) * 512],
                                start=(k == 0), stop=(k == 15))
            return inst
        seq.comp("tensor", st_fn)

        def row_fn(v):
            m = strow[0:1, 0:512]
            var = strow[0:1, 512:1024]
            v.tensor_scalar(m, pst0[:, :], 1.0 / D, None, Op.mult)
            v.tensor_tensor(var, m, m, Op.mult)
            return v.scalar_tensor_tensor(var, pst1[:, :], 1.0 / D, var, Op.mult, Op.subtract)
        seq.comp("vector", row_fn)

        def rstd_fn(s):
            vv = strow[0:1, 512:1024]
            s.activation(out=vv, in_=vv, func=AF.Ln, bias=cst[0:1, 0:1])
            return s.activation(out=strow[0:1, 1024:1536], in_=vv, func=AF.Exp, scale=-0.5)
        seq.comp("scalar", rstd_fn)
        seq.comp("vector", lambda v: v.tensor_tensor(
            strow[0:1, 0:512], strow[0:1, 0:512], strow[0:1, 1024:1536], Op.mult))

        def bc_fn(t):
            t.matmul(pbc[:, 0:512], ones1[:], strow[0:1, 1024:1536], start=True, stop=True)
            return t.matmul(pbc[:, 512:1024], ones1[:], strow[0:1, 0:512], start=True, stop=True)
        seq.comp("tensor", bc_fn)

        def ap_fn(v, t0=t0):
            for k in range(16):
                v.tensor_tensor(sq[:, k * 512:(k + 1) * 512], xsb[:, k * 512:(k + 1) * 512],
                                pbc[:, 0:512], Op.mult)
                inst = v.tensor_tensor(xnr[:, k * W + t0:k * W + t0 + 512],
                                       sq[:, k * 512:(k + 1) * 512], pbc[:, 512:1024], Op.subtract)
            return inst
        seq.comp("vector", ap_fn)
    seq.dma("scalar", xn_out.rearrange("(k p) t -> p k t", p=128),
            xnr[:].rearrange("p (k t) -> p k t", t=W))

    for dt_ in range(17):
        p = 128 if dt_ < 16 else 32
        seq.dma("sync", wsb[:, :16 * p].rearrange("p2 (k m) -> p2 k m", m=p),
                wx_in.rearrange("(k p2) m -> p2 k m", p2=128)[:, :, dt_ * 128:dt_ * 128 + p])

        def mm(t, p=p):
            for c in range(4):
                for k in range(16):
                    inst = t.matmul(pu[c][:p, :], wsb[:, k * p:(k + 1) * p],
                                    xnr[:, k * W + c * 512:k * W + (c + 1) * 512],
                                    start=(k == 0), stop=(k == 15))
            return inst
        seq.comp("tensor", mm)

        def ev(s, p=p):
            for c in range(4):
                inst = s.activation(out=obuf[:p, c * 512:(c + 1) * 512], in_=pu[c][:p, :], func=AF.Identity)
            return inst
        seq.comp("scalar", ev)
        seq.dma("scalar", xp_out[dt_ * 128:dt_ * 128 + p, :], obuf[:p, :])

    es.close()


def build_c1():
    nc = bass.Bass("TRN2", target_bir_lowering=False, debug=False)
    es = ExitStack()
    W = TSH
    DX = 2080
    xfm_in = nc.dram_tensor("xfm", [D, W], F32, kind="ExternalInput").ap()
    wx_in = nc.dram_tensor("wxT", [D, DX], BF16, kind="ExternalInput").ap()
    o1_in = nc.dram_tensor("ones1", [1, 128], F32, kind="ExternalInput").ap()
    xn_out = nc.dram_tensor("xn_o", [D, W], BF16, kind="ExternalOutput").ap()
    xp_out = nc.dram_tensor("xp_o", [DX, W], BF16, kind="ExternalOutput").ap()


    seq = Seq(nc, es, "c1seq")
    io = dict(xfm_in=xfm_in, wx_in=wx_in, o1_in=o1_in, xn_out=xn_out, xp_out=xp_out)
    emit_c1(nc, seq, io)
    seq.finish()
    block = es.enter_context(nc.Block())
    for en in ["sync", "vector", "scalar", "gpsimd", "tensor"]:
        getattr(block, en)(lambda e, en=en: seq.emit(en, e))
    es.close()
    return nc


def emit_c2(nc, seq, io):
    es = ExitStack()
    CS = 256
    NST = 16
    xpd_in = io["xpd_in"]
    xnb_in = io["xnb_in"]
    bc_in = io["bc_in"]
    alog_in = io["alog_in"]
    o1_in = io["o1_in"]
    y_out = io["y_out"]
    sbuf = nc.sbuf_tensor
    dt_ = es.enter_context(sbuf("dt_s", [128, T], F32))
    dtx = es.enter_context(sbuf("dtx", [128, T], F32))
    asb = es.enter_context(sbuf("asb", [128, T], BF16))
    bsb = es.enter_context(sbuf("bsb", [128, T], BF16))
    hsb = es.enter_context(sbuf("hsb", [128, T], BF16))
    ysb = es.enter_context(sbuf("ysb", [128, T], F32))
    Bb = es.enter_context(sbuf("Bb", [128, T], BF16))
    Cb = Bb
    brow1 = es.enter_context(sbuf("brow1", [1, T], BF16))
    crow1 = brow1
    amat = es.enter_context(sbuf("amat", [128, 64], F32))
    ones1 = es.enter_context(sbuf("ones1_s", [1, 128], BF16))
    o1f = es.enter_context(sbuf("o1f", [1, 128], F32))
    pb = [es.enter_context(nc.psum_tensor(f"pb{i}", [128, 512], F32)) for i in range(4)]

    seq.dma("sync", o1f[:], o1_in[:])
    seq.comp("vector", lambda v: v.tensor_copy(ones1[:], o1f[:]))
    seq.dma("sync", amat[:], alog_in[:])
    seq.comp("scalar", lambda s: s.activation(out=amat[:], in_=amat[:], func=AF.Exp))
    seq.comp("vector", lambda v: v.tensor_scalar(amat[:], amat[:], -1.0, None, Op.mult))

    def bcast_row(row_idx, dst):
        rsrc = brow1 if row_idx < 16 else crow1
        seq.dma("sync", rsrc[:], bc_in[row_idx:row_idx + 1, CUR_COLS[0]])
        for half in range(4):
            def mm(t, rsrc=rsrc, half=half):
                for c in range(4):
                    c0 = half * 2048 + c * 512
                    inst = t.matmul(pb[c][:, :], ones1[:], rsrc[0:1, c0:c0 + 512],
                                    start=True, stop=True)
                return inst
            seq.comp("tensor", mm)

            def ev(s, dst=dst, half=half):
                for c in range(4):
                    c0 = half * 2048 + c * 512
                    inst = s.activation(out=dst[:, c0:c0 + 512], in_=pb[c][:, :], func=AF.Identity)
                return inst
            seq.comp("scalar", ev)

    CUR_COLS = [None]
    for b in range(B):
        cols = slice(b * T, (b + 1) * T)
        CUR_COLS[0] = cols
        for ct in range(2):
            rows = slice(ct * 128, (ct + 1) * 128)
            seq.dma("sync", bsb[:], xpd_in[rows, cols])
            seq.dma("sync", hsb[:], xnb_in[rows, cols])

            def sp(v):
                v.tensor_copy(dt_[:], bsb[:])
                v.tensor_scalar(ysb[:], dt_[:], -1.0, None, Op.mult)
                v.tensor_tensor(ysb[:], ysb[:], dt_[:], Op.max)
                return v.tensor_scalar(dtx[:], dt_[:], 0.0, None, Op.max)
            seq.comp("vector", sp)

            def sp2(s):
                s.activation(out=ysb[:], in_=ysb[:], func=AF.Exp, scale=-1.0)
                return s.activation(out=ysb[:], in_=ysb[:], func=AF.Ln, bias=1.0)
            seq.comp("scalar", sp2)
            seq.comp("vector", lambda v: v.tensor_tensor(dt_[:], dtx[:], ysb[:], Op.add))
            seq.comp("vector", lambda v: v.tensor_copy(ysb[:], hsb[:]))
            seq.comp("vector", lambda v: v.tensor_tensor(dtx[:], dt_[:], ysb[:], Op.mult))
            seq.comp("vector", lambda v: v.memset(ysb[:], 0.0))
            for n in range(NST):
                bcast_row(n, Bb)
                seq.comp("scalar", lambda s, ct=ct, n=n: s.activation(
                    out=asb[:], in_=dt_[:], func=AF.Exp, scale=amat[:, ct * 32 + 2 * n:ct * 32 + 2 * n + 1]))
                seq.comp("vector", lambda v: v.tensor_tensor(bsb[:], dtx[:], Bb[:], Op.mult))
                bcast_row(16 + n, Cb)
                seq.comp("vector", lambda v: v.tensor_tensor_scan(hsb[:], asb[:], bsb[:], 0.0, Op.mult, Op.add))
                seq.comp("gpsimd", lambda g: g.tensor_tensor(bsb[:], hsb[:], Cb[:], Op.mult))
                seq.comp("vector", lambda v: v.tensor_tensor(ysb[:], ysb[:], bsb[:], Op.add))
            seq.dma("scalar", y_out[rows, cols], ysb[:])

    es.close()


def build_c2():
    nc = bass.Bass("TRN2", target_bir_lowering=False, debug=False)
    es = ExitStack()
    CS = 256
    NST = 16
    xpd_in = nc.dram_tensor("xpd", [CS, TOK], BF16, kind="ExternalInput").ap()
    xnb_in = nc.dram_tensor("xnb", [CS, TOK], BF16, kind="ExternalInput").ap()
    bc_in = nc.dram_tensor("bcr", [32, TOK], BF16, kind="ExternalInput").ap()
    alog_in = nc.dram_tensor("alog", [128, 64], F32, kind="ExternalInput").ap()
    o1_in = nc.dram_tensor("ones1", [1, 128], F32, kind="ExternalInput").ap()
    y_out = nc.dram_tensor("y_o", [CS, TOK], F32, kind="ExternalOutput").ap()


    seq = Seq(nc, es, "c2seq")
    io = dict(xpd_in=xpd_in, xnb_in=xnb_in, bc_in=bc_in, alog_in=alog_in, o1_in=o1_in, y_out=y_out)
    emit_c2(nc, seq, io)
    seq.finish()
    block = es.enter_context(nc.Block())
    for en in ["sync", "vector", "scalar", "gpsimd", "tensor"]:
        getattr(block, en)(lambda e, en=en: seq.emit(en, e))
    es.close()
    return nc


def emit_c2p(nc, seq, io):
    es = ExitStack()
    NST = 16
    xpd_in = io["xpd_in"]
    xnb_in = io["xnb_in"]
    bc_in = io["bc_in"]
    alog_in = io["alog_in"]
    y_out = io["y_out"]
    sbuf = nc.sbuf_tensor
    dt_ = es.enter_context(sbuf("dt_s", [128, T], FP16))
    dtx = es.enter_context(sbuf("dtx", [128, T], BF16))
    ysb = es.enter_context(sbuf("ysb", [128, T], F32))
    asb = es.enter_context(sbuf("asb", [128, T], BF16))
    bsb = es.enter_context(sbuf("bsb", [128, T], BF16))
    hsb = es.enter_context(sbuf("hsb", [128, T], BF16))
    prod = es.enter_context(sbuf("prod", [128, T], BF16))
    BC4 = es.enter_context(sbuf("BC4", [128, 4 * T], BF16))
    Bb = [BC4[:, 0:T], BC4[:, T:2 * T]]
    Cb = [BC4[:, 2 * T:3 * T], BC4[:, 3 * T:4 * T]]
    scr = BC4[:].bitcast(F32)[:, 0:T]
    amat = es.enter_context(sbuf("amat", [128, 64], F32))

    seq.dma("sync", amat[:], alog_in[:])
    seq.comp("scalar", lambda s: s.activation(out=amat[:], in_=amat[:], func=AF.Exp))
    seq.comp("vector", lambda v: v.tensor_scalar(amat[:], amat[:], -1.0, None, Op.mult))

    for b in range(B):
        cols = slice(b * T, (b + 1) * T)
        for ct in range(2):
            seq.barrier()
            rows = slice(ct * 128, (ct + 1) * 128)
            ld1 = seq.dma("sync", bsb[:], xpd_in[rows, cols], lane="c2a")
            ld2 = seq.dma("sync", hsb[:], xnb_in[rows, cols], lane="c2b")

            def sp(v):
                v.tensor_copy(dt_[:], bsb[:])
                v.tensor_scalar(scr[:], dt_[:], -1.0, None, Op.mult)
                v.tensor_tensor(scr[:], scr[:], dt_[:], Op.max)
                return v.tensor_scalar(ysb[:], dt_[:], 0.0, None, Op.max)
            i1 = seq.comp("vector", sp, deps=[ld1])

            def sp2(s):
                s.activation(out=scr[:], in_=scr[:], func=AF.Exp, scale=-1.0)
                return s.activation(out=scr[:], in_=scr[:], func=AF.Ln, bias=1.0)
            i2 = seq.comp("scalar", sp2, deps=[i1])
            i3 = seq.comp("vector", lambda v: v.tensor_tensor(dt_[:], ysb[:], scr[:], Op.add),
                          deps=[i2])
            i4 = seq.comp("vector", lambda v: v.tensor_copy(scr[:], hsb[:]), deps=[ld2, i3])
            i5 = seq.comp("vector", lambda v: v.tensor_tensor(dtx[:], dt_[:], scr[:], Op.mult),
                          deps=[i4])
            sp_done = seq.comp("vector", lambda v: v.memset(ysb[:], 0.0), deps=[i5])

            bs_k = {}
            sc_k = {}
            pr_k = {}
            ad_k = {}
            ex_k = {}
            for n in range(NST):
                bi = n % 2
                lB = seq.dma("sync", Bb[bi], bc_in[n, cols].partition_broadcast(128),
                             deps=[bs_k.get(n - 2), sp_done if n < 2 else None], lane=f"bb{bi}")
                lC = seq.dma("sync", Cb[bi], bc_in[16 + n, cols].partition_broadcast(128),
                             deps=[pr_k.get(n - 2), sp_done if n < 2 else None], lane=f"cc{bi}")
                ex_k[n] = seq.comp("scalar", lambda s, ct=ct, n=n: s.activation(
                    out=asb[:], in_=dt_[:], func=AF.Exp,
                    scale=amat[:, ct * 32 + 2 * n:ct * 32 + 2 * n + 1]),
                    deps=[sc_k.get(n - 1), sp_done])
                bs_k[n] = seq.comp("vector", lambda v, bi=bi: v.tensor_tensor(
                    bsb[:], dtx[:], Bb[bi], Op.mult), deps=[lB, sp_done])
                sc_k[n] = seq.comp("vector", lambda v: v.tensor_tensor_scan(
                    hsb[:], asb[:], bsb[:], 0.0, Op.mult, Op.add),
                    deps=[ex_k[n], pr_k.get(n - 1)])
                pr_k[n] = seq.comp("gpsimd", lambda g, bi=bi: g.tensor_tensor(
                    prod[:], hsb[:], Cb[bi], Op.mult), deps=[sc_k[n], lC])
                ad_k[n] = seq.comp("gpsimd", lambda g: g.tensor_tensor(
                    ysb[:], ysb[:], prod[:], Op.add), deps=[pr_k[n]])
            seq.dma("scalar", y_out[rows, cols], ysb[:], deps=[ad_k[NST - 1]], lane="c2y")
    seq.barrier()
    es.close()


def emit_c3(nc, seq, io):
    es = ExitStack()
    W = TSH
    H2 = 5120
    xfm_in = io["xfm_in"]
    xn_in = io["xn_in"]
    y_in = io["y_in"]
    dv_in = io["dv_in"]
    wfc_in = io.get("wfc_in")
    wpr_in = io.get("wpr_in")
    xo_out = io["xo_out"]
    hscr = io["hscr"]
    wfc_src = io.get("wfc_src") or (
        lambda m0: wfc_in.rearrange("(k p) m -> p k m", p=128)[:, :, m0:m0 + 128])
    wpr_srcs = io.get("wpr_srcs") or (
        lambda k, g: [(0, 1024, wpr_in[k * 128:(k + 1) * 128, g * 1024:(g + 1) * 1024])])
    sbuf = nc.sbuf_tensor
    xnr = es.enter_context(sbuf("xnr", [128, 16 * W], BF16))
    wg = es.enter_context(sbuf("wg", [128, 16 * 128], BF16))
    gsb = es.enter_context(sbuf("gsb", [128, 512], F32))
    hsb = es.enter_context(sbuf("hsb", [128, 2048], BF16))
    dv = es.enter_context(sbuf("dv_s", [128, 32], F32))
    kch = es.enter_context(sbuf("kch", [128, 512], BF16))
    wch = es.enter_context(sbuf("wch", [128, 8 * 128], BF16))
    xres = es.enter_context(sbuf("xres", [128, 512], F32))
    ych = es.enter_context(sbuf("ych", [128, 512], F32))
    ob = es.enter_context(sbuf("ob", [128, 512], F32))
    P = [es.enter_context(nc.psum_tensor(f"pp{i}", [128, 512], F32)) for i in range(8)]

    seq.dma("sync", dv[:].rearrange("p (h two) -> p h two", two=2),
            dv_in.rearrange("(h p) two -> p h two", p=128))
    seq.dma("sync", xnr[:].rearrange("p (k t) -> p k t", t=W),
            xn_in.rearrange("(k p) t -> p k t", p=128))

    NHT = H2 // 128
    for ht in range(NHT):
        seq.dma("sync", wg[:].rearrange("p (k m) -> p k m", m=128), wfc_src(ht * 128))
        for tc in range(4):
            def mmg(t, tc=tc):
                for k in range(16):
                    inst = t.matmul(P[tc % 2][:, :], wg[:, k * 128:(k + 1) * 128],
                                    xnr[:, k * W + tc * 512:k * W + (tc + 1) * 512],
                                    start=(k == 0), stop=(k == 15))
                return inst
            seq.comp("tensor", mmg)

            def ev(s, tc=tc):
                s.activation(out=gsb[:, :], in_=P[tc % 2][:, :], func=AF.Relu)
                return s.activation(out=hsb[:, tc * 512:(tc + 1) * 512], in_=gsb[:, :], func=AF.Square)
            seq.comp("scalar", ev)
        seq.dma("scalar", hscr.ap()[ht * 128:(ht + 1) * 128, :], hsb[:, :])

    for tc in range(4):
        cs = slice(tc * 512, (tc + 1) * 512)
        for g in range(2):
            for k in range(NHT):
                seq.dma("sync", kch[:, :], hscr.ap()[k * 128:(k + 1) * 128, cs])
                for c0w, wdw, srcw in wpr_srcs(k, g):
                    seq.dma("sync", wch[:, c0w:c0w + wdw], srcw)

                def mmd(t, k=k):
                    for j in range(8):
                        inst = t.matmul(P[j][:, :], wch[:, j * 128:(j + 1) * 128], kch[:, :],
                                        start=(k == 0), stop=(k == NHT - 1))
                    return inst
                seq.comp("tensor", mmd)
            for j in range(8):
                dout = g * 8 + j
                seq.dma("sync", xres[:, :], xfm_in.rearrange("(kk p) t -> p kk t", p=128)[:, dout, cs])
                seq.dma("sync", ych[:, :], y_in.rearrange("(kk p) t -> p kk t", p=128)[:, dout, cs])

                def evd(v, j=j, dout=dout, tc=tc):
                    v.scalar_tensor_tensor(ych[:, :], xnr[:, dout * W + tc * 512:dout * W + (tc + 1) * 512],
                                           dv[:, 2 * dout:2 * dout + 1], ych[:, :], Op.mult, Op.add)
                    v.tensor_tensor(ob[:, :], P[j][:, :], xres[:, :], Op.add)
                    return v.tensor_tensor(ob[:, :], ob[:, :], ych[:, :], Op.add)
                seq.comp("vector", evd)
                seq.dma("scalar", xo_out.rearrange("(kk p) t -> p kk t", p=128)[:, dout, cs], ob[:, :])

    es.close()


def build_c3():
    nc = bass.Bass("TRN2", target_bir_lowering=False, debug=False)
    es = ExitStack()
    W = TSH
    H2 = 5120
    xfm_in = nc.dram_tensor("xfm", [D, W], F32, kind="ExternalInput").ap()
    xn_in = nc.dram_tensor("xn", [D, W], BF16, kind="ExternalInput").ap()
    y_in = nc.dram_tensor("yssm", [D, W], F32, kind="ExternalInput").ap()
    dv_in = nc.dram_tensor("dv", [D, 2], F32, kind="ExternalInput").ap()
    wfc_in = nc.dram_tensor("wfcT", [D, H2], BF16, kind="ExternalInput").ap()
    wpr_in = nc.dram_tensor("wprT", [H2, D], BF16, kind="ExternalInput").ap()
    xo_out = nc.dram_tensor("xo_o", [D, W], F32, kind="ExternalOutput").ap()
    hscr = nc.dram_tensor("hscr", [H2, W], BF16)


    seq = Seq(nc, es, "c3seq")
    io = dict(xfm_in=xfm_in, xn_in=xn_in, y_in=y_in, dv_in=dv_in, wfc_in=wfc_in, wpr_in=wpr_in, xo_out=xo_out, hscr=hscr)
    emit_c3(nc, seq, io)
    seq.finish()
    block = es.enter_context(nc.Block())
    for en in ["sync", "vector", "scalar", "gpsimd", "tensor"]:
        getattr(block, en)(lambda e, en=en: seq.emit(en, e))
    es.close()
    return nc


def emit_d1(nc, seq, io):
    es = ExitStack()
    W = TSH
    xfm_in = io["xfm_in"]
    wq_in = io["wq_in"]
    wk_in = io["wk_in"]
    wv_in = io["wv_in"]
    cosr_in = io["cosr_in"]
    sinr_in = io["sinr_in"]
    o1_in = io["o1_in"]
    xn_out = io["xn_out"]
    q_out = io["q_out"]
    k_out = io["k_out"]
    v_out = io["v_out"]
    sbuf = nc.sbuf_tensor
    xsb = es.enter_context(sbuf("xsb", [128, 16 * 512], F32))
    sq = es.enter_context(sbuf("sq", [128, 16 * 512], F32))
    xnr = es.enter_context(sbuf("xnr", [128, 16 * W], BF16))
    strow = es.enter_context(sbuf("strow", [1, 3 * 512], F32))
    ones = es.enter_context(sbuf("ones", [128, 2], F32))
    ones1 = es.enter_context(sbuf("ones1_s", [1, 128], F32))
    cst = es.enter_context(sbuf("cst", [1, 2], F32))
    wsb2 = [es.enter_context(sbuf(f"wsb{i}", [128, 8192], BF16)) for i in range(2)]
    qtm2 = [es.enter_context(sbuf(f"qtm{i}", [128, 2048], BF16)) for i in range(2)]
    qrl = es.enter_context(sbuf("qrl", [128, 2048], BF16))
    tcos = es.enter_context(sbuf("tcos", [128, 2048], BF16))
    tsin = es.enter_context(sbuf("tsin", [128, 2048], BF16))
    t1s = es.enter_context(sbuf("t1s", [128, 2048], BF16))
    pst0 = es.enter_context(nc.psum_tensor("pst0", [1, 512], F32))
    pst1 = es.enter_context(nc.psum_tensor("pst1", [1, 512], F32))
    pbc = es.enter_context(nc.psum_tensor("pbc", [128, 1024], F32))
    pu = [es.enter_context(nc.psum_tensor(f"pu{i}", [128, 512], F32)) for i in range(4)]

    seq.dma("sync", ones1[:], o1_in[:])
    seq.comp("vector", lambda v: v.memset(ones[:], 1.0))
    seq.comp("vector", lambda v: v.memset(cst[:, 0:1], 1e-5))
    for bi in range(4):
        t0 = bi * 512
        seq.dma("sync", xsb[:].rearrange("p (k x) -> p k x", x=512),
                xfm_in.rearrange("(k p) t -> p k t", p=128)[:, :, t0:t0 + 512])
        seq.comp("scalar", lambda s: s.activation(out=sq[:], in_=xsb[:], func=AF.Square))

        def st_fn(t):
            for k in range(16):
                t.matmul(pst0[:, :], ones[:, 0:1], xsb[:, k * 512:(k + 1) * 512],
                         start=(k == 0), stop=(k == 15))
            for k in range(16):
                inst = t.matmul(pst1[:, :], ones[:, 0:1], sq[:, k * 512:(k + 1) * 512],
                                start=(k == 0), stop=(k == 15))
            return inst
        seq.comp("tensor", st_fn)

        def row_fn(v):
            m = strow[0:1, 0:512]
            var = strow[0:1, 512:1024]
            v.tensor_scalar(m, pst0[:, :], 1.0 / D, None, Op.mult)
            v.tensor_tensor(var, m, m, Op.mult)
            return v.scalar_tensor_tensor(var, pst1[:, :], 1.0 / D, var, Op.mult, Op.subtract)
        seq.comp("vector", row_fn)

        def rstd_fn(s):
            vv = strow[0:1, 512:1024]
            s.activation(out=vv, in_=vv, func=AF.Ln, bias=cst[0:1, 0:1])
            return s.activation(out=strow[0:1, 1024:1536], in_=vv, func=AF.Exp, scale=-0.5)
        seq.comp("scalar", rstd_fn)
        seq.comp("vector", lambda v: v.tensor_tensor(
            strow[0:1, 0:512], strow[0:1, 0:512], strow[0:1, 1024:1536], Op.mult))

        def bc_fn(t):
            t.matmul(pbc[:, 0:512], ones1[:], strow[0:1, 1024:1536], start=True, stop=True)
            return t.matmul(pbc[:, 512:1024], ones1[:], strow[0:1, 0:512], start=True, stop=True)
        seq.comp("tensor", bc_fn)

        def ap_fn(v, t0=t0):
            for k in range(16):
                v.tensor_tensor(sq[:, k * 512:(k + 1) * 512], xsb[:, k * 512:(k + 1) * 512],
                                pbc[:, 0:512], Op.mult)
                inst = v.tensor_tensor(xnr[:, k * W + t0:k * W + t0 + 512],
                                       sq[:, k * 512:(k + 1) * 512], pbc[:, 512:1024], Op.subtract)
            return inst
        seq.comp("vector", ap_fn)
    seq.dma("scalar", xn_out.rearrange("(k p) t -> p k t", p=128),
            xnr[:].rearrange("p (k t) -> p k t", t=W))

    xn_done = len(seq.steps) - 1
    D1S = {"mm": [None, None], "evp": {}, "st": [None, None], "rp": xn_done,
           "lc": [None, None], "ti": 0}

    def qkv3(w_ap, OD, out_base_col, out_ap, rope):
        NOC = OD // 512
        for oc in range(NOC):
            wp = D1S["ti"] % 2
            wsbp = wsb2[wp]
            ldid = seq.dma("sync", wsbp[:].rearrange("p (k m) -> p k m", m=512),
                           w_ap.rearrange("(k p) m -> p k m", p=128)[:, :, oc * 512:(oc + 1) * 512],
                           deps=[D1S["mm"][wp]], lane=f"wq{wp}")
            D1S["ti"] += 1
            for tt in range(16):
                bk = tt % 4
                qp = tt % 2
                qtmp = qtm2[qp]

                def mm(t, tt=tt, bk=bk, wsbp=wsbp):
                    for kk in range(16):
                        inst = t.matmul(pu[bk][:, :], xnr[:, kk * W + tt * 128:kk * W + (tt + 1) * 128],
                                        wsbp[:, kk * 512:(kk + 1) * 512],
                                        start=(kk == 0), stop=(kk == 15))
                    return inst
                mmid = seq.comp("tensor", mm, deps=[ldid, D1S["evp"].get(bk), xn_done])
                if tt == 15:
                    D1S["mm"][wp] = mmid
                evid = seq.comp("scalar", lambda s, bk=bk, qtmp=qtmp: s.activation(
                    out=qtmp[:, 0:512], in_=pu[bk][:, :], func=AF.Identity),
                    deps=[mmid, D1S["st"][qp], D1S["rp"] if rope else None])
                D1S["evp"][bk] = evid
                last = evid
                if rope:
                    lc = seq.dma("sync", tcos[:, 0:512],
                                 cosr_in[tt * 128:(tt + 1) * 128, out_base_col + oc * 512:out_base_col + (oc + 1) * 512],
                                 deps=[D1S["rp"]], lane="rc")
                    ls = seq.dma("sync", tsin[:, 0:512],
                                 sinr_in[tt * 128:(tt + 1) * 128, out_base_col + oc * 512:out_base_col + (oc + 1) * 512],
                                 deps=[D1S["rp"]], lane="rs")

                    def rope_fn(v, qtmp=qtmp):
                        q3 = qtmp[:, 0:512].rearrange("p (h e) -> p h e", e=128)
                        r3 = qrl[:, 0:512].rearrange("p (h e) -> p h e", e=128)
                        v.tensor_copy(r3[:, :, 1:128], q3[:, :, 0:127])
                        v.tensor_copy(r3[:, :, 0:1], q3[:, :, 127:128])
                        v.tensor_tensor(t1s[:, 0:512], qtmp[:, 0:512], tcos[:, 0:512], Op.mult)
                        v.tensor_tensor(qrl[:, 0:512], qrl[:, 0:512], tsin[:, 0:512], Op.mult)
                        return v.tensor_tensor(qtmp[:, 0:512], t1s[:, 0:512], qrl[:, 0:512], Op.add)
                    last = seq.comp("vector", rope_fn, deps=[evid, lc, ls])
                    D1S["rp"] = last
                D1S["st"][qp] = seq.dma("scalar",
                                        out_ap[tt * 128:(tt + 1) * 128, oc * 512:(oc + 1) * 512],
                                        qtmp[:, 0:512], deps=[last], lane=f"qo{qp}")
        seq.barrier()

    qkv3(wq_in, D, 0, q_out, True)
    qkv3(wk_in, 512, 0, k_out, True)
    qkv3(wv_in, 512, 0, v_out, False)

    es.close()


def build_d1():
    nc = bass.Bass("TRN2", target_bir_lowering=False, debug=False)
    es = ExitStack()
    W = TSH
    xfm_in = nc.dram_tensor("xfm", [D, W], F32, kind="ExternalInput").ap()
    wq_in = nc.dram_tensor("wq", [D, D], BF16, kind="ExternalInput").ap()
    wk_in = nc.dram_tensor("wk", [D, 512], BF16, kind="ExternalInput").ap()
    wv_in = nc.dram_tensor("wv", [D, 512], BF16, kind="ExternalInput").ap()
    cosr_in = nc.dram_tensor("cosr", [W, D], BF16, kind="ExternalInput").ap()
    sinr_in = nc.dram_tensor("sinr", [W, D], BF16, kind="ExternalInput").ap()
    o1_in = nc.dram_tensor("ones1", [1, 128], F32, kind="ExternalInput").ap()
    xn_out = nc.dram_tensor("xn_o", [D, W], BF16, kind="ExternalOutput").ap()
    q_out = nc.dram_tensor("q_o", [W, D], BF16, kind="ExternalOutput").ap()
    k_out = nc.dram_tensor("k_o", [W, 512], BF16, kind="ExternalOutput").ap()
    v_out = nc.dram_tensor("v_o", [W, 512], BF16, kind="ExternalOutput").ap()


    seq = Seq(nc, es, "d1seq")
    io = dict(xfm_in=xfm_in, wq_in=wq_in, wk_in=wk_in, wv_in=wv_in, cosr_in=cosr_in, sinr_in=sinr_in, o1_in=o1_in, xn_out=xn_out, q_out=q_out, k_out=k_out, v_out=v_out)
    emit_d1(nc, seq, io)
    seq.finish()
    block = es.enter_context(nc.Block())
    for en in ["sync", "vector", "scalar", "gpsimd", "tensor"]:
        getattr(block, en)(lambda e, en=en: seq.emit(en, e))
    es.close()
    return nc


def emit_d2(nc, seq, io):
    es = ExitStack()
    E = 128
    q_in = io["q_in"]
    k_in = io["k_in"]
    v_in = io["v_in"]
    i128_in = io["i128_in"]
    o1_in = io["o1_in"]
    o_out = io["o_out"]
    DBGON = io.get("dbg", True)
    dbg_out = io.get("dbg_out")
    dbg2_out = io.get("dbg2_out")
    dbg3_out = io.get("dbg3_out")
    bscr = io["bscr"]
    sbuf = nc.sbuf_tensor
    qe = es.enter_context(sbuf("qe", [128, T], BF16))
    ke = es.enter_context(sbuf("ke", [128, T], BF16))
    vt = es.enter_context(sbuf("vt", [128, T], BF16))
    pT = es.enter_context(sbuf("pT", [128, 64 * 512], BF16))
    obuf = es.enter_context(sbuf("obuf", [128, T], BF16))
    iden = es.enter_context(sbuf("iden_s", [128, 128], BF16))
    idf = es.enter_context(sbuf("idf_s", [128, 128], F32))
    ones1 = es.enter_context(sbuf("ones1_s", [1, 128], F32))
    onesc = es.enter_context(sbuf("onesc", [128, 2], BF16))
    onesf = es.enter_context(sbuf("onesf", [128, 2], F32))
    red = es.enter_context(sbuf("red", [128, 2], F32))
    redr = es.enter_context(sbuf("redr", [1, 128], F32))
    brow = es.enter_context(sbuf("brow", [1, 16], F32))
    bcol = es.enter_context(sbuf("bcol", [128, 2], F32))
    srow = es.enter_context(sbuf("srow", [1, 512], F32))
    ob2 = es.enter_context(sbuf("ob2", [128, 512], F32))
    pqk = [es.enter_context(nc.psum_tensor(f"pqk{i}", [128, 512], F32)) for i in range(4)]
    po = es.enter_context(nc.psum_tensor("po", [128, 512], F32))
    psum_s = es.enter_context(nc.psum_tensor("psum_s", [1, 512], F32))
    prd = es.enter_context(nc.psum_tensor("prd", [1, 128], F32))
    pbc2 = es.enter_context(nc.psum_tensor("pbc2", [128, 512], F32))

    seq.dma("sync", iden[:], i128_in[:])
    seq.dma("sync", ones1[:], o1_in[:])
    seq.comp("vector", lambda v: v.memset(onesc[:], 1.0))
    seq.comp("vector", lambda v: v.memset(onesf[:], 1.0))
    seq.comp("vector", lambda v: v.tensor_copy(idf[:], iden[:]))

    for pi in range(4):
        rows = slice(pi * T, (pi + 1) * T)
        seq.dma_t("sync", qe[:], q_in[rows, :])
        seq.dma_t("sync", ke[:], k_in[rows, :])
        seq.dma("sync", vt[:].rearrange("p (c e) -> p c e", e=E),
                v_in[rows, :].rearrange("(c p) e -> p c e", p=128))
        qqv = pT[:].bitcast(F32)
        for which, src_t, dst_col in ((0, qe, 0), (1, ke, 2)):
            seq.comp("scalar", lambda s, src_t=src_t: s.activation(
                out=qqv[:, 0:T], in_=src_t[:], func=AF.Square))
            seq.comp("vector", lambda v, dst_col=dst_col: v.memset(brow[0:1, dst_col:dst_col + 1], 0.0))
            for ch in range(16):
                seq.comp("tensor", lambda t, ch=ch: t.matmul(
                    psum_s[:, :], onesf[:, 0:1], qqv[:, ch * 512:(ch + 1) * 512],
                    start=True, stop=True))

                def mxr(v, dst_col=dst_col):
                    v.tensor_reduce(brow[0:1, 8:9], psum_s[0:1, :], axis=AX.X, op=Op.max)
                    return v.tensor_tensor(brow[0:1, dst_col:dst_col + 1],
                                           brow[0:1, dst_col:dst_col + 1], brow[0:1, 8:9], Op.max)
                seq.comp("vector", mxr)

        def bias_fn(v):
            return v.tensor_tensor(brow[0:1, 4:5], brow[0:1, 0:1], brow[0:1, 2:3], Op.mult)
        seq.comp("vector", bias_fn)

        def bias_fn2(s):
            s.activation(out=brow[0:1, 4:5], in_=brow[0:1, 4:5], func=AF.Ln)
            return s.activation(out=brow[0:1, 4:5], in_=brow[0:1, 4:5], func=AF.Exp, scale=0.5)
        seq.comp("scalar", bias_fn2)

        seq.comp("vector", lambda v: v.tensor_scalar(
            brow[0:1, 6:7], brow[0:1, 4:5], 0.08838834764831845, -30.0, Op.mult, Op.add))
        seq.comp("vector", lambda v: v.tensor_scalar(
            brow[0:1, 4:5], brow[0:1, 6:7], 0.0, -1.0, Op.max, Op.mult))
        seq.dma("scalar", bscr.ap(), brow[:])
        if DBGON:
            seq.dma("scalar", dbg_out[pi:pi + 1, :], brow[:])
        seq.dma("sync", bcol[:, 0:2], bscr.ap()[0, 4:6].partition_broadcast(128))

        for tc in range(16):
            tcs = slice(tc * 512, (tc + 1) * 512)
            for scg in range(16):
                def mqk(t, scg=scg, tc=tc):
                    for j in range(4):
                        sc = scg * 4 + j
                        inst = t.matmul(pqk[j][:, :], ke[:, sc * 128:(sc + 1) * 128],
                                        qe[:, tcs], start=True, stop=True)
                    return inst
                seq.comp("tensor", mqk)
                seq.comp("scalar", lambda s, scg=scg: s.activation(
                    out=pT[:, scg * 2048:(scg + 1) * 2048].rearrange("p (j t) -> p j t", j=4),
                    in_=pqk[0][:, :].to_broadcast((128, 4, 512)) if False else pqk[0][:, :],
                    func=AF.Exp, scale=0.08838834764831845, bias=bcol[:, 0:1])) if False else None
                def ex(s, scg=scg):
                    for j in range(4):
                        inst = s.activation(out=pT[:, (scg * 4 + j) * 512:(scg * 4 + j + 1) * 512],
                                            in_=pqk[j][:, :], func=AF.Exp,
                                            scale=0.08838834764831845, bias=bcol[:, 0:1])
                    return inst
                seq.comp("scalar", ex)
            if DBGON and tc == 0:
                seq.comp("vector", lambda v: v.tensor_copy(ob2[0:1, 0:512], pT[0:1, 0:512]))
                seq.dma("scalar", dbg3_out[pi:pi + 1, :], ob2[0:1, 0:512])
            for scg in range(16):
                def mav(t, scg=scg):
                    for j in range(4):
                        sc = scg * 4 + j
                        t.matmul(po[:, :], vt[:, sc * 128:(sc + 1) * 128],
                                 pT[:, sc * 512:(sc + 1) * 512],
                                 start=(sc == 0), stop=(sc == 63))
                        inst = t.matmul(psum_s[:, :], onesc[:, 0:1],
                                        pT[:, sc * 512:(sc + 1) * 512],
                                        start=(sc == 0), stop=(sc == 63))
                    return inst
                seq.comp("tensor", mav)
            seq.comp("vector", lambda v: v.reciprocal(srow[0:1, :], psum_s[0:1, :]))
            if DBGON:
                seq.comp("vector", lambda v: v.tensor_copy(ob2[0:1, 0:512], psum_s[0:1, :]))
                seq.dma("scalar", dbg2_out[pi:pi + 1, :], ob2[0:1, 0:512])
            seq.comp("tensor", lambda t: t.matmul(pbc2[:, :], ones1[:], srow[0:1, :], start=True, stop=True))
            seq.comp("scalar", lambda s: s.activation(out=ob2[:, :], in_=po[:, :], func=AF.Identity))
            seq.comp("vector", lambda v, tc=tc: v.tensor_tensor(obuf[:, tcs], ob2[:, :], pbc2[:, :], Op.mult))
        seq.dma("scalar", o_out[pi, :, :], obuf[:])

    es.close()


def build_d2():
    nc = bass.Bass("TRN2", target_bir_lowering=False, debug=False)
    es = ExitStack()
    E = 128
    q_in = nc.dram_tensor("qtm", [4 * T, E], BF16, kind="ExternalInput").ap()
    k_in = nc.dram_tensor("ktm", [4 * T, E], BF16, kind="ExternalInput").ap()
    v_in = nc.dram_tensor("vtm", [4 * T, E], BF16, kind="ExternalInput").ap()
    i128_in = nc.dram_tensor("iden", [128, 128], BF16, kind="ExternalInput").ap()
    o1_in = nc.dram_tensor("ones1", [1, 128], F32, kind="ExternalInput").ap()
    o_out = nc.dram_tensor("o_o", [4, E, T], BF16, kind="ExternalOutput").ap()
    dbg_out = nc.dram_tensor("dbg_o", [4, 16], F32, kind="ExternalOutput").ap()
    dbg2_out = nc.dram_tensor("dbg2_o", [4, 512], F32, kind="ExternalOutput").ap()
    dbg3_out = nc.dram_tensor("dbg3_o", [4, 512], F32, kind="ExternalOutput").ap()
    bscr = nc.dram_tensor("bscr", [1, 16], F32)


    seq = Seq(nc, es, "d2seq")
    io = dict(q_in=q_in, k_in=k_in, v_in=v_in, i128_in=i128_in, o1_in=o1_in, o_out=o_out, dbg_out=dbg_out, dbg2_out=dbg2_out, dbg3_out=dbg3_out, bscr=bscr)
    emit_d2(nc, seq, io)
    seq.finish()
    block = es.enter_context(nc.Block())
    for en in ["sync", "vector", "scalar", "gpsimd", "tensor"]:
        getattr(block, en)(lambda e, en=en: seq.emit(en, e))
    es.close()
    return nc


def emit_d2p(nc, seq, io):
    es = ExitStack()
    E = 128
    q_in = io["q_in"]
    k_in = io["k_in"]
    v_in = io["v_in"]
    o1_in = io["o1_in"]
    o_out = io["o_out"]
    bscr = io["bscr"]
    sbuf = nc.sbuf_tensor
    qe = es.enter_context(sbuf("qe", [128, T], BF16))
    ke = es.enter_context(sbuf("ke", [128, T], BF16))
    vt = es.enter_context(sbuf("vt", [128, T], BF16))
    pT = es.enter_context(sbuf("pT", [128, 64 * 512], BF16))
    obuf = es.enter_context(sbuf("obuf", [128, T], BF16))
    ones1 = es.enter_context(sbuf("ones1_s", [1, 128], F32))
    onesc = es.enter_context(sbuf("onesc", [128, 2], BF16))
    onesf = es.enter_context(sbuf("onesf", [128, 2], F32))
    brow = es.enter_context(sbuf("brow", [1, 16], F32))
    bcol = es.enter_context(sbuf("bcol", [128, 2], F32))
    srow = es.enter_context(sbuf("srow", [1, 512], F32))
    ob2 = es.enter_context(sbuf("ob2", [128, 512], F32))
    pqk = [es.enter_context(nc.psum_tensor(f"pqk{i}", [128, 1024], F32)) for i in range(3)]
    po = es.enter_context(nc.psum_tensor("po", [128, 512], F32))
    psum_s = es.enter_context(nc.psum_tensor("psum_s", [1, 512], F32))
    pbc2 = pqk[2][:, 0:512]

    seq.dma("sync", ones1[:], o1_in[:])
    seq.comp("vector", lambda v: v.memset(onesc[:], 1.0))
    seq.comp("vector", lambda v: v.memset(onesf[:], 1.0))

    SCALE = 0.08838834764831845
    for pi in range(4):
        seq.barrier()
        rows = slice(pi * T, (pi + 1) * T)
        ldq = seq.dma_t("sync", qe[:], q_in[rows, :], lane="dq")
        ldk = seq.dma_t("sync", ke[:], k_in[rows, :], lane="dk")
        ldv = seq.dma("sync", vt[:].rearrange("p (c e) -> p c e", e=E),
                      v_in[rows, :].rearrange("(c p) e -> p c e", p=128), lane="dv")
        qqv = pT[:].bitcast(F32)
        for which, src_t, dst_col, ld in ((0, qe, 0, ldq), (1, ke, 2, ldk)):
            seq.comp("scalar", lambda s, src_t=src_t: s.activation(
                out=qqv[:, 0:T], in_=src_t[:], func=AF.Square), deps=[ld, seq.last_id])
            seq.comp("vector", lambda v, dst_col=dst_col: v.memset(brow[0:1, dst_col:dst_col + 1], 0.0))
            for ch in range(16):
                seq.comp("tensor", lambda t, ch=ch: t.matmul(
                    psum_s[:, :], onesf[:, 0:1], qqv[:, ch * 512:(ch + 1) * 512],
                    start=True, stop=True))

                def mxr(v, dst_col=dst_col):
                    v.tensor_reduce(brow[0:1, 8:9], psum_s[0:1, :], axis=AX.X, op=Op.max)
                    return v.tensor_tensor(brow[0:1, dst_col:dst_col + 1],
                                           brow[0:1, dst_col:dst_col + 1], brow[0:1, 8:9], Op.max)
                seq.comp("vector", mxr)
        seq.comp("vector", lambda v: v.tensor_tensor(
            brow[0:1, 4:5], brow[0:1, 0:1], brow[0:1, 2:3], Op.mult))

        def bias_fn2(s):
            s.activation(out=brow[0:1, 4:5], in_=brow[0:1, 4:5], func=AF.Ln)
            return s.activation(out=brow[0:1, 4:5], in_=brow[0:1, 4:5], func=AF.Exp, scale=0.5)
        seq.comp("scalar", bias_fn2)
        seq.comp("vector", lambda v: v.tensor_scalar(
            brow[0:1, 6:7], brow[0:1, 4:5], SCALE, -30.0, Op.mult, Op.add))
        seq.comp("vector", lambda v: v.tensor_scalar(
            brow[0:1, 4:5], brow[0:1, 6:7], 0.0, -1.0, Op.max, Op.mult))
        seq.dma("scalar", bscr.ap(), brow[:])
        bias_done = seq.dma("sync", bcol[:, 0:2], bscr.ap()[0, 4:6].partition_broadcast(128))

        av_prev = {}
        norm_prev = []
        exp_lastS = {}
        store_id = None
        for tc in range(16):
            tcs = slice(tc * 512, (tc + 1) * 512)
            av_cur = {}
            for scg in range(32):
                S = scg % 3

                def mqk(t, scg=scg, tcs=tcs, S=S):
                    for j in range(2):
                        sc = scg * 2 + j
                        inst = t.matmul(pqk[S][:, j * 512:(j + 1) * 512],
                                        ke[:, sc * 128:(sc + 1) * 128], qe[:, tcs],
                                        start=True, stop=True)
                    return inst
                m_id = seq.comp("tensor", mqk, deps=[exp_lastS.get(S), bias_done,
                                                     av_prev.get(scg), ldq, ldk]
                                + (norm_prev if scg < 3 else []))

                def ex(s, scg=scg, S=S):
                    return s.activation(out=pT[:, scg * 1024:(scg + 1) * 1024],
                                        in_=pqk[S][:, :], func=AF.Exp,
                                        scale=SCALE, bias=bcol[:, 0:1])
                e_id = seq.comp("scalar", ex, deps=[m_id, av_prev.get(scg), bias_done])
                exp_lastS[S] = e_id

                def mav(t, scg=scg, tc=tc):
                    for j in range(2):
                        sc = scg * 2 + j
                        t.matmul(po[:, :], vt[:, sc * 128:(sc + 1) * 128],
                                 pT[:, sc * 512:(sc + 1) * 512],
                                 start=(sc == 0), stop=(sc == 63))
                        inst = t.matmul(psum_s[:, :], onesc[:, 0:1],
                                        pT[:, sc * 512:(sc + 1) * 512],
                                        start=(sc == 0), stop=(sc == 63))
                    return inst
                a_id = seq.comp("tensor", mav, deps=[e_id, ldv] + (norm_prev if scg == 0 else []))
                av_cur[scg] = a_id
            rec = seq.comp("vector", lambda v: v.reciprocal(srow[0:1, :], psum_s[0:1, :]),
                           deps=[av_cur[31]])
            bcm = seq.comp("tensor", lambda t: t.matmul(pbc2, ones1[:], srow[0:1, :],
                                                        start=True, stop=True),
                           deps=[rec, exp_lastS.get(2)] + norm_prev)
            oba = seq.comp("scalar", lambda s: s.activation(out=ob2[:, :], in_=po[:, :],
                                                            func=AF.Identity),
                           deps=[av_cur[31]] + norm_prev)
            obm = seq.comp("vector", lambda v, tcs=tcs: v.tensor_tensor(
                obuf[:, tcs], ob2[:, :], pbc2, Op.mult), deps=[oba, bcm, store_id])
            norm_prev = [obm]
            av_prev = av_cur
        store_id = seq.dma("scalar", o_out[pi, :, :], obuf[:], deps=[norm_prev[0]], lane="os")
    seq.barrier()
    es.close()


def emit_d3(nc, seq, io):
    es = ExitStack()
    W = TSH
    H3 = 16384
    xfm_in = io["xfm_in"]
    xn_in = io["xn_in"]
    o_in = io.get("o_in")
    wo_in = io["wo_in"]
    wg_in = io.get("wg_in")
    wu_in = io.get("wu_in")
    wd_in = io.get("wd_in")
    xo_out = io["xo_out"]
    hscr = io["hscr"]
    wg_src = io.get("wg_src") or (
        lambda m0: wg_in.rearrange("(k p) m -> p k m", p=128)[:, :, m0:m0 + 128])
    wu_src = io.get("wu_src") or (
        lambda m0: wu_in.rearrange("(k p) m -> p k m", p=128)[:, :, m0:m0 + 128])
    wd_srcs = io.get("wd_srcs") or (
        lambda k, g: [(0, 1024, wd_in[k * 128:(k + 1) * 128, g * 1024:(g + 1) * 1024])])
    onr_fill = io.get("onr_fill")
    sbuf = nc.sbuf_tensor
    xnr = es.enter_context(sbuf("xnr", [128, 16 * W], BF16))
    onr = es.enter_context(sbuf("onr", [128, 16 * W], BF16))
    wg = es.enter_context(sbuf("wg", [128, 16 * 128], BF16))
    wu = es.enter_context(sbuf("wu", [128, 16 * 128], BF16))
    gsb = es.enter_context(sbuf("gsb", [128, 512], F32))
    g2 = es.enter_context(sbuf("g2", [128, 512], F32))
    g3 = es.enter_context(sbuf("g3", [128, 512], F32))
    hsb = es.enter_context(sbuf("hsb", [128, 2048], BF16))
    kch = es.enter_context(sbuf("kch", [128, 512], BF16))
    wch = es.enter_context(sbuf("wch", [128, 8 * 128], BF16))
    xres = es.enter_context(sbuf("xres", [128, 512], F32))
    ob = es.enter_context(sbuf("ob", [128, 512], F32))
    P = [es.enter_context(nc.psum_tensor(f"pp{i}", [128, 512], F32)) for i in range(8)]

    seq.dma("sync", xnr[:].rearrange("p (k t) -> p k t", t=W),
            xn_in.rearrange("(k p) t -> p k t", p=128))
    if onr_fill is None:
        seq.dma("sync", onr[:].rearrange("p (k t) -> p k t", t=W),
                o_in.rearrange("(k p) t -> p k t", p=128))
    else:
        onr_fill(seq, onr)

    NHT = H3 // 128
    C1 = 0.044715
    C2 = 0.7978845608028654
    for ht in range(NHT):
        seq.dma("sync", wg[:].rearrange("p (k m) -> p k m", m=128), wg_src(ht * 128))
        seq.dma("sync", wu[:].rearrange("p (k m) -> p k m", m=128), wu_src(ht * 128))
        for tc in range(4):
            def mmg(t, tc=tc):
                for k in range(16):
                    t.matmul(P[0][:, :], wg[:, k * 128:(k + 1) * 128],
                             xnr[:, k * W + tc * 512:k * W + (tc + 1) * 512],
                             start=(k == 0), stop=(k == 15))
                for k in range(16):
                    inst = t.matmul(P[1][:, :], wu[:, k * 128:(k + 1) * 128],
                                    xnr[:, k * W + tc * 512:k * W + (tc + 1) * 512],
                                    start=(k == 0), stop=(k == 15))
                return inst
            seq.comp("tensor", mmg)

            def ge1(s):
                s.activation(out=gsb[:, :], in_=P[0][:, :], func=AF.Identity)
                return s.activation(out=g2[:, :], in_=P[0][:, :], func=AF.Square)
            seq.comp("scalar", ge1)

            def ge2(v):
                v.tensor_tensor(g3[:, :], g2[:, :], gsb[:, :], Op.mult)
                return v.scalar_tensor_tensor(g3[:, :], g3[:, :], C1, gsb[:, :], Op.mult, Op.add)
            seq.comp("vector", ge2)
            seq.comp("scalar", lambda s: s.activation(out=g3[:, :], in_=g3[:, :], func=AF.Tanh, scale=C2))

            def ge3(v, tc=tc):
                v.scalar_tensor_tensor(g3[:, :], g3[:, :], 1.0, gsb[:, :], Op.add, Op.mult)
                return v.scalar_tensor_tensor(hsb[:, tc * 512:(tc + 1) * 512], g3[:, :], 0.5,
                                              P[1][:, :], Op.mult, Op.mult)
            seq.comp("vector", ge3)
        seq.dma("scalar", hscr.ap()[ht * 128:(ht + 1) * 128, :], hsb[:, :])

    for tc in range(4):
        cs = slice(tc * 512, (tc + 1) * 512)
        for g in range(2):
            for k in range(NHT):
                seq.dma("sync", kch[:, :], hscr.ap()[k * 128:(k + 1) * 128, cs])
                for c0w, wdw, srcw in wd_srcs(k, g):
                    seq.dma("sync", wch[:, c0w:c0w + wdw], srcw)

                def mmd(t, k=k):
                    for j in range(8):
                        inst = t.matmul(P[j][:, :], wch[:, j * 128:(j + 1) * 128], kch[:, :],
                                        start=(k == 0), stop=False)
                    return inst
                seq.comp("tensor", mmd)
            for k in range(16):
                seq.dma("sync", wch[:], wo_in[k * 128:(k + 1) * 128, g * 1024:(g + 1) * 1024])

                def mma(t, k=k, tc=tc):
                    for j in range(8):
                        inst = t.matmul(P[j][:, :], wch[:, j * 128:(j + 1) * 128],
                                        onr[:, k * W + tc * 512:k * W + (tc + 1) * 512],
                                        start=False, stop=(k == 15))
                    return inst
                seq.comp("tensor", mma)
            for j in range(8):
                dout = g * 8 + j
                seq.dma("sync", xres[:, :], xfm_in.rearrange("(kk p) t -> p kk t", p=128)[:, dout, cs])
                seq.comp("vector", lambda v, j=j: v.tensor_tensor(ob[:, :], P[j][:, :], xres[:, :], Op.add))
                seq.dma("scalar", xo_out.rearrange("(kk p) t -> p kk t", p=128)[:, dout, cs], ob[:, :])

    es.close()


def build_d3():
    nc = bass.Bass("TRN2", target_bir_lowering=False, debug=False)
    es = ExitStack()
    W = TSH
    H3 = 16384
    xfm_in = nc.dram_tensor("xfm", [D, W], F32, kind="ExternalInput").ap()
    xn_in = nc.dram_tensor("xn", [D, W], BF16, kind="ExternalInput").ap()
    o_in = nc.dram_tensor("ofm", [D, W], BF16, kind="ExternalInput").ap()
    wo_in = nc.dram_tensor("wo", [D, D], BF16, kind="ExternalInput").ap()
    wg_in = nc.dram_tensor("wgT", [D, H3], BF16, kind="ExternalInput").ap()
    wu_in = nc.dram_tensor("wuT", [D, H3], BF16, kind="ExternalInput").ap()
    wd_in = nc.dram_tensor("wdT", [H3, D], BF16, kind="ExternalInput").ap()
    xo_out = nc.dram_tensor("xo_o", [D, W], F32, kind="ExternalOutput").ap()
    hscr = nc.dram_tensor("hscr", [H3, W], BF16)


    seq = Seq(nc, es, "d3seq")
    io = dict(xfm_in=xfm_in, xn_in=xn_in, o_in=o_in, wo_in=wo_in, wg_in=wg_in, wu_in=wu_in, wd_in=wd_in, xo_out=xo_out, hscr=hscr)
    emit_d3(nc, seq, io)
    seq.finish()
    block = es.enter_context(nc.Block())
    for en in ["sync", "vector", "scalar", "gpsimd", "tensor"]:
        getattr(block, en)(lambda e, en=en: seq.emit(en, e))
    es.close()
    return nc


FFN_SER1 = False
FFN_SER1K = set()
FFN_P1X = set()
FFN_SER2 = False
FFN_SER2A = False
FFN_SER2E = False


def emit_ffn2(nc, seq, cfg):
    es = ExitStack()
    kind = cfg["kind"]

    def D1(l):
        return None if (FFN_SER1 or kind in FFN_SER1K) else l

    def D2(l):
        return None if FFN_SER2 else l

    def D2a(l):
        return None if (FFN_SER2 or FFN_SER2A) else l

    def D2e(l):
        return None if (FFN_SER2 or FFN_SER2E) else l
    W = TSH
    Hdim = cfg["Hdim"]
    NHT = Hdim // 128
    gated = kind in ("a3", "d3")
    has_rms = kind in ("a3", "b3")
    has_attn = kind in ("a3", "b3", "d3")
    xfm_in = cfg["xfm_in"]
    xn_in = cfg["xn_in"]
    xo_out = cfg["xo_out"]
    hscr = cfg["hscr"]
    gate_src = cfg["gate_src"]
    up_src = cfg.get("up_src")
    down_srcs = cfg["down_srcs"]
    sbuf = nc.sbuf_tensor
    xnr = es.enter_context(sbuf("xnr", [128, 16 * W], BF16))
    wgb = [es.enter_context(sbuf(f"wgb{i}", [128, 16 * 128], BF16)) for i in range(2)]
    wub = [es.enter_context(sbuf(f"wub{i}", [128, 16 * 128], BF16)) for i in range(2)] if gated else None
    hsb = [es.enter_context(sbuf(f"hsb{i}", [128, 2048], BF16)) for i in range(2)]
    gsb = [es.enter_context(sbuf(f"gsb{i}", [128, 512], F32)) for i in range(2)]
    g2 = [es.enter_context(sbuf(f"g2_{i}", [128, 512], F32)) for i in range(2)] if kind == "d3" else None
    g3 = [es.enter_context(sbuf(f"g3_{i}", [128, 512], F32)) for i in range(2)] if kind == "d3" else None
    hq = es.enter_context(sbuf("hq", [128, 2048], F32)) if has_rms or kind == "b3" else None
    rstdB = es.enter_context(sbuf("rstdB", [128, 2048], F32)) if has_rms else None
    strow = es.enter_context(sbuf("strow", [1, 2048], F32)) if has_rms else None
    ones = es.enter_context(sbuf("ones", [128, 2], F32))
    ones1 = es.enter_context(sbuf("ones1_s", [1, 128], F32)) if has_rms else None
    cst = es.enter_context(sbuf("cst", [1, 4], F32))
    onr = es.enter_context(sbuf("onr", [128, 16 * W], BF16)) if kind in ("b3", "d3") else None
    nw = es.enter_context(sbuf("nw_s", [128, 32], F32)) if kind == "b3" else None
    opb = es.enter_context(sbuf("opb_s", [128, 32], F32)) if kind == "a3" else None
    dv = es.enter_context(sbuf("dv_s", [128, 32], F32)) if kind == "c3" else None
    NB = 4
    kch = [es.enter_context(sbuf(f"kch{i}", [128, 512], BF16)) for i in range(NB)]
    kch2 = [es.enter_context(sbuf(f"kch2_{i}", [128, 512], BF16)) for i in range(NB)] if has_rms else None
    kchy = [es.enter_context(sbuf(f"kchy{i}", [128, 512], BF16)) for i in range(NB)] if kind == "a3" else None
    wch = [es.enter_context(sbuf(f"wch{i}", [128, 1024], BF16)) for i in range(NB)]
    xres = [es.enter_context(sbuf(f"xres{i}", [128, 512], F32)) for i in range(2)]
    ych = [es.enter_context(sbuf(f"ych{i}", [128, 512], F32)) for i in range(2)] if kind == "c3" else None
    ob = [es.enter_context(sbuf(f"ob{i}", [128, 512], F32)) for i in range(2)]
    P = [es.enter_context(nc.psum_tensor(f"pp{i}", [128, 512], F32)) for i in range(8)]

    id_xnr = seq.dma("sync", xnr[:].rearrange("p (k t) -> p k t", t=W),
                     xn_in.rearrange("(k p) t -> p k t", p=128))
    seq.comp("vector", lambda v: v.memset(ones[:], 1.0))
    if kind == "a3":
        seq.dma("sync", opb[:].rearrange("p (h two) -> p h two", two=2),
                cfg["opb_in"].rearrange("(h p) two -> p h two", p=128))
        seq.comp("vector", lambda v: v.memset(cst[:, 0:1], 1.1920929e-7))
    elif kind == "b3":
        seq.dma("sync", nw[:].rearrange("p (h two) -> p h two", two=2),
                cfg["nw_in"].rearrange("(h p) two -> p h two", p=128))
        seq.comp("vector", lambda v: v.memset(cst[:, 0:1], 1e-6))
        seq.comp("vector", lambda v: v.memset(cst[:, 2:3], 1.1920929e-7))
    elif kind == "c3":
        seq.dma("sync", dv[:].rearrange("p (h two) -> p h two", two=2),
                cfg["dv_in"].rearrange("(h p) two -> p h two", p=128))
    if has_rms:
        seq.dma("sync", ones1[:], cfg["o1_in"][:])
    id_onr = None
    if kind in ("b3", "d3"):
        cfg["onr_fill"](seq, onr)
        id_onr = len(seq.steps) - 1

    def rstd_chain(cstcol, scl, deps0=None, depsT=None):
        sid = seq.comp("scalar", lambda s: _rstd_fn(s, cstcol, scl), deps=deps0)
        tid = seq.comp("tensor", lambda t: _bc_fn(t), deps=[sid] + (depsT or []))
        vid = seq.comp("vector", lambda v: _bc_ev(v), deps=[tid])
        return vid

    def _rstd_fn(s, cstcol, scl):
        for tcc in range(4):
            s.activation(out=strow[0:1, tcc * 512:(tcc + 1) * 512], in_=P[4 + tcc][0:1, :],
                         func=AF.Ln, bias=cst[0:1, cstcol:cstcol + 1], scale=scl)
            inst = s.activation(out=strow[0:1, tcc * 512:(tcc + 1) * 512],
                                in_=strow[0:1, tcc * 512:(tcc + 1) * 512], func=AF.Exp, scale=-0.5)
        return inst

    def _bc_fn(t):
        for tcc in range(4):
            inst = t.matmul(P[tcc][:, :], ones1[:], strow[0:1, tcc * 512:(tcc + 1) * 512],
                            start=True, stop=True)
        return inst

    def _bc_ev(v):
        for tcc in range(4):
            inst = v.tensor_copy(rstdB[:, tcc * 512:(tcc + 1) * 512], P[tcc][:, :])
        return inst

    pass0_dep = [None]
    if kind == "b3":
        seq.barrier()
        for k in range(16):
            seq.comp("scalar", lambda s, k=k: s.activation(
                out=hq[:, :], in_=onr[:, k * W:(k + 1) * W], func=AF.Square))

            def stm0(t, k=k):
                for tcc in range(4):
                    inst = t.matmul(P[4 + tcc][0:1, :], ones[:, 0:1], hq[:, tcc * 512:(tcc + 1) * 512],
                                    start=(k == 0), stop=(k == 15))
                return inst
            seq.comp("tensor", stm0)
        rstd_chain(0, 1.0 / D)

        def apply_on(v):
            for k in range(16):
                v.tensor_tensor(onr[:, k * W:(k + 1) * W], onr[:, k * W:(k + 1) * W], rstdB[:, :], Op.mult)
                inst = v.tensor_scalar(onr[:, k * W:(k + 1) * W], onr[:, k * W:(k + 1) * W],
                                       nw[:, 2 * k:2 * k + 1], None, Op.mult)
            return inst
        p0v = seq.comp("vector", apply_on)
        pass0_dep[0] = p0v
        seq.barrier()

    ld_prev = {}
    mm_last_on_buf = {}
    ev_last_on_bank = {}
    hs_store = {}
    ev_final = {}
    stat_prev = [None]
    for ht in range(NHT):
        bi = ht % 2
        prs = [(wgb[bi][:].rearrange("p (k m) -> p k m", m=128), gate_src(ht * 128))]
        if gated:
            prs.append((wub[bi][:].rearrange("p (k m) -> p k m", m=128), up_src(ht * 128)))
        idL = seq.dmas("sync", prs,
                       deps=(None if "ld" in FFN_P1X else D1([mm_last_on_buf.get(bi), id_xnr, id_onr])),
                       lane=(None if "ld" in FFN_P1X else f"wg{bi}"))
        ld_prev[bi] = idL
        for tc in range(4):
            bk = tc % 2
            g0, g1 = 2 * bk, 2 * bk + 1

            def mm1(t, ht=ht, tc=tc, bi=bi, g0=g0, g1=g1):
                for k in range(16):
                    inst = t.matmul(P[g0][:, :], wgb[bi][:, k * 128:(k + 1) * 128],
                                    xnr[:, k * W + tc * 512:k * W + (tc + 1) * 512],
                                    start=(k == 0), stop=(k == 15))
                if gated:
                    for k in range(16):
                        inst = t.matmul(P[g1][:, :], wub[bi][:, k * 128:(k + 1) * 128],
                                        xnr[:, k * W + tc * 512:k * W + (tc + 1) * 512],
                                        start=(k == 0), stop=(k == 15))
                return inst
            idM = seq.comp("tensor", mm1, deps=D1([idL, ev_last_on_bank.get(bk), id_xnr, pass0_dep[0]]))
            mm_last_on_buf[bi] = idM
            hdst = hsb[bi][:, tc * 512:(tc + 1) * 512]
            dep0 = [idM, hs_store.get(ht - 2)]
            if kind == "a3":
                s1 = seq.comp("scalar", lambda s, g0=g0, bk=bk: s.activation(
                    out=gsb[bk][:, :], in_=P[g0][:, :], func=AF.Silu), deps=D1(dep0))
                v1 = seq.comp("vector", lambda v, g1=g1, bk=bk, hdst=hdst: v.tensor_tensor(
                    hdst, gsb[bk][:, :], P[g1][:, :], Op.mult), deps=D1([s1]))
                evl = v1
            elif kind in ("b3", "c3"):
                s1 = seq.comp("scalar", lambda s, g0=g0, bk=bk: s.activation(
                    out=gsb[bk][:, :], in_=P[g0][:, :], func=AF.Relu), deps=D1(dep0))
                evl = seq.comp("vector", lambda v, bk=bk, hdst=hdst: v.tensor_tensor(
                    hdst, gsb[bk][:, :], gsb[bk][:, :], Op.mult), deps=D1([s1]))
            else:
                C1 = 0.044715
                C2 = 0.7978845608028654

                def s1f(s, g0=g0, bk=bk):
                    s.activation(out=gsb[bk][:, :], in_=P[g0][:, :], func=AF.Identity)
                    return s.activation(out=g2[bk][:, :], in_=P[g0][:, :], func=AF.Square)
                s1 = seq.comp("scalar", s1f, deps=D1(dep0))

                def v1f(v, bk=bk, C1=C1):
                    v.tensor_tensor(g3[bk][:, :], g2[bk][:, :], gsb[bk][:, :], Op.mult)
                    return v.scalar_tensor_tensor(g3[bk][:, :], g3[bk][:, :], C1, gsb[bk][:, :],
                                                  Op.mult, Op.add)
                v1 = seq.comp("vector", v1f, deps=D1([s1]))
                s2 = seq.comp("scalar", lambda s, bk=bk, C2=C2: s.activation(
                    out=g3[bk][:, :], in_=g3[bk][:, :], func=AF.Tanh, scale=C2), deps=D1([v1]))

                def v2f(v, bk=bk, g1=g1, hdst=hdst):
                    v.scalar_tensor_tensor(g3[bk][:, :], g3[bk][:, :], 1.0, gsb[bk][:, :],
                                           Op.add, Op.mult)
                    return v.scalar_tensor_tensor(hdst, g3[bk][:, :], 0.5, P[g1][:, :],
                                                  Op.mult, Op.mult)
                evl = seq.comp("vector", v2f, deps=D1([s2]))
            ev_last_on_bank[bk] = evl
            ev_final[(ht, tc)] = evl
        evs = [ev_final[(ht, tc)] for tc in range(4)]
        if has_rms:
            sq = seq.comp("scalar", lambda s, bi=bi: s.activation(
                out=hq[:, :], in_=hsb[bi][:, :], func=AF.Square),
                deps=(None if "sq" in FFN_P1X else D1(evs + [stat_prev[0]])))

            def stm(t, ht=ht):
                for tcc in range(4):
                    inst = t.matmul(P[4 + tcc][0:1, :], ones[:, 0:1], hq[:, tcc * 512:(tcc + 1) * 512],
                                    start=(ht == 0), stop=(ht == NHT - 1))
                return inst
            stat_prev[0] = seq.comp("tensor", stm, deps=D1([sq]))
        hs_store[ht] = seq.dma("scalar", hscr.ap()[ht * 128:(ht + 1) * 128, :], hsb[bi][:, :],
                               deps=(None if "st" in FFN_P1X else D1(evs)),
                               lane=(None if "st" in FFN_P1X else f"hs{bi}"))

    if has_rms:
        cstcol = 0 if kind == "a3" else 2
        rstd_chain(cstcol, 1.0 / Hdim, deps0=[stat_prev[0]],
                   depsT=[ev_final[(NHT - 1, tcx)] for tcx in range(4)])
    seq.barrier()

    NDK = NHT
    prev_group_ev = list(seq._bar)
    ld_k = {}
    ld_w = {}
    sc_k = {}
    mm_k = {}
    st_ob = {}
    for tc in range(4):
        cs = slice(tc * 512, (tc + 1) * 512)
        for g in range(2):
            first_deps = list(prev_group_ev)
            for k in range(NDK):
                bi = k % NB
                extra = first_deps if k < NB else []
                dsl = down_srcs(k, g)
                kprs = [(kch[bi][:, :], hscr.ap()[k * 128:(k + 1) * 128, cs])]
                if len(dsl) > 1:
                    c0w, wdw, srcw = dsl[-1]
                    kprs.append((wch[bi][:, c0w:c0w + wdw], srcw))
                    dsl = dsl[:-1]
                idk = seq.dmas("sync", kprs,
                               deps=D2([(sc_k if has_rms else mm_k).get(k - NB),
                                        mm_k.get(k - NB)] + extra), lane=f"kc{bi}")
                idw = seq.dmas("gpsimd", [(wch[bi][:, c0w:c0w + wdw], srcw)
                                          for c0w, wdw, srcw in dsl],
                               deps=D2([mm_k.get(k - NB)] + extra), lane=f"wd{bi}")
                idw2 = idk
                if has_rms:
                    ids = seq.comp("vector", lambda v, bi=bi, tc=tc: v.tensor_tensor(
                        kch2[bi][:, :], kch[bi][:, :], rstdB[:, tc * 512:(tc + 1) * 512], Op.mult),
                        deps=D2([idk, mm_k.get(k - NB)] + extra))
                    sc_k[k] = ids
                    rhs = kch2[bi]
                    mmdeps = [ids, idw, idw2]
                else:
                    rhs = kch[bi]
                    mmdeps = [idk, idw, idw2]
                if k == 0:
                    mmdeps += first_deps

                def mmd(t, k=k, rhs=rhs, last=(k == NDK - 1 and not has_attn)):
                    for j in range(8):
                        inst = t.matmul(P[j][:, :], wch[k % NB][:, j * 128:(j + 1) * 128], rhs[:, :],
                                        start=(k == 0), stop=last)
                    return inst
                mm_k[k] = seq.comp("tensor", mmd, deps=D2(mmdeps))
            last_mm = mm_k[NDK - 1]
            if has_attn:
                wsrc = cfg["attn_w_in"]
                for k in range(16):
                    bi = k % NB
                    idw = seq.dmas("gpsimd", [(wch[bi][:, 0:512], wsrc[k * 128:(k + 1) * 128, g * 1024:g * 1024 + 512])],
                                   deps=D2a([mm_k.get(NDK + k - NB, last_mm if k < NB else None)]), lane=f"wd{bi}")
                    idw2 = seq.dmas("sync", [(wch[bi][:, 512:1024], wsrc[k * 128:(k + 1) * 128, g * 1024 + 512:(g + 1) * 1024])],
                                    deps=D2a([mm_k.get(NDK + k - NB, last_mm if k < NB else None)]), lane=f"ww{bi}")
                    if kind == "a3":
                        idk = seq.dma("sync", kchy[bi][:, :],
                                      cfg["y_in"].rearrange("(kk p) t -> p kk t", p=128)[:, k, cs],
                                      deps=D2a([mm_k.get(NDK + k - NB)]), lane=f"ky{bi}")
                        rhs2 = kchy[bi]
                        mmdeps = [idw, idw2, idk]
                    else:
                        rhs2 = None
                        mmdeps = [idw, idw2, id_onr]

                    def mma(t, k=k, bi=bi, rhs2=rhs2, tc=tc):
                        for j in range(8):
                            if rhs2 is not None:
                                inst = t.matmul(P[j][:, :], wch[bi][:, j * 128:(j + 1) * 128],
                                                rhs2[:, :], start=False, stop=(k == 15))
                            else:
                                inst = t.matmul(P[j][:, :], wch[bi][:, j * 128:(j + 1) * 128],
                                                onr[:, k * W + tc * 512:k * W + (tc + 1) * 512],
                                                start=False, stop=(k == 15))
                        return inst
                    mm_k[NDK + k] = seq.comp("tensor", mma, deps=D2a(mmdeps))
                last_mm = mm_k[NDK + 15]
            group_ev = []
            for j in range(8):
                dout = g * 8 + j
                bj = j % 2
                idx = seq.dma("sync", xres[bj][:, :],
                              xfm_in.rearrange("(kk p) t -> p kk t", p=128)[:, dout, cs],
                              deps=D2e([st_ob.get((tc, g, j - 2))]), lane=f"xr{bj}")
                if kind == "c3":
                    idy = seq.dma("sync", ych[bj][:, :],
                                  cfg["y_in"].rearrange("(kk p) t -> p kk t", p=128)[:, dout, cs],
                                  deps=D2e([st_ob.get((tc, g, j - 2))]), lane=f"yc{bj}")

                    def evd(v, j=j, bj=bj, dout=dout, tc=tc):
                        v.scalar_tensor_tensor(ych[bj][:, :],
                                               xnr[:, dout * W + tc * 512:dout * W + (tc + 1) * 512],
                                               dv[:, 2 * dout:2 * dout + 1], ych[bj][:, :], Op.mult, Op.add)
                        v.tensor_tensor(ob[bj][:, :], P[j][:, :], xres[bj][:, :], Op.add)
                        return v.tensor_tensor(ob[bj][:, :], ob[bj][:, :], ych[bj][:, :], Op.add)
                    ev = seq.comp("vector", evd, deps=D2e([last_mm, idx, idy, st_ob.get((tc, g, j - 2))]))
                elif kind == "a3":
                    sev = seq.comp("scalar", lambda s, j=j, bj=bj, dout=dout: s.activation(
                        out=ob[bj][:, :], in_=P[j][:, :], func=AF.Identity,
                        bias=opb[:, 2 * dout:2 * dout + 1]),
                        deps=D2e([last_mm, st_ob.get((tc, g, j - 2))]))
                    ev = seq.comp("vector", lambda v, bj=bj: v.tensor_tensor(
                        ob[bj][:, :], ob[bj][:, :], xres[bj][:, :], Op.add), deps=D2e([sev, idx]))
                else:
                    ev = seq.comp("vector", lambda v, j=j, bj=bj: v.tensor_tensor(
                        ob[bj][:, :], P[j][:, :], xres[bj][:, :], Op.add),
                        deps=D2e([last_mm, idx, st_ob.get((tc, g, j - 2))]))
                group_ev.append(ev)
                st_ob[(tc, g, j)] = seq.dma("scalar",
                                            xo_out.rearrange("(kk p) t -> p kk t", p=128)[:, dout, cs],
                                            ob[bj][:, :], deps=D2e([ev]), lane=f"so{bj}")
            prev_group_ev = group_ev + [st_ob[(tc, g, j)] for j in range(8)]
            ld_k.clear()
            ld_w.clear()
            sc_k.clear()
            mm_k.clear()
    seq.barrier()
    es.close()


DBG = {}


def kernel_unfused(**inp):
    import math
    ones1 = np.ones((1, 128), np.float32)
    iden = np.eye(128, dtype=bf16)
    cores = list(range(NCORES))

    res = run_bass_kernel_spmd(build_p0(), p0_host_inputs(inp), core_ids=cores)
    prep = p0_collect(res.results)

    x = np.ascontiguousarray(np.asarray(inp["L0_attn_x"], np.float32).reshape(TOK, D).T)
    res = run_bass_kernel_spmd(build_a1(), a1_host_inputs(inp, prep), core_ids=cores)
    xn0 = np.concatenate([r["xn_o"] for r in res.results], 1)
    x0f = np.concatenate([r["x0_o"] for r in res.results], 1)
    v2f = np.concatenate([r["v2_o"] for r in res.results], 1)

    res = run_bass_kernel_spmd(build_a2(), a2_host_inputs(inp, v2f, x0f), core_ids=cores)
    yf = np.concatenate([r["y_o"] for r in res.results], 0)

    opb = np.zeros((D, 2), np.float32)
    opb[:, 0] = np.asarray(inp["L0_attn_out_proj_b"], np.float32)
    maps = [dict(xfm=np.ascontiguousarray(x[:, c * TSH:(c + 1) * TSH]),
                 xn=np.ascontiguousarray(xn0[:, c * TSH:(c + 1) * TSH]),
                 yv=np.ascontiguousarray(yf[:, c * TSH:(c + 1) * TSH]),
                 opwT=prep["opwT"], opb=opb, wguT=prep["l0wgu"], wprT=prep["l0wpr"],
                 ones1=ones1) for c in range(NCORES)]
    res = run_bass_kernel_spmd(build_a3(), maps, core_ids=cores)
    x1 = np.concatenate([r["xo_o"] for r in res.results], 1)
    DBG["x1"] = x1

    maps = [dict(xfm=np.ascontiguousarray(x1[:, c * TSH:(c + 1) * TSH]), wqT=prep["wqT"],
                 wkT=prep["wkT"], wvT=prep["wvT"], ones1=ones1) for c in range(NCORES)]
    res = run_bass_kernel_spmd(build_b1(), maps, core_ids=cores)
    xn1 = np.concatenate([r["xn_o"] for r in res.results], 1)
    qf = np.concatenate([r["q_o"] for r in res.results], 1)
    kf = np.concatenate([r["k_o"] for r in res.results], 1)
    vf = np.concatenate([r["v_o"] for r in res.results], 1)
    lff = np.concatenate([r["lf_o"] for r in res.results], 1)

    maps = b2_host_inputs(qf, kf, np.ascontiguousarray(kf.T), np.ascontiguousarray(vf.T),
                          np.ascontiguousarray(lff.T))
    res = run_bass_kernel_spmd(build_b2(), maps, core_ids=cores)
    o_tm = np.zeros((TOK, D), np.float32)
    for c in range(NCORES):
        for pi in range(4):
            p = 4 * c + pi
            b, h = p // 16, p % 16
            o_tm[b * T:(b + 1) * T, h * 128:(h + 1) * 128] = res.results[c]["o_o"][pi].astype(np.float32)
    o_fm = np.ascontiguousarray(o_tm.T.astype(bf16))

    nw2 = np.zeros((D, 2), np.float32)
    nw2[:, 0] = np.asarray(inp["L1_attn_norm_w"], np.float32)
    maps = [dict(xfm=np.ascontiguousarray(x1[:, c * TSH:(c + 1) * TSH]),
                 xn=np.ascontiguousarray(xn1[:, c * TSH:(c + 1) * TSH]),
                 ofm=np.ascontiguousarray(o_fm[:, c * TSH:(c + 1) * TSH]),
                 woT=prep["woT"], nw=nw2, wfcT=prep["l1wfc"], wprT=prep["l1wpr"],
                 ones1=ones1) for c in range(NCORES)]
    res = run_bass_kernel_spmd(build_b3(), maps, core_ids=cores)
    x2 = np.concatenate([r["xo_o"] for r in res.results], 1)
    DBG["x2"] = x2

    maps = [dict(xfm=np.ascontiguousarray(x2[:, c * TSH:(c + 1) * TSH]), wxT=prep["wxT"],
                 ones1=ones1) for c in range(NCORES)]
    res = run_bass_kernel_spmd(build_c1(), maps, core_ids=cores)
    xn2 = np.concatenate([r["xn_o"] for r in res.results], 1)
    xpf = np.concatenate([r["xp_o"] for r in res.results], 1)

    alog = np.asarray(inp["L2_attn_A_log"], np.float32)
    maps = []
    for c in range(NCORES):
        rs = slice(c * 256, (c + 1) * 256)
        am = np.zeros((128, 64), np.float32)
        al = alog[rs]
        for ct in range(2):
            am[:, ct * 32:ct * 32 + 32:2] = al[ct * 128:(ct + 1) * 128]
        maps.append(dict(xpd=np.ascontiguousarray(xpf[rs]), xnb=np.ascontiguousarray(xn2[rs]),
                         bcr=np.ascontiguousarray(xpf[2048:2080]), alog=am, ones1=ones1))
    res = run_bass_kernel_spmd(build_c2(), maps, core_ids=cores)
    yssm = np.concatenate([r["y_o"] for r in res.results], 0)

    dv2 = np.zeros((D, 2), np.float32)
    dv2[:, 0] = np.asarray(inp["L2_attn_D"], np.float32)
    maps = [dict(xfm=np.ascontiguousarray(x2[:, c * TSH:(c + 1) * TSH]),
                 xn=np.ascontiguousarray(xn2[:, c * TSH:(c + 1) * TSH]),
                 yssm=np.ascontiguousarray(yssm[:, c * TSH:(c + 1) * TSH]),
                 dv=dv2, wfcT=prep["l2wfc"], wprT=prep["l2wpr"]) for c in range(NCORES)]
    res = run_bass_kernel_spmd(build_c3(), maps, core_ids=cores)
    x3 = np.concatenate([r["xo_o"] for r in res.results], 1)
    DBG["x3"] = x3

    HD = 128
    inv_freq = 1.0 / 10000.0 ** (np.arange(0, HD, 2) / HD)
    pos = np.arange(float(T))
    freqs = pos[:, None] * inv_freq[None, :]
    emb = np.concatenate([freqs, freqs], -1)
    cos_rep = np.tile(np.cos(emb).astype(bf16), (1, 16))
    sin_rep = np.tile(np.sin(emb).astype(bf16), (1, 16))
    maps = []
    for c in range(NCORES):
        pos0 = (c * TSH) % T
        maps.append(dict(xfm=np.ascontiguousarray(x3[:, c * TSH:(c + 1) * TSH]),
                         wq=prep["l3wq"], wk=prep["l3wk"], wv=prep["l3wv"],
                         cosr=np.ascontiguousarray(cos_rep[pos0:pos0 + TSH]),
                         sinr=np.ascontiguousarray(sin_rep[pos0:pos0 + TSH]), ones1=ones1))
    res = run_bass_kernel_spmd(build_d1(), maps, core_ids=cores)
    xn3 = np.concatenate([r["xn_o"] for r in res.results], 1)
    qtm = np.concatenate([r["q_o"] for r in res.results], 0)
    ktm = np.concatenate([r["k_o"] for r in res.results], 0)
    vtm = np.concatenate([r["v_o"] for r in res.results], 0)

    maps = []
    for c in range(NCORES):
        qs, ks, vs = [], [], []
        for pi in range(4):
            p = 4 * c + pi
            b, h = p // 16, p % 16
            hk = h // 4
            qs.append(qtm[b * T:(b + 1) * T, h * 128:(h + 1) * 128])
            ks.append(ktm[b * T:(b + 1) * T, hk * 128:(hk + 1) * 128])
            vs.append(vtm[b * T:(b + 1) * T, hk * 128:(hk + 1) * 128])
        maps.append(dict(qtm=np.ascontiguousarray(np.concatenate(qs, 0)),
                         ktm=np.ascontiguousarray(np.concatenate(ks, 0)),
                         vtm=np.ascontiguousarray(np.concatenate(vs, 0)),
                         iden=iden, ones1=ones1))
    res = run_bass_kernel_spmd(build_d2(), maps, core_ids=cores)
    o3 = np.zeros((D, TOK), np.float32)
    for c in range(NCORES):
        for pi in range(4):
            p = 4 * c + pi
            b, h = p // 16, p % 16
            o3[h * 128:(h + 1) * 128, b * T:(b + 1) * T] = res.results[c]["o_o"][pi].astype(np.float32)
    o3b = o3.astype(bf16)

    maps = [dict(xfm=np.ascontiguousarray(x3[:, c * TSH:(c + 1) * TSH]),
                 xn=np.ascontiguousarray(xn3[:, c * TSH:(c + 1) * TSH]),
                 ofm=np.ascontiguousarray(o3b[:, c * TSH:(c + 1) * TSH]),
                 wo=prep["l3wo"], wgT=prep["l3g"], wuT=prep["l3u"], wdT=prep["l3d"])
            for c in range(NCORES)]
    res = run_bass_kernel_spmd(build_d3(), maps, core_ids=cores)
    x4 = np.concatenate([r["xo_o"] for r in res.results], 1)
    DBG["x4"] = x4
    return np.ascontiguousarray(x4.T).reshape(B, T, D).astype(np.float32)


def _sw_dmas(seq, eng, n, dmas_fn, deps=None):
    def wrap(e, h, tgt):
        for arm in e.switch_core_id(n=128):
            c = arm.logical % NCORES
            for dst, src in dmas_fn(c):
                e.dma_start(out=dst, in_=src).then_inc(h, 16)
    return seq.step(eng, 16 * n, wrap, deps=deps, async_=True)


def _ag_step(seq, nc, src_t, dst_t, name, grouped=False):
    rg = [[0, 1, 2, 3], [4, 5, 6, 7]] if grouped else [list(range(NCORES))]

    def fn(g, h, tgt):
        g.collective_compute("AllGather", Op.bypass, replica_groups=rg,
                             ins=[src_t.ap().opt()], outs=[dst_t.ap().opt()]).then_inc(h, 1)
    return seq.step("gpsimd", 1, fn, async_=True)


class _NcProxy:

    def __init__(self, nc, pfx):
        self._nc = nc
        self._pfx = pfx

    def sbuf_tensor(self, name, shape, dt):
        return self._nc.sbuf_tensor(self._pfx + name, shape, dt)

    def psum_tensor(self, name, shape, dt):
        return self._nc.psum_tensor(self._pfx + name, shape, dt)

    def __getattr__(self, k):
        return getattr(self._nc, k)


def _mk_wcol_src(t, Rs):
    r = t.ap().rearrange("(blk k p) m -> p blk k m", blk=NCORES, p=128)
    return lambda m0: r[:, m0 // Rs, :, m0 % Rs:m0 % Rs + 128]


def _mk_wpr_srcs(t, Ktot):
    ap = t.ap()

    def f(k, g):
        return [(jj * 256, 256,
                 ap[(g * 4 + jj) * Ktot + k * 128:(g * 4 + jj) * Ktot + (k + 1) * 128, 0:256])
                for jj in range(4)]
    return f


def build_full(upto="d"):
    nc = bass.Bass("TRN2", target_bir_lowering=False, debug=False)
    oes = ExitStack()
    seq = Seq(nc, oes, "fseq")
    H = {}

    def ein(name, shape, dt):
        H[name] = nc.dram_tensor(name, shape, dt, kind="ExternalInput")
        return H[name]

    def itn(name, shape, dt, shared=False):
        H[name] = nc.dram_tensor(name, shape, dt,
                                 addr_space="Shared" if shared else "Local")
        return H[name]

    def eout(name, shape, dt):
        H[name] = nc.dram_tensor(name, shape, dt, kind="ExternalOutput")
        return H[name]

    W = TSH
    do_b = upto in ("b", "c", "d")
    do_c = upto in ("c", "d")
    do_d = upto == "d"
    for name, R, K, kind, tr in P0_SPECS:
        Rs = R // NCORES
        ein(name, [Rs, K], F32)
        oshape = [K, Rs] if tr else [Rs, K]
        itn(name + "_o", oshape, BF16)
        itn(name + "_all", [NCORES * oshape[0], oshape[1]], BF16, shared=True)
    for name, R, Kp, Kd in P0_NV:
        Rs = R // NCORES
        ein(name + "_pk", [Rs, Kp], I32)
        ein(name + "_sc", [Rs, Kd // 32], F32)
        itn(name + "_o", [Kd, Rs], BF16)
        itn(name + "_all", [NCORES * Kd, Rs], BF16, shared=True)
    ein("iden", [128, 128], BF16)
    ein("icst", [128, 2], I32)
    itn("gpart", [1, 16], F32)
    itn("gsum", [1, 16], F32)
    ein("a1_xfm", [D, W + 2], F32)
    ein("a1_ipb", [3 * D, 2], F32)
    ein("a1_shw", [3 * D, 8], F32)
    ein("a1_shb", [3 * D, 8], F32)
    ein("a1_hsc", [128, 2], F32)
    ein("ones1", [1, 128], F32)
    ein("a_xres", [D, W], F32)
    itn("xn0", [D, W], BF16)
    itn("v2_loc", [D, W], BF16)
    itn("x0_loc", [D, W], BF16)
    itn("v2_all", [NCORES * D, W], BF16, shared=True)
    itn("x0_all", [NCORES * D, W], BF16, shared=True)
    itn("v2_cs", [256, TOK], BF16)
    itn("x0_cs", [256, TOK], BF16)
    for nm, shape, dt in [("zT", [3, T], F32), ("w0T", [3, 64], F32), ("w1T", [64, 64], F32),
                          ("w2T", [64, 64], F32), ("w3T", [64, 256], F32), ("fvec", [64, 8], F32),
                          ("trow", [1, T], F32), ("nds", [256, 2], F32), ("fbv", [256, 2], F32),
                          ("wre", [128, 128], BF16), ("wim", [128, 128], BF16), ("wimn", [128, 128], BF16),
                          ("wc64", [64, 256], BF16), ("wpd", [128, 256], BF16), ("wpd2", [128, 256], BF16),
                          ("twfr", [128, 128], F32), ("twfi", [128, 128], F32),
                          ("twir", [128, 128], F32), ("twii", [128, 128], F32)]:
        ein("a2_" + nm, shape, dt)
    itn("a2_kscr", [256, T], BF16)
    itn("a2_kfscr", [4, 128, 64 * 256], BF16)
    itn("a2_yscr", [256, B * T], BF16)
    itn("y0_loc", [256, TOK], BF16)
    itn("y0_all", [NCORES * 256, TOK], BF16, shared=True)
    itn("y0_cs", [D, W], BF16)
    itn("a3_hscr", [8192, W], BF16)
    itn("xr1", [D, W], F32)
    ein("a3_opb", [D, 2], F32)
    itn("xn1", [D, W], BF16)
    itn("q1_loc", [D, W], BF16)
    itn("k1_loc", [D, W], BF16)
    itn("v1_loc", [D, W], BF16)
    itn("lf1_loc", [D, W], FP16)
    itn("q1_all", [4 * D, W], BF16)
    itn("k1_all", [4 * D, W], BF16)
    itn("v1_all", [4 * D, W], BF16)
    itn("lf1_all", [4 * D, W], FP16)
    ein("b2_utri", [64, 64], FP16)
    ein("b2_mtri", [64, 64], BF16)
    itn("o1_loc", [512, T], BF16)
    itn("o1_all", [4 * 512, T], BF16)
    ein("b3_nw", [D, 2], F32)
    itn("b3_hscr", [8192, W], BF16)
    itn("xr2", [D, W], F32)
    itn("xn2", [D, W], BF16)
    itn("xp_loc", [2080, W], BF16)
    itn("xn2_all", [NCORES * D, W], BF16, shared=True)
    itn("xp_all", [NCORES * 2080, W], BF16, shared=True)
    itn("c2_xpd", [256, TOK], BF16)
    itn("c2_xnb", [256, TOK], BF16)
    itn("c2_bcr", [32, TOK], BF16)
    ein("c2_alog", [128, 64], F32)
    itn("y2_loc", [256, TOK], F32)
    itn("y2_all", [NCORES * 256, TOK], F32, shared=True)
    itn("y2_cs", [D, W], F32)
    ein("c3_dv", [D, 2], F32)
    itn("c3_hscr", [5120, W], BF16)
    itn("xr3", [D, W], F32)
    itn("xn3", [D, W], BF16)
    ein("d1_cosr", [W, D], BF16)
    ein("d1_sinr", [W, D], BF16)
    itn("q3_loc", [W, D], BF16)
    itn("k3_loc", [W, 512], BF16)
    itn("v3_loc", [W, 512], BF16)
    itn("q3_all", [4 * W, D], BF16)
    itn("k3_all", [4 * W, 512], BF16)
    itn("v3_all", [4 * W, 512], BF16)
    itn("d2_q", [4 * T, 128], BF16)
    itn("d2_k", [4 * T, 128], BF16)
    itn("d2_v", [4 * T, 128], BF16)
    itn("d2_bscr", [1, 16], F32)
    itn("o3_loc", [4, 128, T], BF16)
    itn("o3_all", [4 * 4, 128, T], BF16)
    itn("d3_hscr", [16384, W], BF16)
    if upto == "a":
        eout("xr1_dbg", [D, W], F32)
    elif upto == "b":
        eout("xr2_dbg", [D, W], F32)
    elif upto == "c":
        eout("xr3_dbg", [D, W], F32)
    else:
        eout("xo_f", [D, W], F32)

    ins, outs = {}, {}
    for name, R, K, kind, tr in P0_SPECS:
        ins[name] = H[name].ap()
        outs[name] = H[name + "_o"].ap()
    for name, R, Kp, Kd in P0_NV:
        ins[name + "_pk"] = H[name + "_pk"].ap()
        ins[name + "_sc"] = H[name + "_sc"].ap()
        outs[name] = H[name + "_o"].ap()
    seq.barrier()
    seq.label("p0")
    emit_p0p(_NcProxy(nc, "p0_"), seq, dict(ins=ins, outs=outs, iden_in=H["iden"].ap(), icst_in=H["icst"].ap(),
                            gpart=H["gpart"], gsum=H["gsum"]))
    seq.barrier()
    seq.label("ag_w")
    for name, R, K, kind, tr in P0_SPECS:
        _ag_step(seq, nc, H[name + "_o"], H[name + "_all"], name)
    for name, R, Kp, Kd in P0_NV:
        _ag_step(seq, nc, H[name + "_o"], H[name + "_all"], name)

    seq.barrier()
    seq.label("a1")
    emit_a1(_NcProxy(nc, "a1_"), seq, dict(
        xfm_in=H["a1_xfm"].ap(), ipw_in=H["ipwT_all"].ap(), ipb_in=H["a1_ipb"].ap(),
        shw_in=H["a1_shw"].ap(), shb_in=H["a1_shb"].ap(), hsc_in=H["a1_hsc"].ap(),
        o1_in=H["ones1"].ap(), xn_out=H["xn0"].ap(), x0_out=H["x0_loc"].ap(),
        v2_out=H["v2_loc"].ap()))
    seq.barrier()
    seq.label("ag_a")
    _ag_step(seq, nc, H["v2_loc"], H["v2_all"], "agv2")
    _ag_step(seq, nc, H["x0_loc"], H["x0_all"], "agx0")

    def loc_chan2(dst_t, src_t, blk_rows, r_of_c, RB):
        def dmas(c, dst_t=dst_t, src_t=src_t, blk_rows=blk_rows, r_of_c=r_of_c, RB=RB):
            r0 = r_of_c(c)
            return [(dst_t.ap()[:, j * W:(j + 1) * W],
                     src_t.ap()[j * blk_rows + r0:j * blk_rows + r0 + RB, :])
                    for j in range(NCORES)]
        _sw_dmas(seq, "sync", NCORES, dmas)

    def loc_tok(dst_t, src_t, RB):
        def dmas(c, dst_t=dst_t, src_t=src_t, RB=RB):
            return [(dst_t.ap()[s * RB:(s + 1) * RB, :],
                     src_t.ap()[s * RB:(s + 1) * RB, c * W:(c + 1) * W])
                    for s in range(NCORES)]
        _sw_dmas(seq, "sync", NCORES, dmas)

    loc_chan2(H["v2_cs"], H["v2_all"], D, lambda c: c * 256, 256)
    loc_chan2(H["x0_cs"], H["x0_all"], D, lambda c: c * 256, 256)
    a2io = dict(v2_in=H["v2_cs"].ap(), x0_in=H["x0_cs"].ap(), y_out=H["y0_loc"].ap(),
                kscr=H["a2_kscr"], kfscr=H["a2_kfscr"], yscr=H["a2_yscr"], dbg=False)
    for nm in ["zT", "w0T", "w1T", "w2T", "w3T", "fvec", "trow", "nds", "wre", "wim",
               "wimn", "wc64", "wpd", "wpd2", "twfr", "twfi", "twir", "twii"]:
        a2io[nm + "_in"] = H["a2_" + nm].ap()
    a2io["fb_in"] = H["a2_fbv"].ap()
    seq.barrier()
    seq.label("a2")
    emit_a2(_NcProxy(nc, "a2_"), seq, a2io)
    seq.barrier()
    seq.label("ag_y0")
    _ag_step(seq, nc, H["y0_loc"], H["y0_all"], "agy0")
    loc_tok(H["y0_cs"], H["y0_all"], 256)
    seq.barrier()
    seq.label("a3")
    _gs0 = _mk_wcol_src(H["l0wgu_all"], 2048)
    emit_ffn2(_NcProxy(nc, "a3_"), seq, dict(
        kind="a3", Hdim=8192,
        xfm_in=H["a_xres"].ap(), xn_in=H["xn0"].ap(), y_in=H["y0_cs"].ap(),
        attn_w_in=H["opwT_all"].ap(), opb_in=H["a3_opb"].ap(), o1_in=H["ones1"].ap(),
        xo_out=H["xr1"].ap(), hscr=H["a3_hscr"],
        gate_src=_gs0, up_src=lambda m0: _gs0(8192 + m0),
        down_srcs=_mk_wpr_srcs(H["l0wpr_all"], 8192)))
    if upto == "a":
        seq.dma("scalar", H["xr1_dbg"].ap(), H["xr1"].ap())

    if do_b:
        seq.barrier()
        seq.label("b1")
        emit_b1(_NcProxy(nc, "b1_"), seq, dict(
            xfm_in=H["xr1"].ap(), wq_in=H["wqT_all"].ap(), wk_in=H["wkT_all"].ap(),
            wv_in=H["wvT_all"].ap(), o1_in=H["ones1"].ap(), xn_out=H["xn1"].ap(),
            q_out=H["q1_loc"].ap(), k_out=H["k1_loc"].ap(), v_out=H["v1_loc"].ap(),
            lf_out=H["lf1_loc"].ap()))
        seq.barrier()
        seq.label("ag_b1")
        for nm in ["q1", "k1", "v1", "lf1"]:
            _ag_step(seq, nc, H[nm + "_loc"], H[nm + "_all"], nm, grouped=True)
        seq.barrier()
        seq.label("b2")
        emit_b2f(_NcProxy(nc, "b2_"), seq, dict(
            q_all=H["q1_all"].ap(), k_all=H["k1_all"].ap(), v_all=H["v1_all"].ap(),
            lf_all=H["lf1_all"].ap(), utri_in=H["b2_utri"].ap(), mtri_in=H["b2_mtri"].ap(),
            i128_in=H["iden"].ap(), o_out=H["o1_loc"].ap()))
        seq.barrier()
        seq.label("ag_o1")
        _ag_step(seq, nc, H["o1_loc"], H["o1_all"], "ago1", grouped=True)

        def onr_fill_b(seq2, onr):
            o1a = H["o1_all"].ap()

            def dmas(c):
                jb = c % 4
                L = []
                for kk in range(16):
                    sl, pi = kk // 4, kk % 4
                    L.append((onr[:, kk * W:(kk + 1) * W],
                              o1a[sl * 512 + pi * 128:sl * 512 + pi * 128 + 128,
                                  jb * W:(jb + 1) * W]))
                return L
            _sw_dmas(seq2, "sync", 16, dmas)
        seq.barrier()
        seq.label("b3")
        emit_ffn2(_NcProxy(nc, "b3_"), seq, dict(
            kind="b3", Hdim=8192,
            xfm_in=H["xr1"].ap(), xn_in=H["xn1"].ap(), attn_w_in=H["woT_all"].ap(),
            nw_in=H["b3_nw"].ap(), o1_in=H["ones1"].ap(), xo_out=H["xr2"].ap(),
            hscr=H["b3_hscr"], onr_fill=onr_fill_b,
            gate_src=_mk_wcol_src(H["l1wfc_all"], 1024),
            down_srcs=_mk_wpr_srcs(H["l1wpr_all"], 8192)))
        if upto == "b":
            seq.dma("scalar", H["xr2_dbg"].ap(), H["xr2"].ap())

    if do_c:
        seq.barrier()
        seq.label("c1")
        emit_c1(_NcProxy(nc, "c1_"), seq, dict(
            xfm_in=H["xr2"].ap(), wx_in=H["wxT_all"].ap(), o1_in=H["ones1"].ap(),
            xn_out=H["xn2"].ap(), xp_out=H["xp_loc"].ap()))
        seq.barrier()
        seq.label("ag_c1")
        _ag_step(seq, nc, H["xn2"], H["xn2_all"], "agxn2")
        _ag_step(seq, nc, H["xp_loc"], H["xp_all"], "agxp")
        loc_chan2(H["c2_xpd"], H["xp_all"], 2080, lambda c: c * 256, 256)
        loc_chan2(H["c2_xnb"], H["xn2_all"], D, lambda c: c * 256, 256)
        for j in range(NCORES):
            seq.dma("sync", H["c2_bcr"].ap()[:, j * W:(j + 1) * W],
                    H["xp_all"].ap()[j * 2080 + 2048:j * 2080 + 2080, :])
        seq.barrier()
        seq.label("c2")
        emit_c2p(_NcProxy(nc, "c2_"), seq, dict(
            xpd_in=H["c2_xpd"].ap(), xnb_in=H["c2_xnb"].ap(), bc_in=H["c2_bcr"].ap(),
            alog_in=H["c2_alog"].ap(), y_out=H["y2_loc"].ap()))
        seq.barrier()
        seq.label("ag_y2")
        _ag_step(seq, nc, H["y2_loc"], H["y2_all"], "agy2")
        loc_tok(H["y2_cs"], H["y2_all"], 256)
        seq.barrier()
        seq.label("c3")
        emit_ffn2(_NcProxy(nc, "c3_"), seq, dict(
            kind="c3", Hdim=5120,
            xfm_in=H["xr2"].ap(), xn_in=H["xn2"].ap(), y_in=H["y2_cs"].ap(),
            dv_in=H["c3_dv"].ap(), xo_out=H["xr3"].ap(), hscr=H["c3_hscr"],
            gate_src=_mk_wcol_src(H["l2wfc_all"], 640),
            down_srcs=_mk_wpr_srcs(H["l2wpr_all"], 5120)))
        if upto == "c":
            seq.dma("scalar", H["xr3_dbg"].ap(), H["xr3"].ap())

    if do_d:
        seq.barrier()
        seq.label("d1")
        emit_d1(_NcProxy(nc, "d1_"), seq, dict(
            xfm_in=H["xr3"].ap(), wq_in=H["l3wq_all"].ap(), wk_in=H["l3wk_all"].ap(),
            wv_in=H["l3wv_all"].ap(), cosr_in=H["d1_cosr"].ap(), sinr_in=H["d1_sinr"].ap(),
            o1_in=H["ones1"].ap(), xn_out=H["xn3"].ap(), q_out=H["q3_loc"].ap(),
            k_out=H["k3_loc"].ap(), v_out=H["v3_loc"].ap()))
        seq.barrier()
        seq.label("ag_d1")
        _ag_step(seq, nc, H["q3_loc"], H["q3_all"], "agq3", grouped=True)
        _ag_step(seq, nc, H["k3_loc"], H["k3_all"], "agk3", grouped=True)
        _ag_step(seq, nc, H["v3_loc"], H["v3_all"], "agv3", grouped=True)

        def dmas_d2(c):
            L = []
            for pi in range(4):
                h = (4 * c + pi) % 16
                hk = h // 4
                L.append((H["d2_q"].ap()[pi * T:(pi + 1) * T, :],
                          H["q3_all"].ap()[0:T, h * 128:(h + 1) * 128]))
                L.append((H["d2_k"].ap()[pi * T:(pi + 1) * T, :],
                          H["k3_all"].ap()[0:T, hk * 128:(hk + 1) * 128]))
                L.append((H["d2_v"].ap()[pi * T:(pi + 1) * T, :],
                          H["v3_all"].ap()[0:T, hk * 128:(hk + 1) * 128]))
            return L
        _sw_dmas(seq, "sync", 12, dmas_d2)
        seq.barrier()
        seq.label("d2")
        emit_d2p(_NcProxy(nc, "d2_"), seq, dict(
            q_in=H["d2_q"].ap(), k_in=H["d2_k"].ap(), v_in=H["d2_v"].ap(),
            o1_in=H["ones1"].ap(), o_out=H["o3_loc"].ap(),
            bscr=H["d2_bscr"]))
        seq.barrier()
        seq.label("ag_o3")
        _ag_step(seq, nc, H["o3_loc"], H["o3_all"], "ago3", grouped=True)

        def onr_fill_d(seq2, onr):
            o3a = H["o3_all"].ap()

            def dmas(c):
                jb = c % 4
                L = []
                for kk in range(16):
                    L.append((onr[:, kk * W:(kk + 1) * W],
                              o3a[kk, :, jb * W:(jb + 1) * W]))
                return L
            _sw_dmas(seq2, "sync", 16, dmas)
        seq.barrier()
        seq.label("d3")
        emit_ffn2(_NcProxy(nc, "d3_"), seq, dict(
            kind="d3", Hdim=16384,
            xfm_in=H["xr3"].ap(), xn_in=H["xn3"].ap(), attn_w_in=H["l3wo_all"].ap(),
            xo_out=H["xo_f"].ap(), hscr=H["d3_hscr"], onr_fill=onr_fill_d,
            gate_src=_mk_wcol_src(H["l3g_all"], 2048),
            up_src=_mk_wcol_src(H["l3u_all"], 2048),
            down_srcs=_mk_wpr_srcs(H["l3d_all"], 16384)))

    seq.finish()
    block = oes.enter_context(nc.Block())
    for en in ["sync", "vector", "scalar", "gpsimd", "tensor"]:
        getattr(block, en)(lambda e, en=en: seq.emit(en, e))
    oes.close()
    return nc


def full_host_inputs(inp):
    import math
    maps = p0_host_inputs(inp)
    x = np.asarray(inp["L0_attn_x"], np.float32).reshape(TOK, D)
    xT = x.T
    ones1 = np.ones((1, 128), np.float32)
    ipb = np.zeros((3 * D, 2), np.float32)
    ipb[:, 0] = np.asarray(inp["L0_attn_in_proj_b"], np.float32)
    shw_r = np.asarray(inp["L0_attn_short_w"], np.float32).reshape(3 * D, 3)
    shw = np.zeros((3 * D, 8), np.float32)
    shw[:, 0] = shw_r[:, 0]
    shw[:, 2] = shw_r[:, 1]
    shw[:, 4] = shw_r[:, 2]
    shb = np.zeros((3 * D, 8), np.float32)
    shb[:, 0] = np.asarray(inp["L0_attn_short_b"], np.float32)
    N1 = 128
    Wd = np.exp(-2j * np.pi * np.arange(N1)[:, None] * np.arange(N1)[None, :] / N1)
    TwF = np.exp(-2j * np.pi * np.outer(np.arange(N1), np.arange(N1)) / 16384.0)
    a2sh = dict(
        a2_wre=Wd.real.astype(bf16), a2_wim=Wd.imag.astype(bf16),
        a2_wimn=(-Wd.imag).astype(bf16),
        a2_wc64=np.concatenate([Wd.real[:64, :], Wd.imag[:64, :]], 1).astype(bf16),
        a2_wpd=np.concatenate([Wd.real, -Wd.imag], 1).astype(bf16),
        a2_wpd2=np.concatenate([Wd.imag, Wd.real], 1).astype(bf16),
        a2_twfr=TwF.real.astype(np.float32), a2_twfi=TwF.imag.astype(np.float32),
        a2_twir=TwF.real.astype(np.float32), a2_twii=(-TwF.imag).astype(np.float32))
    Lq = T
    t = np.linspace(0.0, 1.0, Lq)[:, None]
    t_resc = np.linspace(0.0, Lq - 1.0, Lq)
    wv = 2.0 * math.pi * t_resc / Lq
    f = np.linspace(0.0001, 0.0, 1)
    ang = f[None, :] * wv[:, None]
    z = np.concatenate([t, np.cos(ang), -np.sin(ang)], axis=-1)
    a2sh["a2_zT"] = np.ascontiguousarray(z.T).astype(np.float32)
    a2sh["a2_trow"] = np.linspace(0.0, 1.0, Lq).reshape(1, Lq).astype(np.float32)
    max_decay = math.log(0.01) / 0.3
    min_decay = math.log(0.01) / 1.5
    deltas = np.linspace(min_decay, max_decay, D)
    nds_full = np.zeros((D, 2), np.float32)
    nds_full[:, 0] = -np.abs(deltas)
    fb_full = np.zeros((D, 2), np.float32)
    fb_full[:, 0] = np.asarray(inp["L0_attn_filter_bias"], np.float32)
    fvec = np.zeros((64, 8), np.float32)
    fvec[:, 0] = np.asarray(inp["L0_attn_sin_freq"], np.float32).reshape(64)
    fvec[:, 2] = np.asarray(inp["L0_attn_mlp_b0"], np.float32)
    fvec[:, 4] = np.asarray(inp["L0_attn_mlp_b1"], np.float32)
    fvec[:, 6] = np.asarray(inp["L0_attn_mlp_b2"], np.float32)
    a2sh["a2_fvec"] = fvec
    a2sh["a2_w0T"] = np.ascontiguousarray(np.asarray(inp["L0_attn_mlp_w0"], np.float32).T)
    a2sh["a2_w1T"] = np.ascontiguousarray(np.asarray(inp["L0_attn_mlp_w1"], np.float32).T)
    a2sh["a2_w2T"] = np.ascontiguousarray(np.asarray(inp["L0_attn_mlp_w2"], np.float32).T)
    w3 = np.asarray(inp["L0_attn_mlp_w3"], np.float32)
    opb = np.zeros((D, 2), np.float32)
    opb[:, 0] = np.asarray(inp["L0_attn_out_proj_b"], np.float32)
    nw2 = np.zeros((D, 2), np.float32)
    nw2[:, 0] = np.asarray(inp["L1_attn_norm_w"], np.float32)
    utri = np.triu(np.ones((64, 64))).astype(np.float16)
    mtri = np.triu(np.ones((64, 64))).astype(bf16)
    alog = np.asarray(inp["L2_attn_A_log"], np.float32)
    dv2 = np.zeros((D, 2), np.float32)
    dv2[:, 0] = np.asarray(inp["L2_attn_D"], np.float32)
    HD = 128
    inv_freq = 1.0 / 10000.0 ** (np.arange(0, HD, 2) / HD)
    pos = np.arange(float(T))
    freqs = pos[:, None] * inv_freq[None, :]
    emb = np.concatenate([freqs, freqs], -1)
    cos_rep = np.tile(np.cos(emb).astype(bf16), (1, 16))
    sin_rep = np.tile(np.sin(emb).astype(bf16), (1, 16))
    for c in range(NCORES):
        m = maps[c]
        t0 = c * TSH
        xfm = np.zeros((D, TSH + 2), np.float32)
        xfm[:, 2:] = x[t0:t0 + TSH].T
        hs = np.zeros((128, 2), np.float32)
        if c % 4 != 0:
            xfm[:, 0:2] = x[t0 - 2:t0].T
            hs[:] = 1.0
        m["ones1"] = ones1
        m["a1_xfm"] = np.ascontiguousarray(xfm)
        m["a1_ipb"] = ipb
        m["a1_shw"] = shw
        m["a1_shb"] = shb
        m["a1_hsc"] = hs
        m["a_xres"] = np.ascontiguousarray(xT[:, t0:t0 + TSH])
        m.update(a2sh)
        rs = slice(c * 256, (c + 1) * 256)
        m["a2_w3T"] = np.ascontiguousarray(w3[rs].T)
        m["a2_nds"] = np.ascontiguousarray(nds_full[rs])
        m["a2_fbv"] = np.ascontiguousarray(fb_full[rs])
        m["a3_opb"] = opb
        m["b2_utri"] = utri
        m["b2_mtri"] = mtri
        m["b3_nw"] = nw2
        am = np.zeros((128, 64), np.float32)
        al = alog[rs]
        for ct in range(2):
            am[:, ct * 32:ct * 32 + 32:2] = al[ct * 128:(ct + 1) * 128]
        m["c2_alog"] = am
        m["c3_dv"] = dv2
        pos0 = t0 % T
        m["d1_cosr"] = np.ascontiguousarray(cos_rep[pos0:pos0 + TSH])
        m["d1_sinr"] = np.ascontiguousarray(sin_rep[pos0:pos0 + TSH])
    return maps


def kernel(**inp):
    maps = full_host_inputs(inp)
    res = run_bass_kernel_spmd(build_full("d"), maps, core_ids=list(range(NCORES)))
    x4 = np.concatenate([r["xo_f"] for r in res.results], 1)
    return np.ascontiguousarray(x4.T).reshape(B, T, D).astype(np.float32)
